# version 02
# baseline (speedup 1.0000x reference)
import numpy as np
import concourse.bass as bass
import concourse.mybir as mybir
from concourse.bass_utils import run_bass_kernel_spmd
from contextlib import ExitStack

F32 = mybir.dt.float32
BF16 = mybir.dt.bfloat16
AF = mybir.ActivationFunctionType
ALU = mybir.AluOpType
AX = mybir.AxisListType

NCORES = 8
NTOK = 16384
TCORE = NTOK // NCORES
DM = 1024
MMF = 512

CONST_NAMES = ["eps1e5", "eps1e6", "one", "neg1", "half", "eps1e12", "neg30"]
CONST_VALS = {"eps1e5": 1e-5, "eps1e6": 1e-6, "one": 1.0, "neg1": -1.0,
              "half": 0.5, "eps1e12": 1e-12, "neg30": -1e30}


def const_array():
    v = np.array([CONST_VALS[n] for n in CONST_NAMES], np.float32)
    return np.broadcast_to(v, (128, len(CONST_NAMES))).copy()


def ceil_div(a, b):
    return (a + b - 1) // b



class Em:
    ENGINES = ("tensor", "vector", "scalar", "gpsimd", "sync")

    def __init__(self, nc, stack):
        self.nc = nc
        self.stack = stack
        self.q = {e: [] for e in self.ENGINES}
        self.g = stack.enter_context(nc.semaphore("g"))
        self.gcnt = 0
        self.last_eng = None

    def do(self, eng, fn, dma=False):
        pre = self.gcnt
        h = self.g
        if pre > 0:
            self.q[eng].append(lambda e, h=h, v=pre: e.wait_ge(h, v))
        amt = 16 if dma else 1
        def run(e, fn=fn, h=h, amt=amt):
            fn(e).then_inc(h, amt)
        self.q[eng].append(run)
        self.gcnt += amt
        self.last_eng = eng
        return self.gcnt

    def dma(self, eng, out, in_):
        return self.do(eng, lambda e, o=out, i=in_: e.dma_start(out=o, in_=i),
                       dma=True)

    def build(self):
        with self.nc.Block() as blk:
            for eng in self.ENGINES:
                ops = self.q[eng]
                if not ops:
                    continue
                def runner(e, ops=ops):
                    for f in ops:
                        f(e)
                getattr(blk, eng)(runner)
        self.q = {e: [] for e in self.ENGINES}


class Rot:
    def __init__(self, tiles):
        self.tiles = tiles
        self.j = 0

    def next(self):
        t = self.tiles[self.j % len(self.tiles)]
        self.j += 1
        return t


class Prog:
    def __init__(self, name):
        self.nc = bass.Bass("TRN2", target_bir_lowering=False)
        self.stack = ExitStack()
        self.E = Em(self.nc, self.stack)
        self.name = name
        self._holders = []

    def dram_in(self, name, shape, dtype=F32):
        return self.nc.dram_tensor(name, list(shape), dtype,
                                   kind="ExternalInput")

    def dram_out(self, name, shape, dtype=F32):
        return self.nc.dram_tensor(name, list(shape), dtype,
                                   kind="ExternalOutput")

    def dram_tmp(self, name, shape, dtype=F32):
        return self.nc.dram_tensor(name, list(shape), dtype)

    def sbuf(self, name, shape, dtype=F32):
        return self.stack.enter_context(
            self.nc.sbuf_tensor(name, list(shape), dtype))

    def psum(self, name, shape, dtype=F32):
        return self.stack.enter_context(
            self.nc.psum_tensor(name, list(shape), dtype))

    def psum_pool(self, n=8, w=MMF, name="pp"):
        return Rot([self.psum(f"{name}{i}", [128, w]) for i in range(n)])

    def sbuf_pool(self, n, shape, dtype=F32, name="sp"):
        return Rot([self.sbuf(f"{name}{i}", list(shape), dtype)
                    for i in range(n)])

    def load_basics(self, eye=False):
        E = self.E
        cd = self.dram_in("consts", [128, len(CONST_NAMES)])
        self.constsb = self.sbuf("constsb", [128, len(CONST_NAMES)])
        E.dma("sync", self.constsb[:, :], cd.ap())
        od = self.dram_in("onescol", [128, 1])
        self.ones = self.sbuf("onessb", [128, 1])
        E.dma("sync", self.ones[:, :], od.ap())
        orow = self.dram_in("onesrow", [1, 128])
        self.ones_row = self.sbuf("onesrowsb", [1, 128])
        E.dma("sync", self.ones_row[:, :], orow.ap())
        self.eye = None
        if eye:
            ed = self.dram_in("eye128", [128, 128])
            self.eye = self.sbuf("eyesb", [128, 128])
            E.dma("sync", self.eye[:, :], ed.ap())

    def ccol(self, name):
        i = CONST_NAMES.index(name)
        return self.constsb[:, i:i + 1]

    def run(self, in_maps, **kw):
        return run_bass_kernel_spmd(self.nc, in_maps,
                                    core_ids=list(range(len(in_maps))), **kw)

    def finish(self, out_dma_engine="sync"):
        E = self.E
        h = E.g
        v = E.gcnt
        E.q[out_dma_engine].append(lambda e, h=h, v=v: e.wait_ge(h, v))
        E.build()
        self.stack.close()

    @staticmethod
    def basic_inputs(eye=False):
        d = {"consts": const_array(),
             "onescol": np.ones((128, 1), np.float32),
             "onesrow": np.ones((1, 128), np.float32)}
        if eye:
            d["eye128"] = np.eye(128, dtype=np.float32)
        return d



def load_fm(P, name, src_ap, d, T, dtype=F32, pool=None):
    KT = ceil_div(d, 128)
    tiles = []
    for k in range(KT):
        kw = min(128, d - k * 128)
        t = P.sbuf(f"{name}{k}", [128, T], dtype) if pool is None else pool[k]
        P.E.dma("sync", t[0:kw, 0:T], src_ap[k * 128:k * 128 + kw, 0:T])
        tiles.append(t)
    return tiles


def proj_fm(P, pp, wpool, xtiles, wt_dram, d_in, d_out, T, evac, xcol0=0,
            w_row0=0, w_col0=0):
    E = P.E
    KT = ceil_div(d_in, 128)
    MT = ceil_div(d_out, 128)
    CT = ceil_div(T, MMF)
    for m in range(MT):
        mw = min(128, d_out - m * 128)
        banks = [pp.next() for _ in range(CT)]
        for k in range(KT):
            kw = min(128, d_in - k * 128)
            wtile = wpool.next()
            E.dma("sync", wtile[0:kw, 0:mw],
                  wt_dram[w_row0 + k * 128:w_row0 + k * 128 + kw,
                          w_col0 + m * 128:w_col0 + m * 128 + mw])
            for c in range(CT):
                cw = min(MMF, T - c * MMF)
                E.do("tensor",
                     lambda e, o=banks[c][0:mw, 0:cw], w=wtile[0:kw, 0:mw],
                            x=xtiles[k][0:kw, xcol0 + c * MMF:
                                        xcol0 + c * MMF + cw],
                            st=(k == 0), sp=(k == KT - 1):
                         e.matmul(o, w, x, start=st, stop=sp))
        for c in range(CT):
            cw = min(MMF, T - c * MMF)
            evac(m, c, banks[c][0:mw, 0:cw])


def ln_rows(P, pp, xtiles, T, d, scr, ab, rms=False, eps_name="eps1e5",
            xcol0=0):
    brow, arow = ab
    E = P.E
    KT = d // 128
    CT = ceil_div(T, MMF)
    for c in range(CT):
        cw = min(MMF, T - c * MMF)
        bank_s = pp.next()
        bank_q = pp.next()
        if not rms:
            for k in range(KT):
                E.do("tensor",
                     lambda e, o=bank_s[0:1, 0:cw], w=P.ones[:, 0:1],
                            x=xtiles[k][:, xcol0 + c * MMF:xcol0 + c * MMF + cw],
                            st=(k == 0):
                         e.matmul(o, w, x, start=st, stop=(k == KT - 1)))
        for k in range(KT):
            E.do("scalar",
                 lambda e, o=scr[:, 0:cw],
                        x=xtiles[k][:, xcol0 + c * MMF:xcol0 + c * MMF + cw]:
                     e.activation(o, x, AF.Square))
            E.do("tensor",
                 lambda e, o=bank_q[0:1, 0:cw], w=P.ones[:, 0:1],
                        x=scr[:, 0:cw], st=(k == 0):
                     e.matmul(o, w, x, start=st, stop=(k == KT - 1)))
        if not rms:
            E.do("vector",
                 lambda e, o=brow[0:1, c * MMF:c * MMF + cw], i=bank_s[0:1, 0:cw]:
                     e.tensor_scalar(o, i, 1.0 / d, None, ALU.mult))
        E.do("vector",
             lambda e, o=arow[0:1, c * MMF:c * MMF + cw], i=bank_q[0:1, 0:cw]:
                 e.tensor_scalar(o, i, 1.0 / d, None, ALU.mult))
    if not rms:
        for c in range(CT):
            cw = min(MMF, T - c * MMF)
            E.do("vector",
                 lambda e, o=scr[0:1, 0:cw], a=brow[0:1, c * MMF:c * MMF + cw]:
                     e.tensor_mul(o, a, a))
            E.do("vector",
                 lambda e, o=arow[0:1, c * MMF:c * MMF + cw],
                        a=arow[0:1, c * MMF:c * MMF + cw], b=scr[0:1, 0:cw]:
                     e.tensor_sub(o, a, b))
    E.do("scalar", lambda e: e.activation(
        arow[0:1, 0:T], arow[0:1, 0:T], AF.Ln, bias=P.ccol(eps_name)[0:1, :]))
    E.do("scalar", lambda e: e.activation(
        arow[0:1, 0:T], arow[0:1, 0:T], AF.Exp, scale=-0.5))
    if not rms:
        E.do("vector", lambda e: e.tensor_mul(
            brow[0:1, 0:T], brow[0:1, 0:T], arow[0:1, 0:T]))
        E.do("vector", lambda e: e.tensor_scalar(
            brow[0:1, 0:T], brow[0:1, 0:T], -1.0, None, ALU.mult))


def apply_ab(P, pp, xtiles, out_tiles, ab, T, d, rms=False, out_dtype=None):
    brow, arow = ab
    E = P.E
    KT = d // 128
    CT = ceil_div(T, MMF)
    for c in range(CT):
        cw = min(MMF, T - c * MMF)
        ba = pp.next()
        E.do("tensor", lambda e, o=ba[:, 0:cw],
             r=arow[0:1, c * MMF:c * MMF + cw]:
                 e.matmul(o, P.ones_row[0:1, :], r, start=True, stop=True))
        if not rms:
            bb = pp.next()
            E.do("tensor", lambda e, o=bb[:, 0:cw],
                 r=brow[0:1, c * MMF:c * MMF + cw]:
                     e.matmul(o, P.ones_row[0:1, :], r, start=True, stop=True))
        for k in range(KT):
            E.do("vector",
                 lambda e, o=out_tiles[k][:, c * MMF:c * MMF + cw],
                        x=xtiles[k][:, c * MMF:c * MMF + cw], a=ba[:, 0:cw]:
                     e.tensor_mul(o, x, a))
            if not rms:
                E.do("vector",
                     lambda e, o=out_tiles[k][:, c * MMF:c * MMF + cw],
                            b=bb[:, 0:cw]:
                         e.tensor_add(o, o, b))


def softplus_cols(P, rows_tile, p, T, tmp, bias_col=None):
    E = P.E
    kw = {} if bias_col is None else {"bias": bias_col[0:p, :]}
    E.do("scalar", lambda e: e.activation(
        tmp[0:p, 0:T], rows_tile[0:p, 0:T], AF.Abs, **kw))
    E.do("scalar", lambda e: e.activation(
        tmp[0:p, 0:T], tmp[0:p, 0:T], AF.Exp, scale=-1.0))
    E.do("scalar", lambda e: e.activation(
        tmp[0:p, 0:T], tmp[0:p, 0:T], AF.Ln, bias=P.ccol("one")[0:p, :]))
    E.do("scalar", lambda e: e.activation(
        rows_tile[0:p, 0:T], rows_tile[0:p, 0:T], AF.Relu, **kw))
    E.do("vector", lambda e: e.tensor_add(
        rows_tile[0:p, 0:T], rows_tile[0:p, 0:T], tmp[0:p, 0:T]))



L0_DIN = 1024
L0_DIN_PROJ = 4384
L0_DINNER = 2048
L0_CONVD = 2304
L0_NH = 32
L0_HD = 64
L0_DS = 128


def build_l0a():
    P = Prog("l0a")
    E = P.E
    T = TCORE
    TH = T + 3
    x_d = P.dram_in("x", [DM, TH])
    wt_d = P.dram_in("wt_in", [L0_DIN, L0_DIN_PROJ])
    convw_d = P.dram_in("conv_w", [L0_CONVD, 4])
    convb_d = P.dram_in("conv_b", [L0_CONVD, 1])
    dtb_d = P.dram_in("dt_bias", [32, 1])
    xn_d = P.dram_out("xn", [DM, T])
    z_d = P.dram_out("z", [L0_DINNER, T])
    xssm_d = P.dram_out("xssm", [L0_DINNER, T])
    b_d = P.dram_out("bmat", [L0_DS, T])
    c_d = P.dram_out("cmat", [L0_DS, T])
    dtp_d = P.dram_out("dtp", [32, T])

    P.load_basics()
    pp = P.psum_pool()
    xt = [P.sbuf(f"x{k}", [128, TH]) for k in range(8)]
    scr = P.sbuf("scr", [128, MMF])
    ab = (P.sbuf("abb", [1, TH]), P.sbuf("aba", [1, TH]))
    wpool = P.sbuf_pool(2, [128, 128], name="wp")
    convw = P.sbuf("convw", [128, 4 * 18])
    convb = P.sbuf("convb", [128, 18])
    dtb = P.sbuf("dtb", [32, 1])
    xbc_buf = P.sbuf("xbcbuf", [128, TH])
    xcv_buf = P.sbuf("xcvbuf", [128, T])
    zstage = P.sbuf("zstage", [128, TH])
    dt_full = P.sbuf("dtfull", [32, TH])
    dtt = P.sbuf("dtt", [32, T])

    for k in range(8):
        E.dma("sync", xt[k][:, :], x_d[k * 128:(k + 1) * 128, :])
    for k in range(18):
        E.dma("sync", convw[:, 4 * k:4 * k + 4],
              convw_d[k * 128:(k + 1) * 128, :])
        E.dma("sync", convb[:, k:k + 1], convb_d[k * 128:(k + 1) * 128, :])
    E.dma("sync", dtb[:, :], dtb_d.ap())

    ln_rows(P, pp, xt, TH, DM, scr, ab)
    apply_ab(P, pp, xt, xt, ab, TH, DM)
    for k in range(8):
        E.dma("sync", xn_d[k * 128:(k + 1) * 128, :], xt[k][:, 3:TH])

    def conv_tile(kk):
        for j in range(4):
            if j == 0:
                E.do("vector",
                     lambda e, o=xcv_buf[:, 0:T], x=xbc_buf[:, 0:T],
                            w=convw[:, 4 * kk:4 * kk + 1]:
                         e.tensor_scalar(o, x, w, None, ALU.mult))
            else:
                E.do("vector",
                     lambda e, o=xcv_buf[:, 0:T], x=xbc_buf[:, j:j + T],
                            w=convw[:, 4 * kk + j:4 * kk + j + 1]:
                         e.scalar_tensor_tensor(o, x, w, xcv_buf[:, 0:T],
                                                ALU.mult, ALU.add))
        E.do("scalar",
             lambda e, o=xcv_buf[:, 0:T], x=xcv_buf[:, 0:T],
                    b=convb[:, kk:kk + 1]:
                 e.activation(o, x, AF.Silu, bias=b))
        if kk < 16:
            E.dma("sync", xssm_d[kk * 128:(kk + 1) * 128, :], xcv_buf[:, 0:T])
        elif kk == 16:
            E.dma("sync", b_d.ap(), xcv_buf[:, 0:T])
        else:
            E.dma("sync", c_d.ap(), xcv_buf[:, 0:T])

    CT = ceil_div(TH, MMF)

    def evac2(m, c, ps):
        cw = ps.shape[1]
        mw = ps.shape[0]
        base = m * 128
        if base < 2048:
            E.do("vector", lambda e, o=zstage[0:mw, c * MMF:c * MMF + cw],
                 p=ps: e.tensor_copy(o, p))
            if c == CT - 1:
                E.dma("sync", z_d[base:base + 128, :], zstage[:, 3:TH])
        elif base < 2048 + 2304:
            E.do("vector", lambda e, o=xbc_buf[0:mw, c * MMF:c * MMF + cw],
                 p=ps: e.tensor_copy(o, p))
            if c == CT - 1:
                conv_tile((base - 2048) // 128)
        else:
            E.do("vector", lambda e, o=dt_full[0:mw, c * MMF:c * MMF + cw],
                 p=ps: e.tensor_copy(o, p))

    proj_fm(P, pp, wpool, xt, wt_d.ap(), L0_DIN, L0_DIN_PROJ, TH, evac2)

    E.do("vector", lambda e: e.tensor_copy(dtt[0:32, 0:T],
                                           dt_full[0:32, 3:TH]))
    softplus_cols(P, dtt, 32, T, dt_full, bias_col=dtb)
    E.dma("sync", dtp_d.ap(), dtt[0:32, 0:T])

    P.finish()
    return P


def host_l0a(x_fm, Win, conv_w, conv_b, dt_bias):
    P = build_l0a()
    wt_in = np.ascontiguousarray(Win.T)
    cw = np.ascontiguousarray(conv_w)
    cb = np.ascontiguousarray(conv_b.reshape(-1, 1))
    db = np.ascontiguousarray(dt_bias.reshape(-1, 1))
    base = Prog.basic_inputs()
    in_maps = []
    for c in range(NCORES):
        lo = c * TCORE
        xh = np.zeros((DM, TCORE + 3), np.float32)
        xh[:, 3:] = x_fm[:, lo:lo + TCORE]
        if c % 2 == 1:
            xh[:, 0:3] = x_fm[:, lo - 3:lo]
        in_maps.append(dict(x=xh, wt_in=wt_in, conv_w=cw, conv_b=cb,
                            dt_bias=db, **base))
    res = P.run(in_maps)
    out = {}
    for name in ("xn", "z", "xssm", "bmat", "cmat", "dtp"):
        out[name] = np.concatenate([r[name] for r in res.results], axis=1)
    return out



def kernel(**inputs):
    raise NotImplementedError("under construction")



L0B_DEBUG = False

def build_l0b():
    TT_ = 128
    NC = 4096 // TT_
    NHC = 16
    P = Prog("l0b")
    E = P.E
    xssm_d = P.dram_in("xssm", [1024, 4096])
    dtp_d = P.dram_in("dtp", [16, 4096])
    dtpT_d = P.dram_in("dtpT", [4096, 16])
    b_d = P.dram_in("bmat", [128, 4096])
    c_d = P.dram_in("cmat", [128, 4096])
    alog_d = P.dram_in("a_log", [16, 1])
    drow_d = P.dram_in("d_row", [1, 16])
    tmask_d = P.dram_in("tmaskc", [16, 4096])
    ipat_d = P.dram_in("ipatc", [128, 4096])
    hsel_d = P.dram_in("hselc", [16, 16 * 128])
    y_d = P.dram_out("y_tm", [4096, 1024])
    if L0B_DEBUG:
        dbg_lunw = P.dram_out("dbg_lunw", [128, 4096])
        dbg_d2r = P.dram_out("dbg_d2r", [16, 4096])
        dbg_data0 = P.dram_out("dbg_data0", [128, 4096])
        dbg_mt = P.dram_out("dbg_mt", [128, 128])
        dbg_xst = P.dram_out("dbg_xst", [128, 64])
        dbg_yi = P.dram_out("dbg_yi", [128, 64])
        dbg_v2 = P.dram_out("dbg_v2", [128, 64])
        dbg_yst = P.dram_out("dbg_yst", [128, 64])

    P.load_basics(eye=True)
    pp = P.psum_pool()
    dtp = P.sbuf("dtp_sb", [16, 4096])
    dA = P.sbuf("dA", [16, 4096])
    dAm = P.sbuf("dAm", [16, 4096])
    d2r = P.sbuf("d2r", [16, 4096])
    tmask = P.sbuf("tmask", [16, 4096])
    hsel = P.sbuf("hsel", [16, 16 * 128])
    acol = P.sbuf("acol", [16, 1])
    bsb = P.sbuf("bsb", [128, 4096])
    csb = P.sbuf("csb", [128, 4096])
    dtpT = P.sbuf("dtpT_sb", [128, NC * 16])
    d2T = P.sbuf("d2T", [128, NC * 16])
    dAcols = P.sbuf("dAcols", [128, NHC * NC])
    dall = P.sbuf("dall", [128, 16])
    xs_h = P.sbuf("xs_h", [64, 4096])
    lunw = P.sbuf("lunw", [128, 4096])
    ipat = P.sbuf("ipat", [128, 4096])
    data0 = P.sbuf("data0", [128, 4096])
    mt = P.sbuf("mt", [128, 128])
    xst = P.sbuf("xst", [128, 64])
    bt = P.sbuf("bt", [128, 128])
    bwt = P.sbuf("bwt", [128, 128])
    v1sb = P.sbuf("v1sb", [128, 64])
    v2sb = P.sbuf("v2sb", [128, 64])
    ystage = P.sbuf("ystage", [128, NC * 64])
    hstate = [P.sbuf(f"hst{h}", [128, 64]) for h in range(NHC)]

    E.dma("sync", dtp[:, :], dtp_d.ap())
    E.dma("sync", tmask[:, :], tmask_d.ap())
    E.dma("sync", ipat[:, :], ipat_d.ap())
    E.dma("sync", hsel[:, :], hsel_d.ap())
    E.dma("sync", acol[:, :], alog_d.ap())
    E.dma("sync", bsb[:, :], b_d.ap())
    E.dma("sync", csb[:, :], c_d.ap())
    E.dma("sync", dtpT[:, :],
          bass.AP(dtpT_d, 0, [[16, 128], [128 * 16, NC], [1, 16]]))

    E.do("scalar", lambda e: e.activation(acol[:, :], acol[:, :], AF.Exp))
    E.do("vector", lambda e: e.tensor_scalar(acol[:, :], acol[:, :], -1.0,
                                             None, ALU.mult))
    E.do("scalar", lambda e: e.activation(dA[:, :], dtp[:, :], AF.Exp,
                                          scale=acol[0:16, :]))
    E.do("vector", lambda e: e.tensor_mul(dAm[:, :], dA[:, :], tmask[:, :]))
    E.do("vector", lambda e: e.tensor_sub(d2r[:, :], dA[:, :], dAm[:, :]))
    E.do("vector", lambda e: e.tensor_tensor_scan(
        d2r[:, :], dAm[:, :], d2r[:, :], 0.0, ALU.mult, ALU.add))
    for c in range(NC):
        ps = pp.next()
        E.do("tensor", lambda e, c=c, o=ps[0:128, 0:16]:
             e.transpose(o, d2r[0:16, c * 128:(c + 1) * 128], P.eye[0:16, 0:16]))
        E.do("vector", lambda e, c=c, i=ps[0:128, 0:16]:
             e.tensor_copy(d2T[:, c * 16:(c + 1) * 16], i))
    dac = P.sbuf("dac", [16, NC])
    E.do("vector", lambda e: e.tensor_copy(
        dac[:, :], bass.AP(d2r, 127, [[4096, 16], [128, NC]])))
    for h in range(NHC):
        ps = pp.next()
        E.do("tensor", lambda e, h=h, o=ps[0:128, 0:NC]:
             e.matmul(o, hsel[:, h * 128:(h + 1) * 128], dac[0:16, :],
                      start=True, stop=True))
        E.do("vector", lambda e, h=h, i=ps[0:128, 0:NC]:
             e.tensor_copy(dAcols[:, h * NC:(h + 1) * NC], i))
    drow = P.sbuf("drow", [1, 16])
    E.dma("sync", drow[:, :], drow_d.ap())
    psd = pp.next()
    E.do("tensor", lambda e: e.matmul(psd[0:128, 0:16], P.ones_row[0:1, :],
                                      drow[0:1, :], start=True, stop=True))
    E.do("vector", lambda e: e.tensor_copy(dall[:, :], psd[0:128, 0:16]))

    for h in range(NHC):
        E.do("vector", lambda e, h=h: e.memset(hstate[h][:, :], 0.0))
    for h in range(NHC):
        E.dma("sync", xs_h[:, :], xssm_d[h * 64:(h + 1) * 64, :])
        for j in range(8):
            ps = pp.next()
            E.do("tensor", lambda e, h=h, j=j, o=ps[:, 0:512]:
                 e.matmul(o, hsel[:, h * 128:(h + 1) * 128],
                          dAm[0:16, j * 512:(j + 1) * 512],
                          start=True, stop=True))
            E.do("scalar", lambda e, j=j, i=ps[:, 0:512]:
                 e.activation(data0[:, j * 512:(j + 1) * 512], i, AF.Identity))
        E.do("vector", lambda e: e.tensor_tensor_scan(
            lunw[:, :], data0[:, :], ipat[:, :], 0.0, ALU.mult, ALU.add))
        if L0B_DEBUG and h == 0:
            E.dma("sync", dbg_lunw.ap(), lunw[:, :])
            E.dma("sync", dbg_d2r.ap(), d2r[:, :])
            E.dma("sync", dbg_data0.ap(), data0[:, :])
        for c in range(NC):
            t0 = c * TT_
            pst = pp.next()
            E.do("tensor", lambda e, o=pst[0:128, 0:64], t0=t0:
                 e.transpose(o, xs_h[0:64, t0:t0 + 128], P.eye[0:64, 0:64]))
            E.do("vector", lambda e, i=pst[0:128, 0:64]:
                 e.tensor_copy(xst[:, :], i))
            bank_s = pp.next()
            E.do("tensor", lambda e, o=bank_s[0:128, 0:64], t0=t0, h=h:
                 e.matmul(o, csb[:, t0:t0 + 128], hstate[h][:, :],
                          start=True, stop=True))
            gt = pp.next()
            E.do("tensor", lambda e, o=gt[0:128, 0:128], t0=t0:
                 e.matmul(o, bsb[:, t0:t0 + 128], csb[:, t0:t0 + 128],
                          start=True, stop=True))
            E.do("vector", lambda e, t0=t0, c=c, h=h, g=gt[0:128, 0:128]:
                 e.scalar_tensor_tensor(
                     mt[:, :], lunw[:, t0:t0 + 128],
                     dtpT[:, c * 16 + h:c * 16 + h + 1], g,
                     ALU.mult, ALU.mult))
            if L0B_DEBUG and h == 0 and c == 1:
                E.dma("sync", dbg_mt.ap(), mt[:, :])
                E.dma("sync", dbg_xst.ap(), xst[:, :])
            bank_i = pp.next()
            E.do("tensor", lambda e, o=bank_i[0:128, 0:64]:
                 e.matmul(o, mt[:, :], xst[:, :], start=True, stop=True))
            E.do("vector", lambda e, c=c, h=h, i=bank_s[0:128, 0:64]:
                 e.tensor_scalar(v1sb[:, :], i,
                                 d2T[:, c * 16 + h:c * 16 + h + 1], None,
                                 ALU.mult))
            E.do("vector", lambda e, h=h:
                 e.scalar_tensor_tensor(v2sb[:, :], xst[:, :],
                                        dall[:, h:h + 1], v1sb[:, :],
                                        ALU.mult, ALU.add))
            E.do("vector", lambda e, c=c, i=bank_i[0:128, 0:64]:
                 e.tensor_add(ystage[:, c * 64:(c + 1) * 64], v2sb[:, :], i))
            if L0B_DEBUG and h == 0 and c == 1:
                ycopy = P.sbuf("ycopy", [128, 64]) if not hasattr(P, "_yc") else P._yc
                P._yc = ycopy
                E.do("vector", lambda e, i=bank_i[0:128, 0:64]:
                     e.tensor_copy(ycopy[:, :], i))
                E.dma("sync", dbg_yi.ap(), ycopy[:, :])
                E.dma("sync", dbg_v2.ap(), v2sb[:, :])
                E.dma("sync", dbg_yst.ap(), ystage[:, 64:128])
            psb = pp.next()
            E.do("tensor", lambda e, o=psb[0:128, 0:128], t0=t0:
                 e.transpose(o, bsb[:, t0:t0 + 128], P.eye[:, :]))
            E.do("vector", lambda e, i=psb[0:128, 0:128]:
                 e.tensor_copy(bt[:, :], i))
            E.do("vector", lambda e, t0=t0, c=c, h=h:
                 e.tensor_scalar(bwt[:, :], bt[:, :],
                                 lunw[:, t0 + 127:t0 + 128],
                                 dtpT[:, c * 16 + h:c * 16 + h + 1],
                                 ALU.mult, ALU.mult))
            pinc = pp.next()
            E.do("tensor", lambda e, o=pinc[0:128, 0:64]:
                 e.matmul(o, bwt[:, :], xst[:, :], start=True, stop=True))
            E.do("vector", lambda e, h=h, c=c, i=pinc[0:128, 0:64]:
                 e.scalar_tensor_tensor(
                     hstate[h][:, :], hstate[h][:, :],
                     dAcols[:, h * NC + c:h * NC + c + 1], i,
                     ALU.mult, ALU.add))
        E.dma("sync",
              bass.AP(y_d, h * 64, [[1024, 128], [1024 * 128, NC], [1, 64]]),
              bass.AP(ystage, 0, [[NC * 64, 128], [64, NC], [1, 64]]))
    P.finish()
    return P


def host_l0b(xssm_fm, dtp_fm, bmat_fm, cmat_fm, A_log, D):
    P = build_l0b()
    base = Prog.basic_inputs(eye=True)
    tmask = np.ones((16, 4096), np.float32)
    tmask[:, ::128] = 0.0
    ipat_host = np.tile(np.eye(128, dtype=np.float32), (1, 32))
    hsel = np.zeros((16, 16 * 128), np.float32)
    for h in range(16):
        hsel[h, h * 128:(h + 1) * 128] = 1.0
    in_maps = []
    for c in range(NCORES):
        beta, hg = c // 2, c % 2
        tok = slice(beta * 4096, (beta + 1) * 4096)
        hrows = slice(hg * 1024, (hg + 1) * 1024)
        hh = slice(hg * 16, (hg + 1) * 16)
        dtp_sl = np.ascontiguousarray(dtp_fm[hh, tok])
        in_maps.append(dict(
            xssm=np.ascontiguousarray(xssm_fm[hrows, tok]),
            dtp=dtp_sl,
            dtpT=np.ascontiguousarray(dtp_sl.T),
            bmat=np.ascontiguousarray(bmat_fm[:, tok]),
            cmat=np.ascontiguousarray(cmat_fm[:, tok]),
            a_log=np.ascontiguousarray(A_log[hh].reshape(-1, 1)),
            d_row=np.ascontiguousarray(D[hh].reshape(1, -1)),
            tmaskc=tmask, hselc=hsel, ipatc=ipat_host, **base))
    res = P.run(in_maps)
    y = np.zeros((2048, NTOK), np.float32)
    for c in range(NCORES):
        beta, hg = c // 2, c % 2
        y[hg * 1024:(hg + 1) * 1024, beta * 4096:(beta + 1) * 4096] = \
            res.results[c]["y_tm"].T
    return y


def proj_fm_wide(P, pp, wpool, xtiles, wt_dram, d_in, d_out, T, evac,
                 xcol0=0):
    E = P.E
    KT = ceil_div(d_in, 128)
    MG = ceil_div(d_out, MMF)
    CT = ceil_div(T, MMF)
    assert 4 * CT <= 8, "proj_fm_wide: too many live psum banks"
    for mg in range(MG):
        gw = min(MMF, d_out - mg * MMF)
        nm = ceil_div(gw, 128)
        banks = [pp.next() for _ in range(nm * CT)]
        for k in range(KT):
            kw = min(128, d_in - k * 128)
            wtile = wpool.next()
            E.dma("sync", wtile[0:kw, 0:gw],
                  wt_dram[k * 128:k * 128 + kw, mg * MMF:mg * MMF + gw])
            for mi in range(nm):
                mw = min(128, gw - mi * 128)
                for c in range(CT):
                    cw = min(MMF, T - c * MMF)
                    E.do("tensor",
                         lambda e, o=banks[mi * CT + c][0:mw, 0:cw],
                                w=wtile[0:kw, mi * 128:mi * 128 + mw],
                                x=xtiles[k][0:kw, xcol0 + c * MMF:
                                            xcol0 + c * MMF + cw],
                                st=(k == 0), sp=(k == KT - 1):
                             e.matmul(o, w, x, start=st, stop=sp))
        for mi in range(nm):
            mw = min(128, gw - mi * 128)
            for c in range(CT):
                cw = min(MMF, T - c * MMF)
                evac(mg * 4 + mi, c, banks[mi * CT + c][0:mw, 0:cw])



def build_l0c1():
    T = TCORE
    P = Prog("l0c1")
    E = P.E
    y_d = P.dram_in("y", [2048, T])
    z_d = P.dram_in("z", [2048, T])
    nw_d = P.dram_in("norm_w", [2048, 1])
    wt_d = P.dram_in("wt_out", [2048, 1024])
    s_d = P.dram_out("s", [1024, T])
    P.load_basics()
    pp = P.psum_pool()
    yt = [P.sbuf(f"y{k}", [128, T]) for k in range(16)]
    zbuf = P.sbuf("zbuf", [128, T])
    scr = P.sbuf("scr", [128, MMF])
    ab = (P.sbuf("abb", [1, T]), P.sbuf("aba", [1, T]))
    nw = P.sbuf("nw", [128, 16])
    wpool = P.sbuf_pool(2, [128, MMF], name="wp")
    sstage = P.sbuf("sstage", [128, T])
    for k in range(16):
        E.dma("sync", yt[k][:, :], y_d[k * 128:(k + 1) * 128, :])
    for k in range(16):
        E.dma("sync", nw[:, k:k + 1], nw_d[k * 128:(k + 1) * 128, :])
    ln_rows(P, pp, yt, T, 2048, scr, ab, rms=True)
    CT = ceil_div(T, MMF)
    for c in range(CT):
        cw = min(MMF, T - c * MMF)
        ba = pp.next()
        E.do("tensor", lambda e, o=ba[:, 0:cw],
             r=ab[1][0:1, c * MMF:c * MMF + cw]:
                 e.matmul(o, P.ones_row[0:1, :], r, start=True, stop=True))
        for k in range(16):
            E.do("vector",
                 lambda e, k=k, o=yt[k][:, c * MMF:c * MMF + cw],
                        a=ba[:, 0:cw]:
                     e.tensor_mul(o, o, a))
    for k in range(16):
        E.dma("sync", zbuf[:, :], z_d[k * 128:(k + 1) * 128, :])
        E.do("scalar", lambda e: e.activation(zbuf[:, :], zbuf[:, :], AF.Silu))
        E.do("vector", lambda e, k=k: e.tensor_scalar(
            yt[k][:, :], yt[k][:, :], nw[:, k:k + 1], None, ALU.mult))
        E.do("vector", lambda e, k=k: e.tensor_mul(
            yt[k][:, :], yt[k][:, :], zbuf[:, :]))

    def evac(m, c, ps):
        E.do("vector", lambda e, o=sstage[0:ps.shape[0],
                                          c * MMF:c * MMF + ps.shape[1]],
             p=ps: e.tensor_copy(o, p))
        if c == ceil_div(T, MMF) - 1:
            E.dma("sync", s_d[m * 128:m * 128 + 128, :], sstage[:, 0:T])

    proj_fm(P, pp, wpool, yt, wt_d.ap(), 2048, 1024, T, evac)
    P.finish()
    return P


def build_l0c2():
    T = TCORE
    TS = 512
    NS = T // TS
    P = Prog("l0c2")
    E = P.E
    xn_d = P.dram_in("xn", [1024, T])
    x_d = P.dram_in("x", [1024, T])
    s_d = P.dram_in("s", [1024, T])
    w1_d = P.dram_in("w1t", [1024, 4096])
    w2_d = P.dram_in("w2t", [1024, 4096])
    h1_d = P.dram_in("h1t", [4096, 4096])
    h2_d = P.dram_in("h2t", [4096, 4096])
    w3_d = P.dram_in("w3t", [4096, 1024])
    sc_d = P.dram_in("scales", [128, 9])
    x1_d = P.dram_out("x1", [1024, T])
    P.load_basics()
    pp = P.psum_pool()
    sc = P.sbuf("sc", [128, 136])
    sc_d2 = P.dram_in("scales2", [128, 136])
    E.dma("sync", sc[:, :], sc_d2.ap())
    xnt = [P.sbuf(f"xn{k}", [128, T]) for k in range(8)]
    ht = [P.sbuf(f"h{k}", [128, TS]) for k in range(32)]
    gt = [P.sbuf(f"g{k}", [128, TS]) for k in range(32)]
    wpool = P.sbuf_pool(2, [128, MMF], name="wp")
    xbuf = P.sbuf("xbuf", [128, TS])
    sbuf2 = P.sbuf("sbuf2", [128, TS])
    for k in range(8):
        E.dma("sync", xnt[k][:, :], xn_d[k * 128:(k + 1) * 128, :])

    for sschunk in range(NS):
        xc0 = sschunk * TS
        def evac_t2(m, c, ps):
            E.do("vector", lambda e, m=m, o=ht[m][0:ps.shape[0], 0:ps.shape[1]],
                 p=ps: e.tensor_scalar(o, p, sc[:, 32 + m:33 + m], None,
                                       ALU.mult))
        proj_fm_wide(P, pp, wpool, xnt, w2_d.ap(), 1024, 4096, TS, evac_t2,
                     xcol0=xc0)
        def evac_t1(m, c, ps):
            E.do("scalar", lambda e, m=m, o=xbuf[0:ps.shape[0], 0:ps.shape[1]],
                 p=ps: e.activation(o, p, AF.Silu, scale=sc[:, m:m + 1]))
            E.do("vector", lambda e, m=m: e.tensor_mul(
                ht[m][:, 0:TS], ht[m][:, 0:TS], xbuf[:, 0:TS]))
        proj_fm_wide(P, pp, wpool, xnt, w1_d.ap(), 1024, 4096, TS, evac_t1,
                     xcol0=xc0)
        def evac_g2(m, c, ps):
            E.do("scalar", lambda e, m=m, o=gt[m][0:ps.shape[0], 0:ps.shape[1]],
                 p=ps: e.activation(o, p, AF.Gelu, scale=sc[:, 64 + m:65 + m]))
        proj_fm_wide(P, pp, wpool, ht, h1_d.ap(), 4096, 4096, TS, evac_g2)
        def evac_g3(m, c, ps):
            E.do("scalar", lambda e, m=m, o=ht[m][0:ps.shape[0], 0:ps.shape[1]],
                 p=ps: e.activation(o, p, AF.Gelu, scale=sc[:, 96 + m:97 + m]))
        proj_fm_wide(P, pp, wpool, gt, h2_d.ap(), 4096, 4096, TS, evac_g3)
        def evac_g(m, c, ps):
            E.dma("sync", xbuf[:, 0:TS], x_d[m * 128:(m + 1) * 128,
                                             xc0:xc0 + TS])
            E.dma("sync", sbuf2[:, 0:TS], s_d[m * 128:(m + 1) * 128,
                                              xc0:xc0 + TS])
            E.do("scalar", lambda e, m=m, o=ps, p=ps:
                 e.activation(o, p, AF.Silu, scale=sc[:, 128 + m:129 + m]))
            E.do("vector", lambda e: e.tensor_mul(
                sbuf2[:, 0:TS], sbuf2[:, 0:TS], ps))
            E.do("vector", lambda e: e.tensor_add(
                xbuf[:, 0:TS], xbuf[:, 0:TS], sbuf2[:, 0:TS]))
            E.dma("sync", x1_d[m * 128:(m + 1) * 128, xc0:xc0 + TS],
                  xbuf[:, 0:TS])
        proj_fm_wide(P, pp, wpool, ht, w3_d.ap(), 4096, 1024, TS, evac_g)
    P.finish()
    return P


def host_l0c(y_fm, z_fm, xn_fm, x_fm, norm_w, Wout, ffn):
    P1 = build_l0c1()
    base = Prog.basic_inputs()
    wt_out = np.ascontiguousarray(Wout.T)
    ims = []
    for c in range(NCORES):
        tok = slice(c * TCORE, (c + 1) * TCORE)
        ims.append(dict(y=np.ascontiguousarray(y_fm[:, tok]),
                        z=np.ascontiguousarray(z_fm[:, tok]),
                        norm_w=np.ascontiguousarray(norm_w.reshape(-1, 1)),
                        wt_out=wt_out, **base))
    res1 = P1.run(ims)
    s_fm = np.concatenate([r["s"] for r in res1.results], axis=1)

    P2 = build_l0c2()
    scales = np.zeros((128, 136), np.float32)
    for i, nm in enumerate(("w1_s", "w2_s", "h1_s", "h2_s")):
        scales[:, i * 32:(i + 1) * 32] = ffn[nm].reshape(32, 128).T
    scales[:, 128:136] = ffn["w3_s"].reshape(8, 128).T
    w1t = np.ascontiguousarray(ffn["w1_w"].T)
    w2t = np.ascontiguousarray(ffn["w2_w"].T)
    h1t = np.ascontiguousarray(ffn["h1_w"].T)
    h2t = np.ascontiguousarray(ffn["h2_w"].T)
    w3t = np.ascontiguousarray(ffn["w3_w"].T)
    ims = []
    for c in range(NCORES):
        tok = slice(c * TCORE, (c + 1) * TCORE)
        ims.append(dict(xn=np.ascontiguousarray(xn_fm[:, tok]),
                        x=np.ascontiguousarray(x_fm[:, tok]),
                        s=np.ascontiguousarray(s_fm[:, tok]),
                        w1t=w1t, w2t=w2t, h1t=h1t, h2t=h2t, w3t=w3t,
                        scales=np.zeros((128, 9), np.float32),
                        scales2=scales, **base))
    res2 = P2.run(ims)
    return np.concatenate([r["x1"] for r in res2.results], axis=1)



def build_l1a():
    P = Prog("l1a")
    E = P.E
    T = TCORE
    TH = T + 3
    x_d = P.dram_in("x", [DM, TH])
    wt_d = P.dram_in("wt_in", [1024, 4096])
    convw_d = P.dram_in("conv_w", [2048, 4])
    xpt_d = P.dram_in("wt_xproj", [2048, 17])
    xn_d = P.dram_out("xn", [DM, T])
    u_d = P.dram_out("u", [2048, T])
    z_d = P.dram_out("z", [2048, T])
    dtb_d = P.dram_out("dtb", [17, T])

    P.load_basics()
    pp = P.psum_pool()
    xt = [P.sbuf(f"x{k}", [128, TH]) for k in range(8)]
    scr = P.sbuf("scr", [128, MMF])
    ab = (P.sbuf("abb", [1, TH]), P.sbuf("aba", [1, TH]))
    wpool = P.sbuf_pool(2, [128, MMF], name="wp")
    convw = P.sbuf("convw", [128, 4 * 16])
    ubuf = P.sbuf("ubuf", [128, TH])
    ucv = P.sbuf("ucv", [128, T])
    zstage = P.sbuf("zstage", [128, TH])
    xpw = P.sbuf("xpw", [128, 17 * 16])
    dtbst = P.sbuf("dtbst", [17, T])
    CT = ceil_div(TH, MMF)
    CTm = ceil_div(T, MMF)

    for k in range(8):
        E.dma("sync", xt[k][:, :], x_d[k * 128:(k + 1) * 128, :])
    for k in range(16):
        E.dma("sync", convw[:, 4 * k:4 * k + 4],
              convw_d[k * 128:(k + 1) * 128, :])
        E.dma("sync", xpw[:, 17 * k:17 * k + 17],
              xpt_d[k * 128:(k + 1) * 128, :])

    ln_rows(P, pp, xt, TH, DM, scr, ab)
    apply_ab(P, pp, xt, xt, ab, TH, DM)
    for k in range(8):
        E.dma("sync", xn_d[k * 128:(k + 1) * 128, :], xt[k][:, 3:TH])

    xdbl = P.sbuf("xdbl", [17, T])

    def conv_u_tile(kk):
        for j in range(4):
            if j == 0:
                E.do("vector",
                     lambda e, o=ucv[:, 0:T], x=ubuf[:, 0:T],
                            w=convw[:, 4 * kk:4 * kk + 1]:
                         e.tensor_scalar(o, x, w, None, ALU.mult))
            else:
                E.do("vector",
                     lambda e, o=ucv[:, 0:T], x=ubuf[:, j:j + T],
                            w=convw[:, 4 * kk + j:4 * kk + j + 1]:
                         e.scalar_tensor_tensor(o, x, w, ucv[:, 0:T],
                                                ALU.mult, ALU.add))
        E.do("scalar", lambda e: e.activation(ucv[:, 0:T], ucv[:, 0:T],
                                              AF.Silu))
        E.dma("sync", u_d[kk * 128:(kk + 1) * 128, :], ucv[:, 0:T])
        if kk == 0:
            E.do("vector", lambda e: e.memset(xdbl[:, :], 0.0))
        for c in range(CTm):
            cw = min(MMF, T - c * MMF)
            ps = pp.next()
            E.do("tensor",
                 lambda e, o=ps[0:17, 0:cw],
                        w=xpw[:, 17 * kk:17 * kk + 17],
                        x=ucv[:, c * MMF:c * MMF + cw]:
                     e.matmul(o, w, x, start=True, stop=True))
            E.do("vector",
                 lambda e, o=xdbl[0:17, c * MMF:c * MMF + cw],
                        p=ps[0:17, 0:cw]:
                     e.tensor_add(o, o, p))

    def evac(m, c, ps):
        cw = ps.shape[1]
        mw = ps.shape[0]
        if m < 16:
            E.do("vector", lambda e, o=ubuf[0:mw, c * MMF:c * MMF + cw],
                 p=ps: e.tensor_copy(o, p))
            if c == CT - 1:
                conv_u_tile(m)
        else:
            E.do("vector", lambda e, o=zstage[0:mw, c * MMF:c * MMF + cw],
                 p=ps: e.tensor_copy(o, p))
            if c == CT - 1:
                E.dma("sync", z_d[(m - 16) * 128:(m - 15) * 128, :],
                      zstage[:, 3:TH])

    proj_fm(P, pp, wpool, xt, wt_d.ap(), 1024, 4096, TH, evac)
    E.dma("sync", dtb_d.ap(), xdbl[:, 0:T])
    P.finish()
    return P


def host_l1a(x_fm, in_proj_w, conv_w, x_proj_w):
    P = build_l1a()
    base = Prog.basic_inputs()
    wt_in = np.ascontiguousarray(in_proj_w.T)
    cw = np.ascontiguousarray(conv_w[:, 0, :])
    xpt = np.ascontiguousarray(x_proj_w.T)
    ims = []
    for c in range(NCORES):
        lo = c * TCORE
        xh = np.zeros((DM, TCORE + 3), np.float32)
        xh[:, 3:] = x_fm[:, lo:lo + TCORE]
        if c % 2 == 1:
            xh[:, 0:3] = x_fm[:, lo - 3:lo]
        ims.append(dict(x=xh, wt_in=wt_in, conv_w=cw, wt_xproj=xpt, **base))
    res = P.run(ims)
    out = {}
    for name in ("xn", "u", "z", "dtb"):
        out[name] = np.concatenate([r[name] for r in res.results], axis=1)
    return out



def build_l1b():
    P = Prog("l1b")
    E = P.E
    L = 4096
    dtin_d = P.dram_in("dtin", [1, NTOK])
    b_d = P.dram_in("bmat16", [16, NTOK])
    dtwb_d = P.dram_in("dtwb", [2, 256])
    a_d = P.dram_in("a_sl", [256, 16])
    hsel_d = P.dram_in("hsel16", [16, 16 * 128])
    y_d = P.dram_out("y", [256, NTOK])
    P.load_basics()
    pp = P.psum_pool()
    mov = P.sbuf("mov", [2, 4096])
    dtwb = P.sbuf("dtwb_sb", [2, 256])
    asl = [P.sbuf(f"asl{i}", [128, 16]) for i in range(2)]
    hsel = P.sbuf("hsel", [16, 16 * 128])
    delta = P.sbuf("delta", [128, L])
    tmp = P.sbuf("tmp", [128, L])
    bbc = P.sbuf("bbc", [128, L])
    dB = P.sbuf("dB", [128, L])
    ee = P.sbuf("ee", [128, L])
    hh = P.sbuf("hh", [128, L])
    yac = P.sbuf("yac", [128, L])
    E.do("vector", lambda e: e.memset(mov[0:2, :], 1.0))
    E.dma("sync", dtwb[:, :], dtwb_d.ap())
    for i in range(2):
        E.dma("sync", asl[i][:, :], a_d[i * 128:(i + 1) * 128, :])
        E.do("scalar", lambda e, i=i: e.activation(asl[i][:, :], asl[i][:, :],
                                                   AF.Exp))
        E.do("vector", lambda e, i=i: e.tensor_scalar(
            asl[i][:, :], asl[i][:, :], -1.0, None, ALU.mult))
    E.dma("sync", hsel[:, :], hsel_d.ap())
    bsb = P.sbuf("bsb16", [16, 4096])

    for bidx in range(4):
        t0 = bidx * L
        E.dma("sync", mov[0:1, :], dtin_d[0:1, t0:t0 + L])
        E.dma("sync", bsb[:, :], b_d[0:16, t0:t0 + L])
        for ds in range(2):
            for c in range(8):
                ps = pp.next()
                E.do("tensor", lambda e, c=c, ds=ds, o=ps[:, 0:512]:
                     e.matmul(o, dtwb[:, ds * 128:(ds + 1) * 128],
                              mov[:, c * 512:(c + 1) * 512],
                              start=True, stop=True))
                E.do("scalar", lambda e, c=c, i=ps[:, 0:512]:
                     e.activation(delta[:, c * 512:(c + 1) * 512], i,
                                  AF.Identity))
            softplus_cols(P, delta, 128, L, tmp)
            E.do("vector", lambda e: e.memset(yac[:, :], 0.0))
            for n in range(16):
                if True:
                    for c in range(8):
                        ps = pp.next()
                        E.do("tensor", lambda e, c=c, n=n, o=ps[:, 0:512]:
                             e.matmul(o, hsel[:, n * 128:(n + 1) * 128],
                                      bsb[0:16, c * 512:(c + 1) * 512],
                                      start=True, stop=True))
                        E.do("scalar", lambda e, c=c, i=ps[:, 0:512]:
                             e.activation(bbc[:, c * 512:(c + 1) * 512], i,
                                          AF.Identity))
                E.do("scalar", lambda e, n=n, ds=ds: e.activation(
                    ee[:, :], delta[:, :], AF.Exp,
                    scale=asl[ds][:, n:n + 1]))
                E.do("vector", lambda e: e.tensor_mul(dB[:, :], delta[:, :],
                                                      bbc[:, :]))
                E.do("vector", lambda e: e.tensor_tensor_scan(
                    hh[:, :], ee[:, :], dB[:, :], 0.0, ALU.mult, ALU.add))
                E.do("vector", lambda e: e.tensor_mul(hh[:, :], hh[:, :],
                                                      bbc[:, :]))
                E.do("vector", lambda e: e.tensor_add(yac[:, :], yac[:, :],
                                                      hh[:, :]))
            E.dma("sync", y_d[ds * 128:(ds + 1) * 128, t0:t0 + L],
                  yac[:, :])
    P.finish()
    return P


def host_l1b(dtb_fm, dt_w, dt_b, A_param):
    P = build_l1b()
    base = Prog.basic_inputs()
    hsel = np.zeros((16, 16 * 128), np.float32)
    for h in range(16):
        hsel[h, h * 128:(h + 1) * 128] = 1.0
    ims = []
    for c in range(NCORES):
        dsl = slice(c * 256, (c + 1) * 256)
        dtwb = np.stack([dt_w[dsl, 0], dt_b[dsl]]).astype(np.float32)
        ims.append(dict(dtin=np.ascontiguousarray(dtb_fm[0:1, :]),
                        bmat16=np.ascontiguousarray(dtb_fm[1:17, :]),
                        dtwb=np.ascontiguousarray(dtwb),
                        a_sl=np.ascontiguousarray(A_param[dsl]),
                        hsel16=hsel, **base))
    res = P.run(ims)
    return np.concatenate([r["y"] for r in res.results], axis=0)



def build_l1c():
    T = TCORE
    P = Prog("l1c")
    E = P.E
    y_d = P.dram_in("y", [2048, T])
    u_d = P.dram_in("u", [2048, T])
    z_d = P.dram_in("z", [2048, T])
    dcol_d = P.dram_in("dcol", [2048, 1])
    wt_d = P.dram_in("wt_out", [2048, 1024])
    ob_d = P.dram_in("out_b", [1024, 1])
    s_d = P.dram_out("s", [1024, T])
    P.load_basics()
    pp = P.psum_pool()
    yt = [P.sbuf(f"y{k}", [128, T]) for k in range(16)]
    buf = P.sbuf("buf", [128, T])
    dcol = P.sbuf("dcol_sb", [128, 16])
    ob = P.sbuf("ob_sb", [128, 8])
    wpool = P.sbuf_pool(2, [128, MMF], name="wp")
    sstage = P.sbuf("sstage", [128, T])
    for k in range(16):
        E.dma("sync", yt[k][:, :], y_d[k * 128:(k + 1) * 128, :])
        E.dma("sync", dcol[:, k:k + 1], dcol_d[k * 128:(k + 1) * 128, :])
    for k in range(8):
        E.dma("sync", ob[:, k:k + 1], ob_d[k * 128:(k + 1) * 128, :])
    for k in range(16):
        E.dma("sync", buf[:, :], u_d[k * 128:(k + 1) * 128, :])
        E.do("vector", lambda e, k=k: e.scalar_tensor_tensor(
            yt[k][:, :], buf[:, :], dcol[:, k:k + 1], yt[k][:, :],
            ALU.mult, ALU.add))
        E.dma("sync", buf[:, :], z_d[k * 128:(k + 1) * 128, :])
        E.do("scalar", lambda e: e.activation(buf[:, :], buf[:, :], AF.Silu))
        E.do("vector", lambda e, k=k: e.tensor_mul(yt[k][:, :], yt[k][:, :],
                                                   buf[:, :]))

    def evac(m, c, ps):
        E.do("vector", lambda e, m=m,
             o=sstage[0:ps.shape[0], c * MMF:c * MMF + ps.shape[1]], p=ps:
                 e.tensor_scalar(o, p, 1.0, ob[:, m:m + 1], ALU.mult, ALU.add))
        if c == ceil_div(T, MMF) - 1:
            E.dma("sync", s_d[m * 128:m * 128 + 128, :], sstage[:, 0:T])

    proj_fm(P, pp, wpool, yt, wt_d.ap(), 2048, 1024, T, evac)
    P.finish()
    return P


def host_l1c(y_fm, u_fm, z_fm, D_param, out_w, out_b):
    P = build_l1c()
    base = Prog.basic_inputs()
    wt = np.ascontiguousarray(out_w.T)
    ims = []
    for c in range(NCORES):
        tok = slice(c * TCORE, (c + 1) * TCORE)
        ims.append(dict(y=np.ascontiguousarray(y_fm[:, tok]),
                        u=np.ascontiguousarray(u_fm[:, tok]),
                        z=np.ascontiguousarray(z_fm[:, tok]),
                        dcol=np.ascontiguousarray(D_param.reshape(-1, 1)),
                        wt_out=wt,
                        out_b=np.ascontiguousarray(out_b.reshape(-1, 1)),
                        **base))
    res = P.run(ims)
    return np.concatenate([r["s"] for r in res.results], axis=1)



def build_l1d1():
    P = Prog("l1d1")
    E = P.E
    T = TCORE
    TH = 513 + T
    TK = 512 + T
    x_d = P.dram_in("x", [DM, TH])
    wq_d = P.dram_in("wq", [1024, 512])
    wk_d = P.dram_in("wk", [1024, 512])
    wv_d = P.dram_in("wv", [1024, 512])
    q_d = P.dram_out("q", [512, T])
    k_d = P.dram_out("k", [512, TK])
    v_d = P.dram_out("v", [512, TK])
    P.load_basics()
    pp = P.psum_pool()
    xt = [P.sbuf(f"x{k}", [128, TH]) for k in range(8)]
    wpool = P.sbuf_pool(2, [128, MMF], name="wp")
    stage = P.sbuf("stage", [128, TK])
    for k in range(8):
        E.dma("sync", xt[k][:, :], x_d[k * 128:(k + 1) * 128, :])

    def proj_shift(w_dram, out_d, Tn, col0):
        CT = ceil_div(Tn, MMF)
        for m in range(4):
            banks = [pp.next() for _ in range(CT)]
            for k in range(8):
                wtile = wpool.next()
                E.dma("sync", wtile[:, 0:128],
                      w_dram[k * 128:(k + 1) * 128, m * 128:(m + 1) * 128])
                off = col0 if k < 4 else col0 - 1
                for c in range(CT):
                    cw = min(MMF, Tn - c * MMF)
                    E.do("tensor",
                         lambda e, o=banks[c][0:128, 0:cw],
                                w=wtile[:, 0:128],
                                x=xt[k][:, off + c * MMF:off + c * MMF + cw],
                                st=(k == 0):
                             e.matmul(o, w, x, start=st, stop=(k == 7)))
            for c in range(CT):
                cw = min(MMF, Tn - c * MMF)
                E.do("vector", lambda e, o=stage[:, c * MMF:c * MMF + cw],
                     p=banks[c][0:128, 0:cw]: e.tensor_copy(o, p))
            E.dma("sync", out_d[m * 128:(m + 1) * 128, 0:Tn],
                  stage[:, 0:Tn])

    proj_shift(wq_d, q_d, T, 513)
    proj_shift(wk_d, k_d, TK, 1)
    proj_shift(wv_d, v_d, TK, 1)
    P.finish()
    return P


L1D2_DEBUG = False


def build_l1d2():
    P = Prog("l1d2")
    E = P.E
    T = TCORE
    TK = 512 + T
    NW = T // 512
    q_d = P.dram_in("q", [512, T])
    k_d = P.dram_in("k", [512, TK])
    v_d = P.dram_in("v", [512, TK])
    qs_d = P.dram_in("qs_col", [512, 1])
    ks_d = P.dram_in("ks_col", [512, 1])
    csq_d = P.dram_in("csq", [128, TK])
    ssq_d = P.dram_in("ssq", [128, TK])
    csk_d = P.dram_in("csk", [128, TK])
    ssk_d = P.dram_in("ssk", [128, TK])
    rotm_d = P.dram_in("rotm", [128, 128])
    ind2_d = P.dram_in("ind2", [2, 128])
    mask_d = P.dram_in("mask8", [128, 8 * 1024])
    o_d = P.dram_out("o", [512, T])
    if L1D2_DEBUG:
        dbg_k0 = P.dram_out("dbg_k0", [128, TK])
        dbg_qw = P.dram_out("dbg_qw", [128, 512])
        dbg_e = P.dram_out("dbg_e", [128, 1024])
        dbg_vt = P.dram_out("dbg_vt", [128, 512])
    P.load_basics(eye=True)
    pp = P.psum_pool()
    kt = [P.sbuf(f"k{i}", [128, TK]) for i in range(4)]
    vt = [P.sbuf(f"v{i}", [128, TK]) for i in range(4)]
    csk = P.sbuf("csk_sb", [128, TK])
    ssk = P.sbuf("ssk_sb", [128, TK])
    csq = P.sbuf("csq_sb", [128, TK])
    ssq = P.sbuf("ssq_sb", [128, TK])
    rotm = P.sbuf("rotm_sb", [128, 128])
    rsc = P.sbuf("rsc", [128, MMF])
    rsc2 = P.sbuf("rsc2", [128, MMF])
    qs = P.sbuf("qs_sb", [128, 4])
    ks = P.sbuf("ks_sb", [128, 4])
    ind2 = P.sbuf("ind2_sb", [2, 128])
    mask = P.sbuf("mask_sb", [128, 8 * 1024])
    rn2 = P.sbuf("rn2", [2, TK])
    qw = P.sbuf("qw", [128, 512])
    esb = P.sbuf("esb", [128, 1024])
    etb = P.sbuf("etb", [128, 1024])
    vtb = P.sbuf("vtb", [128, 512])
    ost = [P.sbuf(f"ost{i}", [64, 512]) for i in range(8)]
    nmc = P.sbuf("nmc", [128, 1])
    rdc = P.sbuf("rdc", [128, 1])
    scrt = P.sbuf("scrt", [128, MMF])

    for i in range(4):
        E.dma("sync", kt[i][:, :], k_d[i * 128:(i + 1) * 128, :])
        E.dma("sync", vt[i][:, :], v_d[i * 128:(i + 1) * 128, :])
        E.dma("sync", qs[:, i:i + 1], qs_d[i * 128:(i + 1) * 128, :])
        E.dma("sync", ks[:, i:i + 1], ks_d[i * 128:(i + 1) * 128, :])
    E.dma("sync", csk[:, :], csk_d.ap())
    E.dma("sync", ssk[:, :], ssk_d.ap())
    E.dma("sync", csq[:, :], csq_d.ap())
    E.dma("sync", ssq[:, :], ssq_d.ap())
    E.dma("sync", ind2[:, :], ind2_d.ap())
    E.dma("sync", mask[:, :], mask_d.ap())
    E.dma("sync", rotm[:, :], rotm_d.ap())

    def prep(tile, Tn, sc_col, cs, ss, qk_scale, ccol0):
        CT = ceil_div(Tn, MMF)
        for c in range(CT):
            cw = min(MMF, Tn - c * MMF)
            psA = pp.next()
            E.do("scalar", lambda e, o=scrt[:, 0:cw],
                 x=tile[:, c * MMF:c * MMF + cw]:
                     e.activation(o, x, AF.Square))
            E.do("tensor", lambda e, o=psA[0:2, 0:cw], x=scrt[:, 0:cw]:
                 e.matmul(o, ind2T[:, 0:2], x, start=True, stop=True))
            E.do("vector", lambda e, o=rn2[0:2, c * MMF:c * MMF + cw],
                 p=psA[0:2, 0:cw]: e.tensor_copy(o, p))
        E.do("scalar", lambda e: e.activation(
            rn2[0:2, 0:Tn], rn2[0:2, 0:Tn], AF.Ln,
            bias=P.ccol("eps1e12")[0:2, :]))
        E.do("scalar", lambda e: e.activation(
            rn2[0:2, 0:Tn], rn2[0:2, 0:Tn], AF.Exp, scale=0.5))
        E.do("vector", lambda e: e.tensor_scalar_max(
            rn2[0:2, 0:Tn], rn2[0:2, 0:Tn], 1e-12))
        E.do("vector", lambda e: e.reciprocal(rn2[0:2, 0:Tn],
                                              rn2[0:2, 0:Tn]))
        if qk_scale != 1.0:
            E.do("vector", lambda e: e.tensor_scalar(
                rn2[0:2, 0:Tn], rn2[0:2, 0:Tn], float(qk_scale), None,
                ALU.mult))
        for c in range(CT):
            cw = min(MMF, Tn - c * MMF)
            ps = pp.next()
            E.do("tensor", lambda e, o=ps[:, 0:cw],
                 r=rn2[0:2, c * MMF:c * MMF + cw]:
                     e.matmul(o, ind2[:, :], r, start=True, stop=True))
            E.do("vector", lambda e, o=tile[:, c * MMF:c * MMF + cw],
                 s=sc_col, p=ps[:, 0:cw]:
                     e.scalar_tensor_tensor(o, o, s, p, ALU.mult, ALU.mult))
        for c in range(CT):
            cw = min(MMF, Tn - c * MMF)
            psr = pp.next()
            E.do("tensor", lambda e, o=psr[:, 0:cw],
                 x=tile[:, c * MMF:c * MMF + cw]:
                     e.matmul(o, rotm[:, :], x, start=True, stop=True))
            E.do("vector", lambda e, o=rsc[:, 0:cw],
                 x=tile[:, c * MMF:c * MMF + cw],
                 t=cs[:, ccol0 + c * MMF:ccol0 + c * MMF + cw]:
                     e.tensor_mul(o, x, t))
            E.do("vector", lambda e, o=rsc2[:, 0:cw], p=psr[:, 0:cw],
                 t=ss[:, ccol0 + c * MMF:ccol0 + c * MMF + cw]:
                     e.tensor_mul(o, p, t))
            E.do("vector", lambda e, o=tile[:, c * MMF:c * MMF + cw]:
                 e.tensor_add(o, rsc[:, 0:cw], rsc2[:, 0:cw]))

    ind2T = P.sbuf("ind2T", [128, 2])
    pst = pp.next()
    E.do("tensor", lambda e: e.transpose(pst[0:128, 0:2], ind2[0:2, 0:128],
                                         P.eye[0:2, 0:2]))
    E.do("vector", lambda e: e.tensor_copy(ind2T[:, :], pst[0:128, 0:2]))

    for i in range(4):
        prep(kt[i], TK, ks[:, i:i + 1], csk, ssk, 1.0, 0)
    if L1D2_DEBUG:
        E.dma("sync", dbg_k0.ap(), kt[0][:, :])

    for w in range(NW):
        kc0 = w * 512
        for i in range(4):
            E.dma("sync", qw[:, :], q_d[i * 128:(i + 1) * 128,
                                        w * 512:(w + 1) * 512])
            prep(qw, 512, qs[:, i:i + 1], csq, ssq, 8.0, 512 + w * 512)
            if L1D2_DEBUG and w == 1 and i == 0:
                E.dma("sync", dbg_qw.ap(), qw[:, :])
            for hhh in range(2):
                h = 2 * i + hhh
                vbase = (h % 2) * 64
                for kb in range(8):
                    psv = pp.next()
                    E.do("tensor", lambda e, o=psv[0:128, 0:64],
                         src_=vt[i][vbase:vbase + 64,
                                    kc0 + kb * 128:kc0 + (kb + 1) * 128],
                         idn=P.eye[vbase:vbase + 64, vbase:vbase + 64]:
                         e.transpose(o, src_, idn))
                    E.do("vector", lambda e, kb=kb, i2=psv[0:128, 0:64]:
                         e.tensor_copy(vtb[:, kb * 64:(kb + 1) * 64], i2))
                qb = vbase
                for qsub in range(4):
                    b0 = pp.next()
                    b1 = pp.next()
                    for ci, bank in ((0, b0), (1, b1)):
                        E.do("tensor",
                             lambda e, o=bank[0:128, 0:512],
                                    wq=qw[qb:qb + 64,
                                          qsub * 128:(qsub + 1) * 128],
                                    x=kt[i][qb:qb + 64,
                                            kc0 + ci * 512:
                                            kc0 + (ci + 1) * 512]:
                                 e.matmul(o, wq, x, start=True, stop=True))
                    mslot = (4 + qsub) if w == 0 else qsub
                    for ci, bank in ((0, b0), (1, b1)):
                        E.do("vector",
                             lambda e, o=esb[:, ci * 512:(ci + 1) * 512],
                                    p=bank[0:128, 0:512], ci=ci, ms=mslot:
                                 e.tensor_add(
                                     o, p,
                                     mask[:, ms * 1024 + ci * 512:
                                          ms * 1024 + (ci + 1) * 512]))
                    E.do("vector", lambda e: e.tensor_reduce(
                        nmc[:, :], esb[:, :], AX.X, ALU.max, negate=True))
                    E.do("scalar", lambda e: e.activation(
                        esb[:, :], esb[:, :], AF.Exp, bias=nmc[:, 0:1]))
                    E.do("vector", lambda e: e.tensor_reduce(
                        rdc[:, :], esb[:, :], AX.X, ALU.add))
                    E.do("vector", lambda e: e.reciprocal(rdc[:, :],
                                                          rdc[:, :]))
                    E.do("vector", lambda e: e.tensor_scalar(
                        esb[:, :], esb[:, :], rdc[:, 0:1], None, ALU.mult))
                    if L1D2_DEBUG and w == 1 and i == 0 and hhh == 0 \
                            and qsub == 0:
                        E.dma("sync", dbg_e.ap(), esb[:, :])
                        E.dma("sync", dbg_vt.ap(), vtb[:, :])
                    for kb in range(8):
                        pse = pp.next()
                        E.do("tensor", lambda e, o=pse[0:128, 0:128], kb=kb:
                             e.transpose(o, esb[:, kb * 128:(kb + 1) * 128],
                                         P.eye[:, :]))
                        E.do("vector", lambda e, kb=kb,
                             i2=pse[0:128, 0:128]:
                             e.tensor_copy(etb[:, kb * 128:(kb + 1) * 128],
                                           i2))
                    ps_o = pp.next()
                    for kb in range(8):
                        E.do("tensor",
                             lambda e, o=ps_o[0:64, 0:128], kb=kb:
                                 e.matmul(o, vtb[:, kb * 64:(kb + 1) * 64],
                                          etb[:, kb * 128:(kb + 1) * 128],
                                          start=(kb == 0), stop=(kb == 7)))
                    E.do("vector", lambda e, o=ost[h][
                        0:64, qsub * 128:(qsub + 1) * 128],
                        p=ps_o[0:64, 0:128]: e.tensor_copy(o, p))
        for h in range(8):
            E.dma("sync", o_d[h * 64:(h + 1) * 64, w * 512:(w + 1) * 512],
                  ost[h][:, :])
    P.finish()
    return P


def build_l1d3():
    T = TCORE
    P = Prog("l1d3")
    E = P.E
    o_d = P.dram_in("o", [512, T])
    x_d = P.dram_in("x", [1024, T])
    wo_d = P.dram_in("wo", [512, 1024])
    x2_d = P.dram_out("x2", [1024, T])
    P.load_basics()
    pp = P.psum_pool()
    ot = [P.sbuf(f"o{k}", [128, T]) for k in range(4)]
    xbuf = P.sbuf("xbuf", [128, T])
    wpool = P.sbuf_pool(2, [128, MMF], name="wp")
    for k in range(4):
        E.dma("sync", ot[k][:, :], o_d[k * 128:(k + 1) * 128, :])

    def evac(m, c, ps):
        if c == 0:
            E.dma("sync", xbuf[:, 0:T], x_d[m * 128:(m + 1) * 128, :])
        E.do("vector", lambda e,
             o=xbuf[0:ps.shape[0], c * MMF:c * MMF + ps.shape[1]], p=ps:
                 e.tensor_add(o, o, p))
        if c == ceil_div(T, MMF) - 1:
            E.dma("sync", x2_d[m * 128:(m + 1) * 128, :], xbuf[:, 0:T])

    proj_fm(P, pp, wpool, ot, wo_d.ap(), 512, 1024, T, evac)
    P.finish()
    return P


def build_l1e():
    T = TCORE
    TS = 1024
    NS = T // TS
    P = Prog("l1e")
    E = P.E
    x2_d = P.dram_in("x2", [1024, 1 + T])
    lngb_d = P.dram_in("lngb", [128, 16])
    w1_d = P.dram_in("w1p", [1024, 5632])
    b1_d = P.dram_in("b1p", [128, 44])
    w2_d = P.dram_in("w2p", [2816, 1024])
    b2_d = P.dram_in("b2c", [128, 8])
    x3_d = P.dram_out("x3", [1024, T])
    P.load_basics()
    pp = P.psum_pool()
    ht = [P.sbuf(f"h{k}", [128, TS]) for k in range(8)]
    at = [P.sbuf(f"a{k}", [128, TS]) for k in range(22)]
    lngb = P.sbuf("lngb_sb", [128, 16])
    b1 = P.sbuf("b1_sb", [128, 44])
    b2 = P.sbuf("b2_sb", [128, 8])
    scr = P.sbuf("scr", [128, MMF])
    ab = (P.sbuf("abb", [1, TS]), P.sbuf("aba", [1, TS]))
    wpool = P.sbuf_pool(2, [128, MMF], name="wp")
    xbuf = P.sbuf("xbuf", [128, TS])
    E.dma("sync", lngb[:, :], lngb_d.ap())
    E.dma("sync", b1[:, :], b1_d.ap())
    E.dma("sync", b2[:, :], b2_d.ap())

    for sub in range(NS):
        xc0 = 1 + sub * TS
        for k in range(8):
            off = xc0 if k < 4 else xc0 - 1
            E.dma("sync", ht[k][:, 0:TS],
                  x2_d[k * 128:(k + 1) * 128, off:off + TS])
        ln_rows(P, pp, ht, TS, 1024, scr, ab)
        apply_ab(P, pp, ht, ht, ab, TS, 1024)
        for k in range(8):
            E.do("vector", lambda e, k=k: e.tensor_scalar(
                ht[k][:, 0:TS], ht[k][:, 0:TS], lngb[:, k:k + 1],
                lngb[:, 8 + k:9 + k], ALU.mult, ALU.add))

        def evac1(m, c, ps):
            cw = ps.shape[1]
            mw = ps.shape[0]
            if m < 22:
                E.do("vector", lambda e, m=m,
                     o=at[m][0:mw, c * MMF:c * MMF + cw], p=ps:
                         e.tensor_scalar(o, p, 1.0, b1[:, m:m + 1],
                                         ALU.mult, ALU.add))
            else:
                mm = m - 22
                E.do("scalar", lambda e, m=m,
                     o=scr[0:mw, 0:cw], p=ps:
                         e.activation(o, p, AF.Gelu, bias=b1[:, m:m + 1]))
                E.do("vector", lambda e, mm=mm, cw=cw, c=c:
                     e.tensor_mul(at[mm][:, c * MMF:c * MMF + cw],
                                  at[mm][:, c * MMF:c * MMF + cw],
                                  scr[:, 0:cw]))

        proj_fm_wide(P, pp, wpool, ht, w1_d.ap(), 1024, 5632, TS, evac1)

        def evac2(m, c, ps):
            cw = ps.shape[1]
            if c == 0:
                E.dma("sync", xbuf[:, 0:TS],
                      x2_d[m * 128:(m + 1) * 128, xc0:xc0 + TS])
            E.do("vector", lambda e, m=m, o=ps, p=ps:
                 e.tensor_scalar(o, p, 1.0, b2[:, m:m + 1], ALU.mult,
                                 ALU.add))
            E.do("vector", lambda e, c=c, cw=cw, p=ps:
                 e.tensor_add(xbuf[:, c * MMF:c * MMF + cw],
                              xbuf[:, c * MMF:c * MMF + cw], p))
            if c == ceil_div(TS, MMF) - 1:
                E.dma("sync",
                      x3_d[m * 128:(m + 1) * 128, sub * TS:(sub + 1) * TS],
                      xbuf[:, 0:TS])

        proj_fm_wide(P, pp, wpool, at, w2_d.ap(), 2816, 1024, TS, evac2)
    P.finish()
    return P


def _l1ffn_tables(core):
    T, TK = TCORE, 512 + TCORE
    start = core * TCORE
    half = np.arange(0, 64, 2, dtype=np.float32)
    inv = 1.0 / 10000.0 ** (half / 64.0)
    base = (half + 0.4 * 64) / (1.4 * 64)
    gtok = np.arange(start - 512, start + T)
    pos = (gtok % 4096).astype(np.float32)
    fr = pos[:, None] * inv[None, :]
    fr = np.concatenate([fr, fr], axis=1)
    power = (pos - 2048.0) / 256.0
    sc = base[None, :] ** power[:, None]
    sc = np.concatenate([sc, sc], axis=1)
    csq = np.tile((np.cos(fr) * sc).T.astype(np.float32), (2, 1)).copy()
    ssq = np.tile((np.sin(fr) * sc).T.astype(np.float32), (2, 1)).copy()
    csk = np.tile((np.cos(fr) / sc).T.astype(np.float32), (2, 1)).copy()
    ssk = np.tile((np.sin(fr) / sc).T.astype(np.float32), (2, 1)).copy()
    mask8 = np.zeros((128, 8 * 1024), np.float32)
    first = (core % 2 == 0)
    for qsub in range(4):
        qr = np.arange(128)[:, None]
        kc = np.arange(1024)[None, :]
        d = 512 + qsub * 128 + qr - kc
        bad = (d < 0) | (d > 512)
        m = np.where(bad, -1e30, 0.0).astype(np.float32)
        mask8[:, qsub * 1024:(qsub + 1) * 1024] = m
        mfw = m.copy()
        if first:
            mfw[:, 0:512] = -1e30
        mask8[:, (4 + qsub) * 1024:(5 + qsub) * 1024] = mfw
    ind2 = np.zeros((2, 128), np.float32)
    ind2[0, 0:64] = 1.0
    ind2[1, 64:128] = 1.0
    r = np.zeros((64, 64), np.float32)
    for d in range(32):
        r[d + 32, d] = -1.0
    for d in range(32, 64):
        r[d - 32, d] = 1.0
    rotm = np.zeros((128, 128), np.float32)
    rotm[0:64, 0:64] = r
    rotm[64:128, 64:128] = r
    return dict(csq=csq, ssq=ssq, csk=csk, ssk=ssk, mask8=mask8, ind2=ind2,
                rotm=rotm)


def host_l1ffn_layer(x_fm, wqkv, q_scale, k_scale, wo, ln_g, ln_b,
                     w1, b1, w2, b2, progs=None):
    base = Prog.basic_inputs()
    base_eye = Prog.basic_inputs(eye=True)
    P1 = build_l1d1()
    ims = []
    T = TCORE
    for c in range(NCORES):
        lo = c * T
        xh = np.zeros((DM, 513 + T), np.float32)
        xh[:, 513:] = x_fm[:, lo:lo + T]
        if c % 2 == 1:
            xh[:, 0:513] = x_fm[:, lo - 513:lo]
        ims.append(dict(x=xh, wq=np.ascontiguousarray(wqkv[:, 0:512]),
                        wk=np.ascontiguousarray(wqkv[:, 512:1024]),
                        wv=np.ascontiguousarray(wqkv[:, 1024:1536]), **base))
    r1 = P1.run(ims)

    P2 = build_l1d2()
    qs_col = np.tile(q_scale, 8).reshape(-1, 1).astype(np.float32)
    ks_col = np.tile(k_scale, 8).reshape(-1, 1).astype(np.float32)
    ims = []
    for c in range(NCORES):
        tb = _l1ffn_tables(c)
        ims.append(dict(q=r1.results[c]["q"], k=r1.results[c]["k"],
                        v=r1.results[c]["v"], qs_col=qs_col, ks_col=ks_col,
                        **tb, **base_eye))
    r2 = P2.run(ims)

    P3 = build_l1d3()
    ims = []
    for c in range(NCORES):
        lo = c * T
        ims.append(dict(o=r2.results[c]["o"],
                        x=np.ascontiguousarray(x_fm[:, lo:lo + T]),
                        wo=np.ascontiguousarray(wo), **base))
    r3 = P3.run(ims)
    x2_fm = np.concatenate([r.get("x2") for r in
                            (rr for rr in (res for res in r3.results))],
                           axis=1) if False else \
        np.concatenate([r["x2"] for r in r3.results], axis=1)

    P4 = build_l1e()
    HID = 2730
    w1p = np.zeros((1024, 5632), np.float32)
    w1p[:, 0:HID] = w1[:, 0:HID]
    w1p[:, 2816:2816 + HID] = w1[:, HID:2 * HID]
    b1p = np.zeros((128, 44), np.float32)
    b1pad = np.zeros(5632, np.float32)
    b1pad[0:HID] = b1[0:HID]
    b1pad[2816:2816 + HID] = b1[HID:]
    b1p[:, :] = b1pad.reshape(44, 128).T
    w2p = np.zeros((2816, 1024), np.float32)
    w2p[0:HID, :] = w2
    b2c = b2.reshape(8, 128).T.astype(np.float32).copy()
    lngb = np.zeros((128, 16), np.float32)
    lngb[:, 0:8] = ln_g.reshape(8, 128).T
    lngb[:, 8:16] = ln_b.reshape(8, 128).T
    ims = []
    for c in range(NCORES):
        lo = c * T
        xh = np.zeros((DM, 1 + T), np.float32)
        xh[:, 1:] = x2_fm[:, lo:lo + T]
        if c % 2 == 1:
            xh[:, 0:1] = x2_fm[:, lo - 1:lo]
        ims.append(dict(x2=xh, lngb=lngb, w1p=w1p, b1p=b1p, w2p=w2p,
                        b2c=b2c, **base))
    r4 = P4.run(ims)
    return np.concatenate([r["x3"] for r in r4.results], axis=1)


def build_combine():
    T = TCORE
    P = Prog("combine")
    E = P.E
    x_d = P.dram_in("x", [1024, T])
    s_d = P.dram_in("s", [1024, T])
    g_d = P.dram_in("g", [1024, T])
    o_d = P.dram_out("xn_out", [1024, T])
    P.load_basics()
    xb = P.sbuf("xb", [128, T])
    sb = P.sbuf("sb", [128, T])
    gb = P.sbuf("gb", [128, T])
    for k in range(8):
        E.dma("sync", xb[:, :], x_d[k * 128:(k + 1) * 128, :])
        E.dma("sync", sb[:, :], s_d[k * 128:(k + 1) * 128, :])
        E.dma("sync", gb[:, :], g_d[k * 128:(k + 1) * 128, :])
        E.do("scalar", lambda e: e.activation(gb[:, :], gb[:, :], AF.Silu))
        E.do("vector", lambda e: e.tensor_mul(sb[:, :], sb[:, :], gb[:, :]))
        E.do("vector", lambda e: e.tensor_add(xb[:, :], xb[:, :], sb[:, :]))
        E.dma("sync", o_d[k * 128:(k + 1) * 128, :], xb[:, :])
    P.finish()
    return P


def host_combine(x_fm, s_fm, g_fm):
    P = build_combine()
    base = Prog.basic_inputs()
    ims = []
    for c in range(NCORES):
        tok = slice(c * TCORE, (c + 1) * TCORE)
        ims.append(dict(x=np.ascontiguousarray(x_fm[:, tok]),
                        s=np.ascontiguousarray(s_fm[:, tok]),
                        g=np.ascontiguousarray(g_fm[:, tok]), **base))
    res = P.run(ims)
    return np.concatenate([r["xn_out"] for r in res.results], axis=1)



def build_l2a():
    T = TCORE
    P = Prog("l2a")
    E = P.E
    xn_d = P.dram_in("x_raw", [1024, T])
    xnout_d = P.dram_out("xn", [1024, T])
    w_d = P.dram_in("w_in", [1024, 3328])
    rw_d = P.dram_in("rms_w", [1024, 1])
    gb_d = P.dram_in("gate_b", [1024, 1])
    cmask_d = P.dram_in("cmask", [128, T])
    z_d = P.dram_out("z", [1024, T])
    q_d = P.dram_out("q", [128, T])
    ix_d = P.dram_out("ix", [1024, T])
    ik_d = P.dram_out("ik", [1024, T])
    cs_d = P.dram_out("cs", [2048, T // 64])
    P.load_basics()
    pp = P.psum_pool()
    xt = [P.sbuf(f"x{k}", [128, T]) for k in range(8)]
    gt = [P.sbuf(f"g{k}", [128, T]) for k in range(8)]
    scr = P.sbuf("scr", [128, MMF])
    ab = (P.sbuf("abb", [1, T]), P.sbuf("aba", [1, T]))
    rw = P.sbuf("rw", [128, 8])
    gbc = P.sbuf("gbc", [128, 8])
    cmask = P.sbuf("cmask_sb", [128, T])
    wpool = P.sbuf_pool(2, [128, MMF], name="wp")
    stage = P.sbuf("stage", [128, T])
    ksb = P.sbuf("ksb", [128, T])
    dec = P.sbuf("dec", [128, T])
    vv = P.sbuf("vv", [128, T])
    sout = P.sbuf("sout", [128, T])
    cstage = P.sbuf("cstage", [128, 16 * (T // 64)])
    NCh = T // 64

    for k in range(8):
        E.dma("sync", xt[k][:, :], xn_d[k * 128:(k + 1) * 128, :])
        E.dma("sync", rw[:, k:k + 1], rw_d[k * 128:(k + 1) * 128, :])
        E.dma("sync", gbc[:, k:k + 1], gb_d[k * 128:(k + 1) * 128, :])
    E.dma("sync", cmask[:, :], cmask_d.ap())
    ln_rows(P, pp, xt, T, 1024, scr, ab)
    apply_ab(P, pp, xt, xt, ab, T, 1024)
    for k in range(8):
        E.dma("sync", xnout_d[k * 128:(k + 1) * 128, :], xt[k][:, :])
    ln_rows(P, pp, xt, T, 1024, scr, ab, rms=True, eps_name="eps1e6")
    apply_ab(P, pp, xt, xt, ab, T, 1024, rms=True)
    for k in range(8):
        E.do("vector", lambda e, k=k: e.tensor_scalar(
            xt[k][:, :], xt[k][:, :], rw[:, k:k + 1], None, ALU.mult))

    def scan_tile(i, val_tile):
        gg = gt[i % 8]
        E.do("vector", lambda e: e.tensor_mul(dec[:, :], gg[:, :],
                                              cmask[:, :]))
        E.do("vector", lambda e: e.tensor_scalar(scr[:, 0:1], scr[:, 0:1],
                                                 1.0, None, ALU.mult))
        E.do("vector", lambda e, g2=gg: e.tensor_scalar(
            vv[:, :], g2[:, :], -1.0, 1.0, ALU.mult, ALU.add))
        E.do("vector", lambda e, v2=val_tile: e.tensor_mul(
            vv[:, :], vv[:, :], v2[:, :]))
        E.do("vector", lambda e: e.tensor_tensor_scan(
            sout[:, :], dec[:, :], vv[:, :], 0.0, ALU.mult, ALU.add))
        dst = ix_d if i < 8 else ik_d
        E.dma("sync", dst[(i % 8) * 128:(i % 8 + 1) * 128, :], sout[:, :])
        ci = i % 8
        E.do("vector", lambda e, i=i: e.tensor_copy(
            cstage[:, (i % 8) * NCh:(i % 8 + 1) * NCh],
            bass.AP(sout, 0, [[T, 128], [64, NCh]])))
        if i % 8 == 7:
            half = 0 if i < 8 else 1
            for kk in range(8):
                E.dma("sync",
                      cs_d[half * 1024 + kk * 128:half * 1024 + (kk + 1) * 128,
                           :],
                      cstage[:, kk * NCh:(kk + 1) * NCh])

    def evac(m, c, ps):
        cw = ps.shape[1]
        CT = ceil_div(T, MMF)
        if m < 8:
            E.do("scalar", lambda e, o=stage[:, c * MMF:c * MMF + cw], p=ps:
                 e.activation(o, p, AF.Sigmoid))
            if c == CT - 1:
                E.dma("sync", z_d[m * 128:(m + 1) * 128, :], stage[:, 0:T])
        elif m < 16:
            E.do("scalar", lambda e, m=m, o=gt[m - 8][:, c * MMF:c * MMF + cw],
                 p=ps: e.activation(o, p, AF.Sigmoid, bias=gbc[:, m - 8:m - 7]))
        elif m < 24:
            E.do("vector", lambda e, o=stage[:, c * MMF:c * MMF + cw], p=ps:
                 e.tensor_copy(o, p))
            if c == CT - 1:
                scan_tile(m - 16, stage)
        elif m == 24:
            E.do("vector", lambda e, o=stage[:, c * MMF:c * MMF + cw], p=ps:
                 e.tensor_copy(o, p))
            if c == CT - 1:
                E.dma("sync", q_d.ap(), stage[:, 0:T])
        else:
            E.do("vector", lambda e, o=ksb[:, c * MMF:c * MMF + cw], p=ps:
                 e.tensor_copy(o, p))
            if c == CT - 1:
                for i in range(8, 16):
                    scan_tile(i, ksb)

    proj_fm(P, pp, wpool, xt, w_d.ap(), 1024, 3328, T, evac)
    P.finish()
    return P


L2B_DEBUG = False


def build_l2b():
    T = TCORE
    NCh = 64
    P = Prog("l2b")
    E = P.E
    q_d = P.dram_in("q", [128, T])
    ix_d = P.dram_in("ix", [1024, T])
    ik_d = P.dram_in("ik", [1024, T])
    cs_d = P.dram_in("cs_b", [2048, NCh])
    z_d = P.dram_in("z", [1024, T])
    xn_d = P.dram_in("xn", [1024, T])
    w_d = P.dram_in("w_out", [1024, 1024])
    mask_d = P.dram_in("maskc", [128, 16 * NCh])
    s_d = P.dram_out("s", [1024, T])
    if L2B_DEBUG:
        dbg_w = P.dram_out("dbg_w", [128, 64])
        dbg_ic = P.dram_out("dbg_ic", [128, 4])
        dbg_ot = P.dram_out("dbg_ot", [1024, T])
        dbg_prod = P.dram_out("dbg_prod", [128, 256])
        dbg_ihb = P.dram_out("dbg_ihb", [128, 256])
        dbg_qsb = P.dram_out("dbg_qsb", [128, 256])
    P.load_basics(eye=True)
    pp = P.psum_pool()
    qsb = P.sbuf("qsb", [128, T])
    ihb = P.sbuf("ihb", [128, T])
    ixb = P.sbuf("ixb", [128, T])
    csx = P.sbuf("csx", [128, 8 * NCh])
    csk = P.sbuf("csk2", [128, 8 * NCh])
    cxT = P.sbuf("cxT", [128, NCh])
    mask = P.sbuf("maskc_sb", [128, 16 * NCh])
    prod = P.sbuf("prod", [128, T])
    wsb = P.sbuf("wsb", [128, NCh])
    wtr = P.sbuf("wtr", [128, 128])
    mcol = P.sbuf("mcol", [128, 1])
    icol2dbg = P.sbuf("icol2dbg", [128, 4])
    lcol = P.sbuf("lcol", [128, 1])
    icol = P.sbuf("icol", [128, 1])
    rcol = P.sbuf("rcol", [128, 1])
    irow = P.sbuf("irow", [1, 128])
    ot = [P.sbuf(f"ot{k}", [128, T]) for k in range(8)]
    buf = P.sbuf("buf2", [128, T])
    wpool = P.sbuf_pool(2, [128, MMF], name="wp")

    E.dma("sync", qsb[:, :], q_d.ap())
    E.dma("sync", mask[:, :], mask_d.ap())
    for k in range(8):
        E.dma("sync", csx[:, k * NCh:(k + 1) * NCh],
              cs_d[k * 128:(k + 1) * 128, :])
        E.dma("sync", csk[:, k * NCh:(k + 1) * NCh],
              cs_d[1024 + k * 128:1024 + (k + 1) * 128, :])

    for h in range(16):
        hb = (h % 2) * 64
        E.dma("sync", ihb[hb:hb + 64, :],
              ik_d[h * 64:(h + 1) * 64, :])
        E.dma("sync", ixb[hb:hb + 64, :],
              ix_d[h * 64:(h + 1) * 64, :])
        pst = pp.next()
        E.do("tensor", lambda e, o=pst[0:64, 0:64],
             src_=csx[hb:hb + 64, (h // 2) * NCh:(h // 2) * NCh + NCh],
             idn=P.eye[hb:hb + 64, hb:hb + 64]:
             e.transpose(o, src_, idn))
        E.do("vector", lambda e, p=pst[0:64, 0:64]:
             e.tensor_copy(cxT[0:64, 0:NCh], p))
        E.do("vector", lambda e, o=prod[hb:hb + 64, :],
             a=qsb[hb:hb + 64, :], b=ihb[hb:hb + 64, :]:
             e.tensor_mul(o, a, b))
        for sub in range(16):
            t0 = sub * 128
            psl = pp.next()
            E.do("tensor", lambda e, o=psl[0:128, 0:1],
                 w=prod[hb:hb + 64, t0:t0 + 128], m1=P.ones[hb:hb + 64, 0:1]:
                 e.matmul(o, w, m1, start=True, stop=True))
            E.do("vector", lambda e, p=psl[0:128, 0:1]:
                 e.tensor_scalar(lcol[:, :], p, 0.125, None, ALU.mult))
            pss = pp.next()
            E.do("tensor", lambda e, o=pss[0:128, 0:NCh],
                 w=qsb[hb:hb + 64, t0:t0 + 128],
                 x=csk[hb:hb + 64, (h // 2) * NCh:(h // 2) * NCh + NCh]:
                 e.matmul(o, w, x, start=True, stop=True))
            E.do("vector", lambda e, p=pss[0:128, 0:NCh], s2=sub:
                 e.scalar_tensor_tensor(wsb[:, 0:NCh], p, 0.125,
                                        mask[:, s2 * NCh:(s2 + 1) * NCh],
                                        ALU.mult, ALU.add))
            E.do("vector", lambda e: e.tensor_reduce(
                mcol[:, :], wsb[:, 0:NCh], AX.X, ALU.max))
            E.do("vector", lambda e: e.tensor_max(mcol[:, :], mcol[:, :],
                                                  lcol[:, :]))
            E.do("vector", lambda e: e.tensor_scalar(
                mcol[:, :], mcol[:, :], -1.0, None, ALU.mult))
            E.do("scalar", lambda e: e.activation(
                wsb[:, 0:NCh], wsb[:, 0:NCh], AF.Exp, bias=mcol[:, 0:1]))
            E.do("vector", lambda e: e.tensor_add(icol[:, :], lcol[:, :],
                                                  mcol[:, :]))
            E.do("scalar", lambda e: e.activation(icol[:, :], icol[:, :],
                                                  AF.Exp))
            E.do("vector", lambda e: e.tensor_reduce(
                rcol[:, :], wsb[:, 0:NCh], AX.X, ALU.add))
            E.do("vector", lambda e: e.tensor_add(rcol[:, :], rcol[:, :],
                                                  icol[:, :]))
            E.do("vector", lambda e: e.reciprocal(rcol[:, :], rcol[:, :]))
            E.do("vector", lambda e: e.tensor_scalar(
                wsb[:, 0:NCh], wsb[:, 0:NCh], rcol[:, 0:1], None, ALU.mult))
            E.do("vector", lambda e: e.tensor_mul(icol[:, :], icol[:, :],
                                                  rcol[:, :]))
            psw = pp.next()
            E.do("tensor", lambda e, o=psw[0:NCh, 0:128]:
                 e.transpose(o, wsb[:, 0:NCh], P.eye[:, :]))
            E.do("vector", lambda e, p=psw[0:NCh, 0:128]:
                 e.tensor_copy(wtr[0:NCh, :], p))
            psr = pp.next()
            E.do("tensor", lambda e, o=psr[0:1, 0:128]:
                 e.transpose(o, icol[:, 0:1], P.eye[:, :]))
            E.do("vector", lambda e, p=psr[0:1, 0:128]:
                 e.tensor_copy(irow[0:1, :], p))
            psb = pp.next()
            E.do("tensor", lambda e, o=psb[0:64, 0:128]:
                 e.matmul(o, P.ones_row[0:1, 0:64], irow[0:1, :],
                          start=True, stop=True))
            pso = pp.next()
            E.do("tensor", lambda e, o=pso[0:64, 0:128]:
                 e.matmul(o, cxT[0:64, 0:NCh], wtr[0:NCh, :],
                          start=True, stop=True))
            E.do("vector", lambda e, o=buf[0:64, 0:128],
                 a=ixb[hb:hb + 64, t0:t0 + 128], p=psb[0:64, 0:128]:
                 e.tensor_mul(o, a, p))
            E.do("vector", lambda e, o=ot[h // 2][hb:hb + 64, t0:t0 + 128],
                 a=buf[0:64, 0:128], p=pso[0:64, 0:128]:
                 e.tensor_add(o, a, p))
            if L2B_DEBUG and h == 0 and sub == 0:
                E.dma("sync", dbg_prod.ap(), prod[:, 0:256])
                E.dma("sync", dbg_ihb.ap(), ihb[:, 0:256])
                E.dma("sync", dbg_qsb.ap(), qsb[:, 0:256])
                E.dma("sync", dbg_w.ap(), wsb[:, 0:64])
                E.do("vector", lambda e: e.tensor_copy(icol2dbg[:, 0:1],
                                                       icol[:, :]))
                E.do("vector", lambda e: e.tensor_copy(icol2dbg[:, 1:2],
                                                       lcol[:, :]))
                E.do("vector", lambda e: e.tensor_copy(icol2dbg[:, 2:3],
                                                       mcol[:, :]))
                E.do("vector", lambda e: e.tensor_copy(icol2dbg[:, 3:4],
                                                       rcol[:, :]))
                E.dma("sync", dbg_ic.ap(), icol2dbg[:, :])
    if L2B_DEBUG:
        for k in range(8):
            E.dma("sync", dbg_ot[k * 128:(k + 1) * 128, :], ot[k][:, :])
    for k in range(8):
        E.dma("sync", buf[:, :], z_d[k * 128:(k + 1) * 128, :])
        E.do("vector", lambda e, k=k: e.tensor_mul(ot[k][:, :], ot[k][:, :],
                                                   buf[:, :]))

    def evac(m, c, ps):
        cw = ps.shape[1]
        if c == 0:
            E.dma("sync", buf[:, 0:T], xn_d[m * 128:(m + 1) * 128, :])
        E.do("vector", lambda e, o=buf[0:128, c * MMF:c * MMF + cw], p=ps:
             e.tensor_add(o, o, p))
        if c == ceil_div(T, MMF) - 1:
            E.dma("sync", s_d[m * 128:(m + 1) * 128, :], buf[:, 0:T])

    proj_fm(P, pp, wpool, ot, w_d.ap(), 1024, 1024, T, evac)
    P.finish()
    return P


def host_l2(x_fm, rms_weight, W_in, gate_bias, W_out):
    P1 = build_l2a()
    base = Prog.basic_inputs()
    base_eye = Prog.basic_inputs(eye=True)
    Wp = np.concatenate([W_in[:, 0:1024], W_in[:, 2048:3072],
                         W_in[:, 1024:2048], W_in[:, 3072:3328]],
                        axis=1).astype(np.float32).copy()
    cmask = np.ones((128, TCORE), np.float32)
    cmask[:, ::64] = 0.0
    ims = []
    for c in range(NCORES):
        tok = slice(c * TCORE, (c + 1) * TCORE)
        ims.append(dict(x_raw=np.ascontiguousarray(x_fm[:, tok]), w_in=Wp,
                        rms_w=rms_weight.reshape(-1, 1).astype(np.float32),
                        gate_b=gate_bias.reshape(-1, 1).astype(np.float32),
                        cmask=cmask, **base))
    r1 = P1.run(ims)
    xn_fm = np.concatenate([r["xn"] for r in r1.results], axis=1)
    cs_all = np.concatenate([r["cs"] for r in r1.results], axis=1)
    ix_fm = np.concatenate([r["ix"] for r in r1.results], axis=1)
    ik_fm = np.concatenate([r["ik"] for r in r1.results], axis=1)
    z_fm = np.concatenate([r["z"] for r in r1.results], axis=1)
    q_fm = np.concatenate([r["q"] for r in r1.results], axis=1)

    P2 = build_l2b()
    NCh = 64
    ims = []
    for c in range(NCORES):
        beta = c // 2
        tok = slice(c * TCORE, (c + 1) * TCORE)
        csb = np.ascontiguousarray(cs_all[:, beta * NCh:(beta + 1) * NCh])
        maskc = np.zeros((128, 16 * NCh), np.float32)
        for sub in range(16):
            tglob = (c % 2) * 2048 + sub * 128 + np.arange(128)
            allowed = (tglob[:, None] // 64) > np.arange(NCh)[None, :]
            maskc[:, sub * NCh:(sub + 1) * NCh] = np.where(allowed, 0.0,
                                                           -1e30)
        ims.append(dict(q=np.ascontiguousarray(q_fm[:, tok]),
                        ix=np.ascontiguousarray(ix_fm[:, tok]),
                        ik=np.ascontiguousarray(ik_fm[:, tok]),
                        cs_b=csb,
                        z=np.ascontiguousarray(z_fm[:, tok]),
                        xn=np.ascontiguousarray(xn_fm[:, tok]),
                        w_out=np.ascontiguousarray(W_out),
                        maskc=maskc, **base_eye))
    r2 = P2.run(ims)
    globals()["_last_r2"] = r2
    return np.concatenate([r["s"] for r in r2.results], axis=1), xn_fm



def build_l2c():
    T = TCORE
    TS = 512
    NS = T // TS
    P = Prog("l2c")
    E = P.E
    xn_d = P.dram_in("xn", [1024, T])
    w1_d = P.dram_in("w1t", [1024, 4096])
    w2_d = P.dram_in("w2t", [4096, 1024])
    b1_d = P.dram_in("b1c", [128, 32])
    b2_d = P.dram_in("b2c", [128, 8])
    g_d = P.dram_out("g", [1024, T])
    P.load_basics(eye=True)
    pp = P.psum_pool()
    xt = [P.sbuf(f"x{k}", [128, T]) for k in range(8)]
    ht = [P.sbuf(f"h{k}", [128, TS]) for k in range(32)]
    wpool = P.sbuf_pool(2, [128, MMF], name="wp")
    wsg = P.sbuf("wsg", [128, MMF])
    b1 = P.sbuf("b1c_sb", [128, 32])
    b2 = P.sbuf("b2c_sb", [128, 8])
    a1 = P.sbuf("a1_sb", [128, 32])
    a2 = P.sbuf("a2_sb", [128, 8])
    arow = P.sbuf("arow2", [1, 4096])
    gstage = P.sbuf("gstage", [128, TS])
    for k in range(8):
        E.dma("sync", xt[k][:, :], xn_d[k * 128:(k + 1) * 128, :])
    E.dma("sync", b1[:, :], b1_d.ap())
    E.dma("sync", b2[:, :], b2_d.ap())

    def alphas(w_dram, din, dout, dst, ncols):
        for mg in range(ceil_div(dout, MMF)):
            gw = min(MMF, dout - mg * MMF)
            ps = pp.next()
            for k in range(din // 128):
                wtile = wpool.next()
                E.dma("sync", wtile[:, 0:gw],
                      w_dram[k * 128:(k + 1) * 128, mg * MMF:mg * MMF + gw])
                E.do("scalar", lambda e, o=wsg[:, 0:gw], x=wtile[:, 0:gw]:
                     e.activation(o, x, AF.Abs))
                E.do("tensor", lambda e, o=ps[0:1, 0:gw], x=wsg[:, 0:gw],
                     st=(k == 0), sp=(k == din // 128 - 1):
                     e.matmul(o, P.ones[:, 0:1], x, start=st, stop=sp))
            E.do("vector", lambda e, o=arow[0:1, mg * MMF:mg * MMF + gw],
                 p=ps[0:1, 0:gw], d=din:
                 e.tensor_scalar(o, p, 1.0 / d, None, ALU.mult))
        for m in range(ncols):
            ps = pp.next()
            E.do("tensor", lambda e, o=ps[0:128, 0:1], m=m:
                 e.transpose(o, arow[0:1, m * 128:(m + 1) * 128],
                             P.eye[0:1, 0:1]))
            E.do("vector", lambda e, o=dst[:, m:m + 1], p=ps[0:128, 0:1]:
                 e.tensor_copy(o, p))

    alphas(w1_d.ap(), 1024, 4096, a1, 32)
    alphas(w2_d.ap(), 4096, 1024, a2, 8)

    def proj_sign(xtiles, w_dram, din, dout, TS2, evac, xcol0):
        for m in range(ceil_div(dout, 128)):
            banks = [pp.next() for _ in range(ceil_div(TS2, MMF))]
            for k in range(din // 128):
                wtile = wpool.next()
                E.dma("sync", wtile[:, 0:128],
                      w_dram[k * 128:(k + 1) * 128, m * 128:(m + 1) * 128])
                E.do("scalar", lambda e, o=wsg[:, 0:128], x=wtile[:, 0:128]:
                     e.activation(o, x, AF.Sign))
                for c in range(ceil_div(TS2, MMF)):
                    cw = min(MMF, TS2 - c * MMF)
                    E.do("tensor",
                         lambda e, o=banks[c][0:128, 0:cw], w=wsg[:, 0:128],
                                x=xtiles[k][0:128, xcol0 + c * MMF:
                                            xcol0 + c * MMF + cw],
                                st=(k == 0), sp=(k == din // 128 - 1):
                             e.matmul(o, w, x, start=st, stop=sp))
            for c in range(ceil_div(TS2, MMF)):
                cw = min(MMF, TS2 - c * MMF)
                evac(m, c, banks[c][0:128, 0:cw])

    for sub in range(NS):
        xc0 = sub * TS

        def evac1(m, c, ps):
            E.do("scalar", lambda e, m=m, o=ht[m][:, 0:TS], p=ps:
                 e.activation(o, p, AF.Gelu, bias=b1[:, m:m + 1],
                              scale=a1[:, m:m + 1]))

        proj_sign(xt, w1_d.ap(), 1024, 4096, TS, evac1, xc0)

        def evac2(m, c, ps):
            E.do("vector", lambda e, m=m, o=gstage[:, 0:TS], p=ps:
                 e.tensor_scalar(o, p, a2[:, m:m + 1], b2[:, m:m + 1],
                                 ALU.mult, ALU.add))
            E.dma("sync", g_d[m * 128:(m + 1) * 128, xc0:xc0 + TS],
                  gstage[:, 0:TS])

        proj_sign(ht, w2_d.ap(), 4096, 1024, TS, evac2, 0)
    P.finish()
    return P


def host_l2c(xn_fm, w1, b1, w2, b2):
    P = build_l2c()
    base = Prog.basic_inputs(eye=True)
    w1t = np.ascontiguousarray(w1.T)
    w2t = np.ascontiguousarray(w2.T)
    b1c = b1.reshape(32, 128).T.astype(np.float32).copy()
    b2c = b2.reshape(8, 128).T.astype(np.float32).copy()
    ims = []
    for c in range(NCORES):
        tok = slice(c * TCORE, (c + 1) * TCORE)
        ims.append(dict(xn=np.ascontiguousarray(xn_fm[:, tok]), w1t=w1t,
                        w2t=w2t, b1c=b1c, b2c=b2c, **base))
    res = P.run(ims)
    return np.concatenate([r["g"] for r in res.results], axis=1)



def build_l3a():
    T = TCORE
    P = Prog("l3a")
    E = P.E
    x_d = P.dram_in("x", [1024, T])
    wq_d = P.dram_in("wq", [1024, 1024])
    wk_d = P.dram_in("wk", [1024, 1024])
    wv_d = P.dram_in("wv", [1024, 1024])
    wb_d = P.dram_in("wb", [1024, 16])
    ind2_d = P.dram_in("ind2", [2, 128])
    ind2T_d = P.dram_in("ind2T_in", [128, 2])
    xn_d = P.dram_out("xn", [1024, T])
    q_d = P.dram_out("qn", [1024, T])
    k_d = P.dram_out("kn", [1024, T])
    v_d = P.dram_out("v", [1024, T])
    be_d = P.dram_out("beta", [16, T])
    P.load_basics()
    pp = P.psum_pool()
    xt = [P.sbuf(f"x{k}", [128, T]) for k in range(8)]
    scr = P.sbuf("scr", [128, MMF])
    ab = (P.sbuf("abb", [1, T]), P.sbuf("aba", [1, T]))
    wpool = P.sbuf_pool(2, [128, MMF], name="wp")
    stage = P.sbuf("stage", [128, T])
    rn2 = P.sbuf("rn2", [2, T])
    ind2 = P.sbuf("ind2_sb", [2, 128])
    ind2T = P.sbuf("ind2T_sb", [128, 2])
    bst = P.sbuf("bst", [16, T])
    for k in range(8):
        E.dma("sync", xt[k][:, :], x_d[k * 128:(k + 1) * 128, :])
    E.dma("sync", ind2[:, :], ind2_d.ap())
    E.dma("sync", ind2T[:, :], ind2T_d.ap())
    ln_rows(P, pp, xt, T, 1024, scr, ab)
    apply_ab(P, pp, xt, xt, ab, T, 1024)
    for k in range(8):
        E.dma("sync", xn_d[k * 128:(k + 1) * 128, :], xt[k][:, :])

    def norm_evac(out_d):
        def evac(m, c, ps):
            E.do("vector", lambda e, o=stage[:, c * MMF:c * MMF + ps.shape[1]],
                 p=ps: e.tensor_copy(o, p))
            if c == ceil_div(T, MMF) - 1:
                CT = ceil_div(T, MMF)
                for c2 in range(CT):
                    cw = min(MMF, T - c2 * MMF)
                    psA = pp.next()
                    E.do("scalar", lambda e, o=scr[:, 0:cw],
                         x=stage[:, c2 * MMF:c2 * MMF + cw]:
                             e.activation(o, x, AF.Square))
                    E.do("tensor", lambda e, o=psA[0:2, 0:cw], x=scr[:, 0:cw]:
                         e.matmul(o, ind2T[:, 0:2], x, start=True, stop=True))
                    E.do("vector", lambda e, o=rn2[0:2, c2 * MMF:c2 * MMF + cw],
                         p=psA[0:2, 0:cw]: e.tensor_copy(o, p))
                E.do("scalar", lambda e: e.activation(
                    rn2[0:2, 0:T], rn2[0:2, 0:T], AF.Ln,
                    bias=P.ccol("eps1e12")[0:2, :]))
                E.do("scalar", lambda e: e.activation(
                    rn2[0:2, 0:T], rn2[0:2, 0:T], AF.Exp, scale=0.5))
                E.do("vector", lambda e: e.tensor_scalar_max(
                    rn2[0:2, 0:T], rn2[0:2, 0:T], 1e-12))
                E.do("vector", lambda e: e.reciprocal(rn2[0:2, 0:T],
                                                      rn2[0:2, 0:T]))
                for c2 in range(CT):
                    cw = min(MMF, T - c2 * MMF)
                    psb = pp.next()
                    E.do("tensor", lambda e, o=psb[:, 0:cw],
                         r=rn2[0:2, c2 * MMF:c2 * MMF + cw]:
                             e.matmul(o, ind2[:, :], r, start=True, stop=True))
                    E.do("vector", lambda e, o=stage[:, c2 * MMF:c2 * MMF + cw],
                         p=psb[:, 0:cw]: e.tensor_mul(o, o, p))
                mm = evac.m_holder[0]
                E.dma("sync", out_d[mm * 128:(mm + 1) * 128, :],
                      stage[:, 0:T])
        evac.m_holder = [0]
        def wrapper(m, c, ps):
            evac.m_holder[0] = m
            evac(m, c, ps)
        return wrapper

    proj_fm(P, pp, wpool, xt, wq_d.ap(), 1024, 1024, T, norm_evac(q_d))
    proj_fm(P, pp, wpool, xt, wk_d.ap(), 1024, 1024, T, norm_evac(k_d))

    def evac_v(m, c, ps):
        E.do("vector", lambda e, o=stage[:, c * MMF:c * MMF + ps.shape[1]],
             p=ps: e.tensor_copy(o, p))
        if c == ceil_div(T, MMF) - 1:
            E.dma("sync", v_d[m * 128:(m + 1) * 128, :], stage[:, 0:T])

    proj_fm(P, pp, wpool, xt, wv_d.ap(), 1024, 1024, T, evac_v)

    def evac_b(m, c, ps):
        E.do("scalar", lambda e, o=bst[0:16, c * MMF:c * MMF + ps.shape[1]],
             p=ps[0:16, 0:ps.shape[1]]: e.activation(o, p, AF.Sigmoid))
        if c == ceil_div(T, MMF) - 1:
            E.dma("sync", be_d.ap(), bst[:, 0:T])

    proj_fm(P, pp, wpool, xt, wb_d.ap(), 1024, 16, T, evac_b)
    P.finish()
    return P


def build_l3b():
    L = 4096
    NCK = 8
    P = Prog("l3b")
    E = P.E
    q_d = P.dram_in("qn", [512, L])
    k_d = P.dram_in("kn", [512, L])
    v_d = P.dram_in("v", [512, L])
    be_d = P.dram_in("beta", [8, L])
    ssel_d = P.dram_in("self_sel", [128, 64 * 128])
    csel_d = P.dram_in("colsel", [128, 64 * 128])
    psel_d = P.dram_in("pairsel", [8, 4 * 128])
    y_d = P.dram_out("y", [512, L])
    P.load_basics()
    pp = P.psum_pool(n=6)
    ypsum = P.psum("ypsum_ded", [128, MMF])
    kt = P.sbuf("kt3", [128, L])
    qt = P.sbuf("qt3", [128, L])
    vt3 = P.sbuf("vt3", [128, L])
    bk = P.sbuf("bk3", [128, L])
    be = P.sbuf("be3", [8, L])
    ssel = P.sbuf("ssel_sb", [128, 64 * 128])
    csel = P.sbuf("csel_sb", [128, 64 * 128])
    psel = P.sbuf("psel_sb", [8, 4 * 128])
    scol = P.sbuf("scol3", [128, 64])
    t1 = P.sbuf("t1_3", [128, MMF])
    aa = P.sbuf("aa3", [128, MMF])
    uu = P.sbuf("uu3", [128, MMF])
    ss = P.sbuf("ss3", [128, MMF])
    yt = P.sbuf("yt3", [128, MMF])
    ystage = P.sbuf("ystage3", [128, MMF])
    E.dma("sync", be[:, :], be_d.ap())
    E.dma("sync", ssel[:, :], ssel_d.ap())
    E.dma("sync", csel[:, :], csel_d.ap())
    E.dma("sync", psel[:, :], psel_d.ap())

    for pg in range(4):
        E.dma("sync", kt[:, :], k_d[pg * 128:(pg + 1) * 128, :])
        E.dma("sync", qt[:, :], q_d[pg * 128:(pg + 1) * 128, :])
        E.dma("sync", vt3[:, :], v_d[pg * 128:(pg + 1) * 128, :])
        for c in range(NCK):
            psb = pp.next()
            E.do("tensor", lambda e, o=psb[:, 0:MMF], c=c, pg=pg:
                 e.matmul(o, psel[:, pg * 128:(pg + 1) * 128],
                          be[0:8, c * MMF:(c + 1) * MMF],
                          start=True, stop=True))
            E.do("vector", lambda e, c=c, p=psb[:, 0:MMF]:
                 e.tensor_mul(bk[:, c * MMF:(c + 1) * MMF],
                              kt[:, c * MMF:(c + 1) * MMF], p))
        E.do("vector", lambda e: e.memset(scol[:, :], 0.0))
        for c in range(NCK):
            t0 = c * MMF
            for i in range(64):
                pk = pp.next()
                E.do("tensor", lambda e, o=pk[:, 0:MMF], i=i, t0=t0:
                     e.matmul(o, ssel[:, i * 128:(i + 1) * 128],
                              kt[:, t0:t0 + MMF], start=True, stop=True))
                pv = pp.next()
                E.do("tensor", lambda e, o=pv[:, 0:MMF], i=i, t0=t0:
                     e.matmul(o, ssel[:, i * 128:(i + 1) * 128],
                              vt3[:, t0:t0 + MMF], start=True, stop=True))
                E.do("vector", lambda e, p=pk[:, 0:MMF], t0=t0:
                     e.tensor_mul(t1[:, 0:MMF], bk[:, t0:t0 + MMF], p))
                E.do("vector", lambda e:
                     e.tensor_scalar(aa[:, 0:MMF], t1[:, 0:MMF], -1.0, 1.0,
                                     ALU.mult, ALU.add))
                E.do("vector", lambda e, p=pv[:, 0:MMF], t0=t0:
                     e.tensor_mul(uu[:, 0:MMF], bk[:, t0:t0 + MMF], p))
                E.do("vector", lambda e, i=i: e.tensor_tensor_scan(
                    ss[:, 0:MMF], aa[:, 0:MMF], uu[:, 0:MMF],
                    scol[:, i:i + 1], ALU.mult, ALU.add))
                E.do("vector", lambda e, i=i: e.tensor_copy(
                    scol[:, i:i + 1], ss[:, MMF - 1:MMF]))
                E.do("vector", lambda e, t0=t0: e.tensor_mul(
                    yt[:, 0:MMF], ss[:, 0:MMF], qt[:, t0:t0 + MMF]))
                E.do("tensor", lambda e, o=ypsum[:, 0:MMF], i=i:
                     e.matmul(o, csel[:, i * 128:(i + 1) * 128], yt[:, 0:MMF],
                              start=(i == 0), stop=(i == 63)))
            E.do("vector", lambda e, p=ypsum[:, 0:MMF]:
                 e.tensor_copy(ystage[:, 0:MMF], p))
            E.dma("sync", y_d[pg * 128:(pg + 1) * 128, t0:t0 + MMF],
                  ystage[:, 0:MMF])
    P.finish()
    return P


def host_l3b(q_fm, k_fm, v_fm, beta_fm):
    P = build_l3b()
    base = Prog.basic_inputs()
    ssel = np.zeros((128, 64 * 128), np.float32)
    csel = np.zeros((128, 64 * 128), np.float32)
    for i in range(64):
        for p2 in range(2):
            ssel[p2 * 64 + i, i * 128 + p2 * 64:i * 128 + (p2 + 1) * 64] = 1.0
            csel[p2 * 64:(p2 + 1) * 64, i * 128 + p2 * 64 + i] = 1.0
    psel = np.zeros((8, 4 * 128), np.float32)
    for pg in range(4):
        psel[2 * pg, pg * 128:pg * 128 + 64] = 1.0
        psel[2 * pg + 1, pg * 128 + 64:(pg + 1) * 128] = 1.0
    ims = []
    for c in range(NCORES):
        b = c // 2
        hh = (c % 2) * 8
        rows = slice(hh * 64, (hh + 8) * 64)
        tok = slice(b * 4096, (b + 1) * 4096)
        ims.append(dict(qn=np.ascontiguousarray(q_fm[rows, tok]),
                        kn=np.ascontiguousarray(k_fm[rows, tok]),
                        v=np.ascontiguousarray(v_fm[rows, tok]),
                        beta=np.ascontiguousarray(beta_fm[hh:hh + 8, tok]),
                        self_sel=ssel, colsel=csel, pairsel=psel, **base))
    res = P.run(ims)
    y = np.zeros((1024, NTOK), np.float32)
    for c in range(NCORES):
        b = c // 2
        hh = (c % 2) * 8
        y[hh * 64:(hh + 8) * 64, b * 4096:(b + 1) * 4096] = res.results[c]["y"]
    return y


def build_l3c():
    T = TCORE
    P = Prog("l3c")
    E = P.E
    o_d = P.dram_in("o_in", [1024, T])
    xn_d = P.dram_in("xn", [1024, T])
    lngb_d = P.dram_in("lngb", [128, 16])
    wg_d = P.dram_in("wg", [1024, 1024])
    wo_d = P.dram_in("wout", [1024, 1024])
    s_d = P.dram_out("s", [1024, T])
    P.load_basics()
    pp = P.psum_pool()
    ot = [P.sbuf(f"o{k}", [128, T]) for k in range(8)]
    xnt = [P.sbuf(f"xn{k}", [128, T]) for k in range(8)]
    scr = P.sbuf("scr", [128, MMF])
    ab = (P.sbuf("abb", [1, T]), P.sbuf("aba", [1, T]))
    lngb = P.sbuf("lngb_sb", [128, 16])
    wpool = P.sbuf_pool(2, [128, MMF], name="wp")
    stage = P.sbuf("stage", [128, T])
    for k in range(8):
        E.dma("sync", ot[k][:, :], o_d[k * 128:(k + 1) * 128, :])
        E.dma("sync", xnt[k][:, :], xn_d[k * 128:(k + 1) * 128, :])
    E.dma("sync", lngb[:, :], lngb_d.ap())
    ln_rows(P, pp, ot, T, 1024, scr, ab)
    apply_ab(P, pp, ot, ot, ab, T, 1024)
    for k in range(8):
        E.do("vector", lambda e, k=k: e.tensor_scalar(
            ot[k][:, :], ot[k][:, :], lngb[:, k:k + 1], lngb[:, 8 + k:9 + k],
            ALU.mult, ALU.add))

    def evac_g(m, c, ps):
        E.do("scalar", lambda e, o=stage[:, c * MMF:c * MMF + ps.shape[1]],
             p=ps: e.activation(o, p, AF.Silu))
        if c == ceil_div(T, MMF) - 1:
            E.do("vector", lambda e, m=m: e.tensor_mul(
                ot[m][:, :], ot[m][:, :], stage[:, 0:T]))

    proj_fm(P, pp, wpool, xnt, wg_d.ap(), 1024, 1024, T, evac_g)

    def evac_s(m, c, ps):
        E.do("vector", lambda e, o=stage[:, c * MMF:c * MMF + ps.shape[1]],
             p=ps: e.tensor_copy(o, p))
        if c == ceil_div(T, MMF) - 1:
            E.dma("sync", s_d[m * 128:(m + 1) * 128, :], stage[:, 0:T])

    proj_fm(P, pp, wpool, ot, wo_d.ap(), 1024, 1024, T, evac_s)
    P.finish()
    return P


L3D_DEBUG = False


def build_l3d():
    T = TCORE
    TS = 512
    NS = T // TS
    P = Prog("l3d")
    E = P.E
    xn_d = P.dram_in("xn", [1024, T])
    lte_d = P.dram_in("lte", [1024, 8])
    lto_d = P.dram_in("lto", [1024, 8])
    ltT_d = P.dram_in("ltT", [16, 1024])
    w1_d = P.dram_in("w1t", [1024, 4096])
    b1_d = P.dram_in("b1c", [128, 32])
    w2_d = P.dram_in("w2t", [4096, 1024])
    b2_d = P.dram_in("b2c", [128, 8])
    sc_d = P.dram_in("scale11", [1, 1])
    sel4_d = P.dram_in("sel4", [4, 4 * 128])
    g_d = P.dram_out("g", [1024, T])
    if L3D_DEBUG:
        dbg_h0 = P.dram_out("dbg_h0", [128, 512])
        dbg_h31 = P.dram_out("dbg_h31", [128, 512])
        dbg_p = P.dram_out("dbg_p", [4, T])
    P.load_basics()
    pp = P.psum_pool(n=6)
    rbded = P.psum("rbded", [128, MMF])
    pbded = P.psum("pbded", [128, MMF])
    xt = [P.sbuf(f"x{k}", [128, T]) for k in range(8)]
    ht = [P.sbuf(f"h{k}", [128, TS]) for k in range(32)]
    lte = [P.sbuf(f"lte{k}", [128, 8]) for k in range(8)]
    lto = [P.sbuf(f"lto{k}", [128, 8]) for k in range(8)]
    b1 = P.sbuf("b1d_sb", [128, 32])
    b2 = P.sbuf("b2d_sb", [128, 8])
    sel4 = P.sbuf("sel4_sb", [4, 4 * 128])
    scrow = P.sbuf("scrow", [1, 1])
    sccol = P.sbuf("sccol", [128, 1])
    colset = P.sbuf("colset", [128, 32])
    pstage = P.sbuf("pstage", [4, T])
    rt = [P.sbuf(f"rt{k}", [4, T]) for k in range(4)]
    tsb = P.sbuf("tsb", [128, MMF])
    t2sb = P.sbuf("t2sb", [128, MMF])
    wpool = P.sbuf_pool(2, [128, MMF], name="wp")
    for k in range(8):
        E.dma("sync", xt[k][:, :], xn_d[k * 128:(k + 1) * 128, :])
        E.dma("sync", lte[k][:, :], lte_d[k * 128:(k + 1) * 128, :])
        E.dma("sync", lto[k][:, :], lto_d[k * 128:(k + 1) * 128, :])
    E.dma("sync", b1[:, :], b1_d.ap())
    E.dma("sync", b2[:, :], b2_d.ap())
    E.dma("sync", sel4[:, :], sel4_d.ap())
    E.dma("sync", scrow[:, :], sc_d.ap())
    for m1 in range(2):
        E.dma("sync",
              bass.AP(colset, m1 * 4, [[32, 128], [8, 4], [1, 4]]),
              bass.AP(ltT_d, 8 * 1024 + m1 * 512,
                      [[4, 128], [2048, 4], [1, 4]]))
    psx = pp.next()
    E.do("tensor", lambda e: e.matmul(psx[:, 0:1], P.ones_row[0:1, :],
                                      scrow[0:1, 0:1], start=True, stop=True))
    E.do("vector", lambda e: e.tensor_copy(sccol[:, :], psx[:, 0:1]))
    for k in range(8):
        E.do("vector", lambda e, k=k: e.tensor_scalar(
            lte[k][:, :], lte[k][:, :], sccol[:, 0:1], None, ALU.mult))
    CT = ceil_div(T, MMF)
    for c in range(CT):
        cw = min(MMF, T - c * MMF)
        psp = pp.next()
        for k in range(8):
            E.do("tensor", lambda e, o=psp[0:4, 0:cw], k=k, c=c, cw=cw,
                 st=(k == 0): e.matmul(o, lto[k][:, 0:4],
                                       xt[k][:, c * MMF:c * MMF + cw],
                                       start=st, stop=(k == 7)))
        E.do("vector", lambda e, c=c, cw=cw, p=psp[0:4, 0:cw]:
             e.tensor_copy(pstage[0:4, c * MMF:c * MMF + cw], p))

    for sub in range(NS):
        xc0 = sub * TS

        def evac1(m, c, ps):
            cw = ps.shape[1]
            pb = pbded
            E.do("tensor", lambda e, o=pb[:, 0:cw],
                 w=sel4[:, (m // 8) * 128:(m // 8 + 1) * 128],
                 x=pstage[0:4, xc0:xc0 + cw]:
                 e.matmul(o, w, x, start=True, stop=True))
            E.do("vector", lambda e, m=m, p=pb[:, 0:cw]:
                 e.tensor_scalar(tsb[:, 0:cw], p,
                                 lte[m % 8][:, m // 8:m // 8 + 1], None,
                                 ALU.mult))
            E.do("vector", lambda e, p=ps: e.tensor_add(t2sb[0:128, 0:cw],
                                                        tsb[0:128, 0:cw], p))
            E.do("scalar", lambda e, m=m, o=ht[m][:, 0:cw]:
                 e.activation(o, t2sb[0:128, 0:cw], AF.Gelu,
                              bias=b1[:, m:m + 1]))

        proj_fm_wide(P, pp, wpool, xt, w1_d.ap(), 1024, 4096, TS, evac1,
                     xcol0=xc0)
        if L3D_DEBUG and sub == 0:
            E.dma("sync", dbg_h0.ap(), ht[0][:, :])
            E.dma("sync", dbg_h31.ap(), ht[31][:, :])
            E.dma("sync", dbg_p.ap(), pstage[:, :])
        for hb in range(4):
            psr = pp.next()
            for kk in range(8):
                E.do("tensor", lambda e, o=psr[0:4, 0:TS], hb=hb, kk=kk,
                     st=(kk == 0):
                     e.matmul(o, lto[kk][:, 4:8],
                              ht[hb * 8 + kk][:, 0:TS], start=st,
                              stop=(kk == 7)))
            E.do("vector", lambda e, o=rt[hb][0:4, xc0:xc0 + TS],
                 p=psr[0:4, 0:TS]:
                 e.tensor_scalar(o, p, sccol[0:4, 0:1], None, ALU.mult))

        def evac2(m, c, ps):
            cw = ps.shape[1]
            E.do("vector", lambda e, m=m, p=ps:
                 e.tensor_scalar(tsb[:, 0:cw], p, 1.0, b2[:, m:m + 1],
                                 ALU.mult, ALU.add))
            for hb in range(4):
                rb = rbded
                E.do("tensor", lambda e, o=rb[:, 0:cw],
                     w=sel4[:, (m // 2) * 128:(m // 2 + 1) * 128],
                     x=rt[hb][0:4, xc0:xc0 + cw]:
                     e.matmul(o, w, x, start=True, stop=True))
                E.do("vector", lambda e, m=m, hb=hb, p=rb[:, 0:cw]:
                     e.scalar_tensor_tensor(tsb[:, 0:cw], p,
                                            colset[:, m * 4 + hb:m * 4 + hb + 1],
                                            tsb[:, 0:cw], ALU.mult, ALU.add))
            E.dma("sync", g_d[m * 128:(m + 1) * 128, xc0:xc0 + cw],
                  tsb[:, 0:cw])

        proj_fm_wide(P, pp, wpool, ht, w2_d.ap(), 4096, 1024, TS, evac2)
    P.finish()
    return P



def host_l3a(x_fm, Wq, Wk, Wv, Wbeta):
    P = build_l3a()
    base = Prog.basic_inputs()
    ind2 = np.zeros((2, 128), np.float32)
    ind2[0, 0:64] = 1.0
    ind2[1, 64:128] = 1.0
    ims = []
    for c in range(NCORES):
        tok = slice(c * TCORE, (c + 1) * TCORE)
        ims.append(dict(x=np.ascontiguousarray(x_fm[:, tok]),
                        wq=np.ascontiguousarray(Wq),
                        wk=np.ascontiguousarray(Wk),
                        wv=np.ascontiguousarray(Wv),
                        wb=np.ascontiguousarray(Wbeta),
                        ind2=ind2,
                        ind2T_in=np.ascontiguousarray(ind2.T), **base))
    res = P.run(ims)
    out = {}
    for nm in ("xn", "qn", "kn", "v", "beta"):
        out[nm] = np.concatenate([r[nm] for r in res.results], axis=1)
    return out


def host_l3c(y_fm, xn_fm, ln_g, ln_b, Wg, Wout):
    P = build_l3c()
    base = Prog.basic_inputs()
    lngb = np.zeros((128, 16), np.float32)
    lngb[:, 0:8] = ln_g.reshape(8, 128).T
    lngb[:, 8:16] = ln_b.reshape(8, 128).T
    ims = []
    for c in range(NCORES):
        tok = slice(c * TCORE, (c + 1) * TCORE)
        ims.append(dict(o_in=np.ascontiguousarray(y_fm[:, tok]),
                        xn=np.ascontiguousarray(xn_fm[:, tok]),
                        lngb=lngb, wg=np.ascontiguousarray(Wg),
                        wout=np.ascontiguousarray(Wout), **base))
    res = P.run(ims)
    return np.concatenate([r["s"] for r in res.results], axis=1)


def host_l3d(xn_fm, W1, b1, W2, b2, scale):
    P = build_l3d()
    base = Prog.basic_inputs()
    sel4 = np.zeros((4, 4 * 128), np.float32)
    for k in range(4):
        sel4[k, k * 128:(k + 1) * 128] = 1.0
    w1t = np.ascontiguousarray(W1.T)
    w2t = np.ascontiguousarray(W2.T)
    b1c = b1.reshape(32, 128).T.astype(np.float32).copy()
    b2c = b2.reshape(8, 128).T.astype(np.float32).copy()
    ims = []
    for c in range(NCORES):
        b = c // 2
        tok = slice(c * TCORE, (c + 1) * TCORE)
        lt = np.ascontiguousarray(xn_fm[:, b * 4096 + 1:b * 4096 + 17])
        ims.append(dict(xn=np.ascontiguousarray(xn_fm[:, tok]),
                        lte=np.ascontiguousarray(lt[:, ::2]),
                        lto=np.ascontiguousarray(lt[:, 1::2]),
                        ltT=np.ascontiguousarray(lt.T),
                        w1t=w1t, b1c=b1c, w2t=w2t, b2c=b2c,
                        scale11=np.asarray(scale, np.float32).reshape(1, 1),
                        sel4=sel4, **base))
    res = P.run(ims)
    return np.concatenate([r["g"] for r in res.results], axis=1)



def kernel(**inputs):
    g = lambda n: np.ascontiguousarray(np.asarray(inputs[n], np.float32))
    x = g("L0_attn_x")
    x_fm = np.ascontiguousarray(x.reshape(NTOK, DM).T)

    o = host_l0a(x_fm, g("L0_attn_Win"), g("L0_attn_conv_w"),
                 g("L0_attn_conv_b"), g("L0_attn_dt_bias"))
    y = host_l0b(o["xssm"], o["dtp"], o["bmat"], o["cmat"],
                 g("L0_attn_A_log"), g("L0_attn_D"))
    ffn0 = {k[7:]: g(k) for k in inputs if k.startswith("L0_ffn_")}
    x_fm = host_l0c(y, o["z"], o["xn"], x_fm, g("L0_attn_norm_w"),
                    g("L0_attn_Wout"), ffn0)

    o1 = host_l1a(x_fm, g("L1_attn_in_proj_w"), g("L1_attn_conv_w"),
                  g("L1_attn_x_proj_w"))
    y1 = host_l1b(o1["dtb"], g("L1_attn_dt_w"), g("L1_attn_dt_b"),
                  g("L1_attn_A_param"))
    s1 = host_l1c(y1, o1["u"], o1["z"], g("L1_attn_D_param"),
                  g("L1_attn_out_w"), g("L1_attn_out_b"))
    xf = o1["xn"]
    for li in range(2):
        xf = host_l1ffn_layer(xf, g("L1_ffn_wqkv")[li],
                              g("L1_ffn_q_scale")[li], g("L1_ffn_k_scale")[li],
                              g("L1_ffn_wo")[li], g("L1_ffn_ln_g")[li],
                              g("L1_ffn_ln_b")[li], g("L1_ffn_w1")[li],
                              g("L1_ffn_b1")[li], g("L1_ffn_w2")[li],
                              g("L1_ffn_b2")[li])
    x_fm = host_combine(x_fm, s1, xf)

    s2, xn2 = host_l2(x_fm, g("L2_attn_rms_weight"), g("L2_attn_W_in"),
                      g("L2_attn_gate_bias"), g("L2_attn_W_out"))
    g2 = host_l2c(xn2, g("L2_ffn_w1"), g("L2_ffn_b1"), g("L2_ffn_w2"),
                  g("L2_ffn_b2"))
    x_fm = host_combine(x_fm, s2, g2)

    o3 = host_l3a(x_fm, g("L3_attn_Wq"), g("L3_attn_Wk"), g("L3_attn_Wv"),
                  g("L3_attn_Wbeta"))
    y3 = host_l3b(o3["qn"], o3["kn"], o3["v"], o3["beta"])
    s3 = host_l3c(y3, o3["xn"], g("L3_attn_ln_g"), g("L3_attn_ln_b"),
                  g("L3_attn_Wg"), g("L3_attn_Wout"))
    g3 = host_l3d(o3["xn"], g("L3_ffn_W1"), g("L3_ffn_b1"), g("L3_ffn_W2"),
                  g("L3_ffn_b2"), g("L3_ffn_scale"))
    x_fm = host_combine(x_fm, s3, g3)

    return np.ascontiguousarray(x_fm.T).reshape(4, 4096, DM)


# version 06
# speedup vs baseline: 1.1617x; 1.1617x over previous
import numpy as np
import concourse.bass as bass
import concourse.mybir as mybir
from concourse.bass_utils import run_bass_kernel_spmd
from contextlib import ExitStack

F32 = mybir.dt.float32
BF16 = mybir.dt.bfloat16
AF = mybir.ActivationFunctionType
ALU = mybir.AluOpType
AX = mybir.AxisListType

NCORES = 8
NTOK = 16384
TCORE = NTOK // NCORES
DM = 1024
MMF = 512

CONST_NAMES = ["eps1e5", "eps1e6", "one", "neg1", "half", "eps1e12", "neg30"]
CONST_VALS = {"eps1e5": 1e-5, "eps1e6": 1e-6, "one": 1.0, "neg1": -1.0,
              "half": 0.5, "eps1e12": 1e-12, "neg30": -1e30}


def const_array():
    v = np.array([CONST_VALS[n] for n in CONST_NAMES], np.float32)
    return np.broadcast_to(v, (128, len(CONST_NAMES))).copy()


def ceil_div(a, b):
    return (a + b - 1) // b



class Em:
    ENGINES = ("tensor", "vector", "scalar", "gpsimd", "sync")

    def __init__(self, nc, stack):
        self.nc = nc
        self.stack = stack
        self.q = {e: [] for e in self.ENGINES}
        self.g = stack.enter_context(nc.semaphore("g"))
        self.gcnt = 0
        self.last_eng = None

    def do(self, eng, fn, dma=False):
        pre = self.gcnt
        h = self.g
        if pre > 0:
            self.q[eng].append(lambda e, h=h, v=pre: e.wait_ge(h, v))
        amt = 16 if dma else 1
        def run(e, fn=fn, h=h, amt=amt):
            fn(e).then_inc(h, amt)
        self.q[eng].append(run)
        self.gcnt += amt
        self.last_eng = eng
        return self.gcnt

    def dma(self, eng, out, in_, allow_slow=False):
        if allow_slow:
            return self.do(eng, lambda e, o=out, i=in_: e.dma_start(
                out=o, in_=i, allow_slow_non_contiguous=True), dma=True)
        return self.do(eng, lambda e, o=out, i=in_: e.dma_start(out=o, in_=i),
                       dma=True)

    def build(self):
        with self.nc.Block() as blk:
            for eng in self.ENGINES:
                ops = self.q[eng]
                if not ops:
                    continue
                def runner(e, ops=ops):
                    for f in ops:
                        f(e)
                getattr(blk, eng)(runner)
        self.q = {e: [] for e in self.ENGINES}


class Rot:
    def __init__(self, tiles):
        self.tiles = tiles
        self.j = 0

    def next(self):
        t = self.tiles[self.j % len(self.tiles)]
        self.j += 1
        return t


class Prog:
    def __init__(self, name):
        self.nc = bass.Bass("TRN2", target_bir_lowering=False)
        self.stack = ExitStack()
        self.E = Em(self.nc, self.stack)
        self.name = name
        self._holders = []
        self._phase = name
        self._phase_stack = None
        self.bind = {}
        self.fused = False
        self.constsb = None
        self.eye = None

    def begin_phase(self, prefix):
        if getattr(self, "next_tag", None):
            prefix = self.next_tag
            self.next_tag = None
        self._phase = prefix
        if self.fused:
            if self._phase_stack is not None:
                self._phase_stack.close()
            self._phase_stack = ExitStack()

    def _alloc_stack(self):
        return self._phase_stack if (self.fused and
                                     self._phase_stack is not None)             else self.stack

    def dram_in(self, name, shape, dtype=F32):
        if name in self.bind:
            return self.bind[name]
        return self.nc.dram_tensor(self._phase + "_" + name, list(shape),
                                   dtype, kind="ExternalInput")

    def dram_out(self, name, shape, dtype=F32):
        if name in self.bind:
            return self.bind[name]
        kind = None if self.fused else "ExternalOutput"
        if kind:
            return self.nc.dram_tensor(self._phase + "_" + name, list(shape),
                                       dtype, kind=kind)
        return self.nc.dram_tensor(self._phase + "_" + name, list(shape),
                                   dtype)

    def dram_tmp(self, name, shape, dtype=F32):
        return self.nc.dram_tensor(self._phase + "_" + name, list(shape),
                                   dtype)

    def sbuf(self, name, shape, dtype=F32):
        return self._alloc_stack().enter_context(
            self.nc.sbuf_tensor(self._phase + "_" + name, list(shape), dtype))

    def psum(self, name, shape, dtype=F32):
        return self._alloc_stack().enter_context(
            self.nc.psum_tensor(self._phase + "_" + name, list(shape), dtype))

    def psum_pool(self, n=8, w=MMF, name="pp"):
        return Rot([self.psum(f"{name}{i}", [128, w]) for i in range(n)])

    def sbuf_pool(self, n, shape, dtype=F32, name="sp"):
        return Rot([self.sbuf(f"{name}{i}", list(shape), dtype)
                    for i in range(n)])

    def load_basics(self, eye=False):
        if getattr(self, "constsb", None) is not None:
            if eye and self.eye is None:
                E = self.E
                ed = self.nc.dram_tensor("eye128", [128, 128], F32,
                                         kind="ExternalInput")
                self.eye = self.stack.enter_context(
                    self.nc.sbuf_tensor("eyesb", [128, 128], F32))
                E.dma("sync", self.eye[:, :], ed.ap())
            return
        E = self.E
        cd = self.nc.dram_tensor("consts", [128, len(CONST_NAMES)], F32,
                                 kind="ExternalInput")
        self.constsb = self.stack.enter_context(
            self.nc.sbuf_tensor("constsb", [128, len(CONST_NAMES)], F32))
        E.dma("sync", self.constsb[:, :], cd.ap())
        od = self.nc.dram_tensor("onescol", [128, 1], F32,
                                 kind="ExternalInput")
        self.ones = self.stack.enter_context(
            self.nc.sbuf_tensor("onessb", [128, 1], F32))
        E.dma("sync", self.ones[:, :], od.ap())
        orow = self.nc.dram_tensor("onesrow", [1, 128], F32,
                                   kind="ExternalInput")
        self.ones_row = self.stack.enter_context(
            self.nc.sbuf_tensor("onesrowsb", [1, 128], F32))
        E.dma("sync", self.ones_row[:, :], orow.ap())
        self.eye = None
        if eye:
            ed = self.nc.dram_tensor("eye128", [128, 128], F32,
                                     kind="ExternalInput")
            self.eye = self.stack.enter_context(
                self.nc.sbuf_tensor("eyesb", [128, 128], F32))
            E.dma("sync", self.eye[:, :], ed.ap())

    def ccol(self, name):
        i = CONST_NAMES.index(name)
        return self.constsb[:, i:i + 1]

    def run(self, in_maps, **kw):
        return run_bass_kernel_spmd(self.nc, in_maps,
                                    core_ids=list(range(len(in_maps))), **kw)

    def finish(self, out_dma_engine="sync"):
        E = self.E
        h = E.g
        v = E.gcnt
        E.q[out_dma_engine].append(lambda e, h=h, v=v: e.wait_ge(h, v))
        E.build()
        if self._phase_stack is not None:
            self._phase_stack.close()
            self._phase_stack = None
        self.stack.close()

    def wrap(self, d):
        basics = {"consts", "onescol", "onesrow", "eye128"}
        return {(k if k in basics else self._phase + "_" + k): v
                for k, v in d.items()}

    @staticmethod
    def basic_inputs(eye=False):
        d = {"consts": const_array(),
             "onescol": np.ones((128, 1), np.float32),
             "onesrow": np.ones((1, 128), np.float32)}
        if eye:
            d["eye128"] = np.eye(128, dtype=np.float32)
        return d


def _W(P, d):
    return P.wrap(d)



def load_fm(P, name, src_ap, d, T, dtype=F32, pool=None):
    KT = ceil_div(d, 128)
    tiles = []
    for k in range(KT):
        kw = min(128, d - k * 128)
        t = P.sbuf(f"{name}{k}", [128, T], dtype) if pool is None else pool[k]
        P.E.dma("sync", t[0:kw, 0:T], src_ap[k * 128:k * 128 + kw, 0:T])
        tiles.append(t)
    return tiles


def cast_tiles(P, name, xtiles, T, eng="scalar"):
    out = []
    for i, t in enumerate(xtiles):
        b = P.sbuf(f"{name}{i}", [128, T], BF16)
        P.E.do(eng, lambda e, o=b[:, 0:T], x=t[:, 0:T]:
               e.activation(o, x, AF.Copy))
        out.append(b)
    return out


def cast_into(P, dst, src, T, eng="scalar"):
    for d, s in zip(dst, src):
        P.E.do(eng, lambda e, o=d[:, 0:T], x=s[:, 0:T]:
               e.activation(o, x, AF.Copy))


def proj_fm(P, pp, wpool, xtiles, wt_dram, d_in, d_out, T, evac, xcol0=0,
            w_row0=0, w_col0=0, bfpool=None):
    E = P.E
    KT = ceil_div(d_in, 128)
    MT = ceil_div(d_out, 128)
    CT = ceil_div(T, MMF)
    for m in range(MT):
        mw = min(128, d_out - m * 128)
        banks = [pp.next() for _ in range(CT)]
        for k in range(KT):
            kw = min(128, d_in - k * 128)
            wtile = wpool.next()
            E.dma("sync", wtile[0:kw, 0:mw],
                  wt_dram[w_row0 + k * 128:w_row0 + k * 128 + kw,
                          w_col0 + m * 128:w_col0 + m * 128 + mw])
            if bfpool is not None:
                wb = bfpool.next()
                E.do("scalar", lambda e, o=wb[0:kw, 0:mw],
                     x=wtile[0:kw, 0:mw]: e.activation(o, x, AF.Copy))
                wtile = wb
            for c in range(CT):
                cw = min(MMF, T - c * MMF)
                E.do("tensor",
                     lambda e, o=banks[c][0:mw, 0:cw], w=wtile[0:kw, 0:mw],
                            x=xtiles[k][0:kw, xcol0 + c * MMF:
                                        xcol0 + c * MMF + cw],
                            st=(k == 0), sp=(k == KT - 1):
                         e.matmul(o, w, x, start=st, stop=sp))
        for c in range(CT):
            cw = min(MMF, T - c * MMF)
            evac(m, c, banks[c][0:mw, 0:cw])


def ln_rows(P, pp, xtiles, T, d, scr, ab, rms=False, eps_name="eps1e5",
            xcol0=0):
    brow, arow = ab
    E = P.E
    KT = d // 128
    CT = ceil_div(T, MMF)
    for c in range(CT):
        cw = min(MMF, T - c * MMF)
        bank_s = pp.next()
        bank_q = pp.next()
        if not rms:
            for k in range(KT):
                E.do("tensor",
                     lambda e, o=bank_s[0:1, 0:cw], w=P.ones[:, 0:1],
                            x=xtiles[k][:, xcol0 + c * MMF:xcol0 + c * MMF + cw],
                            st=(k == 0):
                         e.matmul(o, w, x, start=st, stop=(k == KT - 1)))
        for k in range(KT):
            E.do("scalar",
                 lambda e, o=scr[:, 0:cw],
                        x=xtiles[k][:, xcol0 + c * MMF:xcol0 + c * MMF + cw]:
                     e.activation(o, x, AF.Square))
            E.do("tensor",
                 lambda e, o=bank_q[0:1, 0:cw], w=P.ones[:, 0:1],
                        x=scr[:, 0:cw], st=(k == 0):
                     e.matmul(o, w, x, start=st, stop=(k == KT - 1)))
        if not rms:
            E.do("vector",
                 lambda e, o=brow[0:1, c * MMF:c * MMF + cw], i=bank_s[0:1, 0:cw]:
                     e.tensor_scalar(o, i, 1.0 / d, None, ALU.mult))
        E.do("vector",
             lambda e, o=arow[0:1, c * MMF:c * MMF + cw], i=bank_q[0:1, 0:cw]:
                 e.tensor_scalar(o, i, 1.0 / d, None, ALU.mult))
    if not rms:
        for c in range(CT):
            cw = min(MMF, T - c * MMF)
            E.do("vector",
                 lambda e, o=scr[0:1, 0:cw], a=brow[0:1, c * MMF:c * MMF + cw]:
                     e.tensor_mul(o, a, a))
            E.do("vector",
                 lambda e, o=arow[0:1, c * MMF:c * MMF + cw],
                        a=arow[0:1, c * MMF:c * MMF + cw], b=scr[0:1, 0:cw]:
                     e.tensor_sub(o, a, b))
    E.do("scalar", lambda e: e.activation(
        arow[0:1, 0:T], arow[0:1, 0:T], AF.Ln, bias=P.ccol(eps_name)[0:1, :]))
    E.do("scalar", lambda e: e.activation(
        arow[0:1, 0:T], arow[0:1, 0:T], AF.Exp, scale=-0.5))
    if not rms:
        E.do("vector", lambda e: e.tensor_mul(
            brow[0:1, 0:T], brow[0:1, 0:T], arow[0:1, 0:T]))
        E.do("vector", lambda e: e.tensor_scalar(
            brow[0:1, 0:T], brow[0:1, 0:T], -1.0, None, ALU.mult))


def apply_ab(P, pp, xtiles, out_tiles, ab, T, d, rms=False, out_dtype=None):
    brow, arow = ab
    E = P.E
    KT = d // 128
    CT = ceil_div(T, MMF)
    for c in range(CT):
        cw = min(MMF, T - c * MMF)
        ba = pp.next()
        E.do("tensor", lambda e, o=ba[:, 0:cw],
             r=arow[0:1, c * MMF:c * MMF + cw]:
                 e.matmul(o, P.ones_row[0:1, :], r, start=True, stop=True))
        if not rms:
            bb = pp.next()
            E.do("tensor", lambda e, o=bb[:, 0:cw],
                 r=brow[0:1, c * MMF:c * MMF + cw]:
                     e.matmul(o, P.ones_row[0:1, :], r, start=True, stop=True))
        for k in range(KT):
            E.do("vector",
                 lambda e, o=out_tiles[k][:, c * MMF:c * MMF + cw],
                        x=xtiles[k][:, c * MMF:c * MMF + cw], a=ba[:, 0:cw]:
                     e.tensor_mul(o, x, a))
            if not rms:
                E.do("vector",
                     lambda e, o=out_tiles[k][:, c * MMF:c * MMF + cw],
                            b=bb[:, 0:cw]:
                         e.tensor_add(o, o, b))


def softplus_cols(P, rows_tile, p, T, tmp, bias_col=None):
    E = P.E
    kw = {} if bias_col is None else {"bias": bias_col[0:p, :]}
    E.do("scalar", lambda e: e.activation(
        tmp[0:p, 0:T], rows_tile[0:p, 0:T], AF.Abs, **kw))
    E.do("scalar", lambda e: e.activation(
        tmp[0:p, 0:T], tmp[0:p, 0:T], AF.Exp, scale=-1.0))
    E.do("scalar", lambda e: e.activation(
        tmp[0:p, 0:T], tmp[0:p, 0:T], AF.Ln, bias=P.ccol("one")[0:p, :]))
    E.do("scalar", lambda e: e.activation(
        rows_tile[0:p, 0:T], rows_tile[0:p, 0:T], AF.Relu, **kw))
    E.do("vector", lambda e: e.tensor_add(
        rows_tile[0:p, 0:T], rows_tile[0:p, 0:T], tmp[0:p, 0:T]))



L0_DIN = 1024
L0_DIN_PROJ = 4384
L0_DINNER = 2048
L0_CONVD = 2304
L0_NH = 32
L0_HD = 64
L0_DS = 128


def build_l0a(P=None):
    standalone = P is None
    if standalone:
        P = Prog("l0a")
    P.begin_phase("l0a")
    E = P.E
    T = TCORE
    TH = T + 3
    x_d = P.dram_in("x", [DM, TH])
    wt_d = P.dram_in("wt_in", [L0_DIN, L0_DIN_PROJ])
    convw_d = P.dram_in("conv_w", [L0_CONVD, 4])
    convb_d = P.dram_in("conv_b", [L0_CONVD, 1])
    dtb_d = P.dram_in("dt_bias", [32, 1])
    xn_d = P.dram_out("xn", [DM, T])
    z_d = P.dram_out("z", [L0_DINNER, T])
    xssm_d = P.dram_out("xssm", [L0_DINNER, T])
    b_d = P.dram_out("bmat", [L0_DS, T])
    c_d = P.dram_out("cmat", [L0_DS, T])
    dtp_d = P.dram_out("dtp", [32, T])

    P.load_basics()
    pp = P.psum_pool()
    xt = [P.sbuf(f"x{k}", [128, TH]) for k in range(8)]
    scr = P.sbuf("scr", [128, MMF])
    ab = (P.sbuf("abb", [1, TH]), P.sbuf("aba", [1, TH]))
    wpool = P.sbuf_pool(2, [128, 128], name="wp")
    convw = P.sbuf("convw", [128, 4 * 18])
    convb = P.sbuf("convb", [128, 18])
    dtb = P.sbuf("dtb", [32, 1])
    xbc_buf = P.sbuf("xbcbuf", [128, TH])
    xcv_buf = P.sbuf("xcvbuf", [128, T])
    zstage = P.sbuf("zstage", [128, TH])
    dt_full = P.sbuf("dtfull", [32, TH])
    dtt = P.sbuf("dtt", [32, T])

    for k in range(8):
        E.dma("sync", xt[k][:, :], x_d[k * 128:(k + 1) * 128, :])
    for k in range(18):
        E.dma("sync", convw[:, 4 * k:4 * k + 4],
              convw_d[k * 128:(k + 1) * 128, :])
        E.dma("sync", convb[:, k:k + 1], convb_d[k * 128:(k + 1) * 128, :])
    E.dma("sync", dtb[:, :], dtb_d.ap())

    ln_rows(P, pp, xt, TH, DM, scr, ab)
    apply_ab(P, pp, xt, xt, ab, TH, DM)
    for k in range(8):
        E.dma("sync", xn_d[k * 128:(k + 1) * 128, :], xt[k][:, 3:TH])

    def conv_tile(kk):
        for j in range(4):
            if j == 0:
                E.do("vector",
                     lambda e, o=xcv_buf[:, 0:T], x=xbc_buf[:, 0:T],
                            w=convw[:, 4 * kk:4 * kk + 1]:
                         e.tensor_scalar(o, x, w, None, ALU.mult))
            else:
                E.do("vector",
                     lambda e, o=xcv_buf[:, 0:T], x=xbc_buf[:, j:j + T],
                            w=convw[:, 4 * kk + j:4 * kk + j + 1]:
                         e.scalar_tensor_tensor(o, x, w, xcv_buf[:, 0:T],
                                                ALU.mult, ALU.add))
        E.do("scalar",
             lambda e, o=xcv_buf[:, 0:T], x=xcv_buf[:, 0:T],
                    b=convb[:, kk:kk + 1]:
                 e.activation(o, x, AF.Silu, bias=b))
        if kk < 16:
            E.dma("sync", xssm_d[kk * 128:(kk + 1) * 128, :], xcv_buf[:, 0:T])
        elif kk == 16:
            E.dma("sync", b_d.ap(), xcv_buf[:, 0:T])
        else:
            E.dma("sync", c_d.ap(), xcv_buf[:, 0:T])

    CT = ceil_div(TH, MMF)

    def evac2(m, c, ps):
        cw = ps.shape[1]
        mw = ps.shape[0]
        base = m * 128
        if base < 2048:
            E.do("vector", lambda e, o=zstage[0:mw, c * MMF:c * MMF + cw],
                 p=ps: e.tensor_copy(o, p))
            if c == CT - 1:
                E.dma("sync", z_d[base:base + 128, :], zstage[:, 3:TH])
        elif base < 2048 + 2304:
            E.do("vector", lambda e, o=xbc_buf[0:mw, c * MMF:c * MMF + cw],
                 p=ps: e.tensor_copy(o, p))
            if c == CT - 1:
                conv_tile((base - 2048) // 128)
        else:
            E.do("vector", lambda e, o=dt_full[0:mw, c * MMF:c * MMF + cw],
                 p=ps: e.tensor_copy(o, p))

    xtb = cast_tiles(P, "xtb", xt, TH)
    wbf = P.sbuf_pool(2, [128, 128], BF16, name="wbf")
    proj_fm(P, pp, wpool, xtb, wt_d.ap(), L0_DIN, L0_DIN_PROJ, TH, evac2,
            bfpool=wbf)

    E.do("vector", lambda e: e.tensor_copy(dtt[0:32, 0:T],
                                           dt_full[0:32, 3:TH]))
    softplus_cols(P, dtt, 32, T, dt_full, bias_col=dtb)
    E.dma("sync", dtp_d.ap(), dtt[0:32, 0:T])

    if standalone:
        P.finish()
    return P


def host_l0a(x_fm, Win, conv_w, conv_b, dt_bias):
    P = build_l0a()
    wt_in = np.ascontiguousarray(Win.T)
    cw = np.ascontiguousarray(conv_w)
    cb = np.ascontiguousarray(conv_b.reshape(-1, 1))
    db = np.ascontiguousarray(dt_bias.reshape(-1, 1))
    base = Prog.basic_inputs()
    in_maps = []
    for c in range(NCORES):
        lo = c * TCORE
        xh = np.zeros((DM, TCORE + 3), np.float32)
        xh[:, 3:] = x_fm[:, lo:lo + TCORE]
        if c % 2 == 1:
            xh[:, 0:3] = x_fm[:, lo - 3:lo]
        in_maps.append(_W(P, dict(x=xh, wt_in=wt_in, conv_w=cw,
                                  conv_b=cb, dt_bias=db)) | base)
    res = P.run(in_maps)
    out = {}
    for name in ("xn", "z", "xssm", "bmat", "cmat", "dtp"):
        out[name] = np.concatenate([r["l0a_" + name] for r in res.results],
                                   axis=1)
    return out



def kernel(**inputs):
    raise NotImplementedError("under construction")



L0B_DEBUG = False

def build_l0b(P=None):
    TT_ = 128
    NC = 4096 // TT_
    NHC = 16
    standalone = P is None
    if standalone:
        P = Prog("l0b")
    P.begin_phase("l0b")
    E = P.E
    xssm_d = P.dram_in("xssm", [1024, 4096])
    dtp_d = P.dram_in("dtp", [16, 4096])
    dtpT_d = P.dram_in("dtpT", [4096, 16])
    b_d = P.dram_in("bmat", [128, 4096])
    c_d = P.dram_in("cmat", [128, 4096])
    alog_d = P.dram_in("a_log", [16, 1])
    drow_d = P.dram_in("d_row", [1, 16])
    tmask_d = P.dram_in("tmaskc", [16, 4096])
    ipat_d = P.dram_in("ipatc", [128, 4096])
    hsel_d = P.dram_in("hselc", [16, 16 * 128])
    y_d = P.dram_out("y_tm", [4096, 1024])
    if L0B_DEBUG:
        dbg_lunw = P.dram_out("dbg_lunw", [128, 4096])
        dbg_d2r = P.dram_out("dbg_d2r", [16, 4096])
        dbg_data0 = P.dram_out("dbg_data0", [128, 4096])
        dbg_mt = P.dram_out("dbg_mt", [128, 128])
        dbg_xst = P.dram_out("dbg_xst", [128, 64])
        dbg_yi = P.dram_out("dbg_yi", [128, 64])
        dbg_v2 = P.dram_out("dbg_v2", [128, 64])
        dbg_yst = P.dram_out("dbg_yst", [128, 64])

    P.load_basics(eye=True)
    pp = P.psum_pool()
    dtp = P.sbuf("dtp_sb", [16, 4096])
    dA = P.sbuf("dA", [16, 4096])
    dAm = P.sbuf("dAm", [16, 4096])
    d2r = P.sbuf("d2r", [16, 4096])
    tmask = P.sbuf("tmask", [16, 4096])
    hsel = P.sbuf("hsel", [16, 16 * 128])
    acol = P.sbuf("acol", [16, 1])
    bsb = P.sbuf("bsb", [128, 4096])
    csb = P.sbuf("csb", [128, 4096])
    dtpT = P.sbuf("dtpT_sb", [128, NC * 16])
    d2T = P.sbuf("d2T", [128, NC * 16])
    dAcols = P.sbuf("dAcols", [128, NHC * NC])
    dall = P.sbuf("dall", [128, 16])
    xs_h = P.sbuf("xs_h", [64, 4096])
    lunw = P.sbuf("lunw", [128, 4096])
    ipat = P.sbuf("ipat", [128, 4096])
    data0 = P.sbuf("data0", [128, 4096])
    mt = P.sbuf("mt", [128, 128])
    xst = P.sbuf("xst", [128, 64])
    bt = P.sbuf("bt", [128, 128])
    bwt = P.sbuf("bwt", [128, 128])
    v1sb = P.sbuf("v1sb", [128, 64])
    v2sb = P.sbuf("v2sb", [128, 64])
    ystage = P.sbuf("ystage", [128, NC * 64])
    hstate = [P.sbuf(f"hst{h}", [128, 64]) for h in range(NHC)]

    E.dma("sync", dtp[:, :], dtp_d.ap())
    E.dma("sync", tmask[:, :], tmask_d.ap())
    E.dma("sync", ipat[:, :], ipat_d.ap())
    E.dma("sync", hsel[:, :], hsel_d.ap())
    E.dma("sync", acol[:, :], alog_d.ap())
    E.dma("sync", bsb[:, :], b_d.ap())
    E.dma("sync", csb[:, :], c_d.ap())
    E.dma("sync", dtpT[:, :],
          bass.AP(dtpT_d, 0, [[16, 128], [128 * 16, NC], [1, 16]]))

    E.do("scalar", lambda e: e.activation(acol[:, :], acol[:, :], AF.Exp))
    E.do("vector", lambda e: e.tensor_scalar(acol[:, :], acol[:, :], -1.0,
                                             None, ALU.mult))
    E.do("scalar", lambda e: e.activation(dA[:, :], dtp[:, :], AF.Exp,
                                          scale=acol[0:16, :]))
    E.do("vector", lambda e: e.tensor_mul(dAm[:, :], dA[:, :], tmask[:, :]))
    E.do("vector", lambda e: e.tensor_sub(d2r[:, :], dA[:, :], dAm[:, :]))
    E.do("vector", lambda e: e.tensor_tensor_scan(
        d2r[:, :], dAm[:, :], d2r[:, :], 0.0, ALU.mult, ALU.add))
    for c in range(NC):
        ps = pp.next()
        E.do("tensor", lambda e, c=c, o=ps[0:128, 0:16]:
             e.transpose(o, d2r[0:16, c * 128:(c + 1) * 128], P.eye[0:16, 0:16]))
        E.do("vector", lambda e, c=c, i=ps[0:128, 0:16]:
             e.tensor_copy(d2T[:, c * 16:(c + 1) * 16], i))
    dac = P.sbuf("dac", [16, NC])
    E.do("vector", lambda e: e.tensor_copy(
        dac[:, :], bass.AP(d2r, 127, [[4096, 16], [128, NC]])))
    for h in range(NHC):
        ps = pp.next()
        E.do("tensor", lambda e, h=h, o=ps[0:128, 0:NC]:
             e.matmul(o, hsel[:, h * 128:(h + 1) * 128], dac[0:16, :],
                      start=True, stop=True))
        E.do("vector", lambda e, h=h, i=ps[0:128, 0:NC]:
             e.tensor_copy(dAcols[:, h * NC:(h + 1) * NC], i))
    drow = P.sbuf("drow", [1, 16])
    E.dma("sync", drow[:, :], drow_d.ap())
    psd = pp.next()
    E.do("tensor", lambda e: e.matmul(psd[0:128, 0:16], P.ones_row[0:1, :],
                                      drow[0:1, :], start=True, stop=True))
    E.do("vector", lambda e: e.tensor_copy(dall[:, :], psd[0:128, 0:16]))

    for h in range(NHC):
        E.do("vector", lambda e, h=h: e.memset(hstate[h][:, :], 0.0))
    for h in range(NHC):
        E.dma("sync", xs_h[:, :], xssm_d[h * 64:(h + 1) * 64, :])
        for j in range(8):
            ps = pp.next()
            E.do("tensor", lambda e, h=h, j=j, o=ps[:, 0:512]:
                 e.matmul(o, hsel[:, h * 128:(h + 1) * 128],
                          dAm[0:16, j * 512:(j + 1) * 512],
                          start=True, stop=True))
            E.do("scalar", lambda e, j=j, i=ps[:, 0:512]:
                 e.activation(data0[:, j * 512:(j + 1) * 512], i, AF.Identity))
        E.do("vector", lambda e: e.tensor_tensor_scan(
            lunw[:, :], data0[:, :], ipat[:, :], 0.0, ALU.mult, ALU.add))
        if L0B_DEBUG and h == 0:
            E.dma("sync", dbg_lunw.ap(), lunw[:, :])
            E.dma("sync", dbg_d2r.ap(), d2r[:, :])
            E.dma("sync", dbg_data0.ap(), data0[:, :])
        for c in range(NC):
            t0 = c * TT_
            pst = pp.next()
            E.do("tensor", lambda e, o=pst[0:128, 0:64], t0=t0:
                 e.transpose(o, xs_h[0:64, t0:t0 + 128], P.eye[0:64, 0:64]))
            E.do("vector", lambda e, i=pst[0:128, 0:64]:
                 e.tensor_copy(xst[:, :], i))
            bank_s = pp.next()
            E.do("tensor", lambda e, o=bank_s[0:128, 0:64], t0=t0, h=h:
                 e.matmul(o, csb[:, t0:t0 + 128], hstate[h][:, :],
                          start=True, stop=True))
            gt = pp.next()
            E.do("tensor", lambda e, o=gt[0:128, 0:128], t0=t0:
                 e.matmul(o, bsb[:, t0:t0 + 128], csb[:, t0:t0 + 128],
                          start=True, stop=True))
            E.do("vector", lambda e, t0=t0, c=c, h=h, g=gt[0:128, 0:128]:
                 e.scalar_tensor_tensor(
                     mt[:, :], lunw[:, t0:t0 + 128],
                     dtpT[:, c * 16 + h:c * 16 + h + 1], g,
                     ALU.mult, ALU.mult))
            if L0B_DEBUG and h == 0 and c == 1:
                E.dma("sync", dbg_mt.ap(), mt[:, :])
                E.dma("sync", dbg_xst.ap(), xst[:, :])
            bank_i = pp.next()
            E.do("tensor", lambda e, o=bank_i[0:128, 0:64]:
                 e.matmul(o, mt[:, :], xst[:, :], start=True, stop=True))
            E.do("vector", lambda e, c=c, h=h, i=bank_s[0:128, 0:64]:
                 e.tensor_scalar(v1sb[:, :], i,
                                 d2T[:, c * 16 + h:c * 16 + h + 1], None,
                                 ALU.mult))
            E.do("vector", lambda e, h=h:
                 e.scalar_tensor_tensor(v2sb[:, :], xst[:, :],
                                        dall[:, h:h + 1], v1sb[:, :],
                                        ALU.mult, ALU.add))
            E.do("vector", lambda e, c=c, i=bank_i[0:128, 0:64]:
                 e.tensor_add(ystage[:, c * 64:(c + 1) * 64], v2sb[:, :], i))
            if L0B_DEBUG and h == 0 and c == 1:
                ycopy = P.sbuf("ycopy", [128, 64]) if not hasattr(P, "_yc") else P._yc
                P._yc = ycopy
                E.do("vector", lambda e, i=bank_i[0:128, 0:64]:
                     e.tensor_copy(ycopy[:, :], i))
                E.dma("sync", dbg_yi.ap(), ycopy[:, :])
                E.dma("sync", dbg_v2.ap(), v2sb[:, :])
                E.dma("sync", dbg_yst.ap(), ystage[:, 64:128])
            psb = pp.next()
            E.do("tensor", lambda e, o=psb[0:128, 0:128], t0=t0:
                 e.transpose(o, bsb[:, t0:t0 + 128], P.eye[:, :]))
            E.do("vector", lambda e, i=psb[0:128, 0:128]:
                 e.tensor_copy(bt[:, :], i))
            E.do("vector", lambda e, t0=t0, c=c, h=h:
                 e.tensor_scalar(bwt[:, :], bt[:, :],
                                 lunw[:, t0 + 127:t0 + 128],
                                 dtpT[:, c * 16 + h:c * 16 + h + 1],
                                 ALU.mult, ALU.mult))
            pinc = pp.next()
            E.do("tensor", lambda e, o=pinc[0:128, 0:64]:
                 e.matmul(o, bwt[:, :], xst[:, :], start=True, stop=True))
            E.do("vector", lambda e, h=h, c=c, i=pinc[0:128, 0:64]:
                 e.scalar_tensor_tensor(
                     hstate[h][:, :], hstate[h][:, :],
                     dAcols[:, h * NC + c:h * NC + c + 1], i,
                     ALU.mult, ALU.add))
        E.dma("sync",
              bass.AP(y_d, h * 64, [[1024, 128], [1024 * 128, NC], [1, 64]]),
              bass.AP(ystage, 0, [[NC * 64, 128], [64, NC], [1, 64]]))
    if standalone:
        P.finish()
    return P


def host_l0b(xssm_fm, dtp_fm, bmat_fm, cmat_fm, A_log, D):
    P = build_l0b()
    base = Prog.basic_inputs(eye=True)
    tmask = np.ones((16, 4096), np.float32)
    tmask[:, ::128] = 0.0
    ipat_host = np.tile(np.eye(128, dtype=np.float32), (1, 32))
    hsel = np.zeros((16, 16 * 128), np.float32)
    for h in range(16):
        hsel[h, h * 128:(h + 1) * 128] = 1.0
    in_maps = []
    for c in range(NCORES):
        beta, hg = c // 2, c % 2
        tok = slice(beta * 4096, (beta + 1) * 4096)
        hrows = slice(hg * 1024, (hg + 1) * 1024)
        hh = slice(hg * 16, (hg + 1) * 16)
        dtp_sl = np.ascontiguousarray(dtp_fm[hh, tok])
        in_maps.append(_W(P, dict(
            xssm=np.ascontiguousarray(xssm_fm[hrows, tok]),
            dtp=dtp_sl,
            dtpT=np.ascontiguousarray(dtp_sl.T),
            bmat=np.ascontiguousarray(bmat_fm[:, tok]),
            cmat=np.ascontiguousarray(cmat_fm[:, tok]),
            a_log=np.ascontiguousarray(A_log[hh].reshape(-1, 1)),
            d_row=np.ascontiguousarray(D[hh].reshape(1, -1)),
            tmaskc=tmask, hselc=hsel, ipatc=ipat_host)) | base)
    res = P.run(in_maps)
    y = np.zeros((2048, NTOK), np.float32)
    for c in range(NCORES):
        beta, hg = c // 2, c % 2
        y[hg * 1024:(hg + 1) * 1024, beta * 4096:(beta + 1) * 4096] = \
            res.results[c]["l0b_y_tm"].T
    return y


def proj_fm_wide(P, pp, wpool, xtiles, wt_dram, d_in, d_out, T, evac,
                 xcol0=0, bfpool=None):
    E = P.E
    KT = ceil_div(d_in, 128)
    MG = ceil_div(d_out, MMF)
    CT = ceil_div(T, MMF)
    assert 4 * CT <= 8, "proj_fm_wide: too many live psum banks"
    for mg in range(MG):
        gw = min(MMF, d_out - mg * MMF)
        nm = ceil_div(gw, 128)
        banks = [pp.next() for _ in range(nm * CT)]
        for k in range(KT):
            kw = min(128, d_in - k * 128)
            wtile = wpool.next()
            E.dma("sync", wtile[0:kw, 0:gw],
                  wt_dram[k * 128:k * 128 + kw, mg * MMF:mg * MMF + gw])
            if bfpool is not None:
                wb = bfpool.next()
                E.do("scalar", lambda e, o=wb[0:kw, 0:gw],
                     x=wtile[0:kw, 0:gw]: e.activation(o, x, AF.Copy))
                wtile = wb
            for mi in range(nm):
                mw = min(128, gw - mi * 128)
                for c in range(CT):
                    cw = min(MMF, T - c * MMF)
                    E.do("tensor",
                         lambda e, o=banks[mi * CT + c][0:mw, 0:cw],
                                w=wtile[0:kw, mi * 128:mi * 128 + mw],
                                x=xtiles[k][0:kw, xcol0 + c * MMF:
                                            xcol0 + c * MMF + cw],
                                st=(k == 0), sp=(k == KT - 1):
                             e.matmul(o, w, x, start=st, stop=sp))
        for mi in range(nm):
            mw = min(128, gw - mi * 128)
            for c in range(CT):
                cw = min(MMF, T - c * MMF)
                evac(mg * 4 + mi, c, banks[mi * CT + c][0:mw, 0:cw])



def build_l0c1(P=None):
    T = TCORE
    standalone = P is None
    if standalone:
        P = Prog("l0c1")
    P.begin_phase("l0c1")
    E = P.E
    y_d = P.dram_in("y", [2048, T])
    z_d = P.dram_in("z", [2048, T])
    nw_d = P.dram_in("norm_w", [2048, 1])
    wt_d = P.dram_in("wt_out", [2048, 1024])
    s_d = P.dram_out("s", [1024, T])
    P.load_basics()
    pp = P.psum_pool()
    yt = [P.sbuf(f"y{k}", [128, T]) for k in range(16)]
    zbuf = P.sbuf("zbuf", [128, T])
    scr = P.sbuf("scr", [128, MMF])
    ab = (P.sbuf("abb", [1, T]), P.sbuf("aba", [1, T]))
    nw = P.sbuf("nw", [128, 16])
    wpool = P.sbuf_pool(2, [128, MMF], name="wp")
    sstage = P.sbuf("sstage", [128, T])
    for k in range(16):
        E.dma("sync", yt[k][:, :], y_d[k * 128:(k + 1) * 128, :])
    for k in range(16):
        E.dma("sync", nw[:, k:k + 1], nw_d[k * 128:(k + 1) * 128, :])
    ln_rows(P, pp, yt, T, 2048, scr, ab, rms=True)
    CT = ceil_div(T, MMF)
    for c in range(CT):
        cw = min(MMF, T - c * MMF)
        ba = pp.next()
        E.do("tensor", lambda e, o=ba[:, 0:cw],
             r=ab[1][0:1, c * MMF:c * MMF + cw]:
                 e.matmul(o, P.ones_row[0:1, :], r, start=True, stop=True))
        for k in range(16):
            E.do("vector",
                 lambda e, k=k, o=yt[k][:, c * MMF:c * MMF + cw],
                        a=ba[:, 0:cw]:
                     e.tensor_mul(o, o, a))
    for k in range(16):
        E.dma("sync", zbuf[:, :], z_d[k * 128:(k + 1) * 128, :])
        E.do("scalar", lambda e: e.activation(zbuf[:, :], zbuf[:, :], AF.Silu))
        E.do("vector", lambda e, k=k: e.tensor_scalar(
            yt[k][:, :], yt[k][:, :], nw[:, k:k + 1], None, ALU.mult))
        E.do("vector", lambda e, k=k: e.tensor_mul(
            yt[k][:, :], yt[k][:, :], zbuf[:, :]))

    def evac(m, c, ps):
        E.do("vector", lambda e, o=sstage[0:ps.shape[0],
                                          c * MMF:c * MMF + ps.shape[1]],
             p=ps: e.tensor_copy(o, p))
        if c == ceil_div(T, MMF) - 1:
            E.dma("sync", s_d[m * 128:m * 128 + 128, :], sstage[:, 0:T])

    proj_fm(P, pp, wpool, yt, wt_d.ap(), 2048, 1024, T, evac)
    if standalone:
        P.finish()
    return P


def build_l0c2(P=None):
    T = TCORE
    TS = 512
    NS = T // TS
    standalone = P is None
    if standalone:
        P = Prog("l0c2")
    P.begin_phase("l0c2")
    E = P.E
    xn_d = P.dram_in("xn", [1024, T])
    x_d = P.dram_in("x", [1024, T])
    s_d = P.dram_in("s", [1024, T])
    w1_d = P.dram_in("w1t", [1024, 4096])
    w2_d = P.dram_in("w2t", [1024, 4096])
    h1_d = P.dram_in("h1t", [4096, 4096])
    h2_d = P.dram_in("h2t", [4096, 4096])
    w3_d = P.dram_in("w3t", [4096, 1024])
    sc_d = P.dram_in("scales", [128, 9])
    x1_d = P.dram_out("x1", [1024, T])
    P.load_basics()
    pp = P.psum_pool()
    sc = P.sbuf("sc", [128, 136])
    sc_d2 = P.dram_in("scales2", [128, 136])
    E.dma("sync", sc[:, :], sc_d2.ap())
    xnt = [P.sbuf(f"xn{k}", [128, T]) for k in range(8)]
    ht = [P.sbuf(f"h{k}", [128, TS], BF16) for k in range(32)]
    gt = [P.sbuf(f"g{k}", [128, TS], BF16) for k in range(32)]
    wpool = P.sbuf_pool(2, [128, MMF], name="wp")
    wbf = P.sbuf_pool(2, [128, MMF], BF16, name="wbf")
    xbuf = P.sbuf("xbuf", [128, TS])
    xbufb = P.sbuf("xbufb", [128, TS], BF16)
    sbuf2 = P.sbuf("sbuf2", [128, TS])
    for k in range(8):
        E.dma("sync", xnt[k][:, :], xn_d[k * 128:(k + 1) * 128, :])
    xnb = cast_tiles(P, "xnb", xnt, T)

    for sschunk in range(NS):
        xc0 = sschunk * TS
        def evac_t2(m, c, ps):
            E.do("vector", lambda e, m=m, o=ht[m][0:ps.shape[0], 0:ps.shape[1]],
                 p=ps: e.tensor_scalar(o, p, sc[:, 32 + m:33 + m], None,
                                       ALU.mult))
        proj_fm_wide(P, pp, wpool, xnb, w2_d.ap(), 1024, 4096, TS, evac_t2,
                     xcol0=xc0, bfpool=wbf)
        def evac_t1(m, c, ps):
            E.do("scalar", lambda e, m=m, o=xbufb[0:ps.shape[0], 0:ps.shape[1]],
                 p=ps: e.activation(o, p, AF.Silu, scale=sc[:, m:m + 1]))
            E.do("vector", lambda e, m=m: e.tensor_mul(
                ht[m][:, 0:TS], ht[m][:, 0:TS], xbufb[:, 0:TS]))
        proj_fm_wide(P, pp, wpool, xnb, w1_d.ap(), 1024, 4096, TS, evac_t1,
                     xcol0=xc0, bfpool=wbf)
        def evac_g2(m, c, ps):
            E.do("scalar", lambda e, m=m, o=gt[m][0:ps.shape[0], 0:ps.shape[1]],
                 p=ps: e.activation(o, p, AF.Gelu, scale=sc[:, 64 + m:65 + m]))
        proj_fm_wide(P, pp, wpool, ht, h1_d.ap(), 4096, 4096, TS, evac_g2,
                     bfpool=wbf)
        def evac_g3(m, c, ps):
            E.do("scalar", lambda e, m=m, o=ht[m][0:ps.shape[0], 0:ps.shape[1]],
                 p=ps: e.activation(o, p, AF.Gelu, scale=sc[:, 96 + m:97 + m]))
        proj_fm_wide(P, pp, wpool, gt, h2_d.ap(), 4096, 4096, TS, evac_g3,
                     bfpool=wbf)
        def evac_g(m, c, ps):
            E.dma("sync", xbuf[:, 0:TS], x_d[m * 128:(m + 1) * 128,
                                             xc0:xc0 + TS])
            E.dma("sync", sbuf2[:, 0:TS], s_d[m * 128:(m + 1) * 128,
                                              xc0:xc0 + TS])
            E.do("scalar", lambda e, m=m, o=ps, p=ps:
                 e.activation(o, p, AF.Silu, scale=sc[:, 128 + m:129 + m]))
            E.do("vector", lambda e: e.tensor_mul(
                sbuf2[:, 0:TS], sbuf2[:, 0:TS], ps))
            E.do("vector", lambda e: e.tensor_add(
                xbuf[:, 0:TS], xbuf[:, 0:TS], sbuf2[:, 0:TS]))
            E.dma("sync", x1_d[m * 128:(m + 1) * 128, xc0:xc0 + TS],
                  xbuf[:, 0:TS])
        proj_fm_wide(P, pp, wpool, ht, w3_d.ap(), 4096, 1024, TS, evac_g,
                     bfpool=wbf)
    if standalone:
        P.finish()
    return P


def host_l0c(y_fm, z_fm, xn_fm, x_fm, norm_w, Wout, ffn):
    P1 = build_l0c1()
    base = Prog.basic_inputs()
    wt_out = np.ascontiguousarray(Wout.T)
    ims = []
    for c in range(NCORES):
        tok = slice(c * TCORE, (c + 1) * TCORE)
        ims.append(_W(P1, dict(y=np.ascontiguousarray(y_fm[:, tok]),
                        z=np.ascontiguousarray(z_fm[:, tok]),
                        norm_w=np.ascontiguousarray(norm_w.reshape(-1, 1)),
                        wt_out=wt_out)) | base)
    res1 = P1.run(ims)
    s_fm = np.concatenate([r["l0c1_s"] for r in res1.results], axis=1)

    P2 = build_l0c2()
    scales = np.zeros((128, 136), np.float32)
    for i, nm in enumerate(("w1_s", "w2_s", "h1_s", "h2_s")):
        scales[:, i * 32:(i + 1) * 32] = ffn[nm].reshape(32, 128).T
    scales[:, 128:136] = ffn["w3_s"].reshape(8, 128).T
    w1t = np.ascontiguousarray(ffn["w1_w"].T)
    w2t = np.ascontiguousarray(ffn["w2_w"].T)
    h1t = np.ascontiguousarray(ffn["h1_w"].T)
    h2t = np.ascontiguousarray(ffn["h2_w"].T)
    w3t = np.ascontiguousarray(ffn["w3_w"].T)
    ims = []
    for c in range(NCORES):
        tok = slice(c * TCORE, (c + 1) * TCORE)
        ims.append(_W(P2, dict(xn=np.ascontiguousarray(xn_fm[:, tok]),
                        x=np.ascontiguousarray(x_fm[:, tok]),
                        s=np.ascontiguousarray(s_fm[:, tok]),
                        w1t=w1t, w2t=w2t, h1t=h1t, h2t=h2t, w3t=w3t,
                        scales=np.zeros((128, 9), np.float32),
                        scales2=scales)) | base)
    res2 = P2.run(ims)
    return np.concatenate([r["l0c2_x1"] for r in res2.results], axis=1)



def build_l1a(P=None):
    standalone = P is None
    if standalone:
        P = Prog("l1a")
    P.begin_phase("l1a")
    E = P.E
    T = TCORE
    TH = T + 3
    x_d = P.dram_in("x", [DM, T])
    xh_d = P.dram_in("xhalo", [DM, 3])
    hf_d = P.dram_in("haloflag", [128, 1])
    wt_d = P.dram_in("wt_in", [1024, 4096])
    convw_d = P.dram_in("conv_w", [2048, 4])
    xpt_d = P.dram_in("wt_xproj", [2048, 17])
    xn_d = P.dram_out("xn", [DM, T])
    u_d = P.dram_out("u", [2048, T])
    z_d = P.dram_out("z", [2048, T])
    dtb_d = P.dram_out("dtb", [17, T])

    P.load_basics()
    pp = P.psum_pool()
    xt = [P.sbuf(f"x{k}", [128, TH]) for k in range(8)]
    scr = P.sbuf("scr", [128, MMF])
    ab = (P.sbuf("abb", [1, TH]), P.sbuf("aba", [1, TH]))
    wpool = P.sbuf_pool(2, [128, MMF], name="wp")
    convw = P.sbuf("convw", [128, 4 * 16])
    ubuf = P.sbuf("ubuf", [128, TH])
    ucv = P.sbuf("ucv", [128, T])
    zstage = P.sbuf("zstage", [128, TH])
    xpw = P.sbuf("xpw", [128, 17 * 16])
    dtbst = P.sbuf("dtbst", [17, T])
    CT = ceil_div(TH, MMF)
    CTm = ceil_div(T, MMF)

    hfc = P.sbuf("hfc", [128, 1])
    E.dma("sync", hfc[:, :], hf_d.ap())
    for k in range(8):
        E.dma("sync", xt[k][:, 3:TH], x_d[k * 128:(k + 1) * 128, :])
        E.dma("sync", xt[k][:, 0:3], xh_d[k * 128:(k + 1) * 128, :])
        E.do("vector", lambda e, k=k: e.tensor_scalar(
            xt[k][:, 0:3], xt[k][:, 0:3], hfc[:, 0:1], None, ALU.mult))
    for k in range(16):
        E.dma("sync", convw[:, 4 * k:4 * k + 4],
              convw_d[k * 128:(k + 1) * 128, :])
        E.dma("sync", xpw[:, 17 * k:17 * k + 17],
              xpt_d[k * 128:(k + 1) * 128, :])

    ln_rows(P, pp, xt, TH, DM, scr, ab)
    apply_ab(P, pp, xt, xt, ab, TH, DM)
    for k in range(8):
        E.dma("sync", xn_d[k * 128:(k + 1) * 128, :], xt[k][:, 3:TH])

    xdbl = P.sbuf("xdbl", [17, T])

    def conv_u_tile(kk):
        for j in range(4):
            if j == 0:
                E.do("vector",
                     lambda e, o=ucv[:, 0:T], x=ubuf[:, 0:T],
                            w=convw[:, 4 * kk:4 * kk + 1]:
                         e.tensor_scalar(o, x, w, None, ALU.mult))
            else:
                E.do("vector",
                     lambda e, o=ucv[:, 0:T], x=ubuf[:, j:j + T],
                            w=convw[:, 4 * kk + j:4 * kk + j + 1]:
                         e.scalar_tensor_tensor(o, x, w, ucv[:, 0:T],
                                                ALU.mult, ALU.add))
        E.do("scalar", lambda e: e.activation(ucv[:, 0:T], ucv[:, 0:T],
                                              AF.Silu))
        E.dma("sync", u_d[kk * 128:(kk + 1) * 128, :], ucv[:, 0:T])
        if kk == 0:
            E.do("vector", lambda e: e.memset(xdbl[:, :], 0.0))
        for c in range(CTm):
            cw = min(MMF, T - c * MMF)
            ps = pp.next()
            E.do("tensor",
                 lambda e, o=ps[0:17, 0:cw],
                        w=xpw[:, 17 * kk:17 * kk + 17],
                        x=ucv[:, c * MMF:c * MMF + cw]:
                     e.matmul(o, w, x, start=True, stop=True))
            E.do("vector",
                 lambda e, o=xdbl[0:17, c * MMF:c * MMF + cw],
                        p=ps[0:17, 0:cw]:
                     e.tensor_add(o, o, p))

    def evac(m, c, ps):
        cw = ps.shape[1]
        mw = ps.shape[0]
        if m < 16:
            E.do("vector", lambda e, o=ubuf[0:mw, c * MMF:c * MMF + cw],
                 p=ps: e.tensor_copy(o, p))
            if c == CT - 1:
                conv_u_tile(m)
        else:
            E.do("vector", lambda e, o=zstage[0:mw, c * MMF:c * MMF + cw],
                 p=ps: e.tensor_copy(o, p))
            if c == CT - 1:
                E.dma("sync", z_d[(m - 16) * 128:(m - 15) * 128, :],
                      zstage[:, 3:TH])

    xtb = cast_tiles(P, "xtb", xt, TH)
    wbf = P.sbuf_pool(2, [128, MMF], BF16, name="wbf")
    proj_fm(P, pp, wpool, xtb, wt_d.ap(), 1024, 4096, TH, evac, bfpool=wbf)
    E.dma("sync", dtb_d.ap(), xdbl[:, 0:T])
    if standalone:
        P.finish()
    return P


def host_l1a(x_fm, in_proj_w, conv_w, x_proj_w):
    P = build_l1a()
    base = Prog.basic_inputs()
    wt_in = np.ascontiguousarray(in_proj_w.T)
    cw = np.ascontiguousarray(conv_w[:, 0, :])
    xpt = np.ascontiguousarray(x_proj_w.T)
    ims = []
    for c in range(NCORES):
        lo = c * TCORE
        halo = np.zeros((DM, 3), np.float32)
        flag = np.zeros((128, 1), np.float32)
        if c % 2 == 1:
            halo = np.ascontiguousarray(x_fm[:, lo - 3:lo])
            flag[:] = 1.0
        ims.append(_W(P, dict(x=np.ascontiguousarray(x_fm[:, lo:lo + TCORE]),
                              xhalo=halo, haloflag=flag, wt_in=wt_in,
                              conv_w=cw, wt_xproj=xpt)) | base)
    res = P.run(ims)
    out = {}
    for name in ("xn", "u", "z", "dtb"):
        out[name] = np.concatenate([r["l1a_" + name] for r in res.results],
                                   axis=1)
    return out



def build_l1b(P=None):
    standalone = P is None
    if standalone:
        P = Prog("l1b")
    P.begin_phase("l1b")
    E = P.E
    L = 4096
    dtbag_d = P.dram_in("dtbag", [8 * 17, TCORE])
    dtwb_d = P.dram_in("dtwb", [2, 256])
    a_d = P.dram_in("a_sl", [256, 16])
    hsel_d = P.dram_in("hsel16", [16, 16 * 128])
    y_d = P.dram_out("y", [8 * 256, TCORE])
    P.load_basics()
    pp = P.psum_pool()
    mov = P.sbuf("mov", [2, 4096])
    dtwb = P.sbuf("dtwb_sb", [2, 256])
    asl = [P.sbuf(f"asl{i}", [128, 16]) for i in range(2)]
    hsel = P.sbuf("hsel", [16, 16 * 128])
    delta = P.sbuf("delta", [128, L])
    tmp = P.sbuf("tmp", [128, L])
    bbc = P.sbuf("bbc", [128, L])
    dB = P.sbuf("dB", [128, L])
    ee = P.sbuf("ee", [128, L])
    hh = P.sbuf("hh", [128, L])
    yac = P.sbuf("yac", [128, L])
    E.do("vector", lambda e: e.memset(mov[0:2, :], 1.0))
    E.dma("sync", dtwb[:, :], dtwb_d.ap())
    for i in range(2):
        E.dma("sync", asl[i][:, :], a_d[i * 128:(i + 1) * 128, :])
        E.do("scalar", lambda e, i=i: e.activation(asl[i][:, :], asl[i][:, :],
                                                   AF.Exp))
        E.do("vector", lambda e, i=i: e.tensor_scalar(
            asl[i][:, :], asl[i][:, :], -1.0, None, ALU.mult))
    E.dma("sync", hsel[:, :], hsel_d.ap())
    bsb = P.sbuf("bsb16", [16, 4096])

    for bidx in range(4):
        t0 = bidx * L
        r0 = (2 * bidx) * 17
        r1 = (2 * bidx + 1) * 17
        E.dma("sync", mov[0:1, 0:TCORE], dtbag_d[r0:r0 + 1, :])
        E.dma("sync", mov[0:1, TCORE:L], dtbag_d[r1:r1 + 1, :])
        E.dma("sync", bsb[:, 0:TCORE], dtbag_d[r0 + 1:r0 + 17, :])
        E.dma("sync", bsb[:, TCORE:L], dtbag_d[r1 + 1:r1 + 17, :])
        for ds in range(2):
            for c in range(8):
                ps = pp.next()
                E.do("tensor", lambda e, c=c, ds=ds, o=ps[:, 0:512]:
                     e.matmul(o, dtwb[:, ds * 128:(ds + 1) * 128],
                              mov[:, c * 512:(c + 1) * 512],
                              start=True, stop=True))
                E.do("scalar", lambda e, c=c, i=ps[:, 0:512]:
                     e.activation(delta[:, c * 512:(c + 1) * 512], i,
                                  AF.Identity))
            softplus_cols(P, delta, 128, L, tmp)
            E.do("vector", lambda e: e.memset(yac[:, :], 0.0))
            for n in range(16):
                if True:
                    for c in range(8):
                        ps = pp.next()
                        E.do("tensor", lambda e, c=c, n=n, o=ps[:, 0:512]:
                             e.matmul(o, hsel[:, n * 128:(n + 1) * 128],
                                      bsb[0:16, c * 512:(c + 1) * 512],
                                      start=True, stop=True))
                        E.do("scalar", lambda e, c=c, i=ps[:, 0:512]:
                             e.activation(bbc[:, c * 512:(c + 1) * 512], i,
                                          AF.Identity))
                E.do("scalar", lambda e, n=n, ds=ds: e.activation(
                    ee[:, :], delta[:, :], AF.Exp,
                    scale=asl[ds][:, n:n + 1]))
                E.do("vector", lambda e: e.tensor_mul(dB[:, :], delta[:, :],
                                                      bbc[:, :]))
                E.do("vector", lambda e: e.tensor_tensor_scan(
                    hh[:, :], ee[:, :], dB[:, :], 0.0, ALU.mult, ALU.add))
                E.do("vector", lambda e: e.tensor_mul(hh[:, :], hh[:, :],
                                                      bbc[:, :]))
                E.do("vector", lambda e: e.tensor_add(yac[:, :], yac[:, :],
                                                      hh[:, :]))
            E.dma("sync", y_d[(2 * bidx) * 256 + ds * 128:
                              (2 * bidx) * 256 + (ds + 1) * 128, :],
                  yac[:, 0:TCORE])
            E.dma("sync", y_d[(2 * bidx + 1) * 256 + ds * 128:
                              (2 * bidx + 1) * 256 + (ds + 1) * 128, :],
                  yac[:, TCORE:L])
    if standalone:
        P.finish()
    return P


def host_l1b(dtb_fm, dt_w, dt_b, A_param):
    P = build_l1b()
    base = Prog.basic_inputs()
    hsel = np.zeros((16, 16 * 128), np.float32)
    for h in range(16):
        hsel[h, h * 128:(h + 1) * 128] = 1.0
    dtbag = np.concatenate([dtb_fm[:, j * TCORE:(j + 1) * TCORE]
                            for j in range(8)], axis=0).astype(np.float32)
    ims = []
    for c in range(NCORES):
        dsl = slice(c * 256, (c + 1) * 256)
        dtwb = np.stack([dt_w[dsl, 0], dt_b[dsl]]).astype(np.float32)
        ims.append(_W(P, dict(dtbag=np.ascontiguousarray(dtbag),
                        dtwb=np.ascontiguousarray(dtwb),
                        a_sl=np.ascontiguousarray(A_param[dsl]),
                        hsel16=hsel)) | base)
    res = P.run(ims)
    y = np.zeros((2048, NTOK), np.float32)
    for c in range(NCORES):
        yc = res.results[c]["l1b_y"]
        for j in range(8):
            y[c * 256:(c + 1) * 256, j * TCORE:(j + 1) * TCORE] = \
                yc[j * 256:(j + 1) * 256, :]
    return y



def build_l1c(P=None):
    T = TCORE
    standalone = P is None
    if standalone:
        P = Prog("l1c")
    P.begin_phase("l1c")
    E = P.E
    y_d = P.dram_in("y", [2048, T])
    u_d = P.dram_in("u", [2048, T])
    z_d = P.dram_in("z", [2048, T])
    dcol_d = P.dram_in("dcol", [2048, 1])
    wt_d = P.dram_in("wt_out", [2048, 1024])
    ob_d = P.dram_in("out_b", [1024, 1])
    s_d = P.dram_out("s", [1024, T])
    P.load_basics()
    pp = P.psum_pool()
    yt = [P.sbuf(f"y{k}", [128, T]) for k in range(16)]
    buf = P.sbuf("buf", [128, T])
    dcol = P.sbuf("dcol_sb", [128, 16])
    ob = P.sbuf("ob_sb", [128, 8])
    wpool = P.sbuf_pool(2, [128, MMF], name="wp")
    sstage = P.sbuf("sstage", [128, T])
    for k in range(16):
        E.dma("sync", yt[k][:, :], y_d[k * 128:(k + 1) * 128, :])
        E.dma("sync", dcol[:, k:k + 1], dcol_d[k * 128:(k + 1) * 128, :])
    for k in range(8):
        E.dma("sync", ob[:, k:k + 1], ob_d[k * 128:(k + 1) * 128, :])
    for k in range(16):
        E.dma("sync", buf[:, :], u_d[k * 128:(k + 1) * 128, :])
        E.do("vector", lambda e, k=k: e.scalar_tensor_tensor(
            yt[k][:, :], buf[:, :], dcol[:, k:k + 1], yt[k][:, :],
            ALU.mult, ALU.add))
        E.dma("sync", buf[:, :], z_d[k * 128:(k + 1) * 128, :])
        E.do("scalar", lambda e: e.activation(buf[:, :], buf[:, :], AF.Silu))
        E.do("vector", lambda e, k=k: e.tensor_mul(yt[k][:, :], yt[k][:, :],
                                                   buf[:, :]))

    def evac(m, c, ps):
        E.do("vector", lambda e, m=m,
             o=sstage[0:ps.shape[0], c * MMF:c * MMF + ps.shape[1]], p=ps:
                 e.tensor_scalar(o, p, 1.0, ob[:, m:m + 1], ALU.mult, ALU.add))
        if c == ceil_div(T, MMF) - 1:
            E.dma("sync", s_d[m * 128:m * 128 + 128, :], sstage[:, 0:T])

    proj_fm(P, pp, wpool, yt, wt_d.ap(), 2048, 1024, T, evac)
    if standalone:
        P.finish()
    return P


def host_l1c(y_fm, u_fm, z_fm, D_param, out_w, out_b):
    P = build_l1c()
    base = Prog.basic_inputs()
    wt = np.ascontiguousarray(out_w.T)
    ims = []
    for c in range(NCORES):
        tok = slice(c * TCORE, (c + 1) * TCORE)
        ims.append(_W(P, dict(y=np.ascontiguousarray(y_fm[:, tok]),
                        u=np.ascontiguousarray(u_fm[:, tok]),
                        z=np.ascontiguousarray(z_fm[:, tok]),
                        dcol=np.ascontiguousarray(D_param.reshape(-1, 1)),
                        wt_out=wt,
                        out_b=np.ascontiguousarray(out_b.reshape(-1, 1))))
                   | base)
    res = P.run(ims)
    return np.concatenate([r["l1c_s"] for r in res.results], axis=1)



def build_l1d1(P=None):
    standalone = P is None
    if standalone:
        P = Prog("l1d1")
    P.begin_phase("l1d1")
    E = P.E
    T = TCORE
    TH = 513 + T
    TK = 512 + T
    x_d = P.dram_in("x", [DM, T])
    xh_d = P.dram_in("xhalo", [DM, 513])
    hf_d = P.dram_in("haloflag", [128, 1])
    wq_d = P.dram_in("wq", [1024, 512])
    wk_d = P.dram_in("wk", [1024, 512])
    wv_d = P.dram_in("wv", [1024, 512])
    q_d = P.dram_out("q", [512, T])
    k_d = P.dram_out("k", [512, TK])
    v_d = P.dram_out("v", [512, TK])
    P.load_basics()
    pp = P.psum_pool()
    xt = [P.sbuf(f"x{k}", [128, TH]) for k in range(8)]
    wpool = P.sbuf_pool(2, [128, MMF], name="wp")
    stage = P.sbuf("stage", [128, TK])
    hfc = P.sbuf("hfc", [128, 1])
    E.dma("sync", hfc[:, :], hf_d.ap())
    for k in range(8):
        E.dma("sync", xt[k][:, 513:TH], x_d[k * 128:(k + 1) * 128, :])
        E.dma("sync", xt[k][:, 0:513], xh_d[k * 128:(k + 1) * 128, :])
        E.do("vector", lambda e, k=k: e.tensor_scalar(
            xt[k][:, 0:513], xt[k][:, 0:513], hfc[:, 0:1], None, ALU.mult))

    def proj_shift(w_dram, out_d, Tn, col0):
        CT = ceil_div(Tn, MMF)
        for m in range(4):
            banks = [pp.next() for _ in range(CT)]
            for k in range(8):
                wtile = wpool.next()
                E.dma("sync", wtile[:, 0:128],
                      w_dram[k * 128:(k + 1) * 128, m * 128:(m + 1) * 128])
                off = col0 if k < 4 else col0 - 1
                for c in range(CT):
                    cw = min(MMF, Tn - c * MMF)
                    E.do("tensor",
                         lambda e, o=banks[c][0:128, 0:cw],
                                w=wtile[:, 0:128],
                                x=xt[k][:, off + c * MMF:off + c * MMF + cw],
                                st=(k == 0):
                             e.matmul(o, w, x, start=st, stop=(k == 7)))
            for c in range(CT):
                cw = min(MMF, Tn - c * MMF)
                E.do("vector", lambda e, o=stage[:, c * MMF:c * MMF + cw],
                     p=banks[c][0:128, 0:cw]: e.tensor_copy(o, p))
            E.dma("sync", out_d[m * 128:(m + 1) * 128, 0:Tn],
                  stage[:, 0:Tn])

    proj_shift(wq_d, q_d, T, 513)
    proj_shift(wk_d, k_d, TK, 1)
    proj_shift(wv_d, v_d, TK, 1)
    if standalone:
        P.finish()
    return P


L1D2_DEBUG = False


def build_l1d2(P=None):
    standalone = P is None
    if standalone:
        P = Prog("l1d2")
    P.begin_phase("l1d2")
    E = P.E
    T = TCORE
    TK = 512 + T
    NW = T // 512
    q_d = P.dram_in("q", [512, T])
    k_d = P.dram_in("k", [512, TK])
    v_d = P.dram_in("v", [512, TK])
    qs_d = P.dram_in("qs_col", [512, 1])
    ks_d = P.dram_in("ks_col", [512, 1])
    csq_d = P.dram_in("csq", [128, TK])
    ssq_d = P.dram_in("ssq", [128, TK])
    csk_d = P.dram_in("csk", [128, TK])
    ssk_d = P.dram_in("ssk", [128, TK])
    rotm_d = P.dram_in("rotm", [128, 128])
    ind2_d = P.dram_in("ind2", [2, 128])
    mask_d = P.dram_in("mask8", [128, 8 * 1024])
    o_d = P.dram_out("o", [512, T])
    if L1D2_DEBUG:
        dbg_k0 = P.dram_out("dbg_k0", [128, TK])
        dbg_qw = P.dram_out("dbg_qw", [128, 512])
        dbg_e = P.dram_out("dbg_e", [128, 1024])
        dbg_vt = P.dram_out("dbg_vt", [128, 512])
    P.load_basics(eye=True)
    pp = P.psum_pool()
    kt = [P.sbuf(f"k{i}", [128, TK]) for i in range(4)]
    vt = [P.sbuf(f"v{i}", [128, TK]) for i in range(4)]
    csk = P.sbuf("csk_sb", [128, TK])
    ssk = P.sbuf("ssk_sb", [128, TK])
    csq = P.sbuf("csq_sb", [128, TK])
    ssq = P.sbuf("ssq_sb", [128, TK])
    rotm = P.sbuf("rotm_sb", [128, 128])
    rsc = P.sbuf("rsc", [128, MMF])
    rsc2 = P.sbuf("rsc2", [128, MMF])
    qs = P.sbuf("qs_sb", [128, 4])
    ks = P.sbuf("ks_sb", [128, 4])
    ind2 = P.sbuf("ind2_sb", [2, 128])
    mask = P.sbuf("mask_sb", [128, 8 * 1024])
    rn2 = P.sbuf("rn2", [2, TK])
    qw = P.sbuf("qw", [128, 512])
    esb = P.sbuf("esb", [128, 1024])
    etb = P.sbuf("etb", [128, 1024])
    vtb = P.sbuf("vtb", [128, 512])
    ost = [P.sbuf(f"ost{i}", [64, 512]) for i in range(8)]
    nmc = P.sbuf("nmc", [128, 1])
    rdc = P.sbuf("rdc", [128, 1])
    scrt = P.sbuf("scrt", [128, MMF])

    for i in range(4):
        E.dma("sync", kt[i][:, :], k_d[i * 128:(i + 1) * 128, :])
        E.dma("sync", vt[i][:, :], v_d[i * 128:(i + 1) * 128, :])
        E.dma("sync", qs[:, i:i + 1], qs_d[i * 128:(i + 1) * 128, :])
        E.dma("sync", ks[:, i:i + 1], ks_d[i * 128:(i + 1) * 128, :])
    E.dma("sync", csk[:, :], csk_d.ap())
    E.dma("sync", ssk[:, :], ssk_d.ap())
    E.dma("sync", csq[:, :], csq_d.ap())
    E.dma("sync", ssq[:, :], ssq_d.ap())
    E.dma("sync", ind2[:, :], ind2_d.ap())
    E.dma("sync", mask[:, :], mask_d.ap())
    E.dma("sync", rotm[:, :], rotm_d.ap())

    def prep(tile, Tn, sc_col, cs, ss, qk_scale, ccol0):
        CT = ceil_div(Tn, MMF)
        for c in range(CT):
            cw = min(MMF, Tn - c * MMF)
            psA = pp.next()
            E.do("scalar", lambda e, o=scrt[:, 0:cw],
                 x=tile[:, c * MMF:c * MMF + cw]:
                     e.activation(o, x, AF.Square))
            E.do("tensor", lambda e, o=psA[0:2, 0:cw], x=scrt[:, 0:cw]:
                 e.matmul(o, ind2T[:, 0:2], x, start=True, stop=True))
            E.do("vector", lambda e, o=rn2[0:2, c * MMF:c * MMF + cw],
                 p=psA[0:2, 0:cw]: e.tensor_copy(o, p))
        E.do("scalar", lambda e: e.activation(
            rn2[0:2, 0:Tn], rn2[0:2, 0:Tn], AF.Ln,
            bias=P.ccol("eps1e12")[0:2, :]))
        E.do("scalar", lambda e: e.activation(
            rn2[0:2, 0:Tn], rn2[0:2, 0:Tn], AF.Exp, scale=0.5))
        E.do("vector", lambda e: e.tensor_scalar_max(
            rn2[0:2, 0:Tn], rn2[0:2, 0:Tn], 1e-12))
        E.do("vector", lambda e: e.reciprocal(rn2[0:2, 0:Tn],
                                              rn2[0:2, 0:Tn]))
        if qk_scale != 1.0:
            E.do("vector", lambda e: e.tensor_scalar(
                rn2[0:2, 0:Tn], rn2[0:2, 0:Tn], float(qk_scale), None,
                ALU.mult))
        for c in range(CT):
            cw = min(MMF, Tn - c * MMF)
            ps = pp.next()
            E.do("tensor", lambda e, o=ps[:, 0:cw],
                 r=rn2[0:2, c * MMF:c * MMF + cw]:
                     e.matmul(o, ind2[:, :], r, start=True, stop=True))
            E.do("vector", lambda e, o=tile[:, c * MMF:c * MMF + cw],
                 s=sc_col, p=ps[:, 0:cw]:
                     e.scalar_tensor_tensor(o, o, s, p, ALU.mult, ALU.mult))
        for c in range(CT):
            cw = min(MMF, Tn - c * MMF)
            psr = pp.next()
            E.do("tensor", lambda e, o=psr[:, 0:cw],
                 x=tile[:, c * MMF:c * MMF + cw]:
                     e.matmul(o, rotm[:, :], x, start=True, stop=True))
            E.do("vector", lambda e, o=rsc[:, 0:cw],
                 x=tile[:, c * MMF:c * MMF + cw],
                 t=cs[:, ccol0 + c * MMF:ccol0 + c * MMF + cw]:
                     e.tensor_mul(o, x, t))
            E.do("vector", lambda e, o=rsc2[:, 0:cw], p=psr[:, 0:cw],
                 t=ss[:, ccol0 + c * MMF:ccol0 + c * MMF + cw]:
                     e.tensor_mul(o, p, t))
            E.do("vector", lambda e, o=tile[:, c * MMF:c * MMF + cw]:
                 e.tensor_add(o, rsc[:, 0:cw], rsc2[:, 0:cw]))

    ind2T = P.sbuf("ind2T", [128, 2])
    pst = pp.next()
    E.do("tensor", lambda e: e.transpose(pst[0:128, 0:2], ind2[0:2, 0:128],
                                         P.eye[0:2, 0:2]))
    E.do("vector", lambda e: e.tensor_copy(ind2T[:, :], pst[0:128, 0:2]))

    for i in range(4):
        prep(kt[i], TK, ks[:, i:i + 1], csk, ssk, 1.0, 0)
    if L1D2_DEBUG:
        E.dma("sync", dbg_k0.ap(), kt[0][:, :])

    for w in range(NW):
        kc0 = w * 512
        for i in range(4):
            E.dma("sync", qw[:, :], q_d[i * 128:(i + 1) * 128,
                                        w * 512:(w + 1) * 512])
            prep(qw, 512, qs[:, i:i + 1], csq, ssq, 8.0, 512 + w * 512)
            if L1D2_DEBUG and w == 1 and i == 0:
                E.dma("sync", dbg_qw.ap(), qw[:, :])
            for hhh in range(2):
                h = 2 * i + hhh
                vbase = (h % 2) * 64
                for kb in range(8):
                    psv = pp.next()
                    E.do("tensor", lambda e, o=psv[0:128, 0:64],
                         src_=vt[i][vbase:vbase + 64,
                                    kc0 + kb * 128:kc0 + (kb + 1) * 128],
                         idn=P.eye[vbase:vbase + 64, vbase:vbase + 64]:
                         e.transpose(o, src_, idn))
                    E.do("vector", lambda e, kb=kb, i2=psv[0:128, 0:64]:
                         e.tensor_copy(vtb[:, kb * 64:(kb + 1) * 64], i2))
                qb = vbase
                for qsub in range(4):
                    b0 = pp.next()
                    b1 = pp.next()
                    for ci, bank in ((0, b0), (1, b1)):
                        E.do("tensor",
                             lambda e, o=bank[0:128, 0:512],
                                    wq=qw[qb:qb + 64,
                                          qsub * 128:(qsub + 1) * 128],
                                    x=kt[i][qb:qb + 64,
                                            kc0 + ci * 512:
                                            kc0 + (ci + 1) * 512]:
                                 e.matmul(o, wq, x, start=True, stop=True))
                    mslot = (4 + qsub) if w == 0 else qsub
                    for ci, bank in ((0, b0), (1, b1)):
                        E.do("vector",
                             lambda e, o=esb[:, ci * 512:(ci + 1) * 512],
                                    p=bank[0:128, 0:512], ci=ci, ms=mslot:
                                 e.tensor_add(
                                     o, p,
                                     mask[:, ms * 1024 + ci * 512:
                                          ms * 1024 + (ci + 1) * 512]))
                    E.do("vector", lambda e: e.tensor_reduce(
                        nmc[:, :], esb[:, :], AX.X, ALU.max, negate=True))
                    E.do("scalar", lambda e: e.activation(
                        esb[:, :], esb[:, :], AF.Exp, bias=nmc[:, 0:1]))
                    E.do("vector", lambda e: e.tensor_reduce(
                        rdc[:, :], esb[:, :], AX.X, ALU.add))
                    E.do("vector", lambda e: e.reciprocal(rdc[:, :],
                                                          rdc[:, :]))
                    E.do("vector", lambda e: e.tensor_scalar(
                        esb[:, :], esb[:, :], rdc[:, 0:1], None, ALU.mult))
                    if L1D2_DEBUG and w == 1 and i == 0 and hhh == 0 \
                            and qsub == 0:
                        E.dma("sync", dbg_e.ap(), esb[:, :])
                        E.dma("sync", dbg_vt.ap(), vtb[:, :])
                    for kb in range(8):
                        pse = pp.next()
                        E.do("tensor", lambda e, o=pse[0:128, 0:128], kb=kb:
                             e.transpose(o, esb[:, kb * 128:(kb + 1) * 128],
                                         P.eye[:, :]))
                        E.do("vector", lambda e, kb=kb,
                             i2=pse[0:128, 0:128]:
                             e.tensor_copy(etb[:, kb * 128:(kb + 1) * 128],
                                           i2))
                    ps_o = pp.next()
                    for kb in range(8):
                        E.do("tensor",
                             lambda e, o=ps_o[0:64, 0:128], kb=kb:
                                 e.matmul(o, vtb[:, kb * 64:(kb + 1) * 64],
                                          etb[:, kb * 128:(kb + 1) * 128],
                                          start=(kb == 0), stop=(kb == 7)))
                    E.do("vector", lambda e, o=ost[h][
                        0:64, qsub * 128:(qsub + 1) * 128],
                        p=ps_o[0:64, 0:128]: e.tensor_copy(o, p))
        for h in range(8):
            E.dma("sync", o_d[h * 64:(h + 1) * 64, w * 512:(w + 1) * 512],
                  ost[h][:, :])
    if standalone:
        P.finish()
    return P


def build_l1d3(P=None):
    T = TCORE
    standalone = P is None
    if standalone:
        P = Prog("l1d3")
    P.begin_phase("l1d3")
    E = P.E
    o_d = P.dram_in("o", [512, T])
    x_d = P.dram_in("x", [1024, T])
    wo_d = P.dram_in("wo", [512, 1024])
    x2_d = P.dram_out("x2", [1024, T])
    P.load_basics()
    pp = P.psum_pool()
    ot = [P.sbuf(f"o{k}", [128, T]) for k in range(4)]
    xbuf = P.sbuf("xbuf", [128, T])
    wpool = P.sbuf_pool(2, [128, MMF], name="wp")
    for k in range(4):
        E.dma("sync", ot[k][:, :], o_d[k * 128:(k + 1) * 128, :])

    def evac(m, c, ps):
        if c == 0:
            E.dma("sync", xbuf[:, 0:T], x_d[m * 128:(m + 1) * 128, :])
        E.do("vector", lambda e,
             o=xbuf[0:ps.shape[0], c * MMF:c * MMF + ps.shape[1]], p=ps:
                 e.tensor_add(o, o, p))
        if c == ceil_div(T, MMF) - 1:
            E.dma("sync", x2_d[m * 128:(m + 1) * 128, :], xbuf[:, 0:T])

    proj_fm(P, pp, wpool, ot, wo_d.ap(), 512, 1024, T, evac)
    if standalone:
        P.finish()
    return P


def build_l1e(P=None):
    T = TCORE
    TS = 1024
    NS = T // TS
    standalone = P is None
    if standalone:
        P = Prog("l1e")
    P.begin_phase("l1e")
    E = P.E
    x2_d = P.dram_in("x2", [1024, T])
    x2h_d = P.dram_in("x2halo", [1024, 1])
    hf_d = P.dram_in("haloflag", [128, 1])
    lngb_d = P.dram_in("lngb", [128, 16])
    w1_d = P.dram_in("w1p", [1024, 5632])
    b1_d = P.dram_in("b1p", [128, 44])
    w2_d = P.dram_in("w2p", [2816, 1024])
    b2_d = P.dram_in("b2c", [128, 8])
    x3_d = P.dram_out("x3", [1024, T])
    P.load_basics()
    pp = P.psum_pool()
    ht = [P.sbuf(f"h{k}", [128, TS]) for k in range(8)]
    htb = [P.sbuf(f"hb{k}", [128, TS], BF16) for k in range(8)]
    at = [P.sbuf(f"a{k}", [128, TS], BF16) for k in range(22)]
    lngb = P.sbuf("lngb_sb", [128, 16])
    b1 = P.sbuf("b1_sb", [128, 44])
    b2 = P.sbuf("b2_sb", [128, 8])
    scr = P.sbuf("scr", [128, MMF])
    scrb = P.sbuf("scrb", [128, MMF], BF16)
    ab = (P.sbuf("abb", [1, TS]), P.sbuf("aba", [1, TS]))
    wpool = P.sbuf_pool(2, [128, MMF], name="wp")
    wbf = P.sbuf_pool(2, [128, MMF], BF16, name="wbf")
    xbuf = P.sbuf("xbuf", [128, TS])
    hfc = P.sbuf("hfc", [128, 1])
    E.dma("sync", hfc[:, :], hf_d.ap())
    E.dma("sync", lngb[:, :], lngb_d.ap())
    E.dma("sync", b1[:, :], b1_d.ap())
    E.dma("sync", b2[:, :], b2_d.ap())

    for sub in range(NS):
        xc0 = sub * TS
        for k in range(8):
            if k < 4:
                E.dma("sync", ht[k][:, 0:TS],
                      x2_d[k * 128:(k + 1) * 128, xc0:xc0 + TS])
            elif sub > 0:
                E.dma("sync", ht[k][:, 0:TS],
                      x2_d[k * 128:(k + 1) * 128, xc0 - 1:xc0 - 1 + TS])
            else:
                E.dma("sync", ht[k][:, 1:TS],
                      x2_d[k * 128:(k + 1) * 128, 0:TS - 1])
                E.dma("sync", ht[k][:, 0:1],
                      x2h_d[k * 128:(k + 1) * 128, :])
                E.do("vector", lambda e, k=k: e.tensor_scalar(
                    ht[k][:, 0:1], ht[k][:, 0:1], hfc[:, 0:1], None,
                    ALU.mult))
        ln_rows(P, pp, ht, TS, 1024, scr, ab)
        apply_ab(P, pp, ht, ht, ab, TS, 1024)
        for k in range(8):
            E.do("vector", lambda e, k=k: e.tensor_scalar(
                htb[k][:, 0:TS], ht[k][:, 0:TS], lngb[:, k:k + 1],
                lngb[:, 8 + k:9 + k], ALU.mult, ALU.add))

        def evac1(m, c, ps):
            cw = ps.shape[1]
            mw = ps.shape[0]
            if m < 22:
                E.do("vector", lambda e, m=m,
                     o=at[m][0:mw, c * MMF:c * MMF + cw], p=ps:
                         e.tensor_scalar(o, p, 1.0, b1[:, m:m + 1],
                                         ALU.mult, ALU.add))
            else:
                mm = m - 22
                E.do("scalar", lambda e, m=m,
                     o=scrb[0:mw, 0:cw], p=ps:
                         e.activation(o, p, AF.Gelu, bias=b1[:, m:m + 1]))
                E.do("vector", lambda e, mm=mm, cw=cw, c=c:
                     e.tensor_mul(at[mm][:, c * MMF:c * MMF + cw],
                                  at[mm][:, c * MMF:c * MMF + cw],
                                  scrb[:, 0:cw]))

        proj_fm_wide(P, pp, wpool, htb, w1_d.ap(), 1024, 5632, TS, evac1,
                     bfpool=wbf)

        def evac2(m, c, ps):
            cw = ps.shape[1]
            if c == 0:
                E.dma("sync", xbuf[:, 0:TS],
                      x2_d[m * 128:(m + 1) * 128, sub * TS:(sub + 1) * TS])
            E.do("vector", lambda e, m=m, o=ps, p=ps:
                 e.tensor_scalar(o, p, 1.0, b2[:, m:m + 1], ALU.mult,
                                 ALU.add))
            E.do("vector", lambda e, c=c, cw=cw, p=ps:
                 e.tensor_add(xbuf[:, c * MMF:c * MMF + cw],
                              xbuf[:, c * MMF:c * MMF + cw], p))
            if c == ceil_div(TS, MMF) - 1:
                E.dma("sync",
                      x3_d[m * 128:(m + 1) * 128, sub * TS:(sub + 1) * TS],
                      xbuf[:, 0:TS])

        proj_fm_wide(P, pp, wpool, at, w2_d.ap(), 2816, 1024, TS, evac2,
                     bfpool=wbf)
    if standalone:
        P.finish()
    return P


def _l1ffn_tables(core):
    T, TK = TCORE, 512 + TCORE
    start = core * TCORE
    half = np.arange(0, 64, 2, dtype=np.float32)
    inv = 1.0 / 10000.0 ** (half / 64.0)
    base = (half + 0.4 * 64) / (1.4 * 64)
    gtok = np.arange(start - 512, start + T)
    pos = (gtok % 4096).astype(np.float32)
    fr = pos[:, None] * inv[None, :]
    fr = np.concatenate([fr, fr], axis=1)
    power = (pos - 2048.0) / 256.0
    sc = base[None, :] ** power[:, None]
    sc = np.concatenate([sc, sc], axis=1)
    csq = np.tile((np.cos(fr) * sc).T.astype(np.float32), (2, 1)).copy()
    ssq = np.tile((np.sin(fr) * sc).T.astype(np.float32), (2, 1)).copy()
    csk = np.tile((np.cos(fr) / sc).T.astype(np.float32), (2, 1)).copy()
    ssk = np.tile((np.sin(fr) / sc).T.astype(np.float32), (2, 1)).copy()
    mask8 = np.zeros((128, 8 * 1024), np.float32)
    first = (core % 2 == 0)
    for qsub in range(4):
        qr = np.arange(128)[:, None]
        kc = np.arange(1024)[None, :]
        d = 512 + qsub * 128 + qr - kc
        bad = (d < 0) | (d > 512)
        m = np.where(bad, -1e30, 0.0).astype(np.float32)
        mask8[:, qsub * 1024:(qsub + 1) * 1024] = m
        mfw = m.copy()
        if first:
            mfw[:, 0:512] = -1e30
        mask8[:, (4 + qsub) * 1024:(5 + qsub) * 1024] = mfw
    ind2 = np.zeros((2, 128), np.float32)
    ind2[0, 0:64] = 1.0
    ind2[1, 64:128] = 1.0
    r = np.zeros((64, 64), np.float32)
    for d in range(32):
        r[d + 32, d] = -1.0
    for d in range(32, 64):
        r[d - 32, d] = 1.0
    rotm = np.zeros((128, 128), np.float32)
    rotm[0:64, 0:64] = r
    rotm[64:128, 64:128] = r
    return dict(csq=csq, ssq=ssq, csk=csk, ssk=ssk, mask8=mask8, ind2=ind2,
                rotm=rotm)


def host_l1ffn_layer(x_fm, wqkv, q_scale, k_scale, wo, ln_g, ln_b,
                     w1, b1, w2, b2, progs=None):
    base = Prog.basic_inputs()
    base_eye = Prog.basic_inputs(eye=True)
    P1 = build_l1d1()
    ims = []
    T = TCORE
    for c in range(NCORES):
        lo = c * T
        halo = np.zeros((DM, 513), np.float32)
        flag = np.zeros((128, 1), np.float32)
        if c % 2 == 1:
            halo = np.ascontiguousarray(x_fm[:, lo - 513:lo])
            flag[:] = 1.0
        ims.append(_W(P1, dict(x=np.ascontiguousarray(x_fm[:, lo:lo + T]),
                        xhalo=halo, haloflag=flag,
                        wq=np.ascontiguousarray(wqkv[:, 0:512]),
                        wk=np.ascontiguousarray(wqkv[:, 512:1024]),
                        wv=np.ascontiguousarray(wqkv[:, 1024:1536]))) | base)
    r1 = P1.run(ims)

    P2 = build_l1d2()
    qs_col = np.tile(q_scale, 8).reshape(-1, 1).astype(np.float32)
    ks_col = np.tile(k_scale, 8).reshape(-1, 1).astype(np.float32)
    ims = []
    for c in range(NCORES):
        tb = _l1ffn_tables(c)
        ims.append(_W(P2, dict(q=r1.results[c]["l1d1_q"],
                        k=r1.results[c]["l1d1_k"],
                        v=r1.results[c]["l1d1_v"], qs_col=qs_col,
                        ks_col=ks_col, **tb)) | base_eye)
    r2 = P2.run(ims)

    P3 = build_l1d3()
    ims = []
    for c in range(NCORES):
        lo = c * T
        ims.append(_W(P3, dict(o=r2.results[c]["l1d2_o"],
                        x=np.ascontiguousarray(x_fm[:, lo:lo + T]),
                        wo=np.ascontiguousarray(wo))) | base)
    r3 = P3.run(ims)
    x2_fm = np.concatenate([r["l1d3_x2"] for r in r3.results], axis=1)

    P4 = build_l1e()
    HID = 2730
    w1p = np.zeros((1024, 5632), np.float32)
    w1p[:, 0:HID] = w1[:, 0:HID]
    w1p[:, 2816:2816 + HID] = w1[:, HID:2 * HID]
    b1p = np.zeros((128, 44), np.float32)
    b1pad = np.zeros(5632, np.float32)
    b1pad[0:HID] = b1[0:HID]
    b1pad[2816:2816 + HID] = b1[HID:]
    b1p[:, :] = b1pad.reshape(44, 128).T
    w2p = np.zeros((2816, 1024), np.float32)
    w2p[0:HID, :] = w2
    b2c = b2.reshape(8, 128).T.astype(np.float32).copy()
    lngb = np.zeros((128, 16), np.float32)
    lngb[:, 0:8] = ln_g.reshape(8, 128).T
    lngb[:, 8:16] = ln_b.reshape(8, 128).T
    ims = []
    for c in range(NCORES):
        lo = c * T
        halo = np.zeros((1024, 1), np.float32)
        flag = np.zeros((128, 1), np.float32)
        if c % 2 == 1:
            halo = np.ascontiguousarray(x2_fm[:, lo - 1:lo])
            flag[:] = 1.0
        ims.append(_W(P4, dict(x2=np.ascontiguousarray(x2_fm[:, lo:lo + T]),
                        x2halo=halo, haloflag=flag, lngb=lngb, w1p=w1p,
                        b1p=b1p, w2p=w2p, b2c=b2c)) | base)
    r4 = P4.run(ims)
    return np.concatenate([r["l1e_x3"] for r in r4.results], axis=1)


def build_combine(P=None):
    T = TCORE
    standalone = P is None
    if standalone:
        P = Prog("combine")
    P.begin_phase("combine")
    E = P.E
    x_d = P.dram_in("x", [1024, T])
    s_d = P.dram_in("s", [1024, T])
    g_d = P.dram_in("g", [1024, T])
    o_d = P.dram_out("xn_out", [1024, T])
    P.load_basics()
    xb = P.sbuf("xb", [128, T])
    sb = P.sbuf("sb", [128, T])
    gb = P.sbuf("gb", [128, T])
    for k in range(8):
        E.dma("sync", xb[:, :], x_d[k * 128:(k + 1) * 128, :])
        E.dma("sync", sb[:, :], s_d[k * 128:(k + 1) * 128, :])
        E.dma("sync", gb[:, :], g_d[k * 128:(k + 1) * 128, :])
        E.do("scalar", lambda e: e.activation(gb[:, :], gb[:, :], AF.Silu))
        E.do("vector", lambda e: e.tensor_mul(sb[:, :], sb[:, :], gb[:, :]))
        E.do("vector", lambda e: e.tensor_add(xb[:, :], xb[:, :], sb[:, :]))
        E.dma("sync", o_d[k * 128:(k + 1) * 128, :], xb[:, :])
    if standalone:
        P.finish()
    return P


def host_combine(x_fm, s_fm, g_fm):
    P = build_combine()
    base = Prog.basic_inputs()
    ims = []
    for c in range(NCORES):
        tok = slice(c * TCORE, (c + 1) * TCORE)
        ims.append(_W(P, dict(x=np.ascontiguousarray(x_fm[:, tok]),
                        s=np.ascontiguousarray(s_fm[:, tok]),
                        g=np.ascontiguousarray(g_fm[:, tok]))) | base)
    res = P.run(ims)
    return np.concatenate([r["combine_xn_out"] for r in res.results], axis=1)



def build_l2a(P=None):
    T = TCORE
    standalone = P is None
    if standalone:
        P = Prog("l2a")
    P.begin_phase("l2a")
    E = P.E
    xn_d = P.dram_in("x_raw", [1024, T])
    xnout_d = P.dram_out("xn", [1024, T])
    w_d = P.dram_in("w_in", [1024, 3328])
    rw_d = P.dram_in("rms_w", [1024, 1])
    gb_d = P.dram_in("gate_b", [1024, 1])
    cmask_d = P.dram_in("cmask", [128, T])
    z_d = P.dram_out("z", [1024, T])
    q_d = P.dram_out("q", [128, T])
    ix_d = P.dram_out("ix", [1024, T])
    ik_d = P.dram_out("ik", [1024, T])
    cs_d = P.dram_out("cs", [2048, T // 64])
    P.load_basics()
    pp = P.psum_pool()
    xt = [P.sbuf(f"x{k}", [128, T]) for k in range(8)]
    gt = [P.sbuf(f"g{k}", [128, T]) for k in range(8)]
    scr = P.sbuf("scr", [128, MMF])
    ab = (P.sbuf("abb", [1, T]), P.sbuf("aba", [1, T]))
    rw = P.sbuf("rw", [128, 8])
    gbc = P.sbuf("gbc", [128, 8])
    cmask = P.sbuf("cmask_sb", [128, T])
    wpool = P.sbuf_pool(2, [128, MMF], name="wp")
    stage = P.sbuf("stage", [128, T])
    ksb = P.sbuf("ksb", [128, T])
    dec = P.sbuf("dec", [128, T])
    vv = P.sbuf("vv", [128, T])
    sout = P.sbuf("sout", [128, T])
    cstage = P.sbuf("cstage", [128, 16 * (T // 64)])
    NCh = T // 64

    for k in range(8):
        E.dma("sync", xt[k][:, :], xn_d[k * 128:(k + 1) * 128, :])
        E.dma("sync", rw[:, k:k + 1], rw_d[k * 128:(k + 1) * 128, :])
        E.dma("sync", gbc[:, k:k + 1], gb_d[k * 128:(k + 1) * 128, :])
    E.dma("sync", cmask[:, :], cmask_d.ap())
    ln_rows(P, pp, xt, T, 1024, scr, ab)
    apply_ab(P, pp, xt, xt, ab, T, 1024)
    for k in range(8):
        E.dma("sync", xnout_d[k * 128:(k + 1) * 128, :], xt[k][:, :])
    ln_rows(P, pp, xt, T, 1024, scr, ab, rms=True, eps_name="eps1e6")
    apply_ab(P, pp, xt, xt, ab, T, 1024, rms=True)
    for k in range(8):
        E.do("vector", lambda e, k=k: e.tensor_scalar(
            xt[k][:, :], xt[k][:, :], rw[:, k:k + 1], None, ALU.mult))

    def scan_tile(i, val_tile):
        gg = gt[i % 8]
        E.do("vector", lambda e: e.tensor_mul(dec[:, :], gg[:, :],
                                              cmask[:, :]))
        E.do("vector", lambda e: e.tensor_scalar(scr[:, 0:1], scr[:, 0:1],
                                                 1.0, None, ALU.mult))
        E.do("vector", lambda e, g2=gg: e.tensor_scalar(
            vv[:, :], g2[:, :], -1.0, 1.0, ALU.mult, ALU.add))
        E.do("vector", lambda e, v2=val_tile: e.tensor_mul(
            vv[:, :], vv[:, :], v2[:, :]))
        E.do("vector", lambda e: e.tensor_tensor_scan(
            sout[:, :], dec[:, :], vv[:, :], 0.0, ALU.mult, ALU.add))
        dst = ix_d if i < 8 else ik_d
        E.dma("sync", dst[(i % 8) * 128:(i % 8 + 1) * 128, :], sout[:, :])
        ci = i % 8
        E.do("vector", lambda e, i=i: e.tensor_copy(
            cstage[:, (i % 8) * NCh:(i % 8 + 1) * NCh],
            bass.AP(sout, 0, [[T, 128], [64, NCh]])))
        if i % 8 == 7:
            half = 0 if i < 8 else 1
            for kk in range(8):
                E.dma("sync",
                      cs_d[half * 1024 + kk * 128:half * 1024 + (kk + 1) * 128,
                           :],
                      cstage[:, kk * NCh:(kk + 1) * NCh])

    def evac(m, c, ps):
        cw = ps.shape[1]
        CT = ceil_div(T, MMF)
        if m < 8:
            E.do("scalar", lambda e, o=stage[:, c * MMF:c * MMF + cw], p=ps:
                 e.activation(o, p, AF.Sigmoid))
            if c == CT - 1:
                E.dma("sync", z_d[m * 128:(m + 1) * 128, :], stage[:, 0:T])
        elif m < 16:
            E.do("scalar", lambda e, m=m, o=gt[m - 8][:, c * MMF:c * MMF + cw],
                 p=ps: e.activation(o, p, AF.Sigmoid, bias=gbc[:, m - 8:m - 7]))
        elif m < 24:
            E.do("vector", lambda e, o=stage[:, c * MMF:c * MMF + cw], p=ps:
                 e.tensor_copy(o, p))
            if c == CT - 1:
                scan_tile(m - 16, stage)
        elif m == 24:
            E.do("vector", lambda e, o=stage[:, c * MMF:c * MMF + cw], p=ps:
                 e.tensor_copy(o, p))
            if c == CT - 1:
                E.dma("sync", q_d.ap(), stage[:, 0:T])
        else:
            E.do("vector", lambda e, o=ksb[:, c * MMF:c * MMF + cw], p=ps:
                 e.tensor_copy(o, p))
            if c == CT - 1:
                for i in range(8, 16):
                    scan_tile(i, ksb)

    proj_fm(P, pp, wpool, xt, w_d.ap(), 1024, 3328, T, evac)
    if standalone:
        P.finish()
    return P


L2B_DEBUG = False


def build_l2b(P=None):
    T = TCORE
    NCh = 64
    standalone = P is None
    if standalone:
        P = Prog("l2b")
    P.begin_phase("l2b")
    E = P.E
    q_d = P.dram_in("q", [128, T])
    ix_d = P.dram_in("ix", [1024, T])
    ik_d = P.dram_in("ik", [1024, T])
    cs_d = P.dram_in("cs_b", [4096, NCh // 2])
    z_d = P.dram_in("z", [1024, T])
    xn_d = P.dram_in("xn", [1024, T])
    w_d = P.dram_in("w_out", [1024, 1024])
    mask_d = P.dram_in("maskc", [128, 16 * NCh])
    s_d = P.dram_out("s", [1024, T])
    if L2B_DEBUG:
        dbg_w = P.dram_out("dbg_w", [128, 64])
        dbg_ic = P.dram_out("dbg_ic", [128, 4])
        dbg_ot = P.dram_out("dbg_ot", [1024, T])
        dbg_prod = P.dram_out("dbg_prod", [128, 256])
        dbg_ihb = P.dram_out("dbg_ihb", [128, 256])
        dbg_qsb = P.dram_out("dbg_qsb", [128, 256])
    P.load_basics(eye=True)
    pp = P.psum_pool()
    qsb = P.sbuf("qsb", [128, T])
    ihb = P.sbuf("ihb", [128, T])
    ixb = P.sbuf("ixb", [128, T])
    csx = P.sbuf("csx", [128, 8 * NCh])
    csk = P.sbuf("csk2", [128, 8 * NCh])
    cxT = P.sbuf("cxT", [128, NCh])
    mask = P.sbuf("maskc_sb", [128, 16 * NCh])
    prod = P.sbuf("prod", [128, T])
    wsb = P.sbuf("wsb", [128, NCh])
    wtr = P.sbuf("wtr", [128, 128])
    mcol = P.sbuf("mcol", [128, 1])
    icol2dbg = P.sbuf("icol2dbg", [128, 4])
    lcol = P.sbuf("lcol", [128, 1])
    icol = P.sbuf("icol", [128, 1])
    rcol = P.sbuf("rcol", [128, 1])
    irow = P.sbuf("irow", [1, 128])
    ot = [P.sbuf(f"ot{k}", [128, T]) for k in range(8)]
    buf = P.sbuf("buf2", [128, T])
    wpool = P.sbuf_pool(2, [128, MMF], name="wp")

    E.dma("sync", qsb[:, :], q_d.ap())
    E.dma("sync", mask[:, :], mask_d.ap())
    for k in range(8):
        E.dma("sync", csx[:, k * NCh:k * NCh + 32],
              cs_d[k * 128:(k + 1) * 128, :])
        E.dma("sync", csx[:, k * NCh + 32:(k + 1) * NCh],
              cs_d[2048 + k * 128:2048 + (k + 1) * 128, :])
        E.dma("sync", csk[:, k * NCh:k * NCh + 32],
              cs_d[1024 + k * 128:1024 + (k + 1) * 128, :])
        E.dma("sync", csk[:, k * NCh + 32:(k + 1) * NCh],
              cs_d[3072 + k * 128:3072 + (k + 1) * 128, :])

    for h in range(16):
        hb = (h % 2) * 64
        E.dma("sync", ihb[hb:hb + 64, :],
              ik_d[h * 64:(h + 1) * 64, :])
        E.dma("sync", ixb[hb:hb + 64, :],
              ix_d[h * 64:(h + 1) * 64, :])
        pst = pp.next()
        E.do("tensor", lambda e, o=pst[0:64, 0:64],
             src_=csx[hb:hb + 64, (h // 2) * NCh:(h // 2) * NCh + NCh],
             idn=P.eye[hb:hb + 64, hb:hb + 64]:
             e.transpose(o, src_, idn))
        E.do("vector", lambda e, p=pst[0:64, 0:64]:
             e.tensor_copy(cxT[0:64, 0:NCh], p))
        E.do("vector", lambda e, o=prod[hb:hb + 64, :],
             a=qsb[hb:hb + 64, :], b=ihb[hb:hb + 64, :]:
             e.tensor_mul(o, a, b))
        for sub in range(16):
            t0 = sub * 128
            psl = pp.next()
            E.do("tensor", lambda e, o=psl[0:128, 0:1],
                 w=prod[hb:hb + 64, t0:t0 + 128], m1=P.ones[hb:hb + 64, 0:1]:
                 e.matmul(o, w, m1, start=True, stop=True))
            E.do("vector", lambda e, p=psl[0:128, 0:1]:
                 e.tensor_scalar(lcol[:, :], p, 0.125, None, ALU.mult))
            pss = pp.next()
            E.do("tensor", lambda e, o=pss[0:128, 0:NCh],
                 w=qsb[hb:hb + 64, t0:t0 + 128],
                 x=csk[hb:hb + 64, (h // 2) * NCh:(h // 2) * NCh + NCh]:
                 e.matmul(o, w, x, start=True, stop=True))
            E.do("vector", lambda e, p=pss[0:128, 0:NCh], s2=sub:
                 e.scalar_tensor_tensor(wsb[:, 0:NCh], p, 0.125,
                                        mask[:, s2 * NCh:(s2 + 1) * NCh],
                                        ALU.mult, ALU.add))
            E.do("vector", lambda e: e.tensor_reduce(
                mcol[:, :], wsb[:, 0:NCh], AX.X, ALU.max))
            E.do("vector", lambda e: e.tensor_max(mcol[:, :], mcol[:, :],
                                                  lcol[:, :]))
            E.do("vector", lambda e: e.tensor_scalar(
                mcol[:, :], mcol[:, :], -1.0, None, ALU.mult))
            E.do("scalar", lambda e: e.activation(
                wsb[:, 0:NCh], wsb[:, 0:NCh], AF.Exp, bias=mcol[:, 0:1]))
            E.do("vector", lambda e: e.tensor_add(icol[:, :], lcol[:, :],
                                                  mcol[:, :]))
            E.do("scalar", lambda e: e.activation(icol[:, :], icol[:, :],
                                                  AF.Exp))
            E.do("vector", lambda e: e.tensor_reduce(
                rcol[:, :], wsb[:, 0:NCh], AX.X, ALU.add))
            E.do("vector", lambda e: e.tensor_add(rcol[:, :], rcol[:, :],
                                                  icol[:, :]))
            E.do("vector", lambda e: e.reciprocal(rcol[:, :], rcol[:, :]))
            E.do("vector", lambda e: e.tensor_scalar(
                wsb[:, 0:NCh], wsb[:, 0:NCh], rcol[:, 0:1], None, ALU.mult))
            E.do("vector", lambda e: e.tensor_mul(icol[:, :], icol[:, :],
                                                  rcol[:, :]))
            psw = pp.next()
            E.do("tensor", lambda e, o=psw[0:NCh, 0:128]:
                 e.transpose(o, wsb[:, 0:NCh], P.eye[:, :]))
            E.do("vector", lambda e, p=psw[0:NCh, 0:128]:
                 e.tensor_copy(wtr[0:NCh, :], p))
            psr = pp.next()
            E.do("tensor", lambda e, o=psr[0:1, 0:128]:
                 e.transpose(o, icol[:, 0:1], P.eye[:, :]))
            E.do("vector", lambda e, p=psr[0:1, 0:128]:
                 e.tensor_copy(irow[0:1, :], p))
            psb = pp.next()
            E.do("tensor", lambda e, o=psb[0:64, 0:128]:
                 e.matmul(o, P.ones_row[0:1, 0:64], irow[0:1, :],
                          start=True, stop=True))
            pso = pp.next()
            E.do("tensor", lambda e, o=pso[0:64, 0:128]:
                 e.matmul(o, cxT[0:64, 0:NCh], wtr[0:NCh, :],
                          start=True, stop=True))
            E.do("vector", lambda e, o=buf[0:64, 0:128],
                 a=ixb[hb:hb + 64, t0:t0 + 128], p=psb[0:64, 0:128]:
                 e.tensor_mul(o, a, p))
            E.do("vector", lambda e, o=ot[h // 2][hb:hb + 64, t0:t0 + 128],
                 a=buf[0:64, 0:128], p=pso[0:64, 0:128]:
                 e.tensor_add(o, a, p))
            if L2B_DEBUG and h == 0 and sub == 0:
                E.dma("sync", dbg_prod.ap(), prod[:, 0:256])
                E.dma("sync", dbg_ihb.ap(), ihb[:, 0:256])
                E.dma("sync", dbg_qsb.ap(), qsb[:, 0:256])
                E.dma("sync", dbg_w.ap(), wsb[:, 0:64])
                E.do("vector", lambda e: e.tensor_copy(icol2dbg[:, 0:1],
                                                       icol[:, :]))
                E.do("vector", lambda e: e.tensor_copy(icol2dbg[:, 1:2],
                                                       lcol[:, :]))
                E.do("vector", lambda e: e.tensor_copy(icol2dbg[:, 2:3],
                                                       mcol[:, :]))
                E.do("vector", lambda e: e.tensor_copy(icol2dbg[:, 3:4],
                                                       rcol[:, :]))
                E.dma("sync", dbg_ic.ap(), icol2dbg[:, :])
    if L2B_DEBUG:
        for k in range(8):
            E.dma("sync", dbg_ot[k * 128:(k + 1) * 128, :], ot[k][:, :])
    for k in range(8):
        E.dma("sync", buf[:, :], z_d[k * 128:(k + 1) * 128, :])
        E.do("vector", lambda e, k=k: e.tensor_mul(ot[k][:, :], ot[k][:, :],
                                                   buf[:, :]))

    def evac(m, c, ps):
        cw = ps.shape[1]
        if c == 0:
            E.dma("sync", buf[:, 0:T], xn_d[m * 128:(m + 1) * 128, :])
        E.do("vector", lambda e, o=buf[0:128, c * MMF:c * MMF + cw], p=ps:
             e.tensor_add(o, o, p))
        if c == ceil_div(T, MMF) - 1:
            E.dma("sync", s_d[m * 128:(m + 1) * 128, :], buf[:, 0:T])

    proj_fm(P, pp, wpool, ot, w_d.ap(), 1024, 1024, T, evac)
    if standalone:
        P.finish()
    return P


def host_l2(x_fm, rms_weight, W_in, gate_bias, W_out):
    P1 = build_l2a()
    base = Prog.basic_inputs()
    base_eye = Prog.basic_inputs(eye=True)
    Wp = np.concatenate([W_in[:, 0:1024], W_in[:, 2048:3072],
                         W_in[:, 1024:2048], W_in[:, 3072:3328]],
                        axis=1).astype(np.float32).copy()
    cmask = np.ones((128, TCORE), np.float32)
    cmask[:, ::64] = 0.0
    ims = []
    for c in range(NCORES):
        tok = slice(c * TCORE, (c + 1) * TCORE)
        ims.append(_W(P1, dict(x_raw=np.ascontiguousarray(x_fm[:, tok]),
                        w_in=Wp,
                        rms_w=rms_weight.reshape(-1, 1).astype(np.float32),
                        gate_b=gate_bias.reshape(-1, 1).astype(np.float32),
                        cmask=cmask)) | base)
    r1 = P1.run(ims)
    xn_fm = np.concatenate([r["l2a_xn"] for r in r1.results], axis=1)
    cs_all = np.concatenate([r["l2a_cs"] for r in r1.results], axis=1)
    ix_fm = np.concatenate([r["l2a_ix"] for r in r1.results], axis=1)
    ik_fm = np.concatenate([r["l2a_ik"] for r in r1.results], axis=1)
    z_fm = np.concatenate([r["l2a_z"] for r in r1.results], axis=1)
    q_fm = np.concatenate([r["l2a_q"] for r in r1.results], axis=1)

    P2 = build_l2b()
    NCh = 64
    ims = []
    for c in range(NCORES):
        beta = c // 2
        tok = slice(c * TCORE, (c + 1) * TCORE)
        csb = np.ascontiguousarray(np.concatenate(
            [cs_all[:, beta * NCh:beta * NCh + 32],
             cs_all[:, beta * NCh + 32:(beta + 1) * NCh]], axis=0))
        maskc = np.zeros((128, 16 * NCh), np.float32)
        for sub in range(16):
            tglob = (c % 2) * 2048 + sub * 128 + np.arange(128)
            allowed = (tglob[:, None] // 64) > np.arange(NCh)[None, :]
            maskc[:, sub * NCh:(sub + 1) * NCh] = np.where(allowed, 0.0,
                                                           -1e30)
        ims.append(_W(P2, dict(q=np.ascontiguousarray(q_fm[:, tok]),
                        ix=np.ascontiguousarray(ix_fm[:, tok]),
                        ik=np.ascontiguousarray(ik_fm[:, tok]),
                        cs_b=csb,
                        z=np.ascontiguousarray(z_fm[:, tok]),
                        xn=np.ascontiguousarray(xn_fm[:, tok]),
                        w_out=np.ascontiguousarray(W_out),
                        maskc=maskc)) | base_eye)
    r2 = P2.run(ims)
    globals()["_last_r2"] = r2
    return np.concatenate([r["l2b_s"] for r in r2.results], axis=1), xn_fm



def build_l2c(P=None):
    T = TCORE
    TS = 512
    NS = T // TS
    standalone = P is None
    if standalone:
        P = Prog("l2c")
    P.begin_phase("l2c")
    E = P.E
    xn_d = P.dram_in("xn", [1024, T])
    w1_d = P.dram_in("w1t", [1024, 4096])
    w2_d = P.dram_in("w2t", [4096, 1024])
    b1_d = P.dram_in("b1c", [128, 32])
    b2_d = P.dram_in("b2c", [128, 8])
    g_d = P.dram_out("g", [1024, T])
    P.load_basics(eye=True)
    pp = P.psum_pool()
    xt = [P.sbuf(f"x{k}", [128, T]) for k in range(8)]
    ht = [P.sbuf(f"h{k}", [128, TS], BF16) for k in range(32)]
    wpool = P.sbuf_pool(2, [128, MMF], name="wp")
    wsg = P.sbuf("wsg", [128, MMF])
    wsgb = P.sbuf_pool(2, [128, 128], BF16, name="wsgb")
    b1 = P.sbuf("b1c_sb", [128, 32])
    b2 = P.sbuf("b2c_sb", [128, 8])
    a1 = P.sbuf("a1_sb", [128, 32])
    a2 = P.sbuf("a2_sb", [128, 8])
    arow = P.sbuf("arow2", [1, 4096])
    gstage = P.sbuf("gstage", [128, TS])
    for k in range(8):
        E.dma("sync", xt[k][:, :], xn_d[k * 128:(k + 1) * 128, :])
    E.dma("sync", b1[:, :], b1_d.ap())
    E.dma("sync", b2[:, :], b2_d.ap())

    def alphas(w_dram, din, dout, dst, ncols):
        for mg in range(ceil_div(dout, MMF)):
            gw = min(MMF, dout - mg * MMF)
            ps = pp.next()
            for k in range(din // 128):
                wtile = wpool.next()
                E.dma("sync", wtile[:, 0:gw],
                      w_dram[k * 128:(k + 1) * 128, mg * MMF:mg * MMF + gw])
                E.do("scalar", lambda e, o=wsg[:, 0:gw], x=wtile[:, 0:gw]:
                     e.activation(o, x, AF.Abs))
                E.do("tensor", lambda e, o=ps[0:1, 0:gw], x=wsg[:, 0:gw],
                     st=(k == 0), sp=(k == din // 128 - 1):
                     e.matmul(o, P.ones[:, 0:1], x, start=st, stop=sp))
            E.do("vector", lambda e, o=arow[0:1, mg * MMF:mg * MMF + gw],
                 p=ps[0:1, 0:gw], d=din:
                 e.tensor_scalar(o, p, 1.0 / d, None, ALU.mult))
        for m in range(ncols):
            ps = pp.next()
            E.do("tensor", lambda e, o=ps[0:128, 0:1], m=m:
                 e.transpose(o, arow[0:1, m * 128:(m + 1) * 128],
                             P.eye[0:1, 0:1]))
            E.do("vector", lambda e, o=dst[:, m:m + 1], p=ps[0:128, 0:1]:
                 e.tensor_copy(o, p))

    alphas(w1_d.ap(), 1024, 4096, a1, 32)
    alphas(w2_d.ap(), 4096, 1024, a2, 8)

    def proj_sign(xtiles, w_dram, din, dout, TS2, evac, xcol0):
        for m in range(ceil_div(dout, 128)):
            banks = [pp.next() for _ in range(ceil_div(TS2, MMF))]
            for k in range(din // 128):
                wtile = wpool.next()
                E.dma("sync", wtile[:, 0:128],
                      w_dram[k * 128:(k + 1) * 128, m * 128:(m + 1) * 128])
                wb = wsgb.next()
                E.do("scalar", lambda e, o=wb[:, 0:128], x=wtile[:, 0:128]:
                     e.activation(o, x, AF.Sign))
                for c in range(ceil_div(TS2, MMF)):
                    cw = min(MMF, TS2 - c * MMF)
                    E.do("tensor",
                         lambda e, o=banks[c][0:128, 0:cw], w=wb[:, 0:128],
                                x=xtiles[k][0:128, xcol0 + c * MMF:
                                            xcol0 + c * MMF + cw],
                                st=(k == 0), sp=(k == din // 128 - 1):
                             e.matmul(o, w, x, start=st, stop=sp))
            for c in range(ceil_div(TS2, MMF)):
                cw = min(MMF, TS2 - c * MMF)
                evac(m, c, banks[c][0:128, 0:cw])

    xtb = cast_tiles(P, "xtb", xt, T)

    for sub in range(NS):
        xc0 = sub * TS

        def evac1(m, c, ps):
            E.do("scalar", lambda e, m=m, o=ht[m][:, 0:TS], p=ps:
                 e.activation(o, p, AF.Gelu, bias=b1[:, m:m + 1],
                              scale=a1[:, m:m + 1]))

        proj_sign(xtb, w1_d.ap(), 1024, 4096, TS, evac1, xc0)

        def evac2(m, c, ps):
            E.do("vector", lambda e, m=m, o=gstage[:, 0:TS], p=ps:
                 e.tensor_scalar(o, p, a2[:, m:m + 1], b2[:, m:m + 1],
                                 ALU.mult, ALU.add))
            E.dma("sync", g_d[m * 128:(m + 1) * 128, xc0:xc0 + TS],
                  gstage[:, 0:TS])

        proj_sign(ht, w2_d.ap(), 4096, 1024, TS, evac2, 0)
    if standalone:
        P.finish()
    return P


def host_l2c(xn_fm, w1, b1, w2, b2):
    P = build_l2c()
    base = Prog.basic_inputs(eye=True)
    w1t = np.ascontiguousarray(w1.T)
    w2t = np.ascontiguousarray(w2.T)
    b1c = b1.reshape(32, 128).T.astype(np.float32).copy()
    b2c = b2.reshape(8, 128).T.astype(np.float32).copy()
    ims = []
    for c in range(NCORES):
        tok = slice(c * TCORE, (c + 1) * TCORE)
        ims.append(_W(P, dict(xn=np.ascontiguousarray(xn_fm[:, tok]),
                        w1t=w1t, w2t=w2t, b1c=b1c, b2c=b2c)) | base)
    res = P.run(ims)
    return np.concatenate([r["l2c_g"] for r in res.results], axis=1)



def build_l3a(P=None):
    T = TCORE
    standalone = P is None
    if standalone:
        P = Prog("l3a")
    P.begin_phase("l3a")
    E = P.E
    x_d = P.dram_in("x", [1024, T])
    wq_d = P.dram_in("wq", [1024, 1024])
    wk_d = P.dram_in("wk", [1024, 1024])
    wv_d = P.dram_in("wv", [1024, 1024])
    wb_d = P.dram_in("wb", [1024, 16])
    ind2_d = P.dram_in("ind2", [2, 128])
    ind2T_d = P.dram_in("ind2T_in", [128, 2])
    xn_d = P.dram_out("xn", [1024, T])
    q_d = P.dram_out("qn", [1024, T])
    k_d = P.dram_out("kn", [1024, T])
    v_d = P.dram_out("v", [1024, T])
    be_d = P.dram_out("beta", [16, T])
    P.load_basics()
    pp = P.psum_pool()
    xt = [P.sbuf(f"x{k}", [128, T]) for k in range(8)]
    scr = P.sbuf("scr", [128, MMF])
    ab = (P.sbuf("abb", [1, T]), P.sbuf("aba", [1, T]))
    wpool = P.sbuf_pool(2, [128, MMF], name="wp")
    stage = P.sbuf("stage", [128, T])
    rn2 = P.sbuf("rn2", [2, T])
    ind2 = P.sbuf("ind2_sb", [2, 128])
    ind2T = P.sbuf("ind2T_sb", [128, 2])
    bst = P.sbuf("bst", [16, T])
    for k in range(8):
        E.dma("sync", xt[k][:, :], x_d[k * 128:(k + 1) * 128, :])
    E.dma("sync", ind2[:, :], ind2_d.ap())
    E.dma("sync", ind2T[:, :], ind2T_d.ap())
    ln_rows(P, pp, xt, T, 1024, scr, ab)
    apply_ab(P, pp, xt, xt, ab, T, 1024)
    for k in range(8):
        E.dma("sync", xn_d[k * 128:(k + 1) * 128, :], xt[k][:, :])

    def norm_evac(out_d):
        def evac(m, c, ps):
            E.do("vector", lambda e, o=stage[:, c * MMF:c * MMF + ps.shape[1]],
                 p=ps: e.tensor_copy(o, p))
            if c == ceil_div(T, MMF) - 1:
                CT = ceil_div(T, MMF)
                for c2 in range(CT):
                    cw = min(MMF, T - c2 * MMF)
                    psA = pp.next()
                    E.do("scalar", lambda e, o=scr[:, 0:cw],
                         x=stage[:, c2 * MMF:c2 * MMF + cw]:
                             e.activation(o, x, AF.Square))
                    E.do("tensor", lambda e, o=psA[0:2, 0:cw], x=scr[:, 0:cw]:
                         e.matmul(o, ind2T[:, 0:2], x, start=True, stop=True))
                    E.do("vector", lambda e, o=rn2[0:2, c2 * MMF:c2 * MMF + cw],
                         p=psA[0:2, 0:cw]: e.tensor_copy(o, p))
                E.do("scalar", lambda e: e.activation(
                    rn2[0:2, 0:T], rn2[0:2, 0:T], AF.Ln,
                    bias=P.ccol("eps1e12")[0:2, :]))
                E.do("scalar", lambda e: e.activation(
                    rn2[0:2, 0:T], rn2[0:2, 0:T], AF.Exp, scale=0.5))
                E.do("vector", lambda e: e.tensor_scalar_max(
                    rn2[0:2, 0:T], rn2[0:2, 0:T], 1e-12))
                E.do("vector", lambda e: e.reciprocal(rn2[0:2, 0:T],
                                                      rn2[0:2, 0:T]))
                for c2 in range(CT):
                    cw = min(MMF, T - c2 * MMF)
                    psb = pp.next()
                    E.do("tensor", lambda e, o=psb[:, 0:cw],
                         r=rn2[0:2, c2 * MMF:c2 * MMF + cw]:
                             e.matmul(o, ind2[:, :], r, start=True, stop=True))
                    E.do("vector", lambda e, o=stage[:, c2 * MMF:c2 * MMF + cw],
                         p=psb[:, 0:cw]: e.tensor_mul(o, o, p))
                mm = evac.m_holder[0]
                E.dma("sync", out_d[mm * 128:(mm + 1) * 128, :],
                      stage[:, 0:T])
        evac.m_holder = [0]
        def wrapper(m, c, ps):
            evac.m_holder[0] = m
            evac(m, c, ps)
        return wrapper

    xtb = cast_tiles(P, "xtb", xt, T)
    wbf = P.sbuf_pool(2, [128, 128], BF16, name="wbf")
    proj_fm(P, pp, wpool, xtb, wq_d.ap(), 1024, 1024, T, norm_evac(q_d),
            bfpool=wbf)
    proj_fm(P, pp, wpool, xtb, wk_d.ap(), 1024, 1024, T, norm_evac(k_d),
            bfpool=wbf)

    def evac_v(m, c, ps):
        E.do("vector", lambda e, o=stage[:, c * MMF:c * MMF + ps.shape[1]],
             p=ps: e.tensor_copy(o, p))
        if c == ceil_div(T, MMF) - 1:
            E.dma("sync", v_d[m * 128:(m + 1) * 128, :], stage[:, 0:T])

    proj_fm(P, pp, wpool, xtb, wv_d.ap(), 1024, 1024, T, evac_v, bfpool=wbf)

    def evac_b(m, c, ps):
        E.do("scalar", lambda e, o=bst[0:16, c * MMF:c * MMF + ps.shape[1]],
             p=ps[0:16, 0:ps.shape[1]]: e.activation(o, p, AF.Sigmoid))
        if c == ceil_div(T, MMF) - 1:
            E.dma("sync", be_d.ap(), bst[:, 0:T])

    proj_fm(P, pp, wpool, xtb, wb_d.ap(), 1024, 16, T, evac_b, bfpool=wbf)
    if standalone:
        P.finish()
    return P


def build_l3b(P=None):
    L = 4096
    NCK = 8
    standalone = P is None
    if standalone:
        P = Prog("l3b")
    P.begin_phase("l3b")
    E = P.E
    q_d = P.dram_in("qn", [1024, TCORE])
    k_d = P.dram_in("kn", [1024, TCORE])
    v_d = P.dram_in("v", [1024, TCORE])
    be_d = P.dram_in("beta", [16, TCORE])
    ssel_d = P.dram_in("self_sel", [128, 64 * 128])
    csel_d = P.dram_in("colsel", [128, 64 * 128])
    psel_d = P.dram_in("pairsel", [8, 4 * 128])
    y_d = P.dram_out("y", [1024, TCORE])
    P.load_basics()
    pp = P.psum_pool(n=6)
    ypsum = P.psum("ypsum_ded", [128, MMF])
    kt = P.sbuf("kt3", [128, L])
    qt = P.sbuf("qt3", [128, L])
    vt3 = P.sbuf("vt3", [128, L])
    bk = P.sbuf("bk3", [128, L])
    be = P.sbuf("be3", [8, L])
    ssel = P.sbuf("ssel_sb", [128, 64 * 128])
    csel = P.sbuf("csel_sb", [128, 64 * 128])
    psel = P.sbuf("psel_sb", [8, 4 * 128])
    scol = P.sbuf("scol3", [128, 64])
    t1 = P.sbuf("t1_3", [128, MMF])
    aa = P.sbuf("aa3", [128, MMF])
    uu = P.sbuf("uu3", [128, MMF])
    ss = P.sbuf("ss3", [128, MMF])
    yt = P.sbuf("yt3", [128, MMF])
    ystage = P.sbuf("ystage3", [128, MMF])
    E.dma("sync", be[0:8, 0:TCORE], be_d[0:8, :])
    E.dma("sync", be[0:8, TCORE:L], be_d[8:16, :])
    E.dma("sync", ssel[:, :], ssel_d.ap())
    E.dma("sync", csel[:, :], csel_d.ap())
    E.dma("sync", psel[:, :], psel_d.ap())

    for pg in range(4):
        E.dma("sync", kt[:, 0:TCORE], k_d[pg * 128:(pg + 1) * 128, :])
        E.dma("sync", kt[:, TCORE:L], k_d[512 + pg * 128:512 + (pg + 1) * 128, :])
        E.dma("sync", qt[:, 0:TCORE], q_d[pg * 128:(pg + 1) * 128, :])
        E.dma("sync", qt[:, TCORE:L], q_d[512 + pg * 128:512 + (pg + 1) * 128, :])
        E.dma("sync", vt3[:, 0:TCORE], v_d[pg * 128:(pg + 1) * 128, :])
        E.dma("sync", vt3[:, TCORE:L], v_d[512 + pg * 128:512 + (pg + 1) * 128, :])
        for c in range(NCK):
            psb = pp.next()
            E.do("tensor", lambda e, o=psb[:, 0:MMF], c=c, pg=pg:
                 e.matmul(o, psel[:, pg * 128:(pg + 1) * 128],
                          be[0:8, c * MMF:(c + 1) * MMF],
                          start=True, stop=True))
            E.do("vector", lambda e, c=c, p=psb[:, 0:MMF]:
                 e.tensor_mul(bk[:, c * MMF:(c + 1) * MMF],
                              kt[:, c * MMF:(c + 1) * MMF], p))
        E.do("vector", lambda e: e.memset(scol[:, :], 0.0))
        for c in range(NCK):
            t0 = c * MMF
            for i in range(64):
                pk = pp.next()
                E.do("tensor", lambda e, o=pk[:, 0:MMF], i=i, t0=t0:
                     e.matmul(o, ssel[:, i * 128:(i + 1) * 128],
                              kt[:, t0:t0 + MMF], start=True, stop=True))
                pv = pp.next()
                E.do("tensor", lambda e, o=pv[:, 0:MMF], i=i, t0=t0:
                     e.matmul(o, ssel[:, i * 128:(i + 1) * 128],
                              vt3[:, t0:t0 + MMF], start=True, stop=True))
                E.do("vector", lambda e, p=pk[:, 0:MMF], t0=t0:
                     e.tensor_mul(t1[:, 0:MMF], bk[:, t0:t0 + MMF], p))
                E.do("vector", lambda e:
                     e.tensor_scalar(aa[:, 0:MMF], t1[:, 0:MMF], -1.0, 1.0,
                                     ALU.mult, ALU.add))
                E.do("vector", lambda e, p=pv[:, 0:MMF], t0=t0:
                     e.tensor_mul(uu[:, 0:MMF], bk[:, t0:t0 + MMF], p))
                E.do("vector", lambda e, i=i: e.tensor_tensor_scan(
                    ss[:, 0:MMF], aa[:, 0:MMF], uu[:, 0:MMF],
                    scol[:, i:i + 1], ALU.mult, ALU.add))
                E.do("vector", lambda e, i=i: e.tensor_copy(
                    scol[:, i:i + 1], ss[:, MMF - 1:MMF]))
                E.do("vector", lambda e, t0=t0: e.tensor_mul(
                    yt[:, 0:MMF], ss[:, 0:MMF], qt[:, t0:t0 + MMF]))
                E.do("tensor", lambda e, o=ypsum[:, 0:MMF], i=i:
                     e.matmul(o, csel[:, i * 128:(i + 1) * 128], yt[:, 0:MMF],
                              start=(i == 0), stop=(i == 63)))
            E.do("vector", lambda e, p=ypsum[:, 0:MMF]:
                 e.tensor_copy(ystage[:, 0:MMF], p))
            shard = 0 if c < 4 else 512
            E.dma("sync",
                  y_d[shard + pg * 128:shard + (pg + 1) * 128,
                      (c % 4) * MMF:(c % 4) * MMF + MMF],
                  ystage[:, 0:MMF])
    if standalone:
        P.finish()
    return P


def host_l3b(q_fm, k_fm, v_fm, beta_fm):
    P = build_l3b()
    base = Prog.basic_inputs()
    ssel = np.zeros((128, 64 * 128), np.float32)
    csel = np.zeros((128, 64 * 128), np.float32)
    for i in range(64):
        for p2 in range(2):
            ssel[p2 * 64 + i, i * 128 + p2 * 64:i * 128 + (p2 + 1) * 64] = 1.0
            csel[p2 * 64:(p2 + 1) * 64, i * 128 + p2 * 64 + i] = 1.0
    psel = np.zeros((8, 4 * 128), np.float32)
    for pg in range(4):
        psel[2 * pg, pg * 128:pg * 128 + 64] = 1.0
        psel[2 * pg + 1, pg * 128 + 64:(pg + 1) * 128] = 1.0
    ims = []
    for c in range(NCORES):
        b = c // 2
        hh = (c % 2) * 8
        rows = slice(hh * 64, (hh + 8) * 64)
        t1 = slice(b * 4096, b * 4096 + TCORE)
        t2 = slice(b * 4096 + TCORE, (b + 1) * 4096)
        stk = lambda a: np.concatenate([a[rows, t1], a[rows, t2]], axis=0)
        ims.append(_W(P, dict(qn=np.ascontiguousarray(stk(q_fm)),
                        kn=np.ascontiguousarray(stk(k_fm)),
                        v=np.ascontiguousarray(stk(v_fm)),
                        beta=np.ascontiguousarray(np.concatenate(
                            [beta_fm[hh:hh + 8, t1],
                             beta_fm[hh:hh + 8, t2]], axis=0)),
                        self_sel=ssel, colsel=csel, pairsel=psel)) | base)
    res = P.run(ims)
    y = np.zeros((1024, NTOK), np.float32)
    for c in range(NCORES):
        b = c // 2
        hh = (c % 2) * 8
        yc = res.results[c]["l3b_y"]
        y[hh * 64:(hh + 8) * 64, b * 4096:b * 4096 + TCORE] = yc[0:512, :]
        y[hh * 64:(hh + 8) * 64, b * 4096 + TCORE:(b + 1) * 4096] = \
            yc[512:1024, :]
    return y


def build_l3c(P=None):
    T = TCORE
    standalone = P is None
    if standalone:
        P = Prog("l3c")
    P.begin_phase("l3c")
    E = P.E
    o_d = P.dram_in("o_in", [1024, T])
    xn_d = P.dram_in("xn", [1024, T])
    lngb_d = P.dram_in("lngb", [128, 16])
    wg_d = P.dram_in("wg", [1024, 1024])
    wo_d = P.dram_in("wout", [1024, 1024])
    s_d = P.dram_out("s", [1024, T])
    P.load_basics()
    pp = P.psum_pool()
    ot = [P.sbuf(f"o{k}", [128, T]) for k in range(8)]
    xnt = [P.sbuf(f"xn{k}", [128, T]) for k in range(8)]
    scr = P.sbuf("scr", [128, MMF])
    ab = (P.sbuf("abb", [1, T]), P.sbuf("aba", [1, T]))
    lngb = P.sbuf("lngb_sb", [128, 16])
    wpool = P.sbuf_pool(2, [128, MMF], name="wp")
    stage = P.sbuf("stage", [128, T])
    for k in range(8):
        E.dma("sync", ot[k][:, :], o_d[k * 128:(k + 1) * 128, :])
        E.dma("sync", xnt[k][:, :], xn_d[k * 128:(k + 1) * 128, :])
    E.dma("sync", lngb[:, :], lngb_d.ap())
    ln_rows(P, pp, ot, T, 1024, scr, ab)
    apply_ab(P, pp, ot, ot, ab, T, 1024)
    for k in range(8):
        E.do("vector", lambda e, k=k: e.tensor_scalar(
            ot[k][:, :], ot[k][:, :], lngb[:, k:k + 1], lngb[:, 8 + k:9 + k],
            ALU.mult, ALU.add))

    def evac_g(m, c, ps):
        E.do("scalar", lambda e, o=stage[:, c * MMF:c * MMF + ps.shape[1]],
             p=ps: e.activation(o, p, AF.Silu))
        if c == ceil_div(T, MMF) - 1:
            E.do("vector", lambda e, m=m: e.tensor_mul(
                ot[m][:, :], ot[m][:, :], stage[:, 0:T]))

    xb = cast_tiles(P, "xb", xnt, T)
    wbf = P.sbuf_pool(2, [128, 128], BF16, name="wbf")
    proj_fm(P, pp, wpool, xb, wg_d.ap(), 1024, 1024, T, evac_g, bfpool=wbf)

    def evac_s(m, c, ps):
        E.do("vector", lambda e, o=stage[:, c * MMF:c * MMF + ps.shape[1]],
             p=ps: e.tensor_copy(o, p))
        if c == ceil_div(T, MMF) - 1:
            E.dma("sync", s_d[m * 128:(m + 1) * 128, :], stage[:, 0:T])

    cast_into(P, xb, ot, T)
    proj_fm(P, pp, wpool, xb, wo_d.ap(), 1024, 1024, T, evac_s, bfpool=wbf)
    if standalone:
        P.finish()
    return P


L3D_DEBUG = False


def build_l3d(P=None):
    T = TCORE
    TS = 512
    NS = T // TS
    standalone = P is None
    if standalone:
        P = Prog("l3d")
    P.begin_phase("l3d")
    E = P.E
    xn_d = P.dram_in("xn", [1024, T])
    lt_d = P.dram_in("lt", [1024, 16])
    ltT_d = P.dram_tmp("ltT_t", [16, 1024])
    w1_d = P.dram_in("w1t", [1024, 4096])
    b1_d = P.dram_in("b1c", [128, 32])
    w2_d = P.dram_in("w2t", [4096, 1024])
    b2_d = P.dram_in("b2c", [128, 8])
    sc_d = P.dram_in("scale11", [1, 1])
    sel4_d = P.dram_in("sel4", [4, 4 * 128])
    g_d = P.dram_out("g", [1024, T])
    if L3D_DEBUG:
        dbg_h0 = P.dram_out("dbg_h0", [128, 512])
        dbg_h31 = P.dram_out("dbg_h31", [128, 512])
        dbg_p = P.dram_out("dbg_p", [4, T])
    P.load_basics()
    pp = P.psum_pool(n=6)
    rbded = P.psum("rbded", [128, MMF])
    pbded = P.psum("pbded", [128, MMF])
    xt = [P.sbuf(f"x{k}", [128, T]) for k in range(8)]
    ht = [P.sbuf(f"h{k}", [128, TS], BF16) for k in range(32)]
    hscr = P.sbuf("hscr", [128, TS])
    wbf = P.sbuf_pool(2, [128, MMF], BF16, name="wbf")
    lts = [P.sbuf(f"lts{k}", [128, 16]) for k in range(8)]
    ltTs = P.sbuf("ltTs", [16, 1024])
    b1 = P.sbuf("b1d_sb", [128, 32])
    b2 = P.sbuf("b2d_sb", [128, 8])
    sel4 = P.sbuf("sel4_sb", [4, 4 * 128])
    scrow = P.sbuf("scrow", [1, 1])
    sccol = P.sbuf("sccol", [128, 1])
    colset = P.sbuf("colset", [128, 32])
    pstage = P.sbuf("pstage", [4, T])
    rt = [P.sbuf(f"rt{k}", [4, T]) for k in range(4)]
    tsb = P.sbuf("tsb", [128, MMF])
    t2sb = P.sbuf("t2sb", [128, MMF])
    wpool = P.sbuf_pool(2, [128, MMF], name="wp")
    P.load_basics(eye=True)
    for k in range(8):
        E.dma("sync", xt[k][:, :], xn_d[k * 128:(k + 1) * 128, :])
        E.dma("sync", lts[k][:, :], lt_d[k * 128:(k + 1) * 128, :])
    for k in range(8):
        pst = pp.next()
        E.do("tensor", lambda e, k=k, o=pst[0:16, 0:128]:
             e.transpose(o, lts[k][:, :], P.eye[:, :]))
        E.do("vector", lambda e, k=k, p=pst[0:16, 0:128]:
             e.tensor_copy(ltTs[0:16, k * 128:(k + 1) * 128], p))
    E.dma("sync", ltT_d.ap(), ltTs[:, :])
    E.dma("sync", b1[:, :], b1_d.ap())
    E.dma("sync", b2[:, :], b2_d.ap())
    E.dma("sync", sel4[:, :], sel4_d.ap())
    E.dma("sync", scrow[:, :], sc_d.ap())
    for m1 in range(2):
        E.dma("sync",
              bass.AP(colset, m1 * 4, [[32, 128], [8, 4], [1, 4]]),
              bass.AP(ltT_d, 8 * 1024 + m1 * 512,
                      [[4, 128], [2048, 4], [1, 4]]))
    psx = pp.next()
    E.do("tensor", lambda e: e.matmul(psx[:, 0:1], P.ones_row[0:1, :],
                                      scrow[0:1, 0:1], start=True, stop=True))
    E.do("vector", lambda e: e.tensor_copy(sccol[:, :], psx[:, 0:1]))
    for k in range(8):
        ev = bass.AP(lts[k], 0, [[16, 128], [2, 8]])
        E.do("vector", lambda e, ev=ev: e.tensor_scalar(
            ev, ev, sccol[:, 0:1], None, ALU.mult))
    CT = ceil_div(T, MMF)
    for c in range(CT):
        cw = min(MMF, T - c * MMF)
        psp = pp.next()
        for k in range(8):
            E.do("tensor", lambda e, o=psp[0:4, 0:cw],
                 w=bass.AP(lts[k], 1, [[16, 128], [2, 4]]),
                 x=xt[k][:, c * MMF:c * MMF + cw],
                 st=(k == 0): e.matmul(o, w, x, start=st, stop=(k == 7)))
        E.do("vector", lambda e, c=c, cw=cw, p=psp[0:4, 0:cw]:
             e.tensor_copy(pstage[0:4, c * MMF:c * MMF + cw], p))

    xtb = cast_tiles(P, "xtb", xt, T)

    for sub in range(NS):
        xc0 = sub * TS

        def evac1(m, c, ps):
            cw = ps.shape[1]
            pb = pbded
            E.do("tensor", lambda e, o=pb[:, 0:cw],
                 w=sel4[:, (m // 8) * 128:(m // 8 + 1) * 128],
                 x=pstage[0:4, xc0:xc0 + cw]:
                 e.matmul(o, w, x, start=True, stop=True))
            E.do("vector", lambda e, p=pb[:, 0:cw],
                 s=lts[m % 8][:, 2 * (m // 8):2 * (m // 8) + 1]:
                 e.tensor_scalar(tsb[:, 0:cw], p, s, None, ALU.mult))
            E.do("vector", lambda e, p=ps: e.tensor_add(t2sb[0:128, 0:cw],
                                                        tsb[0:128, 0:cw], p))
            E.do("scalar", lambda e, m=m, o=ht[m][:, 0:cw]:
                 e.activation(o, t2sb[0:128, 0:cw], AF.Gelu,
                              bias=b1[:, m:m + 1]))

        proj_fm_wide(P, pp, wpool, xtb, w1_d.ap(), 1024, 4096, TS, evac1,
                     xcol0=xc0, bfpool=wbf)
        if L3D_DEBUG and sub == 0:
            E.dma("sync", dbg_h0.ap(), ht[0][:, :])
            E.dma("sync", dbg_h31.ap(), ht[31][:, :])
            E.dma("sync", dbg_p.ap(), pstage[:, :])
        for hb in range(4):
            psr = pp.next()
            for kk in range(8):
                E.do("scalar", lambda e, hb=hb, kk=kk, o=hscr[:, 0:TS]:
                     e.activation(o, ht[hb * 8 + kk][:, 0:TS], AF.Copy))
                E.do("tensor", lambda e, o=psr[0:4, 0:TS],
                     w=bass.AP(lts[kk], 9, [[16, 128], [2, 4]]),
                     x=hscr[:, 0:TS],
                     st=(kk == 0):
                     e.matmul(o, w, x, start=st, stop=(kk == 7)))
            E.do("vector", lambda e, o=rt[hb][0:4, xc0:xc0 + TS],
                 p=psr[0:4, 0:TS]:
                 e.tensor_scalar(o, p, sccol[0:4, 0:1], None, ALU.mult))

        def evac2(m, c, ps):
            cw = ps.shape[1]
            E.do("vector", lambda e, m=m, p=ps:
                 e.tensor_scalar(tsb[:, 0:cw], p, 1.0, b2[:, m:m + 1],
                                 ALU.mult, ALU.add))
            for hb in range(4):
                rb = rbded
                E.do("tensor", lambda e, o=rb[:, 0:cw],
                     w=sel4[:, (m // 2) * 128:(m // 2 + 1) * 128],
                     x=rt[hb][0:4, xc0:xc0 + cw]:
                     e.matmul(o, w, x, start=True, stop=True))
                E.do("vector", lambda e, m=m, hb=hb, p=rb[:, 0:cw]:
                     e.scalar_tensor_tensor(tsb[:, 0:cw], p,
                                            colset[:, m * 4 + hb:m * 4 + hb + 1],
                                            tsb[:, 0:cw], ALU.mult, ALU.add))
            E.dma("sync", g_d[m * 128:(m + 1) * 128, xc0:xc0 + cw],
                  tsb[:, 0:cw])

        proj_fm_wide(P, pp, wpool, ht, w2_d.ap(), 4096, 1024, TS, evac2,
                     bfpool=wbf)
    if standalone:
        P.finish()
    return P



def host_l3a(x_fm, Wq, Wk, Wv, Wbeta):
    P = build_l3a()
    base = Prog.basic_inputs()
    ind2 = np.zeros((2, 128), np.float32)
    ind2[0, 0:64] = 1.0
    ind2[1, 64:128] = 1.0
    ims = []
    for c in range(NCORES):
        tok = slice(c * TCORE, (c + 1) * TCORE)
        ims.append(_W(P, dict(x=np.ascontiguousarray(x_fm[:, tok]),
                        wq=np.ascontiguousarray(Wq),
                        wk=np.ascontiguousarray(Wk),
                        wv=np.ascontiguousarray(Wv),
                        wb=np.ascontiguousarray(Wbeta),
                        ind2=ind2,
                        ind2T_in=np.ascontiguousarray(ind2.T))) | base)
    res = P.run(ims)
    out = {}
    for nm in ("xn", "qn", "kn", "v", "beta"):
        out[nm] = np.concatenate([r["l3a_" + nm] for r in res.results],
                                 axis=1)
    return out


def host_l3c(y_fm, xn_fm, ln_g, ln_b, Wg, Wout):
    P = build_l3c()
    base = Prog.basic_inputs()
    lngb = np.zeros((128, 16), np.float32)
    lngb[:, 0:8] = ln_g.reshape(8, 128).T
    lngb[:, 8:16] = ln_b.reshape(8, 128).T
    ims = []
    for c in range(NCORES):
        tok = slice(c * TCORE, (c + 1) * TCORE)
        ims.append(_W(P, dict(o_in=np.ascontiguousarray(y_fm[:, tok]),
                        xn=np.ascontiguousarray(xn_fm[:, tok]),
                        lngb=lngb, wg=np.ascontiguousarray(Wg),
                        wout=np.ascontiguousarray(Wout))) | base)
    res = P.run(ims)
    return np.concatenate([r["l3c_s"] for r in res.results], axis=1)


def host_l3d(xn_fm, W1, b1, W2, b2, scale):
    P = build_l3d()
    base = Prog.basic_inputs(eye=True)
    sel4 = np.zeros((4, 4 * 128), np.float32)
    for k in range(4):
        sel4[k, k * 128:(k + 1) * 128] = 1.0
    w1t = np.ascontiguousarray(W1.T)
    w2t = np.ascontiguousarray(W2.T)
    b1c = b1.reshape(32, 128).T.astype(np.float32).copy()
    b2c = b2.reshape(8, 128).T.astype(np.float32).copy()
    ims = []
    for c in range(NCORES):
        b = c // 2
        tok = slice(c * TCORE, (c + 1) * TCORE)
        lt = np.ascontiguousarray(xn_fm[:, b * 4096 + 1:b * 4096 + 17])
        ims.append(_W(P, dict(xn=np.ascontiguousarray(xn_fm[:, tok]),
                        lt=lt,
                        w1t=w1t, b1c=b1c, w2t=w2t, b2c=b2c,
                        scale11=np.asarray(scale, np.float32).reshape(1, 1),
                        sel4=sel4)) | base)
    res = P.run(ims)
    return np.concatenate([r["l3d_g"] for r in res.results], axis=1)



def kernel_multilaunch(**inputs):
    g = lambda n: np.ascontiguousarray(np.asarray(inputs[n], np.float32))
    x = g("L0_attn_x")
    x_fm = np.ascontiguousarray(x.reshape(NTOK, DM).T)

    o = host_l0a(x_fm, g("L0_attn_Win"), g("L0_attn_conv_w"),
                 g("L0_attn_conv_b"), g("L0_attn_dt_bias"))
    y = host_l0b(o["xssm"], o["dtp"], o["bmat"], o["cmat"],
                 g("L0_attn_A_log"), g("L0_attn_D"))
    ffn0 = {k[7:]: g(k) for k in inputs if k.startswith("L0_ffn_")}
    x_fm = host_l0c(y, o["z"], o["xn"], x_fm, g("L0_attn_norm_w"),
                    g("L0_attn_Wout"), ffn0)

    o1 = host_l1a(x_fm, g("L1_attn_in_proj_w"), g("L1_attn_conv_w"),
                  g("L1_attn_x_proj_w"))
    y1 = host_l1b(o1["dtb"], g("L1_attn_dt_w"), g("L1_attn_dt_b"),
                  g("L1_attn_A_param"))
    s1 = host_l1c(y1, o1["u"], o1["z"], g("L1_attn_D_param"),
                  g("L1_attn_out_w"), g("L1_attn_out_b"))
    xf = o1["xn"]
    for li in range(2):
        xf = host_l1ffn_layer(xf, g("L1_ffn_wqkv")[li],
                              g("L1_ffn_q_scale")[li], g("L1_ffn_k_scale")[li],
                              g("L1_ffn_wo")[li], g("L1_ffn_ln_g")[li],
                              g("L1_ffn_ln_b")[li], g("L1_ffn_w1")[li],
                              g("L1_ffn_b1")[li], g("L1_ffn_w2")[li],
                              g("L1_ffn_b2")[li])
    x_fm = host_combine(x_fm, s1, xf)

    s2, xn2 = host_l2(x_fm, g("L2_attn_rms_weight"), g("L2_attn_W_in"),
                      g("L2_attn_gate_bias"), g("L2_attn_W_out"))
    g2 = host_l2c(xn2, g("L2_ffn_w1"), g("L2_ffn_b1"), g("L2_ffn_w2"),
                  g("L2_ffn_b2"))
    x_fm = host_combine(x_fm, s2, g2)

    o3 = host_l3a(x_fm, g("L3_attn_Wq"), g("L3_attn_Wk"), g("L3_attn_Wv"),
                  g("L3_attn_Wbeta"))
    y3 = host_l3b(o3["qn"], o3["kn"], o3["v"], o3["beta"])
    s3 = host_l3c(y3, o3["xn"], g("L3_attn_ln_g"), g("L3_attn_ln_b"),
                  g("L3_attn_Wg"), g("L3_attn_Wout"))
    g3 = host_l3d(o3["xn"], g("L3_ffn_W1"), g("L3_ffn_b1"), g("L3_ffn_W2"),
                  g("L3_ffn_b2"), g("L3_ffn_scale"))
    x_fm = host_combine(x_fm, s3, g3)

    return np.ascontiguousarray(x_fm.T).reshape(4, 4096, DM)



PAIRS = [[0, 1], [2, 3], [4, 5], [6, 7]]
ALL8 = [[0, 1, 2, 3, 4, 5, 6, 7]]


def _cc(P, kind, contrib, out_shape, name, groups):
    from concourse.aot_env import allow_ici_all_to_all
    out = P.nc.dram_tensor(name, list(out_shape), F32)

    def run(e, contrib=contrib, out=out, kind=kind, groups=groups):
        if kind == "AllToAll":
            with allow_ici_all_to_all():
                return e.collective_compute(
                    kind, ALU.bypass, replica_groups=groups,
                    ins=[contrib.ap().opt()], outs=[out.ap().opt()])
        return e.collective_compute(
            kind, ALU.bypass, replica_groups=groups,
            ins=[contrib.ap().opt()], outs=[out.ap().opt()])

    P.E.do("gpsimd", run, dma=False)
    return out


def glue_blend(P, tag, jobs):
    E = P.E
    P.begin_phase(tag)
    P.load_basics()
    if getattr(P, "bflag", None) is None:
        bd = P.nc.dram_tensor("bflag", [128, 2], F32, kind="ExternalInput")
        P.bflag = P.stack.enter_context(
            P.nc.sbuf_tensor("bflagsb", [128, 2], F32))
        E.dma("sync", P.bflag[:, :], bd.ap())
    a = P.sbuf("ba", [128, 2048])
    b = P.sbuf("bb", [128, 2048])
    for (src_h, a0, b0, dst_h, d0, R, C) in jobs:
        assert C <= 2048
        for r0 in range(0, R, 128):
            rw = min(128, R - r0)
            E.dma("sync", a[0:rw, 0:C], src_h[a0 + r0:a0 + r0 + rw, 0:C])
            E.dma("sync", b[0:rw, 0:C], src_h[b0 + r0:b0 + r0 + rw, 0:C])
            E.do("vector", lambda e, rw=rw, C=C: e.tensor_scalar(
                b[0:rw, 0:C], b[0:rw, 0:C], P.bflag[0:rw, 1:2], None,
                ALU.mult))
            E.do("vector", lambda e, rw=rw, C=C: e.scalar_tensor_tensor(
                a[0:rw, 0:C], a[0:rw, 0:C], P.bflag[0:rw, 0:1], b[0:rw, 0:C],
                ALU.mult, ALU.add))
            E.dma("sync", dst_h[d0 + r0:d0 + r0 + rw, 0:C], a[0:rw, 0:C])


def glue_transpose(P, src_h, sr0, sc0, dst_h, dr0, dc0, R, C, tag):
    E = P.E
    P.begin_phase(tag)
    P.load_basics(eye=True)
    a = P.sbuf("ta", [128, 128])
    b = P.sbuf("tb", [128, 128])
    ps = P.psum("tp", [128, 128])
    for r0 in range(0, R, 128):
        rw = min(128, R - r0)
        for c0 in range(0, C, 128):
            cw = min(128, C - c0)
            E.dma("sync", a[0:rw, 0:cw],
                  src_h[sr0 + r0:sr0 + r0 + rw, sc0 + c0:sc0 + c0 + cw])
            E.do("tensor", lambda e, o=ps[0:128, 0:128], rw=rw, cw=cw:
                 e.transpose(o[0:cw, 0:rw], a[0:rw, 0:cw],
                             P.eye[0:rw, 0:rw]))
            E.do("vector", lambda e, rw=rw, cw=cw, p=ps[0:128, 0:128]:
                 e.tensor_copy(b[0:cw, 0:rw], p[0:cw, 0:rw]))
            E.dma("sync",
                  dst_h[dr0 + c0:dr0 + c0 + cw, dc0 + r0:dc0 + r0 + rw],
                  b[0:cw, 0:rw])


def build_fused():
    P = Prog("fz")
    P.fused = True
    E = P.E
    nc = P.nc
    T = TCORE

    def itn(name, shape):
        return nc.dram_tensor("i_" + name, list(shape), F32)

    xn0 = itn("xn0", [1024, T]); z0 = itn("z0", [2048, T])
    xssm = itn("xssm", [2048, T]); bm = itn("bm", [128, T])
    cm = itn("cm", [128, T]); dtp0 = itn("dtp0", [32, T])
    P.bind = dict(xn=xn0, z=z0, xssm=xssm, bmat=bm, cmat=cm, dtp=dtp0)
    build_l0a(P=P)
    xssm_x = _cc(P, "AllGather", xssm, [4096, T], "cc_xssm", PAIRS)
    dtp_x = _cc(P, "AllGather", dtp0, [64, T], "cc_dtp", PAIRS)
    bm_x = _cc(P, "AllGather", bm, [256, T], "cc_bm", PAIRS)
    cm_x = _cc(P, "AllGather", cm, [256, T], "cc_cm", PAIRS)
    xssmL = itn("xssmL", [1024, 4096]); dtpL = itn("dtpL", [16, 4096])
    bmL = itn("bmL", [128, 4096]); cmL = itn("cmL", [128, 4096])
    xssmT2 = itn("xssmT2", [2048, T])
    glue_blend(P, "bl0", [
        (xssm_x, 0, 1024, xssmT2, 0, 1024, T),
        (xssm_x, 2048, 3072, xssmT2, 1024, 1024, T),
    ])
    E.dma("sync", xssmL[:, 0:T], xssmT2[0:1024, :])
    E.dma("sync", xssmL[:, T:4096], xssmT2[1024:2048, :])
    dtpT2 = itn("dtpT2", [32, T])
    glue_blend(P, "bl0d", [
        (dtp_x, 0, 16, dtpT2, 0, 16, T),
        (dtp_x, 32, 48, dtpT2, 16, 16, T),
    ])
    E.dma("sync", dtpL[:, 0:T], dtpT2[0:16, :])
    E.dma("sync", dtpL[:, T:4096], dtpT2[16:32, :])
    E.dma("sync", bmL[:, 0:T], bm_x[0:128, :])
    E.dma("sync", bmL[:, T:4096], bm_x[128:256, :])
    E.dma("sync", cmL[:, 0:T], cm_x[0:128, :])
    E.dma("sync", cmL[:, T:4096], cm_x[128:256, :])
    dtpTL = itn("dtpTL", [4096, 16])
    glue_transpose(P, dtpL, 0, 0, dtpTL, 0, 0, 16, 4096, "gt0")
    y0 = itn("y0", [4096, 1024])
    P.bind = dict(xssm=xssmL, dtp=dtpL, dtpT=dtpTL, bmat=bmL, cmat=cmL,
                  y_tm=y0)
    build_l0b(P=P)
    y0x = _cc(P, "AllGather", y0, [8192, 1024], "cc_y0", PAIRS)
    ysel = itn("ysel", [4096, 1024])
    glue_blend(P, "bly0", [
        (y0x, 0, 2048, ysel, 0, 2048, 1024),
        (y0x, 4096, 6144, ysel, 2048, 2048, 1024),
    ])
    yfm = itn("yfm", [2048, T])
    glue_transpose(P, ysel, 0, 0, yfm, 0, 0, 2048, 1024, "gt1")
    glue_transpose(P, ysel, 2048, 0, yfm, 1024, 0, 2048, 1024, "gt2")
    s0 = itn("s0", [1024, T])
    P.bind = dict(y=yfm, z=z0, s=s0)
    build_l0c1(P=P)
    x1 = itn("x1", [1024, T])
    P.bind = dict(xn=xn0, s=s0, x1=x1)
    build_l0c2(P=P)

    tail3 = itn("tail3", [1024, 3])
    E.dma("sync", tail3.ap(), x1[:, T - 3:T])
    h3 = _cc(P, "AllGather", tail3, [2048, 3], "cc_h3", PAIRS)
    halo3 = itn("halo3", [1024, 3])
    E.dma("sync", halo3.ap(), h3[0:1024, :])
    xn1 = itn("xn1", [1024, T]); u1 = itn("u1", [2048, T])
    z1 = itn("z1", [2048, T]); dtb1 = itn("dtb1", [17, T])
    P.bind = dict(x=x1, xhalo=halo3, xn=xn1, u=u1, z=z1, dtb=dtb1)
    build_l1a(P=P)
    dtbag = _cc(P, "AllGather", dtb1, [136, T], "cc_dtb", ALL8)
    y1s = itn("y1s", [2048, T])
    P.bind = dict(dtbag=dtbag, y=y1s)
    build_l1b(P=P)
    y1 = _cc(P, "AllToAll", y1s, [2048, T], "cc_y1", ALL8)
    s1 = itn("s1", [1024, T])
    P.bind = dict(y=y1, u=u1, z=z1, s=s1)
    build_l1c(P=P)
    xf = xn1
    for li in range(2):
        t513 = itn(f"t513_{li}", [1024, 513])
        E.dma("sync", t513.ap(), xf[:, T - 513:T])
        h513 = _cc(P, "AllGather", t513, [2048, 513], f"cc_h513_{li}", PAIRS)
        halo513 = itn(f"halo513_{li}", [1024, 513])
        E.dma("sync", halo513.ap(), h513[0:1024, :])
        qq = itn(f"q_{li}", [512, T]); kk = itn(f"k_{li}", [512, 512 + T])
        vv = itn(f"v_{li}", [512, 512 + T])
        P.next_tag = f"l1d1_{li}"
        P.bind = dict(x=xf, xhalo=halo513, q=qq, k=kk, v=vv)
        build_l1d1(P=P)
        oo = itn(f"o_{li}", [512, T])
        P.next_tag = f"l1d2_{li}"
        P.bind = dict(q=qq, k=kk, v=vv, o=oo)
        build_l1d2(P=P)
        xf2 = itn(f"xf2_{li}", [1024, T])
        P.next_tag = f"l1d3_{li}"
        P.bind = dict(o=oo, x=xf, x2=xf2)
        build_l1d3(P=P)
        t1t = itn(f"t1_{li}", [1024, 1])
        E.dma("sync", t1t.ap(), xf2[:, T - 1:T], allow_slow=True)
        h1t = _cc(P, "AllGather", t1t, [2048, 1], f"cc_h1_{li}", PAIRS)
        halo1 = itn(f"halo1_{li}", [1024, 1])
        E.dma("sync", halo1.ap(), h1t[0:1024, :])
        xf3 = itn(f"xf3_{li}", [1024, T])
        P.next_tag = f"l1e_{li}"
        P.bind = dict(x2=xf2, x2halo=halo1, x3=xf3)
        build_l1e(P=P)
        xf = xf3
    x2g = itn("x2g", [1024, T])
    P.next_tag = "combine_1"
    P.bind = dict(x=x1, s=s1, g=xf, xn_out=x2g)
    build_combine(P=P)

    xn2 = itn("xn2", [1024, T]); z2 = itn("z2", [1024, T])
    q2 = itn("q2", [128, T]); ix2 = itn("ix2", [1024, T])
    ik2 = itn("ik2", [1024, T]); cs2 = itn("cs2", [2048, T // 64])
    P.bind = dict(x_raw=x2g, xn=xn2, z=z2, q=q2, ix=ix2, ik=ik2, cs=cs2)
    build_l2a(P=P)
    csag = _cc(P, "AllGather", cs2, [4096, T // 64], "cc_cs", PAIRS)
    s2 = itn("s2", [1024, T])
    P.bind = dict(q=q2, ix=ix2, ik=ik2, cs_b=csag, z=z2, xn=xn2, s=s2)
    build_l2b(P=P)
    g2 = itn("g2", [1024, T])
    P.bind = dict(xn=xn2, g=g2)
    build_l2c(P=P)
    x3g = itn("x3g", [1024, T])
    P.next_tag = "combine_2"
    P.bind = dict(x=x2g, s=s2, g=g2, xn_out=x3g)
    build_combine(P=P)

    xn3 = itn("xn3", [1024, T]); qn3 = itn("qn3", [1024, T])
    kn3 = itn("kn3", [1024, T]); v3 = itn("v3", [1024, T])
    be3 = itn("be3", [16, T])
    P.bind = dict(x=x3g, xn=xn3, qn=qn3, kn=kn3, v=v3, beta=be3)
    build_l3a(P=P)
    qag = _cc(P, "AllGather", qn3, [2048, T], "cc_q3", PAIRS)
    kag = _cc(P, "AllGather", kn3, [2048, T], "cc_k3", PAIRS)
    vag = _cc(P, "AllGather", v3, [2048, T], "cc_v3", PAIRS)
    beag = _cc(P, "AllGather", be3, [32, T], "cc_b3", PAIRS)
    qx = itn("qx", [1024, T]); kx = itn("kx", [1024, T])
    vx = itn("vx", [1024, T]); bex = itn("bex", [16, T])
    jobs3 = []
    for src, dst in ((qag, qx), (kag, kx), (vag, vx)):
        jobs3.append((src, 0, 512, dst, 0, 512, T))
        jobs3.append((src, 1024, 1536, dst, 512, 512, T))
    jobs3.append((beag, 0, 8, bex, 0, 8, T))
    jobs3.append((beag, 16, 24, bex, 8, 8, T))
    glue_blend(P, "bl3", jobs3)
    y3s = itn("y3s", [1024, T])
    P.bind = dict(qn=qx, kn=kx, v=vx, beta=bex, y=y3s)
    build_l3b(P=P)
    yag3 = _cc(P, "AllGather", y3s, [2048, T], "cc_y3", PAIRS)
    y3 = itn("y3r", [1024, T])
    glue_blend(P, "bl3y", [
        (yag3, 0, 512, y3, 0, 512, T),
        (yag3, 1024, 1536, y3, 512, 512, T),
    ])
    s3 = itn("s3", [1024, T])
    P.bind = dict(o_in=y3, xn=xn3, s=s3)
    build_l3c(P=P)
    lt0 = itn("lt0", [1024, 16])
    E.dma("sync", lt0.ap(), xn3[:, 1:17])
    ltag = _cc(P, "AllGather", lt0, [2048, 16], "cc_lt", PAIRS)
    ltL = itn("ltL", [1024, 16])
    E.dma("sync", ltL.ap(), ltag[0:1024, :])
    g3 = itn("g3", [1024, T])
    P.bind = dict(xn=xn3, lt=ltL, g=g3)
    build_l3d(P=P)
    out_final = nc.dram_tensor("out_final", [1024, T], F32,
                               kind="ExternalOutput")
    P.next_tag = "combine_3"
    P.bind = dict(x=x3g, s=s3, g=g3, xn_out=out_final)
    build_combine(P=P)
    P.bind = {}
    P.finish()
    return P

def host_fused(inputs):
    g = lambda n: np.ascontiguousarray(np.asarray(inputs[n], np.float32))
    x = g("L0_attn_x")
    x_fm = np.ascontiguousarray(x.reshape(NTOK, DM).T)
    P = build_fused()
    base = Prog.basic_inputs(eye=True)

    sh = {}
    sh["l0a_wt_in"] = np.ascontiguousarray(g("L0_attn_Win").T)
    sh["l0a_conv_w"] = g("L0_attn_conv_w")
    sh["l0a_conv_b"] = g("L0_attn_conv_b").reshape(-1, 1)
    sh["l0a_dt_bias"] = g("L0_attn_dt_bias").reshape(-1, 1)
    tmask = np.ones((16, 4096), np.float32)
    tmask[:, ::128] = 0.0
    sh["l0b_tmaskc"] = tmask
    sh["l0b_ipatc"] = np.tile(np.eye(128, dtype=np.float32), (1, 32))
    hsel = np.zeros((16, 16 * 128), np.float32)
    for h in range(16):
        hsel[h, h * 128:(h + 1) * 128] = 1.0
    sh["l0b_hselc"] = hsel
    sh["l0c1_norm_w"] = g("L0_attn_norm_w").reshape(-1, 1)
    sh["l0c1_wt_out"] = np.ascontiguousarray(g("L0_attn_Wout").T)
    ffn0 = {k[7:]: g(k) for k in inputs if k.startswith("L0_ffn_")}
    sc2 = np.zeros((128, 136), np.float32)
    for i, nm in enumerate(("w1_s", "w2_s", "h1_s", "h2_s")):
        sc2[:, i * 32:(i + 1) * 32] = ffn0[nm].reshape(32, 128).T
    sc2[:, 128:136] = ffn0["w3_s"].reshape(8, 128).T
    sh["l0c2_scales2"] = sc2
    sh["l0c2_scales"] = np.zeros((128, 9), np.float32)
    for nm in ("w1", "w2", "h1", "h2", "w3"):
        sh[f"l0c2_{nm}t"] = np.ascontiguousarray(ffn0[nm + "_w"].T)

    sh["l1a_wt_in"] = np.ascontiguousarray(g("L1_attn_in_proj_w").T)
    sh["l1a_conv_w"] = np.ascontiguousarray(g("L1_attn_conv_w")[:, 0, :])
    sh["l1a_wt_xproj"] = np.ascontiguousarray(g("L1_attn_x_proj_w").T)
    sh["l1b_hsel16"] = hsel
    sh["l1c_dcol"] = g("L1_attn_D_param").reshape(-1, 1)
    sh["l1c_wt_out"] = np.ascontiguousarray(g("L1_attn_out_w").T)
    sh["l1c_out_b"] = g("L1_attn_out_b").reshape(-1, 1)

    HID = 2730
    for li in range(2):
        wqkv = g("L1_ffn_wqkv")[li]
        sh[f"l1d1_{li}_wq"] = np.ascontiguousarray(wqkv[:, 0:512])
        sh[f"l1d1_{li}_wk"] = np.ascontiguousarray(wqkv[:, 512:1024])
        sh[f"l1d1_{li}_wv"] = np.ascontiguousarray(wqkv[:, 1024:1536])
        sh[f"l1d2_{li}_qs_col"] = np.tile(g("L1_ffn_q_scale")[li], 8)\
            .reshape(-1, 1).astype(np.float32)
        sh[f"l1d2_{li}_ks_col"] = np.tile(g("L1_ffn_k_scale")[li], 8)\
            .reshape(-1, 1).astype(np.float32)
        sh[f"l1d3_{li}_wo"] = np.ascontiguousarray(g("L1_ffn_wo")[li])
        w1 = g("L1_ffn_w1")[li]; b1 = g("L1_ffn_b1")[li]
        w2 = g("L1_ffn_w2")[li]; b2 = g("L1_ffn_b2")[li]
        w1p = np.zeros((1024, 5632), np.float32)
        w1p[:, 0:HID] = w1[:, 0:HID]
        w1p[:, 2816:2816 + HID] = w1[:, HID:2 * HID]
        b1p = np.zeros((128, 44), np.float32)
        b1pad = np.zeros(5632, np.float32)
        b1pad[0:HID] = b1[0:HID]
        b1pad[2816:2816 + HID] = b1[HID:]
        b1p[:, :] = b1pad.reshape(44, 128).T
        w2p = np.zeros((2816, 1024), np.float32)
        w2p[0:HID, :] = w2
        lngb = np.zeros((128, 16), np.float32)
        lngb[:, 0:8] = g("L1_ffn_ln_g")[li].reshape(8, 128).T
        lngb[:, 8:16] = g("L1_ffn_ln_b")[li].reshape(8, 128).T
        sh[f"l1e_{li}_w1p"] = w1p
        sh[f"l1e_{li}_b1p"] = b1p
        sh[f"l1e_{li}_w2p"] = w2p
        sh[f"l1e_{li}_b2c"] = b2.reshape(8, 128).T.astype(np.float32).copy()
        sh[f"l1e_{li}_lngb"] = lngb

    W_in = g("L2_attn_W_in")
    sh["l2a_w_in"] = np.concatenate(
        [W_in[:, 0:1024], W_in[:, 2048:3072], W_in[:, 1024:2048],
         W_in[:, 3072:3328]], axis=1).astype(np.float32).copy()
    sh["l2a_rms_w"] = g("L2_attn_rms_weight").reshape(-1, 1)
    sh["l2a_gate_b"] = g("L2_attn_gate_bias").reshape(-1, 1)
    cmask = np.ones((128, TCORE), np.float32)
    cmask[:, ::64] = 0.0
    sh["l2a_cmask"] = cmask
    sh["l2b_w_out"] = g("L2_attn_W_out")
    sh["l2c_w1t"] = np.ascontiguousarray(g("L2_ffn_w1").T)
    sh["l2c_w2t"] = np.ascontiguousarray(g("L2_ffn_w2").T)
    sh["l2c_b1c"] = g("L2_ffn_b1").reshape(32, 128).T.copy()
    sh["l2c_b2c"] = g("L2_ffn_b2").reshape(8, 128).T.copy()

    ind2 = np.zeros((2, 128), np.float32)
    ind2[0, 0:64] = 1.0
    ind2[1, 64:128] = 1.0
    sh["l3a_wq"] = g("L3_attn_Wq")
    sh["l3a_wk"] = g("L3_attn_Wk")
    sh["l3a_wv"] = g("L3_attn_Wv")
    sh["l3a_wb"] = g("L3_attn_Wbeta")
    sh["l3a_ind2"] = ind2
    sh["l3a_ind2T_in"] = np.ascontiguousarray(ind2.T)
    ssel = np.zeros((128, 64 * 128), np.float32)
    csel = np.zeros((128, 64 * 128), np.float32)
    for i in range(64):
        for p2 in range(2):
            ssel[p2 * 64 + i, i * 128 + p2 * 64:i * 128 + (p2 + 1) * 64] = 1.0
            csel[p2 * 64:(p2 + 1) * 64, i * 128 + p2 * 64 + i] = 1.0
    psel = np.zeros((8, 4 * 128), np.float32)
    for pg in range(4):
        psel[2 * pg, pg * 128:pg * 128 + 64] = 1.0
        psel[2 * pg + 1, pg * 128 + 64:(pg + 1) * 128] = 1.0
    sh["l3b_self_sel"] = ssel
    sh["l3b_colsel"] = csel
    sh["l3b_pairsel"] = psel
    lngb3 = np.zeros((128, 16), np.float32)
    lngb3[:, 0:8] = g("L3_attn_ln_g").reshape(8, 128).T
    lngb3[:, 8:16] = g("L3_attn_ln_b").reshape(8, 128).T
    sh["l3c_lngb"] = lngb3
    sh["l3c_wg"] = g("L3_attn_Wg")
    sh["l3c_wout"] = g("L3_attn_Wout")
    sh["l3d_w1t"] = np.ascontiguousarray(g("L3_ffn_W1").T)
    sh["l3d_w2t"] = np.ascontiguousarray(g("L3_ffn_W2").T)
    sh["l3d_b1c"] = g("L3_ffn_b1").reshape(32, 128).T.copy()
    sh["l3d_b2c"] = g("L3_ffn_b2").reshape(8, 128).T.copy()
    sh["l3d_scale11"] = np.asarray(g("L3_ffn_scale"), np.float32)\
        .reshape(1, 1)
    sel4 = np.zeros((4, 4 * 128), np.float32)
    for k in range(4):
        sel4[k, k * 128:(k + 1) * 128] = 1.0
    sh["l3d_sel4"] = sel4

    A_log = g("L0_attn_A_log"); D0 = g("L0_attn_D")
    dt_w = g("L1_attn_dt_w"); dt_b = g("L1_attn_dt_b")
    A1 = g("L1_attn_A_param")
    in_maps = []
    for c in range(NCORES):
        hg = c % 2
        lo = c * TCORE
        per = {}
        xh = np.zeros((DM, TCORE + 3), np.float32)
        xh[:, 3:] = x_fm[:, lo:lo + TCORE]
        if hg == 1:
            xh[:, 0:3] = x_fm[:, lo - 3:lo]
        per["l0a_x"] = xh
        hh = slice(hg * 16, (hg + 1) * 16)
        per["l0b_a_log"] = np.ascontiguousarray(A_log[hh].reshape(-1, 1))
        per["l0b_d_row"] = np.ascontiguousarray(D0[hh].reshape(1, -1))
        per["l0c2_x"] = np.ascontiguousarray(x_fm[:, lo:lo + TCORE])
        flag = np.full((128, 1), float(hg), np.float32)
        per["l1a_haloflag"] = flag
        dsl = slice(c * 256, (c + 1) * 256)
        per["l1b_dtwb"] = np.ascontiguousarray(
            np.stack([dt_w[dsl, 0], dt_b[dsl]]).astype(np.float32))
        per["l1b_a_sl"] = np.ascontiguousarray(A1[dsl])
        for li in range(2):
            per[f"l1d1_{li}_haloflag"] = flag
            per[f"l1e_{li}_haloflag"] = flag
            tb = _l1ffn_tables(c)
            for knm, v in tb.items():
                per[f"l1d2_{li}_{knm}"] = v
        NCh = 64
        maskc = np.zeros((128, 16 * NCh), np.float32)
        for sub in range(16):
            tglob = hg * 2048 + sub * 128 + np.arange(128)
            allowed = (tglob[:, None] // 64) > np.arange(NCh)[None, :]
            maskc[:, sub * NCh:(sub + 1) * NCh] = np.where(allowed, 0.0,
                                                           -1e30)
        per["l2b_maskc"] = maskc
        bf = np.zeros((128, 2), np.float32)
        bf[:, hg] = 1.0
        per["bflag"] = bf
        in_maps.append(sh | per | base)
    res = P.run(in_maps)
    x_out = np.concatenate([r["out_final"] for r in res.results], axis=1)
    return np.ascontiguousarray(x_out.T).reshape(4, 4096, DM)


FUSED = True


def kernel(**inputs):
    if FUSED:
        return host_fused(inputs)
    return kernel_multilaunch(**inputs)


# version 08
# speedup vs baseline: 1.4808x; 1.2747x over previous
import numpy as np
import concourse.bass as bass
import concourse.mybir as mybir
from concourse.bass_utils import run_bass_kernel_spmd
from contextlib import ExitStack

F32 = mybir.dt.float32
BF16 = mybir.dt.bfloat16
AF = mybir.ActivationFunctionType
ALU = mybir.AluOpType
AX = mybir.AxisListType

NCORES = 8
NTOK = 16384
TCORE = NTOK // NCORES
DM = 1024
MMF = 512

CONST_NAMES = ["eps1e5", "eps1e6", "one", "neg1", "half", "eps1e12", "neg30"]
CONST_VALS = {"eps1e5": 1e-5, "eps1e6": 1e-6, "one": 1.0, "neg1": -1.0,
              "half": 0.5, "eps1e12": 1e-12, "neg30": -1e30}


def const_array():
    v = np.array([CONST_VALS[n] for n in CONST_NAMES], np.float32)
    return np.broadcast_to(v, (128, len(CONST_NAMES))).copy()


def ceil_div(a, b):
    return (a + b - 1) // b



class Em:
    ENGINES = ("tensor", "vector", "scalar", "gpsimd", "sync")

    def __init__(self, nc, stack):
        self.nc = nc
        self.stack = stack
        self.q = {e: [] for e in self.ENGINES}
        self.g = stack.enter_context(nc.semaphore("g"))
        self.gcnt = 0
        self.last_eng = None

    def do(self, eng, fn, dma=False):
        pre = self.gcnt
        h = self.g
        if pre > 0:
            self.q[eng].append(lambda e, h=h, v=pre: e.wait_ge(h, v))
        amt = 16 if dma else 1
        def run(e, fn=fn, h=h, amt=amt):
            fn(e).then_inc(h, amt)
        self.q[eng].append(run)
        self.gcnt += amt
        self.last_eng = eng
        return self.gcnt

    def dma(self, eng, out, in_, allow_slow=False):
        if allow_slow:
            return self.do(eng, lambda e, o=out, i=in_: e.dma_start(
                out=o, in_=i, allow_slow_non_contiguous=True), dma=True)
        return self.do(eng, lambda e, o=out, i=in_: e.dma_start(out=o, in_=i),
                       dma=True)

    def dma_par(self, jobs):
        pre = self.gcnt
        h = self.g
        for eng, out, in_ in jobs:
            if pre > 0:
                self.q[eng].append(lambda e, h=h, v=pre: e.wait_ge(h, v))
            def run(e, o=out, i=in_, h=h):
                e.dma_start(out=o, in_=i).then_inc(h, 16)
            self.q[eng].append(run)
            self.gcnt += 16
        if jobs:
            self.last_eng = jobs[-1][0]

    def build(self):
        with self.nc.Block() as blk:
            for eng in self.ENGINES:
                ops = self.q[eng]
                if not ops:
                    continue
                def runner(e, ops=ops):
                    for f in ops:
                        f(e)
                getattr(blk, eng)(runner)
        self.q = {e: [] for e in self.ENGINES}


class Rot:
    def __init__(self, tiles):
        self.tiles = tiles
        self.j = 0

    def next(self):
        t = self.tiles[self.j % len(self.tiles)]
        self.j += 1
        return t


class Prog:
    def __init__(self, name):
        self.nc = bass.Bass("TRN2", target_bir_lowering=False)
        self.stack = ExitStack()
        self.E = Em(self.nc, self.stack)
        self.name = name
        self._holders = []
        self._phase = name
        self._phase_stack = None
        self.bind = {}
        self.fused = False
        self.constsb = None
        self.eye = None

    def begin_phase(self, prefix):
        if getattr(self, "next_tag", None):
            prefix = self.next_tag
            self.next_tag = None
        self._phase = prefix
        if self.fused:
            if self._phase_stack is not None:
                self._phase_stack.close()
            self._phase_stack = ExitStack()

    def _alloc_stack(self):
        return self._phase_stack if (self.fused and
                                     self._phase_stack is not None)             else self.stack

    def dram_in(self, name, shape, dtype=F32):
        if name in self.bind:
            return self.bind[name]
        return self.nc.dram_tensor(self._phase + "_" + name, list(shape),
                                   dtype, kind="ExternalInput")

    def dram_out(self, name, shape, dtype=F32):
        if name in self.bind:
            return self.bind[name]
        kind = None if self.fused else "ExternalOutput"
        if kind:
            return self.nc.dram_tensor(self._phase + "_" + name, list(shape),
                                       dtype, kind=kind)
        return self.nc.dram_tensor(self._phase + "_" + name, list(shape),
                                   dtype)

    def dram_tmp(self, name, shape, dtype=F32):
        return self.nc.dram_tensor(self._phase + "_" + name, list(shape),
                                   dtype)

    def sbuf(self, name, shape, dtype=F32):
        return self._alloc_stack().enter_context(
            self.nc.sbuf_tensor(self._phase + "_" + name, list(shape), dtype))

    def psum(self, name, shape, dtype=F32):
        return self._alloc_stack().enter_context(
            self.nc.psum_tensor(self._phase + "_" + name, list(shape), dtype))

    def psum_pool(self, n=8, w=MMF, name="pp"):
        return Rot([self.psum(f"{name}{i}", [128, w]) for i in range(n)])

    def sbuf_pool(self, n, shape, dtype=F32, name="sp"):
        return Rot([self.sbuf(f"{name}{i}", list(shape), dtype)
                    for i in range(n)])

    def load_basics(self, eye=False):
        if getattr(self, "constsb", None) is not None:
            if eye and self.eye is None:
                E = self.E
                ed = self.nc.dram_tensor("eye128", [128, 128], F32,
                                         kind="ExternalInput")
                self.eye = self.stack.enter_context(
                    self.nc.sbuf_tensor("eyesb", [128, 128], F32))
                E.dma("sync", self.eye[:, :], ed.ap())
            return
        E = self.E
        cd = self.nc.dram_tensor("consts", [128, len(CONST_NAMES)], F32,
                                 kind="ExternalInput")
        self.constsb = self.stack.enter_context(
            self.nc.sbuf_tensor("constsb", [128, len(CONST_NAMES)], F32))
        E.dma("sync", self.constsb[:, :], cd.ap())
        od = self.nc.dram_tensor("onescol", [128, 1], F32,
                                 kind="ExternalInput")
        self.ones = self.stack.enter_context(
            self.nc.sbuf_tensor("onessb", [128, 1], F32))
        E.dma("sync", self.ones[:, :], od.ap())
        orow = self.nc.dram_tensor("onesrow", [1, 128], F32,
                                   kind="ExternalInput")
        self.ones_row = self.stack.enter_context(
            self.nc.sbuf_tensor("onesrowsb", [1, 128], F32))
        E.dma("sync", self.ones_row[:, :], orow.ap())
        self.eye = None
        if eye:
            ed = self.nc.dram_tensor("eye128", [128, 128], F32,
                                     kind="ExternalInput")
            self.eye = self.stack.enter_context(
                self.nc.sbuf_tensor("eyesb", [128, 128], F32))
            E.dma("sync", self.eye[:, :], ed.ap())

    def ccol(self, name):
        i = CONST_NAMES.index(name)
        return self.constsb[:, i:i + 1]

    def run(self, in_maps, **kw):
        return run_bass_kernel_spmd(self.nc, in_maps,
                                    core_ids=list(range(len(in_maps))), **kw)

    def finish(self, out_dma_engine="sync"):
        E = self.E
        h = E.g
        v = E.gcnt
        E.q[out_dma_engine].append(lambda e, h=h, v=v: e.wait_ge(h, v))
        E.build()
        if self._phase_stack is not None:
            self._phase_stack.close()
            self._phase_stack = None
        self.stack.close()

    def wrap(self, d):
        basics = {"consts", "onescol", "onesrow", "eye128"}
        return {(k if k in basics else self._phase + "_" + k): v
                for k, v in d.items()}

    @staticmethod
    def basic_inputs(eye=False):
        d = {"consts": const_array(),
             "onescol": np.ones((128, 1), np.float32),
             "onesrow": np.ones((1, 128), np.float32)}
        if eye:
            d["eye128"] = np.eye(128, dtype=np.float32)
        return d


def _W(P, d):
    return P.wrap(d)



def load_fm(P, name, src_ap, d, T, dtype=F32, pool=None):
    KT = ceil_div(d, 128)
    tiles = []
    for k in range(KT):
        kw = min(128, d - k * 128)
        t = P.sbuf(f"{name}{k}", [128, T], dtype) if pool is None else pool[k]
        P.E.dma("sync", t[0:kw, 0:T], src_ap[k * 128:k * 128 + kw, 0:T])
        tiles.append(t)
    return tiles


def cast_tiles(P, name, xtiles, T, eng="scalar"):
    out = []
    for i, t in enumerate(xtiles):
        b = P.sbuf(f"{name}{i}", [128, T], BF16)
        P.E.do(eng, lambda e, o=b[:, 0:T], x=t[:, 0:T]:
               e.activation(o, x, AF.Copy))
        out.append(b)
    return out


def cast_into(P, dst, src, T, eng="scalar"):
    for d, s in zip(dst, src):
        P.E.do(eng, lambda e, o=d[:, 0:T], x=s[:, 0:T]:
               e.activation(o, x, AF.Copy))


def dma_rows_par(P, tiles, dram, n, cols=None, col0=0):
    E = P.E
    for b0 in range(0, n, 3):
        jobs = []
        for j, k in enumerate(range(b0, min(b0 + 3, n))):
            t = tiles[k]
            dst = t[:, :] if cols is None else t[:, col0:col0 + cols]
            jobs.append((DMA_ENGS[j % 3], dst, dram[k * 128:(k + 1) * 128, :]))
        E.dma_par(jobs)


def proj_fm(P, pp, wpool, xtiles, wt_dram, d_in, d_out, T, evac, xcol0=0,
            w_row0=0, w_col0=0, bfpool=None):
    E = P.E
    KT = ceil_div(d_in, 128)
    MT = ceil_div(d_out, 128)
    CT = ceil_div(T, MMF)
    NB = min(len(wpool.tiles),
             len(bfpool.tiles) if bfpool is not None else 8, KT, 4)
    for m in range(MT):
        mw = min(128, d_out - m * 128)
        banks = [pp.next() for _ in range(CT)]
        for kb0 in range(0, KT, NB):
            kbatch = list(range(kb0, min(kb0 + NB, KT)))
            stages = {}
            jobs = []
            for j, k in enumerate(kbatch):
                kw = min(128, d_in - k * 128)
                wtile = wpool.next()
                stages[k] = wtile
                jobs.append((DMA_ENGS[j % 3], wtile[0:kw, 0:mw],
                             wt_dram[w_row0 + k * 128:w_row0 + k * 128 + kw,
                                     w_col0 + m * 128:w_col0 + m * 128 + mw]))
            E.dma_par(jobs)
            for k in kbatch:
                kw = min(128, d_in - k * 128)
                wtile = stages[k]
                if bfpool is not None:
                    wb = bfpool.next()
                    E.do("scalar", lambda e, o=wb[0:kw, 0:mw],
                         x=wtile[0:kw, 0:mw]: e.activation(o, x, AF.Copy))
                    wtile = wb
                for c in range(CT):
                    cw = min(MMF, T - c * MMF)
                    E.do("tensor",
                         lambda e, o=banks[c][0:mw, 0:cw], w=wtile[0:kw, 0:mw],
                                x=xtiles[k][0:kw, xcol0 + c * MMF:
                                            xcol0 + c * MMF + cw],
                                st=(k == 0), sp=(k == KT - 1):
                             e.matmul(o, w, x, start=st, stop=sp))
        for c in range(CT):
            cw = min(MMF, T - c * MMF)
            evac(m, c, banks[c][0:mw, 0:cw])


def ln_rows(P, pp, xtiles, T, d, scr, ab, rms=False, eps_name="eps1e5",
            xcol0=0):
    brow, arow = ab
    E = P.E
    KT = d // 128
    CT = ceil_div(T, MMF)
    for c in range(CT):
        cw = min(MMF, T - c * MMF)
        bank_s = pp.next()
        bank_q = pp.next()
        if not rms:
            for k in range(KT):
                E.do("tensor",
                     lambda e, o=bank_s[0:1, 0:cw], w=P.ones[:, 0:1],
                            x=xtiles[k][:, xcol0 + c * MMF:xcol0 + c * MMF + cw],
                            st=(k == 0):
                         e.matmul(o, w, x, start=st, stop=(k == KT - 1)))
        for k in range(KT):
            E.do("scalar",
                 lambda e, o=scr[:, 0:cw],
                        x=xtiles[k][:, xcol0 + c * MMF:xcol0 + c * MMF + cw]:
                     e.activation(o, x, AF.Square))
            E.do("tensor",
                 lambda e, o=bank_q[0:1, 0:cw], w=P.ones[:, 0:1],
                        x=scr[:, 0:cw], st=(k == 0):
                     e.matmul(o, w, x, start=st, stop=(k == KT - 1)))
        if not rms:
            E.do("vector",
                 lambda e, o=brow[0:1, c * MMF:c * MMF + cw], i=bank_s[0:1, 0:cw]:
                     e.tensor_scalar(o, i, 1.0 / d, None, ALU.mult))
        E.do("vector",
             lambda e, o=arow[0:1, c * MMF:c * MMF + cw], i=bank_q[0:1, 0:cw]:
                 e.tensor_scalar(o, i, 1.0 / d, None, ALU.mult))
    if not rms:
        for c in range(CT):
            cw = min(MMF, T - c * MMF)
            E.do("vector",
                 lambda e, o=scr[0:1, 0:cw], a=brow[0:1, c * MMF:c * MMF + cw]:
                     e.tensor_mul(o, a, a))
            E.do("vector",
                 lambda e, o=arow[0:1, c * MMF:c * MMF + cw],
                        a=arow[0:1, c * MMF:c * MMF + cw], b=scr[0:1, 0:cw]:
                     e.tensor_sub(o, a, b))
    E.do("scalar", lambda e: e.activation(
        arow[0:1, 0:T], arow[0:1, 0:T], AF.Ln, bias=P.ccol(eps_name)[0:1, :]))
    E.do("scalar", lambda e: e.activation(
        arow[0:1, 0:T], arow[0:1, 0:T], AF.Exp, scale=-0.5))
    if not rms:
        E.do("vector", lambda e: e.tensor_mul(
            brow[0:1, 0:T], brow[0:1, 0:T], arow[0:1, 0:T]))
        E.do("vector", lambda e: e.tensor_scalar(
            brow[0:1, 0:T], brow[0:1, 0:T], -1.0, None, ALU.mult))


def apply_ab(P, pp, xtiles, out_tiles, ab, T, d, rms=False, out_dtype=None):
    brow, arow = ab
    E = P.E
    KT = d // 128
    CT = ceil_div(T, MMF)
    for c in range(CT):
        cw = min(MMF, T - c * MMF)
        ba = pp.next()
        E.do("tensor", lambda e, o=ba[:, 0:cw],
             r=arow[0:1, c * MMF:c * MMF + cw]:
                 e.matmul(o, P.ones_row[0:1, :], r, start=True, stop=True))
        if not rms:
            bb = pp.next()
            E.do("tensor", lambda e, o=bb[:, 0:cw],
                 r=brow[0:1, c * MMF:c * MMF + cw]:
                     e.matmul(o, P.ones_row[0:1, :], r, start=True, stop=True))
        for k in range(KT):
            E.do("vector",
                 lambda e, o=out_tiles[k][:, c * MMF:c * MMF + cw],
                        x=xtiles[k][:, c * MMF:c * MMF + cw], a=ba[:, 0:cw]:
                     e.tensor_mul(o, x, a))
            if not rms:
                E.do("vector",
                     lambda e, o=out_tiles[k][:, c * MMF:c * MMF + cw],
                            b=bb[:, 0:cw]:
                         e.tensor_add(o, o, b))


def softplus_cols(P, rows_tile, p, T, tmp, bias_col=None):
    E = P.E
    kw = {} if bias_col is None else {"bias": bias_col[0:p, :]}
    E.do("scalar", lambda e: e.activation(
        tmp[0:p, 0:T], rows_tile[0:p, 0:T], AF.Abs, **kw))
    E.do("scalar", lambda e: e.activation(
        tmp[0:p, 0:T], tmp[0:p, 0:T], AF.Exp, scale=-1.0))
    E.do("scalar", lambda e: e.activation(
        tmp[0:p, 0:T], tmp[0:p, 0:T], AF.Ln, bias=P.ccol("one")[0:p, :]))
    E.do("scalar", lambda e: e.activation(
        rows_tile[0:p, 0:T], rows_tile[0:p, 0:T], AF.Relu, **kw))
    E.do("vector", lambda e: e.tensor_add(
        rows_tile[0:p, 0:T], rows_tile[0:p, 0:T], tmp[0:p, 0:T]))



L0_DIN = 1024
L0_DIN_PROJ = 4384
L0_DINNER = 2048
L0_CONVD = 2304
L0_NH = 32
L0_HD = 64
L0_DS = 128


def build_l0a(P=None):
    standalone = P is None
    if standalone:
        P = Prog("l0a")
    P.begin_phase("l0a")
    E = P.E
    T = TCORE
    TH = T + 3
    x_d = P.dram_in("x", [DM, TH])
    wt_d = P.dram_in("wt_in", [L0_DIN, L0_DIN_PROJ])
    convw_d = P.dram_in("conv_w", [L0_CONVD, 4])
    convb_d = P.dram_in("conv_b", [L0_CONVD, 1])
    dtb_d = P.dram_in("dt_bias", [32, 1])
    xn_d = P.dram_out("xn", [DM, T])
    z_d = P.dram_out("z", [L0_DINNER, T])
    xssm_d = P.dram_out("xssm", [L0_DINNER, T])
    b_d = P.dram_out("bmat", [L0_DS, T])
    c_d = P.dram_out("cmat", [L0_DS, T])
    dtp_d = P.dram_out("dtp", [32, T])

    P.load_basics()
    pp = P.psum_pool()
    xt = [P.sbuf(f"x{k}", [128, TH]) for k in range(8)]
    scr = P.sbuf("scr", [128, MMF])
    ab = (P.sbuf("abb", [1, TH]), P.sbuf("aba", [1, TH]))
    wpool = P.sbuf_pool(4, [128, 128], name="wp")
    convw = P.sbuf("convw", [128, 4 * 18])
    convb = P.sbuf("convb", [128, 18])
    dtb = P.sbuf("dtb", [32, 1])
    xbc_buf = P.sbuf("xbcbuf", [128, TH])
    xcv_buf = P.sbuf("xcvbuf", [128, T])
    zstage = P.sbuf("zstage", [128, TH])
    dt_full = P.sbuf("dtfull", [32, TH])
    dtt = P.sbuf("dtt", [32, T])

    dma_rows_par(P, xt, x_d, 8)
    for k in range(18):
        E.dma("sync", convw[:, 4 * k:4 * k + 4],
              convw_d[k * 128:(k + 1) * 128, :])
        E.dma("sync", convb[:, k:k + 1], convb_d[k * 128:(k + 1) * 128, :])
    E.dma("sync", dtb[:, :], dtb_d.ap())

    ln_rows(P, pp, xt, TH, DM, scr, ab)
    apply_ab(P, pp, xt, xt, ab, TH, DM)
    for k in range(8):
        E.dma("sync", xn_d[k * 128:(k + 1) * 128, :], xt[k][:, 3:TH])

    def conv_tile(kk):
        for j in range(4):
            if j == 0:
                E.do("vector",
                     lambda e, o=xcv_buf[:, 0:T], x=xbc_buf[:, 0:T],
                            w=convw[:, 4 * kk:4 * kk + 1]:
                         e.tensor_scalar(o, x, w, None, ALU.mult))
            else:
                E.do("vector",
                     lambda e, o=xcv_buf[:, 0:T], x=xbc_buf[:, j:j + T],
                            w=convw[:, 4 * kk + j:4 * kk + j + 1]:
                         e.scalar_tensor_tensor(o, x, w, xcv_buf[:, 0:T],
                                                ALU.mult, ALU.add))
        E.do("scalar",
             lambda e, o=xcv_buf[:, 0:T], x=xcv_buf[:, 0:T],
                    b=convb[:, kk:kk + 1]:
                 e.activation(o, x, AF.Silu, bias=b))
        if kk < 16:
            E.dma("sync", xssm_d[kk * 128:(kk + 1) * 128, :], xcv_buf[:, 0:T])
        elif kk == 16:
            E.dma("sync", b_d.ap(), xcv_buf[:, 0:T])
        else:
            E.dma("sync", c_d.ap(), xcv_buf[:, 0:T])

    CT = ceil_div(TH, MMF)

    def evac2(m, c, ps):
        cw = ps.shape[1]
        mw = ps.shape[0]
        base = m * 128
        if base < 2048:
            E.do("vector", lambda e, o=zstage[0:mw, c * MMF:c * MMF + cw],
                 p=ps: e.tensor_copy(o, p))
            if c == CT - 1:
                E.dma("sync", z_d[base:base + 128, :], zstage[:, 3:TH])
        elif base < 2048 + 2304:
            E.do("vector", lambda e, o=xbc_buf[0:mw, c * MMF:c * MMF + cw],
                 p=ps: e.tensor_copy(o, p))
            if c == CT - 1:
                conv_tile((base - 2048) // 128)
        else:
            E.do("vector", lambda e, o=dt_full[0:mw, c * MMF:c * MMF + cw],
                 p=ps: e.tensor_copy(o, p))

    xtb = cast_tiles(P, "xtb", xt, TH)
    wbf = P.sbuf_pool(4, [128, 128], BF16, name="wbf")
    proj_fm(P, pp, wpool, xtb, wt_d.ap(), L0_DIN, L0_DIN_PROJ, TH, evac2,
            bfpool=wbf)

    E.do("vector", lambda e: e.tensor_copy(dtt[0:32, 0:T],
                                           dt_full[0:32, 3:TH]))
    softplus_cols(P, dtt, 32, T, dt_full, bias_col=dtb)
    E.dma("sync", dtp_d.ap(), dtt[0:32, 0:T])

    if standalone:
        P.finish()
    return P


def host_l0a(x_fm, Win, conv_w, conv_b, dt_bias):
    P = build_l0a()
    wt_in = np.ascontiguousarray(Win.T)
    cw = np.ascontiguousarray(conv_w)
    cb = np.ascontiguousarray(conv_b.reshape(-1, 1))
    db = np.ascontiguousarray(dt_bias.reshape(-1, 1))
    base = Prog.basic_inputs()
    in_maps = []
    for c in range(NCORES):
        lo = c * TCORE
        xh = np.zeros((DM, TCORE + 3), np.float32)
        xh[:, 3:] = x_fm[:, lo:lo + TCORE]
        if c % 2 == 1:
            xh[:, 0:3] = x_fm[:, lo - 3:lo]
        in_maps.append(_W(P, dict(x=xh, wt_in=wt_in, conv_w=cw,
                                  conv_b=cb, dt_bias=db)) | base)
    res = P.run(in_maps)
    out = {}
    for name in ("xn", "z", "xssm", "bmat", "cmat", "dtp"):
        out[name] = np.concatenate([r["l0a_" + name] for r in res.results],
                                   axis=1)
    return out



def kernel(**inputs):
    raise NotImplementedError("under construction")



L0B_DEBUG = False

def build_l0b(P=None):
    TT_ = 128
    NC = 4096 // TT_
    NHC = 16
    standalone = P is None
    if standalone:
        P = Prog("l0b")
    P.begin_phase("l0b")
    E = P.E
    xssm_d = P.dram_in("xssm", [1024, 4096])
    dtp_d = P.dram_in("dtp", [16, 4096])
    dtpT_d = P.dram_in("dtpT", [4096, 16])
    b_d = P.dram_in("bmat", [128, 4096])
    c_d = P.dram_in("cmat", [128, 4096])
    alog_d = P.dram_in("a_log", [16, 1])
    drow_d = P.dram_in("d_row", [1, 16])
    tmask_d = P.dram_in("tmaskc", [16, 4096])
    ipat_d = P.dram_in("ipatc", [128, 4096])
    hsel_d = P.dram_in("hselc", [16, 16 * 128])
    y_d = P.dram_out("y_tm", [4096, 1024])
    if L0B_DEBUG:
        dbg_lunw = P.dram_out("dbg_lunw", [128, 4096])
        dbg_d2r = P.dram_out("dbg_d2r", [16, 4096])
        dbg_data0 = P.dram_out("dbg_data0", [128, 4096])
        dbg_mt = P.dram_out("dbg_mt", [128, 128])
        dbg_xst = P.dram_out("dbg_xst", [128, 64])
        dbg_yi = P.dram_out("dbg_yi", [128, 64])
        dbg_v2 = P.dram_out("dbg_v2", [128, 64])
        dbg_yst = P.dram_out("dbg_yst", [128, 64])

    P.load_basics(eye=True)
    pp = P.psum_pool()
    dtp = P.sbuf("dtp_sb", [16, 4096])
    dA = P.sbuf("dA", [16, 4096])
    dAm = P.sbuf("dAm", [16, 4096])
    d2r = P.sbuf("d2r", [16, 4096])
    tmask = P.sbuf("tmask", [16, 4096])
    hsel = P.sbuf("hsel", [16, 16 * 128])
    acol = P.sbuf("acol", [16, 1])
    bsb = P.sbuf("bsb", [128, 4096])
    csb = P.sbuf("csb", [128, 4096])
    dtpT = P.sbuf("dtpT_sb", [128, NC * 16])
    d2T = P.sbuf("d2T", [128, NC * 16])
    dAcols = P.sbuf("dAcols", [128, NHC * NC])
    dall = P.sbuf("dall", [128, 16])
    xs_h = P.sbuf("xs_h", [64, 4096])
    lunw = P.sbuf("lunw", [128, 4096])
    ipat = P.sbuf("ipat", [128, 4096])
    data0 = P.sbuf("data0", [128, 4096])
    mt = P.sbuf("mt", [128, 128])
    xst = P.sbuf("xst", [128, 64])
    bt = P.sbuf("bt", [128, 128])
    bwt = P.sbuf("bwt", [128, 128])
    v1sb = P.sbuf("v1sb", [128, 64])
    v2sb = P.sbuf("v2sb", [128, 64])
    ystage = P.sbuf("ystage", [128, NC * 64])
    hstate = [P.sbuf(f"hst{h}", [128, 64]) for h in range(NHC)]

    E.dma_par([("sync", dtp[:, :], dtp_d.ap()),
               ("gpsimd", tmask[:, :], tmask_d.ap()),
               ("scalar", ipat[:, :], ipat_d.ap())])
    E.dma_par([("sync", hsel[:, :], hsel_d.ap()),
               ("gpsimd", bsb[:, :], b_d.ap()),
               ("scalar", csb[:, :], c_d.ap())])
    E.dma("sync", acol[:, :], alog_d.ap())
    E.dma("sync", dtpT[:, :],
          bass.AP(dtpT_d, 0, [[16, 128], [128 * 16, NC], [1, 16]]))

    E.do("scalar", lambda e: e.activation(acol[:, :], acol[:, :], AF.Exp))
    E.do("vector", lambda e: e.tensor_scalar(acol[:, :], acol[:, :], -1.0,
                                             None, ALU.mult))
    E.do("scalar", lambda e: e.activation(dA[:, :], dtp[:, :], AF.Exp,
                                          scale=acol[0:16, :]))
    E.do("vector", lambda e: e.tensor_mul(dAm[:, :], dA[:, :], tmask[:, :]))
    E.do("vector", lambda e: e.tensor_sub(d2r[:, :], dA[:, :], dAm[:, :]))
    E.do("vector", lambda e: e.tensor_tensor_scan(
        d2r[:, :], dAm[:, :], d2r[:, :], 0.0, ALU.mult, ALU.add))
    for c in range(NC):
        ps = pp.next()
        E.do("tensor", lambda e, c=c, o=ps[0:128, 0:16]:
             e.transpose(o, d2r[0:16, c * 128:(c + 1) * 128], P.eye[0:16, 0:16]))
        E.do("vector", lambda e, c=c, i=ps[0:128, 0:16]:
             e.tensor_copy(d2T[:, c * 16:(c + 1) * 16], i))
    dac = P.sbuf("dac", [16, NC])
    E.do("vector", lambda e: e.tensor_copy(
        dac[:, :], bass.AP(d2r, 127, [[4096, 16], [128, NC]])))
    for h in range(NHC):
        ps = pp.next()
        E.do("tensor", lambda e, h=h, o=ps[0:128, 0:NC]:
             e.matmul(o, hsel[:, h * 128:(h + 1) * 128], dac[0:16, :],
                      start=True, stop=True))
        E.do("vector", lambda e, h=h, i=ps[0:128, 0:NC]:
             e.tensor_copy(dAcols[:, h * NC:(h + 1) * NC], i))
    drow = P.sbuf("drow", [1, 16])
    E.dma("sync", drow[:, :], drow_d.ap())
    psd = pp.next()
    E.do("tensor", lambda e: e.matmul(psd[0:128, 0:16], P.ones_row[0:1, :],
                                      drow[0:1, :], start=True, stop=True))
    E.do("vector", lambda e: e.tensor_copy(dall[:, :], psd[0:128, 0:16]))

    for h in range(NHC):
        E.do("vector", lambda e, h=h: e.memset(hstate[h][:, :], 0.0))
    for h in range(NHC):
        E.dma("sync", xs_h[:, :], xssm_d[h * 64:(h + 1) * 64, :])
        for j in range(8):
            ps = pp.next()
            E.do("tensor", lambda e, h=h, j=j, o=ps[:, 0:512]:
                 e.matmul(o, hsel[:, h * 128:(h + 1) * 128],
                          dAm[0:16, j * 512:(j + 1) * 512],
                          start=True, stop=True))
            E.do("scalar", lambda e, j=j, i=ps[:, 0:512]:
                 e.activation(data0[:, j * 512:(j + 1) * 512], i, AF.Identity))
        E.do("vector", lambda e: e.tensor_tensor_scan(
            lunw[:, :], data0[:, :], ipat[:, :], 0.0, ALU.mult, ALU.add))
        if L0B_DEBUG and h == 0:
            E.dma("sync", dbg_lunw.ap(), lunw[:, :])
            E.dma("sync", dbg_d2r.ap(), d2r[:, :])
            E.dma("sync", dbg_data0.ap(), data0[:, :])
        for c in range(NC):
            t0 = c * TT_
            pst = pp.next()
            E.do("tensor", lambda e, o=pst[0:128, 0:64], t0=t0:
                 e.transpose(o, xs_h[0:64, t0:t0 + 128], P.eye[0:64, 0:64]))
            E.do("vector", lambda e, i=pst[0:128, 0:64]:
                 e.tensor_copy(xst[:, :], i))
            bank_s = pp.next()
            E.do("tensor", lambda e, o=bank_s[0:128, 0:64], t0=t0, h=h:
                 e.matmul(o, csb[:, t0:t0 + 128], hstate[h][:, :],
                          start=True, stop=True))
            gt = pp.next()
            E.do("tensor", lambda e, o=gt[0:128, 0:128], t0=t0:
                 e.matmul(o, bsb[:, t0:t0 + 128], csb[:, t0:t0 + 128],
                          start=True, stop=True))
            E.do("vector", lambda e, t0=t0, c=c, h=h, g=gt[0:128, 0:128]:
                 e.scalar_tensor_tensor(
                     mt[:, :], lunw[:, t0:t0 + 128],
                     dtpT[:, c * 16 + h:c * 16 + h + 1], g,
                     ALU.mult, ALU.mult))
            if L0B_DEBUG and h == 0 and c == 1:
                E.dma("sync", dbg_mt.ap(), mt[:, :])
                E.dma("sync", dbg_xst.ap(), xst[:, :])
            bank_i = pp.next()
            E.do("tensor", lambda e, o=bank_i[0:128, 0:64]:
                 e.matmul(o, mt[:, :], xst[:, :], start=True, stop=True))
            E.do("vector", lambda e, c=c, h=h, i=bank_s[0:128, 0:64]:
                 e.tensor_scalar(v1sb[:, :], i,
                                 d2T[:, c * 16 + h:c * 16 + h + 1], None,
                                 ALU.mult))
            E.do("vector", lambda e, h=h:
                 e.scalar_tensor_tensor(v2sb[:, :], xst[:, :],
                                        dall[:, h:h + 1], v1sb[:, :],
                                        ALU.mult, ALU.add))
            E.do("vector", lambda e, c=c, i=bank_i[0:128, 0:64]:
                 e.tensor_add(ystage[:, c * 64:(c + 1) * 64], v2sb[:, :], i))
            if L0B_DEBUG and h == 0 and c == 1:
                ycopy = P.sbuf("ycopy", [128, 64]) if not hasattr(P, "_yc") else P._yc
                P._yc = ycopy
                E.do("vector", lambda e, i=bank_i[0:128, 0:64]:
                     e.tensor_copy(ycopy[:, :], i))
                E.dma("sync", dbg_yi.ap(), ycopy[:, :])
                E.dma("sync", dbg_v2.ap(), v2sb[:, :])
                E.dma("sync", dbg_yst.ap(), ystage[:, 64:128])
            psb = pp.next()
            E.do("tensor", lambda e, o=psb[0:128, 0:128], t0=t0:
                 e.transpose(o, bsb[:, t0:t0 + 128], P.eye[:, :]))
            E.do("vector", lambda e, i=psb[0:128, 0:128]:
                 e.tensor_copy(bt[:, :], i))
            E.do("vector", lambda e, t0=t0, c=c, h=h:
                 e.tensor_scalar(bwt[:, :], bt[:, :],
                                 lunw[:, t0 + 127:t0 + 128],
                                 dtpT[:, c * 16 + h:c * 16 + h + 1],
                                 ALU.mult, ALU.mult))
            pinc = pp.next()
            E.do("tensor", lambda e, o=pinc[0:128, 0:64]:
                 e.matmul(o, bwt[:, :], xst[:, :], start=True, stop=True))
            E.do("vector", lambda e, h=h, c=c, i=pinc[0:128, 0:64]:
                 e.scalar_tensor_tensor(
                     hstate[h][:, :], hstate[h][:, :],
                     dAcols[:, h * NC + c:h * NC + c + 1], i,
                     ALU.mult, ALU.add))
        E.dma("sync",
              bass.AP(y_d, h * 64, [[1024, 128], [1024 * 128, NC], [1, 64]]),
              bass.AP(ystage, 0, [[NC * 64, 128], [64, NC], [1, 64]]))
    if standalone:
        P.finish()
    return P


def host_l0b(xssm_fm, dtp_fm, bmat_fm, cmat_fm, A_log, D):
    P = build_l0b()
    base = Prog.basic_inputs(eye=True)
    tmask = np.ones((16, 4096), np.float32)
    tmask[:, ::128] = 0.0
    ipat_host = np.tile(np.eye(128, dtype=np.float32), (1, 32))
    hsel = np.zeros((16, 16 * 128), np.float32)
    for h in range(16):
        hsel[h, h * 128:(h + 1) * 128] = 1.0
    in_maps = []
    for c in range(NCORES):
        beta, hg = c // 2, c % 2
        tok = slice(beta * 4096, (beta + 1) * 4096)
        hrows = slice(hg * 1024, (hg + 1) * 1024)
        hh = slice(hg * 16, (hg + 1) * 16)
        dtp_sl = np.ascontiguousarray(dtp_fm[hh, tok])
        in_maps.append(_W(P, dict(
            xssm=np.ascontiguousarray(xssm_fm[hrows, tok]),
            dtp=dtp_sl,
            dtpT=np.ascontiguousarray(dtp_sl.T),
            bmat=np.ascontiguousarray(bmat_fm[:, tok]),
            cmat=np.ascontiguousarray(cmat_fm[:, tok]),
            a_log=np.ascontiguousarray(A_log[hh].reshape(-1, 1)),
            d_row=np.ascontiguousarray(D[hh].reshape(1, -1)),
            tmaskc=tmask, hselc=hsel, ipatc=ipat_host)) | base)
    res = P.run(in_maps)
    y = np.zeros((2048, NTOK), np.float32)
    for c in range(NCORES):
        beta, hg = c // 2, c % 2
        y[hg * 1024:(hg + 1) * 1024, beta * 4096:(beta + 1) * 4096] = \
            res.results[c]["l0b_y_tm"].T
    return y


DMA_ENGS = ("sync", "gpsimd", "scalar")


def proj_fm_wide(P, pp, wpool, xtiles, wt_dram, d_in, d_out, T, evac,
                 xcol0=0, bfpool=None, bf_src=None, bf_sink=None):
    E = P.E
    KT = ceil_div(d_in, 128)
    MG = ceil_div(d_out, MMF)
    CT = ceil_div(T, MMF)
    assert 4 * CT <= 8, "proj_fm_wide: too many live psum banks"
    NB = min(len(wpool.tiles) if bf_src is None else 8,
             len(bfpool.tiles) if bfpool is not None else 8, KT, 4)
    for mg in range(MG):
        gw = min(MMF, d_out - mg * MMF)
        nm = ceil_div(gw, 128)
        banks = [pp.next() for _ in range(nm * CT)]
        for kb0 in range(0, KT, NB):
            kbatch = list(range(kb0, min(kb0 + NB, KT)))
            wtiles = {}
            if bf_src is not None:
                jobs = []
                for j, k in enumerate(kbatch):
                    kw = min(128, d_in - k * 128)
                    wb = bfpool.next()
                    wtiles[k] = wb
                    jobs.append((DMA_ENGS[j % 3], wb[0:kw, 0:gw],
                                 bf_src[k * 128:k * 128 + kw,
                                        mg * MMF:mg * MMF + gw]))
                E.dma_par(jobs)
            else:
                jobs = []
                stages = {}
                for j, k in enumerate(kbatch):
                    kw = min(128, d_in - k * 128)
                    wtile = wpool.next()
                    stages[k] = wtile
                    jobs.append((DMA_ENGS[j % 3], wtile[0:kw, 0:gw],
                                 wt_dram[k * 128:k * 128 + kw,
                                         mg * MMF:mg * MMF + gw]))
                E.dma_par(jobs)
                if bfpool is not None:
                    for k in kbatch:
                        kw = min(128, d_in - k * 128)
                        wb = bfpool.next()
                        E.do("scalar", lambda e, o=wb[0:kw, 0:gw],
                             x=stages[k][0:kw, 0:gw]:
                             e.activation(o, x, AF.Copy))
                        wtiles[k] = wb
                    if bf_sink is not None:
                        E.dma_par([
                            (DMA_ENGS[j % 3],
                             bf_sink[k * 128:k * 128 + min(128, d_in - k * 128),
                                     mg * MMF:mg * MMF + gw],
                             wtiles[k][0:min(128, d_in - k * 128), 0:gw])
                            for j, k in enumerate(kbatch)])
                else:
                    wtiles = stages
            for k in kbatch:
                kw = min(128, d_in - k * 128)
                wtile = wtiles[k]
                for mi in range(nm):
                    mw = min(128, gw - mi * 128)
                    for c in range(CT):
                        cw = min(MMF, T - c * MMF)
                        E.do("tensor",
                             lambda e, o=banks[mi * CT + c][0:mw, 0:cw],
                                    w=wtile[0:kw, mi * 128:mi * 128 + mw],
                                    x=xtiles[k][0:kw, xcol0 + c * MMF:
                                                xcol0 + c * MMF + cw],
                                    st=(k == 0), sp=(k == KT - 1):
                                 e.matmul(o, w, x, start=st, stop=sp))
        for mi in range(nm):
            mw = min(128, gw - mi * 128)
            for c in range(CT):
                cw = min(MMF, T - c * MMF)
                evac(mg * 4 + mi, c, banks[mi * CT + c][0:mw, 0:cw])



def build_l0c1(P=None):
    T = TCORE
    standalone = P is None
    if standalone:
        P = Prog("l0c1")
    P.begin_phase("l0c1")
    E = P.E
    y_d = P.dram_in("y", [2048, T])
    z_d = P.dram_in("z", [2048, T])
    nw_d = P.dram_in("norm_w", [2048, 1])
    wt_d = P.dram_in("wt_out", [2048, 1024])
    s_d = P.dram_out("s", [1024, T])
    P.load_basics()
    pp = P.psum_pool()
    yt = [P.sbuf(f"y{k}", [128, T]) for k in range(16)]
    zbuf = P.sbuf("zbuf", [128, T])
    scr = P.sbuf("scr", [128, MMF])
    ab = (P.sbuf("abb", [1, T]), P.sbuf("aba", [1, T]))
    nw = P.sbuf("nw", [128, 16])
    wpool = P.sbuf_pool(2, [128, MMF], name="wp")
    sstage = P.sbuf("sstage", [128, T])
    dma_rows_par(P, yt, y_d, 16)
    for k in range(16):
        E.dma("sync", nw[:, k:k + 1], nw_d[k * 128:(k + 1) * 128, :])
    ln_rows(P, pp, yt, T, 2048, scr, ab, rms=True)
    CT = ceil_div(T, MMF)
    for c in range(CT):
        cw = min(MMF, T - c * MMF)
        ba = pp.next()
        E.do("tensor", lambda e, o=ba[:, 0:cw],
             r=ab[1][0:1, c * MMF:c * MMF + cw]:
                 e.matmul(o, P.ones_row[0:1, :], r, start=True, stop=True))
        for k in range(16):
            E.do("vector",
                 lambda e, k=k, o=yt[k][:, c * MMF:c * MMF + cw],
                        a=ba[:, 0:cw]:
                     e.tensor_mul(o, o, a))
    for k in range(16):
        E.dma("sync", zbuf[:, :], z_d[k * 128:(k + 1) * 128, :])
        E.do("scalar", lambda e: e.activation(zbuf[:, :], zbuf[:, :], AF.Silu))
        E.do("vector", lambda e, k=k: e.tensor_scalar(
            yt[k][:, :], yt[k][:, :], nw[:, k:k + 1], None, ALU.mult))
        E.do("vector", lambda e, k=k: e.tensor_mul(
            yt[k][:, :], yt[k][:, :], zbuf[:, :]))

    def evac(m, c, ps):
        E.do("vector", lambda e, o=sstage[0:ps.shape[0],
                                          c * MMF:c * MMF + ps.shape[1]],
             p=ps: e.tensor_copy(o, p))
        if c == ceil_div(T, MMF) - 1:
            E.dma("sync", s_d[m * 128:m * 128 + 128, :], sstage[:, 0:T])

    proj_fm(P, pp, wpool, yt, wt_d.ap(), 2048, 1024, T, evac)
    if standalone:
        P.finish()
    return P


def build_l0c2(P=None):
    T = TCORE
    TS = 512
    NS = T // TS
    standalone = P is None
    if standalone:
        P = Prog("l0c2")
    P.begin_phase("l0c2")
    E = P.E
    xn_d = P.dram_in("xn", [1024, T])
    x_d = P.dram_in("x", [1024, T])
    s_d = P.dram_in("s", [1024, T])
    w1_d = P.dram_in("w1t", [1024, 4096])
    w2_d = P.dram_in("w2t", [1024, 4096])
    h1_d = P.dram_in("h1t", [4096, 4096])
    h2_d = P.dram_in("h2t", [4096, 4096])
    w3_d = P.dram_in("w3t", [4096, 1024])
    sc_d = P.dram_in("scales", [128, 9])
    x1_d = P.dram_out("x1", [1024, T])
    P.load_basics()
    pp = P.psum_pool()
    sc = P.sbuf("sc", [128, 136])
    sc_d2 = P.dram_in("scales2", [128, 136])
    E.dma("sync", sc[:, :], sc_d2.ap())
    ht = [P.sbuf(f"h{k}", [128, TS], BF16) for k in range(32)]
    gt = [P.sbuf(f"g{k}", [128, TS], BF16) for k in range(32)]
    wpool = P.sbuf_pool(4, [128, MMF], name="wp")
    wbf = P.sbuf_pool(4, [128, MMF], BF16, name="wbf")
    xbuf = P.sbuf("xbuf", [128, TS])
    xbufb = P.sbuf("xbufb", [128, TS], BF16)
    sbuf2 = P.sbuf("sbuf2", [128, TS])
    xnb = [P.sbuf(f"xnb{k}", [128, T], BF16) for k in range(8)]
    xst2 = [P.sbuf(f"xst{j}", [128, T]) for j in range(2)]
    for k in range(0, 8, 2):
        E.dma_par([("sync", xst2[0][:, :], xn_d[k * 128:(k + 1) * 128, :]),
                   ("gpsimd", xst2[1][:, :],
                    xn_d[(k + 1) * 128:(k + 2) * 128, :])])
        E.do("scalar", lambda e, k=k: e.activation(
            xnb[k][:, :], xst2[0][:, :], AF.Copy))
        E.do("scalar", lambda e, k=k: e.activation(
            xnb[k + 1][:, :], xst2[1][:, :], AF.Copy))
    w1c = P.dram_tmp("w1bf", [1024, 4096], BF16)
    w2c = P.dram_tmp("w2bf", [1024, 4096], BF16)
    h1c = P.dram_tmp("h1bf", [4096, 4096], BF16)
    h2c = P.dram_tmp("h2bf", [4096, 4096], BF16)
    w3c = P.dram_tmp("w3bf", [4096, 1024], BF16)

    def cache_kw(sschunk, wc):
        return ({"bf_sink": wc} if sschunk == 0 else {"bf_src": wc})

    for sschunk in range(NS):
        xc0 = sschunk * TS
        def evac_t2(m, c, ps):
            E.do("vector", lambda e, m=m, o=ht[m][0:ps.shape[0], 0:ps.shape[1]],
                 p=ps: e.tensor_scalar(o, p, sc[:, 32 + m:33 + m], None,
                                       ALU.mult))
        proj_fm_wide(P, pp, wpool, xnb, w2_d.ap(), 1024, 4096, TS, evac_t2,
                     xcol0=xc0, bfpool=wbf, **cache_kw(sschunk, w2c))
        def evac_t1(m, c, ps):
            E.do("scalar", lambda e, m=m, o=xbufb[0:ps.shape[0], 0:ps.shape[1]],
                 p=ps: e.activation(o, p, AF.Silu, scale=sc[:, m:m + 1]))
            E.do("vector", lambda e, m=m: e.tensor_mul(
                ht[m][:, 0:TS], ht[m][:, 0:TS], xbufb[:, 0:TS]))
        proj_fm_wide(P, pp, wpool, xnb, w1_d.ap(), 1024, 4096, TS, evac_t1,
                     xcol0=xc0, bfpool=wbf, **cache_kw(sschunk, w1c))
        def evac_g2(m, c, ps):
            E.do("scalar", lambda e, m=m, o=gt[m][0:ps.shape[0], 0:ps.shape[1]],
                 p=ps: e.activation(o, p, AF.Gelu, scale=sc[:, 64 + m:65 + m]))
        proj_fm_wide(P, pp, wpool, ht, h1_d.ap(), 4096, 4096, TS, evac_g2,
                     bfpool=wbf, **cache_kw(sschunk, h1c))
        def evac_g3(m, c, ps):
            E.do("scalar", lambda e, m=m, o=ht[m][0:ps.shape[0], 0:ps.shape[1]],
                 p=ps: e.activation(o, p, AF.Gelu, scale=sc[:, 96 + m:97 + m]))
        proj_fm_wide(P, pp, wpool, gt, h2_d.ap(), 4096, 4096, TS, evac_g3,
                     bfpool=wbf, **cache_kw(sschunk, h2c))
        def evac_g(m, c, ps):
            E.dma("sync", xbuf[:, 0:TS], x_d[m * 128:(m + 1) * 128,
                                             xc0:xc0 + TS])
            E.dma("sync", sbuf2[:, 0:TS], s_d[m * 128:(m + 1) * 128,
                                              xc0:xc0 + TS])
            E.do("scalar", lambda e, m=m, o=ps, p=ps:
                 e.activation(o, p, AF.Silu, scale=sc[:, 128 + m:129 + m]))
            E.do("vector", lambda e: e.tensor_mul(
                sbuf2[:, 0:TS], sbuf2[:, 0:TS], ps))
            E.do("vector", lambda e: e.tensor_add(
                xbuf[:, 0:TS], xbuf[:, 0:TS], sbuf2[:, 0:TS]))
            E.dma("sync", x1_d[m * 128:(m + 1) * 128, xc0:xc0 + TS],
                  xbuf[:, 0:TS])
        proj_fm_wide(P, pp, wpool, ht, w3_d.ap(), 4096, 1024, TS, evac_g,
                     bfpool=wbf, **cache_kw(sschunk, w3c))
    if standalone:
        P.finish()
    return P


def host_l0c(y_fm, z_fm, xn_fm, x_fm, norm_w, Wout, ffn):
    P1 = build_l0c1()
    base = Prog.basic_inputs()
    wt_out = np.ascontiguousarray(Wout.T)
    ims = []
    for c in range(NCORES):
        tok = slice(c * TCORE, (c + 1) * TCORE)
        ims.append(_W(P1, dict(y=np.ascontiguousarray(y_fm[:, tok]),
                        z=np.ascontiguousarray(z_fm[:, tok]),
                        norm_w=np.ascontiguousarray(norm_w.reshape(-1, 1)),
                        wt_out=wt_out)) | base)
    res1 = P1.run(ims)
    s_fm = np.concatenate([r["l0c1_s"] for r in res1.results], axis=1)

    P2 = build_l0c2()
    scales = np.zeros((128, 136), np.float32)
    for i, nm in enumerate(("w1_s", "w2_s", "h1_s", "h2_s")):
        scales[:, i * 32:(i + 1) * 32] = ffn[nm].reshape(32, 128).T
    scales[:, 128:136] = ffn["w3_s"].reshape(8, 128).T
    w1t = np.ascontiguousarray(ffn["w1_w"].T)
    w2t = np.ascontiguousarray(ffn["w2_w"].T)
    h1t = np.ascontiguousarray(ffn["h1_w"].T)
    h2t = np.ascontiguousarray(ffn["h2_w"].T)
    w3t = np.ascontiguousarray(ffn["w3_w"].T)
    ims = []
    for c in range(NCORES):
        tok = slice(c * TCORE, (c + 1) * TCORE)
        ims.append(_W(P2, dict(xn=np.ascontiguousarray(xn_fm[:, tok]),
                        x=np.ascontiguousarray(x_fm[:, tok]),
                        s=np.ascontiguousarray(s_fm[:, tok]),
                        w1t=w1t, w2t=w2t, h1t=h1t, h2t=h2t, w3t=w3t,
                        scales=np.zeros((128, 9), np.float32),
                        scales2=scales)) | base)
    res2 = P2.run(ims)
    return np.concatenate([r["l0c2_x1"] for r in res2.results], axis=1)



def build_l1a(P=None):
    standalone = P is None
    if standalone:
        P = Prog("l1a")
    P.begin_phase("l1a")
    E = P.E
    T = TCORE
    TH = T + 3
    x_d = P.dram_in("x", [DM, T])
    xh_d = P.dram_in("xhalo", [DM, 3])
    hf_d = P.dram_in("haloflag", [128, 1])
    wt_d = P.dram_in("wt_in", [1024, 4096])
    convw_d = P.dram_in("conv_w", [2048, 4])
    xpt_d = P.dram_in("wt_xproj", [2048, 17])
    xn_d = P.dram_out("xn", [DM, T])
    u_d = P.dram_out("u", [2048, T])
    z_d = P.dram_out("z", [2048, T])
    dtb_d = P.dram_out("dtb", [17, T])

    P.load_basics()
    pp = P.psum_pool()
    xt = [P.sbuf(f"x{k}", [128, TH]) for k in range(8)]
    scr = P.sbuf("scr", [128, MMF])
    ab = (P.sbuf("abb", [1, TH]), P.sbuf("aba", [1, TH]))
    wpool = P.sbuf_pool(4, [128, MMF], name="wp")
    convw = P.sbuf("convw", [128, 4 * 16])
    ubuf = P.sbuf("ubuf", [128, TH])
    ucv = P.sbuf("ucv", [128, T])
    zstage = P.sbuf("zstage", [128, TH])
    xpw = P.sbuf("xpw", [128, 17 * 16])
    dtbst = P.sbuf("dtbst", [17, T])
    CT = ceil_div(TH, MMF)
    CTm = ceil_div(T, MMF)

    hfc = P.sbuf("hfc", [128, 1])
    E.dma("sync", hfc[:, :], hf_d.ap())
    dma_rows_par(P, xt, x_d, 8, cols=TH - 3, col0=3)
    for k in range(8):
        E.dma("sync", xt[k][:, 0:3], xh_d[k * 128:(k + 1) * 128, :])
        E.do("vector", lambda e, k=k: e.tensor_scalar(
            xt[k][:, 0:3], xt[k][:, 0:3], hfc[:, 0:1], None, ALU.mult))
    for k in range(16):
        E.dma("sync", convw[:, 4 * k:4 * k + 4],
              convw_d[k * 128:(k + 1) * 128, :])
        E.dma("sync", xpw[:, 17 * k:17 * k + 17],
              xpt_d[k * 128:(k + 1) * 128, :])

    ln_rows(P, pp, xt, TH, DM, scr, ab)
    apply_ab(P, pp, xt, xt, ab, TH, DM)
    for k in range(8):
        E.dma("sync", xn_d[k * 128:(k + 1) * 128, :], xt[k][:, 3:TH])

    xdbl = P.sbuf("xdbl", [17, T])

    def conv_u_tile(kk):
        for j in range(4):
            if j == 0:
                E.do("vector",
                     lambda e, o=ucv[:, 0:T], x=ubuf[:, 0:T],
                            w=convw[:, 4 * kk:4 * kk + 1]:
                         e.tensor_scalar(o, x, w, None, ALU.mult))
            else:
                E.do("vector",
                     lambda e, o=ucv[:, 0:T], x=ubuf[:, j:j + T],
                            w=convw[:, 4 * kk + j:4 * kk + j + 1]:
                         e.scalar_tensor_tensor(o, x, w, ucv[:, 0:T],
                                                ALU.mult, ALU.add))
        E.do("scalar", lambda e: e.activation(ucv[:, 0:T], ucv[:, 0:T],
                                              AF.Silu))
        E.dma("sync", u_d[kk * 128:(kk + 1) * 128, :], ucv[:, 0:T])
        if kk == 0:
            E.do("vector", lambda e: e.memset(xdbl[:, :], 0.0))
        for c in range(CTm):
            cw = min(MMF, T - c * MMF)
            ps = pp.next()
            E.do("tensor",
                 lambda e, o=ps[0:17, 0:cw],
                        w=xpw[:, 17 * kk:17 * kk + 17],
                        x=ucv[:, c * MMF:c * MMF + cw]:
                     e.matmul(o, w, x, start=True, stop=True))
            E.do("vector",
                 lambda e, o=xdbl[0:17, c * MMF:c * MMF + cw],
                        p=ps[0:17, 0:cw]:
                     e.tensor_add(o, o, p))

    def evac(m, c, ps):
        cw = ps.shape[1]
        mw = ps.shape[0]
        if m < 16:
            E.do("vector", lambda e, o=ubuf[0:mw, c * MMF:c * MMF + cw],
                 p=ps: e.tensor_copy(o, p))
            if c == CT - 1:
                conv_u_tile(m)
        else:
            E.do("vector", lambda e, o=zstage[0:mw, c * MMF:c * MMF + cw],
                 p=ps: e.tensor_copy(o, p))
            if c == CT - 1:
                E.dma("sync", z_d[(m - 16) * 128:(m - 15) * 128, :],
                      zstage[:, 3:TH])

    xtb = cast_tiles(P, "xtb", xt, TH)
    wbf = P.sbuf_pool(4, [128, MMF], BF16, name="wbf")
    proj_fm(P, pp, wpool, xtb, wt_d.ap(), 1024, 4096, TH, evac, bfpool=wbf)
    E.dma("sync", dtb_d.ap(), xdbl[:, 0:T])
    if standalone:
        P.finish()
    return P


def host_l1a(x_fm, in_proj_w, conv_w, x_proj_w):
    P = build_l1a()
    base = Prog.basic_inputs()
    wt_in = np.ascontiguousarray(in_proj_w.T)
    cw = np.ascontiguousarray(conv_w[:, 0, :])
    xpt = np.ascontiguousarray(x_proj_w.T)
    ims = []
    for c in range(NCORES):
        lo = c * TCORE
        halo = np.zeros((DM, 3), np.float32)
        flag = np.zeros((128, 1), np.float32)
        if c % 2 == 1:
            halo = np.ascontiguousarray(x_fm[:, lo - 3:lo])
            flag[:] = 1.0
        ims.append(_W(P, dict(x=np.ascontiguousarray(x_fm[:, lo:lo + TCORE]),
                              xhalo=halo, haloflag=flag, wt_in=wt_in,
                              conv_w=cw, wt_xproj=xpt)) | base)
    res = P.run(ims)
    out = {}
    for name in ("xn", "u", "z", "dtb"):
        out[name] = np.concatenate([r["l1a_" + name] for r in res.results],
                                   axis=1)
    return out



def build_l1b(P=None):
    standalone = P is None
    if standalone:
        P = Prog("l1b")
    P.begin_phase("l1b")
    E = P.E
    L = 4096
    dtbag_d = P.dram_in("dtbag", [8 * 17, TCORE])
    dtwb_d = P.dram_in("dtwb", [2, 256])
    a_d = P.dram_in("a_sl", [256, 16])
    hsel_d = P.dram_in("hsel16", [16, 16 * 128])
    y_d = P.dram_out("y", [8 * 256, TCORE])
    P.load_basics()
    pp = P.psum_pool()
    mov = P.sbuf("mov", [2, 4096])
    dtwb = P.sbuf("dtwb_sb", [2, 256])
    asl = [P.sbuf(f"asl{i}", [128, 16]) for i in range(2)]
    hsel = P.sbuf("hsel", [16, 16 * 128])
    delta = P.sbuf("delta", [128, L])
    tmp = P.sbuf("tmp", [128, L])
    bbc = P.sbuf("bbc", [128, L])
    dB = P.sbuf("dB", [128, L])
    ee = P.sbuf("ee", [128, L])
    hh = P.sbuf("hh", [128, L])
    yac = P.sbuf("yac", [128, L])
    E.do("vector", lambda e: e.memset(mov[0:2, :], 1.0))
    E.dma("sync", dtwb[:, :], dtwb_d.ap())
    for i in range(2):
        E.dma("sync", asl[i][:, :], a_d[i * 128:(i + 1) * 128, :])
        E.do("scalar", lambda e, i=i: e.activation(asl[i][:, :], asl[i][:, :],
                                                   AF.Exp))
        E.do("vector", lambda e, i=i: e.tensor_scalar(
            asl[i][:, :], asl[i][:, :], -1.0, None, ALU.mult))
    E.dma("sync", hsel[:, :], hsel_d.ap())
    bsb = P.sbuf("bsb16", [16, 4096])

    for bidx in range(4):
        t0 = bidx * L
        r0 = (2 * bidx) * 17
        r1 = (2 * bidx + 1) * 17
        E.dma("sync", mov[0:1, 0:TCORE], dtbag_d[r0:r0 + 1, :])
        E.dma("sync", mov[0:1, TCORE:L], dtbag_d[r1:r1 + 1, :])
        E.dma("sync", bsb[:, 0:TCORE], dtbag_d[r0 + 1:r0 + 17, :])
        E.dma("sync", bsb[:, TCORE:L], dtbag_d[r1 + 1:r1 + 17, :])
        for ds in range(2):
            for c in range(8):
                ps = pp.next()
                E.do("tensor", lambda e, c=c, ds=ds, o=ps[:, 0:512]:
                     e.matmul(o, dtwb[:, ds * 128:(ds + 1) * 128],
                              mov[:, c * 512:(c + 1) * 512],
                              start=True, stop=True))
                E.do("scalar", lambda e, c=c, i=ps[:, 0:512]:
                     e.activation(delta[:, c * 512:(c + 1) * 512], i,
                                  AF.Identity))
            softplus_cols(P, delta, 128, L, tmp)
            E.do("vector", lambda e: e.memset(yac[:, :], 0.0))
            for n in range(16):
                if True:
                    for c in range(8):
                        ps = pp.next()
                        E.do("tensor", lambda e, c=c, n=n, o=ps[:, 0:512]:
                             e.matmul(o, hsel[:, n * 128:(n + 1) * 128],
                                      bsb[0:16, c * 512:(c + 1) * 512],
                                      start=True, stop=True))
                        E.do("scalar", lambda e, c=c, i=ps[:, 0:512]:
                             e.activation(bbc[:, c * 512:(c + 1) * 512], i,
                                          AF.Identity))
                E.do("scalar", lambda e, n=n, ds=ds: e.activation(
                    ee[:, :], delta[:, :], AF.Exp,
                    scale=asl[ds][:, n:n + 1]))
                E.do("vector", lambda e: e.tensor_mul(dB[:, :], delta[:, :],
                                                      bbc[:, :]))
                E.do("vector", lambda e: e.tensor_tensor_scan(
                    hh[:, :], ee[:, :], dB[:, :], 0.0, ALU.mult, ALU.add))
                E.do("vector", lambda e: e.tensor_mul(hh[:, :], hh[:, :],
                                                      bbc[:, :]))
                E.do("vector", lambda e: e.tensor_add(yac[:, :], yac[:, :],
                                                      hh[:, :]))
            E.dma("sync", y_d[(2 * bidx) * 256 + ds * 128:
                              (2 * bidx) * 256 + (ds + 1) * 128, :],
                  yac[:, 0:TCORE])
            E.dma("sync", y_d[(2 * bidx + 1) * 256 + ds * 128:
                              (2 * bidx + 1) * 256 + (ds + 1) * 128, :],
                  yac[:, TCORE:L])
    if standalone:
        P.finish()
    return P


def host_l1b(dtb_fm, dt_w, dt_b, A_param):
    P = build_l1b()
    base = Prog.basic_inputs()
    hsel = np.zeros((16, 16 * 128), np.float32)
    for h in range(16):
        hsel[h, h * 128:(h + 1) * 128] = 1.0
    dtbag = np.concatenate([dtb_fm[:, j * TCORE:(j + 1) * TCORE]
                            for j in range(8)], axis=0).astype(np.float32)
    ims = []
    for c in range(NCORES):
        dsl = slice(c * 256, (c + 1) * 256)
        dtwb = np.stack([dt_w[dsl, 0], dt_b[dsl]]).astype(np.float32)
        ims.append(_W(P, dict(dtbag=np.ascontiguousarray(dtbag),
                        dtwb=np.ascontiguousarray(dtwb),
                        a_sl=np.ascontiguousarray(A_param[dsl]),
                        hsel16=hsel)) | base)
    res = P.run(ims)
    y = np.zeros((2048, NTOK), np.float32)
    for c in range(NCORES):
        yc = res.results[c]["l1b_y"]
        for j in range(8):
            y[c * 256:(c + 1) * 256, j * TCORE:(j + 1) * TCORE] = \
                yc[j * 256:(j + 1) * 256, :]
    return y



def build_l1c(P=None):
    T = TCORE
    standalone = P is None
    if standalone:
        P = Prog("l1c")
    P.begin_phase("l1c")
    E = P.E
    y_d = P.dram_in("y", [2048, T])
    u_d = P.dram_in("u", [2048, T])
    z_d = P.dram_in("z", [2048, T])
    dcol_d = P.dram_in("dcol", [2048, 1])
    wt_d = P.dram_in("wt_out", [2048, 1024])
    ob_d = P.dram_in("out_b", [1024, 1])
    s_d = P.dram_out("s", [1024, T])
    P.load_basics()
    pp = P.psum_pool()
    yt = [P.sbuf(f"y{k}", [128, T]) for k in range(16)]
    buf = P.sbuf("buf", [128, T])
    dcol = P.sbuf("dcol_sb", [128, 16])
    ob = P.sbuf("ob_sb", [128, 8])
    wpool = P.sbuf_pool(2, [128, MMF], name="wp")
    sstage = P.sbuf("sstage", [128, T])
    dma_rows_par(P, yt, y_d, 16)
    for k in range(16):
        E.dma("sync", dcol[:, k:k + 1], dcol_d[k * 128:(k + 1) * 128, :])
    for k in range(8):
        E.dma("sync", ob[:, k:k + 1], ob_d[k * 128:(k + 1) * 128, :])
    for k in range(16):
        E.dma("sync", buf[:, :], u_d[k * 128:(k + 1) * 128, :])
        E.do("vector", lambda e, k=k: e.scalar_tensor_tensor(
            yt[k][:, :], buf[:, :], dcol[:, k:k + 1], yt[k][:, :],
            ALU.mult, ALU.add))
        E.dma("sync", buf[:, :], z_d[k * 128:(k + 1) * 128, :])
        E.do("scalar", lambda e: e.activation(buf[:, :], buf[:, :], AF.Silu))
        E.do("vector", lambda e, k=k: e.tensor_mul(yt[k][:, :], yt[k][:, :],
                                                   buf[:, :]))

    def evac(m, c, ps):
        E.do("vector", lambda e, m=m,
             o=sstage[0:ps.shape[0], c * MMF:c * MMF + ps.shape[1]], p=ps:
                 e.tensor_scalar(o, p, 1.0, ob[:, m:m + 1], ALU.mult, ALU.add))
        if c == ceil_div(T, MMF) - 1:
            E.dma("sync", s_d[m * 128:m * 128 + 128, :], sstage[:, 0:T])

    proj_fm(P, pp, wpool, yt, wt_d.ap(), 2048, 1024, T, evac)
    if standalone:
        P.finish()
    return P


def host_l1c(y_fm, u_fm, z_fm, D_param, out_w, out_b):
    P = build_l1c()
    base = Prog.basic_inputs()
    wt = np.ascontiguousarray(out_w.T)
    ims = []
    for c in range(NCORES):
        tok = slice(c * TCORE, (c + 1) * TCORE)
        ims.append(_W(P, dict(y=np.ascontiguousarray(y_fm[:, tok]),
                        u=np.ascontiguousarray(u_fm[:, tok]),
                        z=np.ascontiguousarray(z_fm[:, tok]),
                        dcol=np.ascontiguousarray(D_param.reshape(-1, 1)),
                        wt_out=wt,
                        out_b=np.ascontiguousarray(out_b.reshape(-1, 1))))
                   | base)
    res = P.run(ims)
    return np.concatenate([r["l1c_s"] for r in res.results], axis=1)



def build_l1d1(P=None):
    standalone = P is None
    if standalone:
        P = Prog("l1d1")
    P.begin_phase("l1d1")
    E = P.E
    T = TCORE
    TH = 513 + T
    TK = 512 + T
    x_d = P.dram_in("x", [DM, T])
    xh_d = P.dram_in("xhalo", [DM, 513])
    hf_d = P.dram_in("haloflag", [128, 1])
    wq_d = P.dram_in("wq", [1024, 512])
    wk_d = P.dram_in("wk", [1024, 512])
    wv_d = P.dram_in("wv", [1024, 512])
    q_d = P.dram_out("q", [512, T])
    k_d = P.dram_out("k", [512, TK])
    v_d = P.dram_out("v", [512, TK])
    P.load_basics()
    pp = P.psum_pool()
    xt = [P.sbuf(f"x{k}", [128, TH]) for k in range(8)]
    wpool = P.sbuf_pool(2, [128, MMF], name="wp")
    stage = P.sbuf("stage", [128, TK])
    hfc = P.sbuf("hfc", [128, 1])
    E.dma("sync", hfc[:, :], hf_d.ap())
    dma_rows_par(P, xt, x_d, 8, cols=TH - 513, col0=513)
    for k in range(8):
        E.dma("sync", xt[k][:, 0:513], xh_d[k * 128:(k + 1) * 128, :])
        E.do("vector", lambda e, k=k: e.tensor_scalar(
            xt[k][:, 0:513], xt[k][:, 0:513], hfc[:, 0:1], None, ALU.mult))

    def proj_shift(w_dram, out_d, Tn, col0):
        CT = ceil_div(Tn, MMF)
        for m in range(4):
            banks = [pp.next() for _ in range(CT)]
            for k in range(8):
                wtile = wpool.next()
                E.dma("sync", wtile[:, 0:128],
                      w_dram[k * 128:(k + 1) * 128, m * 128:(m + 1) * 128])
                off = col0 if k < 4 else col0 - 1
                for c in range(CT):
                    cw = min(MMF, Tn - c * MMF)
                    E.do("tensor",
                         lambda e, o=banks[c][0:128, 0:cw],
                                w=wtile[:, 0:128],
                                x=xt[k][:, off + c * MMF:off + c * MMF + cw],
                                st=(k == 0):
                             e.matmul(o, w, x, start=st, stop=(k == 7)))
            for c in range(CT):
                cw = min(MMF, Tn - c * MMF)
                E.do("vector", lambda e, o=stage[:, c * MMF:c * MMF + cw],
                     p=banks[c][0:128, 0:cw]: e.tensor_copy(o, p))
            E.dma("sync", out_d[m * 128:(m + 1) * 128, 0:Tn],
                  stage[:, 0:Tn])

    proj_shift(wq_d, q_d, T, 513)
    proj_shift(wk_d, k_d, TK, 1)
    proj_shift(wv_d, v_d, TK, 1)
    if standalone:
        P.finish()
    return P


L1D2_DEBUG = False


def build_l1d2(P=None):
    standalone = P is None
    if standalone:
        P = Prog("l1d2")
    P.begin_phase("l1d2")
    E = P.E
    T = TCORE
    TK = 512 + T
    NW = T // 512
    q_d = P.dram_in("q", [512, T])
    k_d = P.dram_in("k", [512, TK])
    v_d = P.dram_in("v", [512, TK])
    qs_d = P.dram_in("qs_col", [512, 1])
    ks_d = P.dram_in("ks_col", [512, 1])
    csq_d = P.dram_in("csq", [128, TK])
    ssq_d = P.dram_in("ssq", [128, TK])
    csk_d = P.dram_in("csk", [128, TK])
    ssk_d = P.dram_in("ssk", [128, TK])
    rotm_d = P.dram_in("rotm", [128, 128])
    ind2_d = P.dram_in("ind2", [2, 128])
    mask_d = P.dram_in("mask8", [128, 8 * 1024])
    o_d = P.dram_out("o", [512, T])
    if L1D2_DEBUG:
        dbg_k0 = P.dram_out("dbg_k0", [128, TK])
        dbg_qw = P.dram_out("dbg_qw", [128, 512])
        dbg_e = P.dram_out("dbg_e", [128, 1024])
        dbg_vt = P.dram_out("dbg_vt", [128, 512])
    P.load_basics(eye=True)
    pp = P.psum_pool()
    kt = [P.sbuf(f"k{i}", [128, TK]) for i in range(4)]
    vt = [P.sbuf(f"v{i}", [128, TK]) for i in range(4)]
    csk = P.sbuf("csk_sb", [128, TK])
    ssk = P.sbuf("ssk_sb", [128, TK])
    csq = P.sbuf("csq_sb", [128, TK])
    ssq = P.sbuf("ssq_sb", [128, TK])
    rotm = P.sbuf("rotm_sb", [128, 128])
    rsc = P.sbuf("rsc", [128, MMF])
    rsc2 = P.sbuf("rsc2", [128, MMF])
    qs = P.sbuf("qs_sb", [128, 4])
    ks = P.sbuf("ks_sb", [128, 4])
    ind2 = P.sbuf("ind2_sb", [2, 128])
    mask = P.sbuf("mask_sb", [128, 8 * 1024])
    rn2 = P.sbuf("rn2", [2, TK])
    qw = P.sbuf("qw", [128, 512])
    esb = P.sbuf("esb", [128, 1024])
    etb = P.sbuf("etb", [128, 1024], BF16)
    vtb = P.sbuf("vtb", [128, 512], BF16)
    ost = [P.sbuf(f"ost{i}", [64, 512]) for i in range(8)]
    nmc = P.sbuf("nmc", [128, 1])
    rdc = P.sbuf("rdc", [128, 1])
    scrt = P.sbuf("scrt", [128, MMF])

    dma_rows_par(P, kt, k_d, 4)
    dma_rows_par(P, vt, v_d, 4)
    for i in range(4):
        E.dma("sync", qs[:, i:i + 1], qs_d[i * 128:(i + 1) * 128, :])
        E.dma("sync", ks[:, i:i + 1], ks_d[i * 128:(i + 1) * 128, :])
    E.dma_par([("sync", csk[:, :], csk_d.ap()),
               ("gpsimd", ssk[:, :], ssk_d.ap()),
               ("scalar", csq[:, :], csq_d.ap())])
    E.dma_par([("sync", ssq[:, :], ssq_d.ap()),
               ("gpsimd", ind2[:, :], ind2_d.ap()),
               ("scalar", mask[:, :], mask_d.ap())])
    E.dma("sync", rotm[:, :], rotm_d.ap())

    def prep(tile, Tn, sc_col, cs, ss, qk_scale, ccol0):
        CT = ceil_div(Tn, MMF)
        for c in range(CT):
            cw = min(MMF, Tn - c * MMF)
            psA = pp.next()
            E.do("scalar", lambda e, o=scrt[:, 0:cw],
                 x=tile[:, c * MMF:c * MMF + cw]:
                     e.activation(o, x, AF.Square))
            E.do("tensor", lambda e, o=psA[0:2, 0:cw], x=scrt[:, 0:cw]:
                 e.matmul(o, ind2T[:, 0:2], x, start=True, stop=True))
            E.do("vector", lambda e, o=rn2[0:2, c * MMF:c * MMF + cw],
                 p=psA[0:2, 0:cw]: e.tensor_copy(o, p))
        E.do("scalar", lambda e: e.activation(
            rn2[0:2, 0:Tn], rn2[0:2, 0:Tn], AF.Ln,
            bias=P.ccol("eps1e12")[0:2, :]))
        E.do("scalar", lambda e: e.activation(
            rn2[0:2, 0:Tn], rn2[0:2, 0:Tn], AF.Exp, scale=0.5))
        E.do("vector", lambda e: e.tensor_scalar_max(
            rn2[0:2, 0:Tn], rn2[0:2, 0:Tn], 1e-12))
        E.do("vector", lambda e: e.reciprocal(rn2[0:2, 0:Tn],
                                              rn2[0:2, 0:Tn]))
        if qk_scale != 1.0:
            E.do("vector", lambda e: e.tensor_scalar(
                rn2[0:2, 0:Tn], rn2[0:2, 0:Tn], float(qk_scale), None,
                ALU.mult))
        for c in range(CT):
            cw = min(MMF, Tn - c * MMF)
            ps = pp.next()
            E.do("tensor", lambda e, o=ps[:, 0:cw],
                 r=rn2[0:2, c * MMF:c * MMF + cw]:
                     e.matmul(o, ind2[:, :], r, start=True, stop=True))
            E.do("vector", lambda e, o=tile[:, c * MMF:c * MMF + cw],
                 s=sc_col, p=ps[:, 0:cw]:
                     e.scalar_tensor_tensor(o, o, s, p, ALU.mult, ALU.mult))
        for c in range(CT):
            cw = min(MMF, Tn - c * MMF)
            psr = pp.next()
            E.do("tensor", lambda e, o=psr[:, 0:cw],
                 x=tile[:, c * MMF:c * MMF + cw]:
                     e.matmul(o, rotm[:, :], x, start=True, stop=True))
            E.do("vector", lambda e, o=rsc[:, 0:cw],
                 x=tile[:, c * MMF:c * MMF + cw],
                 t=cs[:, ccol0 + c * MMF:ccol0 + c * MMF + cw]:
                     e.tensor_mul(o, x, t))
            E.do("vector", lambda e, o=rsc2[:, 0:cw], p=psr[:, 0:cw],
                 t=ss[:, ccol0 + c * MMF:ccol0 + c * MMF + cw]:
                     e.tensor_mul(o, p, t))
            E.do("vector", lambda e, o=tile[:, c * MMF:c * MMF + cw]:
                 e.tensor_add(o, rsc[:, 0:cw], rsc2[:, 0:cw]))

    ind2T = P.sbuf("ind2T", [128, 2])
    pst = pp.next()
    E.do("tensor", lambda e: e.transpose(pst[0:128, 0:2], ind2[0:2, 0:128],
                                         P.eye[0:2, 0:2]))
    E.do("vector", lambda e: e.tensor_copy(ind2T[:, :], pst[0:128, 0:2]))

    for i in range(4):
        prep(kt[i], TK, ks[:, i:i + 1], csk, ssk, 1.0, 0)
    if L1D2_DEBUG:
        E.dma("sync", dbg_k0.ap(), kt[0][:, :])

    for w in range(NW):
        kc0 = w * 512
        for i in range(4):
            E.dma("sync", qw[:, :], q_d[i * 128:(i + 1) * 128,
                                        w * 512:(w + 1) * 512])
            prep(qw, 512, qs[:, i:i + 1], csq, ssq, 8.0, 512 + w * 512)
            if L1D2_DEBUG and w == 1 and i == 0:
                E.dma("sync", dbg_qw.ap(), qw[:, :])
            for hhh in range(2):
                h = 2 * i + hhh
                vbase = (h % 2) * 64
                for kb in range(8):
                    psv = pp.next()
                    E.do("tensor", lambda e, o=psv[0:128, 0:64],
                         src_=vt[i][vbase:vbase + 64,
                                    kc0 + kb * 128:kc0 + (kb + 1) * 128],
                         idn=P.eye[vbase:vbase + 64, vbase:vbase + 64]:
                         e.transpose(o, src_, idn))
                    E.do("vector", lambda e, kb=kb, i2=psv[0:128, 0:64]:
                         e.tensor_copy(vtb[:, kb * 64:(kb + 1) * 64], i2))
                qb = vbase
                for qsub in range(4):
                    b0 = pp.next()
                    b1 = pp.next()
                    for ci, bank in ((0, b0), (1, b1)):
                        E.do("tensor",
                             lambda e, o=bank[0:128, 0:512],
                                    wq=qw[qb:qb + 64,
                                          qsub * 128:(qsub + 1) * 128],
                                    x=kt[i][qb:qb + 64,
                                            kc0 + ci * 512:
                                            kc0 + (ci + 1) * 512]:
                                 e.matmul(o, wq, x, start=True, stop=True))
                    mslot = (4 + qsub) if w == 0 else qsub
                    for ci, bank in ((0, b0), (1, b1)):
                        E.do("vector",
                             lambda e, o=esb[:, ci * 512:(ci + 1) * 512],
                                    p=bank[0:128, 0:512], ci=ci, ms=mslot:
                                 e.tensor_add(
                                     o, p,
                                     mask[:, ms * 1024 + ci * 512:
                                          ms * 1024 + (ci + 1) * 512]))
                    E.do("vector", lambda e: e.tensor_reduce(
                        nmc[:, :], esb[:, :], AX.X, ALU.max, negate=True))
                    E.do("scalar", lambda e: e.activation(
                        esb[:, :], esb[:, :], AF.Exp, bias=nmc[:, 0:1]))
                    E.do("vector", lambda e: e.tensor_reduce(
                        rdc[:, :], esb[:, :], AX.X, ALU.add))
                    E.do("vector", lambda e: e.reciprocal(rdc[:, :],
                                                          rdc[:, :]))
                    E.do("vector", lambda e: e.tensor_scalar(
                        esb[:, :], esb[:, :], rdc[:, 0:1], None, ALU.mult))
                    if L1D2_DEBUG and w == 1 and i == 0 and hhh == 0 \
                            and qsub == 0:
                        E.dma("sync", dbg_e.ap(), esb[:, :])
                        E.dma("sync", dbg_vt.ap(), vtb[:, :])
                    for kb in range(8):
                        pse = pp.next()
                        E.do("tensor", lambda e, o=pse[0:128, 0:128], kb=kb:
                             e.transpose(o, esb[:, kb * 128:(kb + 1) * 128],
                                         P.eye[:, :]))
                        E.do("vector", lambda e, kb=kb,
                             i2=pse[0:128, 0:128]:
                             e.tensor_copy(etb[:, kb * 128:(kb + 1) * 128],
                                           i2))
                    ps_o = pp.next()
                    for kb in range(8):
                        E.do("tensor",
                             lambda e, o=ps_o[0:64, 0:128], kb=kb:
                                 e.matmul(o, vtb[:, kb * 64:(kb + 1) * 64],
                                          etb[:, kb * 128:(kb + 1) * 128],
                                          start=(kb == 0), stop=(kb == 7)))
                    E.do("vector", lambda e, o=ost[h][
                        0:64, qsub * 128:(qsub + 1) * 128],
                        p=ps_o[0:64, 0:128]: e.tensor_copy(o, p))
        for h in range(8):
            E.dma("sync", o_d[h * 64:(h + 1) * 64, w * 512:(w + 1) * 512],
                  ost[h][:, :])
    if standalone:
        P.finish()
    return P


def build_l1d3(P=None):
    T = TCORE
    standalone = P is None
    if standalone:
        P = Prog("l1d3")
    P.begin_phase("l1d3")
    E = P.E
    o_d = P.dram_in("o", [512, T])
    x_d = P.dram_in("x", [1024, T])
    wo_d = P.dram_in("wo", [512, 1024])
    x2_d = P.dram_out("x2", [1024, T])
    P.load_basics()
    pp = P.psum_pool()
    ot = [P.sbuf(f"o{k}", [128, T]) for k in range(4)]
    xbuf = P.sbuf("xbuf", [128, T])
    wpool = P.sbuf_pool(2, [128, MMF], name="wp")
    dma_rows_par(P, ot, o_d, 4)

    def evac(m, c, ps):
        if c == 0:
            E.dma("sync", xbuf[:, 0:T], x_d[m * 128:(m + 1) * 128, :])
        E.do("vector", lambda e,
             o=xbuf[0:ps.shape[0], c * MMF:c * MMF + ps.shape[1]], p=ps:
                 e.tensor_add(o, o, p))
        if c == ceil_div(T, MMF) - 1:
            E.dma("sync", x2_d[m * 128:(m + 1) * 128, :], xbuf[:, 0:T])

    proj_fm(P, pp, wpool, ot, wo_d.ap(), 512, 1024, T, evac)
    if standalone:
        P.finish()
    return P


def build_l1e(P=None):
    T = TCORE
    TS = 1024
    NS = T // TS
    standalone = P is None
    if standalone:
        P = Prog("l1e")
    P.begin_phase("l1e")
    E = P.E
    x2_d = P.dram_in("x2", [1024, T])
    x2h_d = P.dram_in("x2halo", [1024, 1])
    hf_d = P.dram_in("haloflag", [128, 1])
    lngb_d = P.dram_in("lngb", [128, 16])
    w1_d = P.dram_in("w1p", [1024, 5632])
    b1_d = P.dram_in("b1p", [128, 44])
    w2_d = P.dram_in("w2p", [2816, 1024])
    b2_d = P.dram_in("b2c", [128, 8])
    x3_d = P.dram_out("x3", [1024, T])
    P.load_basics()
    pp = P.psum_pool()
    ht = [P.sbuf(f"h{k}", [128, TS]) for k in range(8)]
    htb = [P.sbuf(f"hb{k}", [128, TS], BF16) for k in range(8)]
    at = [P.sbuf(f"a{k}", [128, TS], BF16) for k in range(22)]
    lngb = P.sbuf("lngb_sb", [128, 16])
    b1 = P.sbuf("b1_sb", [128, 44])
    b2 = P.sbuf("b2_sb", [128, 8])
    scr = P.sbuf("scr", [128, MMF])
    scrb = P.sbuf("scrb", [128, MMF], BF16)
    ab = (P.sbuf("abb", [1, TS]), P.sbuf("aba", [1, TS]))
    wpool = P.sbuf_pool(4, [128, MMF], name="wp")
    wbf = P.sbuf_pool(4, [128, MMF], BF16, name="wbf")
    w1ce = P.dram_tmp("w1bf", [1024, 5632], BF16)
    w2ce = P.dram_tmp("w2bf", [2816, 1024], BF16)
    xbuf = P.sbuf("xbuf", [128, TS])
    hfc = P.sbuf("hfc", [128, 1])
    E.dma("sync", hfc[:, :], hf_d.ap())
    E.dma("sync", lngb[:, :], lngb_d.ap())
    E.dma("sync", b1[:, :], b1_d.ap())
    E.dma("sync", b2[:, :], b2_d.ap())

    for sub in range(NS):
        xc0 = sub * TS
        for k in range(8):
            if k < 4:
                E.dma("sync", ht[k][:, 0:TS],
                      x2_d[k * 128:(k + 1) * 128, xc0:xc0 + TS])
            elif sub > 0:
                E.dma("sync", ht[k][:, 0:TS],
                      x2_d[k * 128:(k + 1) * 128, xc0 - 1:xc0 - 1 + TS])
            else:
                E.dma("sync", ht[k][:, 1:TS],
                      x2_d[k * 128:(k + 1) * 128, 0:TS - 1])
                E.dma("sync", ht[k][:, 0:1],
                      x2h_d[k * 128:(k + 1) * 128, :])
                E.do("vector", lambda e, k=k: e.tensor_scalar(
                    ht[k][:, 0:1], ht[k][:, 0:1], hfc[:, 0:1], None,
                    ALU.mult))
        ln_rows(P, pp, ht, TS, 1024, scr, ab)
        apply_ab(P, pp, ht, ht, ab, TS, 1024)
        for k in range(8):
            E.do("vector", lambda e, k=k: e.tensor_scalar(
                htb[k][:, 0:TS], ht[k][:, 0:TS], lngb[:, k:k + 1],
                lngb[:, 8 + k:9 + k], ALU.mult, ALU.add))

        def evac1(m, c, ps):
            cw = ps.shape[1]
            mw = ps.shape[0]
            if m < 22:
                E.do("vector", lambda e, m=m,
                     o=at[m][0:mw, c * MMF:c * MMF + cw], p=ps:
                         e.tensor_scalar(o, p, 1.0, b1[:, m:m + 1],
                                         ALU.mult, ALU.add))
            else:
                mm = m - 22
                E.do("scalar", lambda e, m=m,
                     o=scrb[0:mw, 0:cw], p=ps:
                         e.activation(o, p, AF.Gelu, bias=b1[:, m:m + 1]))
                E.do("vector", lambda e, mm=mm, cw=cw, c=c:
                     e.tensor_mul(at[mm][:, c * MMF:c * MMF + cw],
                                  at[mm][:, c * MMF:c * MMF + cw],
                                  scrb[:, 0:cw]))

        proj_fm_wide(P, pp, wpool, htb, w1_d.ap(), 1024, 5632, TS, evac1,
                     bfpool=wbf,
                     **({"bf_sink": w1ce} if sub == 0 else {"bf_src": w1ce}))

        def evac2(m, c, ps):
            cw = ps.shape[1]
            if c == 0:
                E.dma("sync", xbuf[:, 0:TS],
                      x2_d[m * 128:(m + 1) * 128, sub * TS:(sub + 1) * TS])
            E.do("vector", lambda e, m=m, o=ps, p=ps:
                 e.tensor_scalar(o, p, 1.0, b2[:, m:m + 1], ALU.mult,
                                 ALU.add))
            E.do("vector", lambda e, c=c, cw=cw, p=ps:
                 e.tensor_add(xbuf[:, c * MMF:c * MMF + cw],
                              xbuf[:, c * MMF:c * MMF + cw], p))
            if c == ceil_div(TS, MMF) - 1:
                E.dma("sync",
                      x3_d[m * 128:(m + 1) * 128, sub * TS:(sub + 1) * TS],
                      xbuf[:, 0:TS])

        proj_fm_wide(P, pp, wpool, at, w2_d.ap(), 2816, 1024, TS, evac2,
                     bfpool=wbf,
                     **({"bf_sink": w2ce} if sub == 0 else {"bf_src": w2ce}))
    if standalone:
        P.finish()
    return P


def _l1ffn_tables(core):
    T, TK = TCORE, 512 + TCORE
    start = core * TCORE
    half = np.arange(0, 64, 2, dtype=np.float32)
    inv = 1.0 / 10000.0 ** (half / 64.0)
    base = (half + 0.4 * 64) / (1.4 * 64)
    gtok = np.arange(start - 512, start + T)
    pos = (gtok % 4096).astype(np.float32)
    fr = pos[:, None] * inv[None, :]
    fr = np.concatenate([fr, fr], axis=1)
    power = (pos - 2048.0) / 256.0
    sc = base[None, :] ** power[:, None]
    sc = np.concatenate([sc, sc], axis=1)
    csq = np.tile((np.cos(fr) * sc).T.astype(np.float32), (2, 1)).copy()
    ssq = np.tile((np.sin(fr) * sc).T.astype(np.float32), (2, 1)).copy()
    csk = np.tile((np.cos(fr) / sc).T.astype(np.float32), (2, 1)).copy()
    ssk = np.tile((np.sin(fr) / sc).T.astype(np.float32), (2, 1)).copy()
    mask8 = np.zeros((128, 8 * 1024), np.float32)
    first = (core % 2 == 0)
    for qsub in range(4):
        qr = np.arange(128)[:, None]
        kc = np.arange(1024)[None, :]
        d = 512 + qsub * 128 + qr - kc
        bad = (d < 0) | (d > 512)
        m = np.where(bad, -1e30, 0.0).astype(np.float32)
        mask8[:, qsub * 1024:(qsub + 1) * 1024] = m
        mfw = m.copy()
        if first:
            mfw[:, 0:512] = -1e30
        mask8[:, (4 + qsub) * 1024:(5 + qsub) * 1024] = mfw
    ind2 = np.zeros((2, 128), np.float32)
    ind2[0, 0:64] = 1.0
    ind2[1, 64:128] = 1.0
    r = np.zeros((64, 64), np.float32)
    for d in range(32):
        r[d + 32, d] = -1.0
    for d in range(32, 64):
        r[d - 32, d] = 1.0
    rotm = np.zeros((128, 128), np.float32)
    rotm[0:64, 0:64] = r
    rotm[64:128, 64:128] = r
    return dict(csq=csq, ssq=ssq, csk=csk, ssk=ssk, mask8=mask8, ind2=ind2,
                rotm=rotm)


def host_l1ffn_layer(x_fm, wqkv, q_scale, k_scale, wo, ln_g, ln_b,
                     w1, b1, w2, b2, progs=None):
    base = Prog.basic_inputs()
    base_eye = Prog.basic_inputs(eye=True)
    P1 = build_l1d1()
    ims = []
    T = TCORE
    for c in range(NCORES):
        lo = c * T
        halo = np.zeros((DM, 513), np.float32)
        flag = np.zeros((128, 1), np.float32)
        if c % 2 == 1:
            halo = np.ascontiguousarray(x_fm[:, lo - 513:lo])
            flag[:] = 1.0
        ims.append(_W(P1, dict(x=np.ascontiguousarray(x_fm[:, lo:lo + T]),
                        xhalo=halo, haloflag=flag,
                        wq=np.ascontiguousarray(wqkv[:, 0:512]),
                        wk=np.ascontiguousarray(wqkv[:, 512:1024]),
                        wv=np.ascontiguousarray(wqkv[:, 1024:1536]))) | base)
    r1 = P1.run(ims)

    P2 = build_l1d2()
    qs_col = np.tile(q_scale, 8).reshape(-1, 1).astype(np.float32)
    ks_col = np.tile(k_scale, 8).reshape(-1, 1).astype(np.float32)
    ims = []
    for c in range(NCORES):
        tb = _l1ffn_tables(c)
        ims.append(_W(P2, dict(q=r1.results[c]["l1d1_q"],
                        k=r1.results[c]["l1d1_k"],
                        v=r1.results[c]["l1d1_v"], qs_col=qs_col,
                        ks_col=ks_col, **tb)) | base_eye)
    r2 = P2.run(ims)

    P3 = build_l1d3()
    ims = []
    for c in range(NCORES):
        lo = c * T
        ims.append(_W(P3, dict(o=r2.results[c]["l1d2_o"],
                        x=np.ascontiguousarray(x_fm[:, lo:lo + T]),
                        wo=np.ascontiguousarray(wo))) | base)
    r3 = P3.run(ims)
    x2_fm = np.concatenate([r["l1d3_x2"] for r in r3.results], axis=1)

    P4 = build_l1e()
    HID = 2730
    w1p = np.zeros((1024, 5632), np.float32)
    w1p[:, 0:HID] = w1[:, 0:HID]
    w1p[:, 2816:2816 + HID] = w1[:, HID:2 * HID]
    b1p = np.zeros((128, 44), np.float32)
    b1pad = np.zeros(5632, np.float32)
    b1pad[0:HID] = b1[0:HID]
    b1pad[2816:2816 + HID] = b1[HID:]
    b1p[:, :] = b1pad.reshape(44, 128).T
    w2p = np.zeros((2816, 1024), np.float32)
    w2p[0:HID, :] = w2
    b2c = b2.reshape(8, 128).T.astype(np.float32).copy()
    lngb = np.zeros((128, 16), np.float32)
    lngb[:, 0:8] = ln_g.reshape(8, 128).T
    lngb[:, 8:16] = ln_b.reshape(8, 128).T
    ims = []
    for c in range(NCORES):
        lo = c * T
        halo = np.zeros((1024, 1), np.float32)
        flag = np.zeros((128, 1), np.float32)
        if c % 2 == 1:
            halo = np.ascontiguousarray(x2_fm[:, lo - 1:lo])
            flag[:] = 1.0
        ims.append(_W(P4, dict(x2=np.ascontiguousarray(x2_fm[:, lo:lo + T]),
                        x2halo=halo, haloflag=flag, lngb=lngb, w1p=w1p,
                        b1p=b1p, w2p=w2p, b2c=b2c)) | base)
    r4 = P4.run(ims)
    return np.concatenate([r["l1e_x3"] for r in r4.results], axis=1)


def build_combine(P=None):
    T = TCORE
    standalone = P is None
    if standalone:
        P = Prog("combine")
    P.begin_phase("combine")
    E = P.E
    x_d = P.dram_in("x", [1024, T])
    s_d = P.dram_in("s", [1024, T])
    g_d = P.dram_in("g", [1024, T])
    o_d = P.dram_out("xn_out", [1024, T])
    P.load_basics()
    xb = P.sbuf("xb", [128, T])
    sb = P.sbuf("sb", [128, T])
    gb = P.sbuf("gb", [128, T])
    for k in range(8):
        E.dma("sync", xb[:, :], x_d[k * 128:(k + 1) * 128, :])
        E.dma("sync", sb[:, :], s_d[k * 128:(k + 1) * 128, :])
        E.dma("sync", gb[:, :], g_d[k * 128:(k + 1) * 128, :])
        E.do("scalar", lambda e: e.activation(gb[:, :], gb[:, :], AF.Silu))
        E.do("vector", lambda e: e.tensor_mul(sb[:, :], sb[:, :], gb[:, :]))
        E.do("vector", lambda e: e.tensor_add(xb[:, :], xb[:, :], sb[:, :]))
        E.dma("sync", o_d[k * 128:(k + 1) * 128, :], xb[:, :])
    if standalone:
        P.finish()
    return P


def host_combine(x_fm, s_fm, g_fm):
    P = build_combine()
    base = Prog.basic_inputs()
    ims = []
    for c in range(NCORES):
        tok = slice(c * TCORE, (c + 1) * TCORE)
        ims.append(_W(P, dict(x=np.ascontiguousarray(x_fm[:, tok]),
                        s=np.ascontiguousarray(s_fm[:, tok]),
                        g=np.ascontiguousarray(g_fm[:, tok]))) | base)
    res = P.run(ims)
    return np.concatenate([r["combine_xn_out"] for r in res.results], axis=1)



def build_l2a(P=None):
    T = TCORE
    standalone = P is None
    if standalone:
        P = Prog("l2a")
    P.begin_phase("l2a")
    E = P.E
    xn_d = P.dram_in("x_raw", [1024, T])
    xnout_d = P.dram_out("xn", [1024, T])
    w_d = P.dram_in("w_in", [1024, 3328])
    rw_d = P.dram_in("rms_w", [1024, 1])
    gb_d = P.dram_in("gate_b", [1024, 1])
    cmask_d = P.dram_in("cmask", [128, T])
    z_d = P.dram_out("z", [1024, T])
    q_d = P.dram_out("q", [128, T])
    ix_d = P.dram_out("ix", [1024, T])
    ik_d = P.dram_out("ik", [1024, T])
    cs_d = P.dram_out("cs", [2048, T // 64])
    P.load_basics()
    pp = P.psum_pool()
    xt = [P.sbuf(f"x{k}", [128, T]) for k in range(8)]
    gt = [P.sbuf(f"g{k}", [128, T]) for k in range(8)]
    scr = P.sbuf("scr", [128, MMF])
    ab = (P.sbuf("abb", [1, T]), P.sbuf("aba", [1, T]))
    rw = P.sbuf("rw", [128, 8])
    gbc = P.sbuf("gbc", [128, 8])
    cmask = P.sbuf("cmask_sb", [128, T])
    wpool = P.sbuf_pool(2, [128, MMF], name="wp")
    stage = P.sbuf("stage", [128, T])
    ksb = P.sbuf("ksb", [128, T])
    dec = P.sbuf("dec", [128, T])
    vv = P.sbuf("vv", [128, T])
    sout = P.sbuf("sout", [128, T])
    cstage = P.sbuf("cstage", [128, 16 * (T // 64)])
    NCh = T // 64

    dma_rows_par(P, xt, xn_d, 8)
    for k in range(8):
        E.dma("sync", rw[:, k:k + 1], rw_d[k * 128:(k + 1) * 128, :])
        E.dma("sync", gbc[:, k:k + 1], gb_d[k * 128:(k + 1) * 128, :])
    E.dma("sync", cmask[:, :], cmask_d.ap())
    ln_rows(P, pp, xt, T, 1024, scr, ab)
    apply_ab(P, pp, xt, xt, ab, T, 1024)
    for k in range(8):
        E.dma("sync", xnout_d[k * 128:(k + 1) * 128, :], xt[k][:, :])
    ln_rows(P, pp, xt, T, 1024, scr, ab, rms=True, eps_name="eps1e6")
    apply_ab(P, pp, xt, xt, ab, T, 1024, rms=True)
    for k in range(8):
        E.do("vector", lambda e, k=k: e.tensor_scalar(
            xt[k][:, :], xt[k][:, :], rw[:, k:k + 1], None, ALU.mult))

    def scan_tile(i, val_tile):
        gg = gt[i % 8]
        E.do("vector", lambda e: e.tensor_mul(dec[:, :], gg[:, :],
                                              cmask[:, :]))
        E.do("vector", lambda e: e.tensor_scalar(scr[:, 0:1], scr[:, 0:1],
                                                 1.0, None, ALU.mult))
        E.do("vector", lambda e, g2=gg: e.tensor_scalar(
            vv[:, :], g2[:, :], -1.0, 1.0, ALU.mult, ALU.add))
        E.do("vector", lambda e, v2=val_tile: e.tensor_mul(
            vv[:, :], vv[:, :], v2[:, :]))
        E.do("vector", lambda e: e.tensor_tensor_scan(
            sout[:, :], dec[:, :], vv[:, :], 0.0, ALU.mult, ALU.add))
        dst = ix_d if i < 8 else ik_d
        E.dma("sync", dst[(i % 8) * 128:(i % 8 + 1) * 128, :], sout[:, :])
        ci = i % 8
        E.do("vector", lambda e, i=i: e.tensor_copy(
            cstage[:, (i % 8) * NCh:(i % 8 + 1) * NCh],
            bass.AP(sout, 0, [[T, 128], [64, NCh]])))
        if i % 8 == 7:
            half = 0 if i < 8 else 1
            for kk in range(8):
                E.dma("sync",
                      cs_d[half * 1024 + kk * 128:half * 1024 + (kk + 1) * 128,
                           :],
                      cstage[:, kk * NCh:(kk + 1) * NCh])

    def evac(m, c, ps):
        cw = ps.shape[1]
        CT = ceil_div(T, MMF)
        if m < 8:
            E.do("scalar", lambda e, o=stage[:, c * MMF:c * MMF + cw], p=ps:
                 e.activation(o, p, AF.Sigmoid))
            if c == CT - 1:
                E.dma("sync", z_d[m * 128:(m + 1) * 128, :], stage[:, 0:T])
        elif m < 16:
            E.do("scalar", lambda e, m=m, o=gt[m - 8][:, c * MMF:c * MMF + cw],
                 p=ps: e.activation(o, p, AF.Sigmoid, bias=gbc[:, m - 8:m - 7]))
        elif m < 24:
            E.do("vector", lambda e, o=stage[:, c * MMF:c * MMF + cw], p=ps:
                 e.tensor_copy(o, p))
            if c == CT - 1:
                scan_tile(m - 16, stage)
        elif m == 24:
            E.do("vector", lambda e, o=stage[:, c * MMF:c * MMF + cw], p=ps:
                 e.tensor_copy(o, p))
            if c == CT - 1:
                E.dma("sync", q_d.ap(), stage[:, 0:T])
        else:
            E.do("vector", lambda e, o=ksb[:, c * MMF:c * MMF + cw], p=ps:
                 e.tensor_copy(o, p))
            if c == CT - 1:
                for i in range(8, 16):
                    scan_tile(i, ksb)

    proj_fm(P, pp, wpool, xt, w_d.ap(), 1024, 3328, T, evac)
    if standalone:
        P.finish()
    return P


L2B_DEBUG = False


def build_l2b(P=None):
    T = TCORE
    NCh = 64
    standalone = P is None
    if standalone:
        P = Prog("l2b")
    P.begin_phase("l2b")
    E = P.E
    q_d = P.dram_in("q", [128, T])
    ix_d = P.dram_in("ix", [1024, T])
    ik_d = P.dram_in("ik", [1024, T])
    cs_d = P.dram_in("cs_b", [4096, NCh // 2])
    z_d = P.dram_in("z", [1024, T])
    xn_d = P.dram_in("xn", [1024, T])
    w_d = P.dram_in("w_out", [1024, 1024])
    mask_d = P.dram_in("maskc", [128, 16 * NCh])
    s_d = P.dram_out("s", [1024, T])
    if L2B_DEBUG:
        dbg_w = P.dram_out("dbg_w", [128, 64])
        dbg_ic = P.dram_out("dbg_ic", [128, 4])
        dbg_ot = P.dram_out("dbg_ot", [1024, T])
        dbg_prod = P.dram_out("dbg_prod", [128, 256])
        dbg_ihb = P.dram_out("dbg_ihb", [128, 256])
        dbg_qsb = P.dram_out("dbg_qsb", [128, 256])
    P.load_basics(eye=True)
    pp = P.psum_pool()
    qsb = P.sbuf("qsb", [128, T])
    ihb = P.sbuf("ihb", [128, T])
    ixb = P.sbuf("ixb", [128, T])
    csx = P.sbuf("csx", [128, 8 * NCh])
    csk = P.sbuf("csk2", [128, 8 * NCh])
    cxT = P.sbuf("cxT", [128, NCh])
    mask = P.sbuf("maskc_sb", [128, 16 * NCh])
    prod = P.sbuf("prod", [128, T])
    wsb = P.sbuf("wsb", [128, NCh])
    wtr = P.sbuf("wtr", [128, 128])
    mcol = P.sbuf("mcol", [128, 1])
    icol2dbg = P.sbuf("icol2dbg", [128, 4])
    lcol = P.sbuf("lcol", [128, 1])
    icol = P.sbuf("icol", [128, 1])
    rcol = P.sbuf("rcol", [128, 1])
    irow = P.sbuf("irow", [1, 128])
    ot = [P.sbuf(f"ot{k}", [128, T]) for k in range(8)]
    buf = P.sbuf("buf2", [128, T])
    wpool = P.sbuf_pool(2, [128, MMF], name="wp")

    E.dma("sync", qsb[:, :], q_d.ap())
    E.dma("sync", mask[:, :], mask_d.ap())
    for k in range(8):
        E.dma("sync", csx[:, k * NCh:k * NCh + 32],
              cs_d[k * 128:(k + 1) * 128, :])
        E.dma("sync", csx[:, k * NCh + 32:(k + 1) * NCh],
              cs_d[2048 + k * 128:2048 + (k + 1) * 128, :])
        E.dma("sync", csk[:, k * NCh:k * NCh + 32],
              cs_d[1024 + k * 128:1024 + (k + 1) * 128, :])
        E.dma("sync", csk[:, k * NCh + 32:(k + 1) * NCh],
              cs_d[3072 + k * 128:3072 + (k + 1) * 128, :])

    for h in range(16):
        hb = (h % 2) * 64
        E.dma("sync", ihb[hb:hb + 64, :],
              ik_d[h * 64:(h + 1) * 64, :])
        E.dma("sync", ixb[hb:hb + 64, :],
              ix_d[h * 64:(h + 1) * 64, :])
        pst = pp.next()
        E.do("tensor", lambda e, o=pst[0:64, 0:64],
             src_=csx[hb:hb + 64, (h // 2) * NCh:(h // 2) * NCh + NCh],
             idn=P.eye[hb:hb + 64, hb:hb + 64]:
             e.transpose(o, src_, idn))
        E.do("vector", lambda e, p=pst[0:64, 0:64]:
             e.tensor_copy(cxT[0:64, 0:NCh], p))
        E.do("vector", lambda e, o=prod[hb:hb + 64, :],
             a=qsb[hb:hb + 64, :], b=ihb[hb:hb + 64, :]:
             e.tensor_mul(o, a, b))
        for sub in range(16):
            t0 = sub * 128
            psl = pp.next()
            E.do("tensor", lambda e, o=psl[0:128, 0:1],
                 w=prod[hb:hb + 64, t0:t0 + 128], m1=P.ones[hb:hb + 64, 0:1]:
                 e.matmul(o, w, m1, start=True, stop=True))
            E.do("vector", lambda e, p=psl[0:128, 0:1]:
                 e.tensor_scalar(lcol[:, :], p, 0.125, None, ALU.mult))
            pss = pp.next()
            E.do("tensor", lambda e, o=pss[0:128, 0:NCh],
                 w=qsb[hb:hb + 64, t0:t0 + 128],
                 x=csk[hb:hb + 64, (h // 2) * NCh:(h // 2) * NCh + NCh]:
                 e.matmul(o, w, x, start=True, stop=True))
            E.do("vector", lambda e, p=pss[0:128, 0:NCh], s2=sub:
                 e.scalar_tensor_tensor(wsb[:, 0:NCh], p, 0.125,
                                        mask[:, s2 * NCh:(s2 + 1) * NCh],
                                        ALU.mult, ALU.add))
            E.do("vector", lambda e: e.tensor_reduce(
                mcol[:, :], wsb[:, 0:NCh], AX.X, ALU.max))
            E.do("vector", lambda e: e.tensor_max(mcol[:, :], mcol[:, :],
                                                  lcol[:, :]))
            E.do("vector", lambda e: e.tensor_scalar(
                mcol[:, :], mcol[:, :], -1.0, None, ALU.mult))
            E.do("scalar", lambda e: e.activation(
                wsb[:, 0:NCh], wsb[:, 0:NCh], AF.Exp, bias=mcol[:, 0:1]))
            E.do("vector", lambda e: e.tensor_add(icol[:, :], lcol[:, :],
                                                  mcol[:, :]))
            E.do("scalar", lambda e: e.activation(icol[:, :], icol[:, :],
                                                  AF.Exp))
            E.do("vector", lambda e: e.tensor_reduce(
                rcol[:, :], wsb[:, 0:NCh], AX.X, ALU.add))
            E.do("vector", lambda e: e.tensor_add(rcol[:, :], rcol[:, :],
                                                  icol[:, :]))
            E.do("vector", lambda e: e.reciprocal(rcol[:, :], rcol[:, :]))
            E.do("vector", lambda e: e.tensor_scalar(
                wsb[:, 0:NCh], wsb[:, 0:NCh], rcol[:, 0:1], None, ALU.mult))
            E.do("vector", lambda e: e.tensor_mul(icol[:, :], icol[:, :],
                                                  rcol[:, :]))
            psw = pp.next()
            E.do("tensor", lambda e, o=psw[0:NCh, 0:128]:
                 e.transpose(o, wsb[:, 0:NCh], P.eye[:, :]))
            E.do("vector", lambda e, p=psw[0:NCh, 0:128]:
                 e.tensor_copy(wtr[0:NCh, :], p))
            psr = pp.next()
            E.do("tensor", lambda e, o=psr[0:1, 0:128]:
                 e.transpose(o, icol[:, 0:1], P.eye[:, :]))
            E.do("vector", lambda e, p=psr[0:1, 0:128]:
                 e.tensor_copy(irow[0:1, :], p))
            psb = pp.next()
            E.do("tensor", lambda e, o=psb[0:64, 0:128]:
                 e.matmul(o, P.ones_row[0:1, 0:64], irow[0:1, :],
                          start=True, stop=True))
            pso = pp.next()
            E.do("tensor", lambda e, o=pso[0:64, 0:128]:
                 e.matmul(o, cxT[0:64, 0:NCh], wtr[0:NCh, :],
                          start=True, stop=True))
            E.do("vector", lambda e, o=buf[0:64, 0:128],
                 a=ixb[hb:hb + 64, t0:t0 + 128], p=psb[0:64, 0:128]:
                 e.tensor_mul(o, a, p))
            E.do("vector", lambda e, o=ot[h // 2][hb:hb + 64, t0:t0 + 128],
                 a=buf[0:64, 0:128], p=pso[0:64, 0:128]:
                 e.tensor_add(o, a, p))
            if L2B_DEBUG and h == 0 and sub == 0:
                E.dma("sync", dbg_prod.ap(), prod[:, 0:256])
                E.dma("sync", dbg_ihb.ap(), ihb[:, 0:256])
                E.dma("sync", dbg_qsb.ap(), qsb[:, 0:256])
                E.dma("sync", dbg_w.ap(), wsb[:, 0:64])
                E.do("vector", lambda e: e.tensor_copy(icol2dbg[:, 0:1],
                                                       icol[:, :]))
                E.do("vector", lambda e: e.tensor_copy(icol2dbg[:, 1:2],
                                                       lcol[:, :]))
                E.do("vector", lambda e: e.tensor_copy(icol2dbg[:, 2:3],
                                                       mcol[:, :]))
                E.do("vector", lambda e: e.tensor_copy(icol2dbg[:, 3:4],
                                                       rcol[:, :]))
                E.dma("sync", dbg_ic.ap(), icol2dbg[:, :])
    if L2B_DEBUG:
        for k in range(8):
            E.dma("sync", dbg_ot[k * 128:(k + 1) * 128, :], ot[k][:, :])
    for k in range(8):
        E.dma("sync", buf[:, :], z_d[k * 128:(k + 1) * 128, :])
        E.do("vector", lambda e, k=k: e.tensor_mul(ot[k][:, :], ot[k][:, :],
                                                   buf[:, :]))

    def evac(m, c, ps):
        cw = ps.shape[1]
        if c == 0:
            E.dma("sync", buf[:, 0:T], xn_d[m * 128:(m + 1) * 128, :])
        E.do("vector", lambda e, o=buf[0:128, c * MMF:c * MMF + cw], p=ps:
             e.tensor_add(o, o, p))
        if c == ceil_div(T, MMF) - 1:
            E.dma("sync", s_d[m * 128:(m + 1) * 128, :], buf[:, 0:T])

    proj_fm(P, pp, wpool, ot, w_d.ap(), 1024, 1024, T, evac)
    if standalone:
        P.finish()
    return P


def host_l2(x_fm, rms_weight, W_in, gate_bias, W_out):
    P1 = build_l2a()
    base = Prog.basic_inputs()
    base_eye = Prog.basic_inputs(eye=True)
    Wp = np.concatenate([W_in[:, 0:1024], W_in[:, 2048:3072],
                         W_in[:, 1024:2048], W_in[:, 3072:3328]],
                        axis=1).astype(np.float32).copy()
    cmask = np.ones((128, TCORE), np.float32)
    cmask[:, ::64] = 0.0
    ims = []
    for c in range(NCORES):
        tok = slice(c * TCORE, (c + 1) * TCORE)
        ims.append(_W(P1, dict(x_raw=np.ascontiguousarray(x_fm[:, tok]),
                        w_in=Wp,
                        rms_w=rms_weight.reshape(-1, 1).astype(np.float32),
                        gate_b=gate_bias.reshape(-1, 1).astype(np.float32),
                        cmask=cmask)) | base)
    r1 = P1.run(ims)
    xn_fm = np.concatenate([r["l2a_xn"] for r in r1.results], axis=1)
    cs_all = np.concatenate([r["l2a_cs"] for r in r1.results], axis=1)
    ix_fm = np.concatenate([r["l2a_ix"] for r in r1.results], axis=1)
    ik_fm = np.concatenate([r["l2a_ik"] for r in r1.results], axis=1)
    z_fm = np.concatenate([r["l2a_z"] for r in r1.results], axis=1)
    q_fm = np.concatenate([r["l2a_q"] for r in r1.results], axis=1)

    P2 = build_l2b()
    NCh = 64
    ims = []
    for c in range(NCORES):
        beta = c // 2
        tok = slice(c * TCORE, (c + 1) * TCORE)
        csb = np.ascontiguousarray(np.concatenate(
            [cs_all[:, beta * NCh:beta * NCh + 32],
             cs_all[:, beta * NCh + 32:(beta + 1) * NCh]], axis=0))
        maskc = np.zeros((128, 16 * NCh), np.float32)
        for sub in range(16):
            tglob = (c % 2) * 2048 + sub * 128 + np.arange(128)
            allowed = (tglob[:, None] // 64) > np.arange(NCh)[None, :]
            maskc[:, sub * NCh:(sub + 1) * NCh] = np.where(allowed, 0.0,
                                                           -1e30)
        ims.append(_W(P2, dict(q=np.ascontiguousarray(q_fm[:, tok]),
                        ix=np.ascontiguousarray(ix_fm[:, tok]),
                        ik=np.ascontiguousarray(ik_fm[:, tok]),
                        cs_b=csb,
                        z=np.ascontiguousarray(z_fm[:, tok]),
                        xn=np.ascontiguousarray(xn_fm[:, tok]),
                        w_out=np.ascontiguousarray(W_out),
                        maskc=maskc)) | base_eye)
    r2 = P2.run(ims)
    globals()["_last_r2"] = r2
    return np.concatenate([r["l2b_s"] for r in r2.results], axis=1), xn_fm



def build_l2c(P=None):
    T = TCORE
    TS = 512
    NS = T // TS
    standalone = P is None
    if standalone:
        P = Prog("l2c")
    P.begin_phase("l2c")
    E = P.E
    xn_d = P.dram_in("xn", [1024, T])
    w1_d = P.dram_in("w1t", [1024, 4096])
    w2_d = P.dram_in("w2t", [4096, 1024])
    b1_d = P.dram_in("b1c", [128, 32])
    b2_d = P.dram_in("b2c", [128, 8])
    g_d = P.dram_out("g", [1024, T])
    P.load_basics(eye=True)
    pp = P.psum_pool()
    xt = [P.sbuf(f"x{k}", [128, T]) for k in range(8)]
    ht = [P.sbuf(f"h{k}", [128, TS], BF16) for k in range(32)]
    wpool = P.sbuf_pool(4, [128, MMF], name="wp")
    wsg = P.sbuf("wsg", [128, MMF])
    wsgb = P.sbuf_pool(4, [128, 128], BF16, name="wsgb")
    b1 = P.sbuf("b1c_sb", [128, 32])
    b2 = P.sbuf("b2c_sb", [128, 8])
    a1 = P.sbuf("a1_sb", [128, 32])
    a2 = P.sbuf("a2_sb", [128, 8])
    arow = P.sbuf("arow2", [1, 4096])
    gstage = P.sbuf("gstage", [128, TS])
    dma_rows_par(P, xt, xn_d, 8)
    E.dma("sync", b1[:, :], b1_d.ap())
    E.dma("sync", b2[:, :], b2_d.ap())

    def alphas(w_dram, din, dout, dst, ncols):
        for mg in range(ceil_div(dout, MMF)):
            gw = min(MMF, dout - mg * MMF)
            ps = pp.next()
            for k in range(din // 128):
                wtile = wpool.next()
                E.dma("sync", wtile[:, 0:gw],
                      w_dram[k * 128:(k + 1) * 128, mg * MMF:mg * MMF + gw])
                E.do("scalar", lambda e, o=wsg[:, 0:gw], x=wtile[:, 0:gw]:
                     e.activation(o, x, AF.Abs))
                E.do("tensor", lambda e, o=ps[0:1, 0:gw], x=wsg[:, 0:gw],
                     st=(k == 0), sp=(k == din // 128 - 1):
                     e.matmul(o, P.ones[:, 0:1], x, start=st, stop=sp))
            E.do("vector", lambda e, o=arow[0:1, mg * MMF:mg * MMF + gw],
                 p=ps[0:1, 0:gw], d=din:
                 e.tensor_scalar(o, p, 1.0 / d, None, ALU.mult))
        for m in range(ncols):
            ps = pp.next()
            E.do("tensor", lambda e, o=ps[0:128, 0:1], m=m:
                 e.transpose(o, arow[0:1, m * 128:(m + 1) * 128],
                             P.eye[0:1, 0:1]))
            E.do("vector", lambda e, o=dst[:, m:m + 1], p=ps[0:128, 0:1]:
                 e.tensor_copy(o, p))

    alphas(w1_d.ap(), 1024, 4096, a1, 32)
    alphas(w2_d.ap(), 4096, 1024, a2, 8)

    def proj_sign(xtiles, w_dram, din, dout, TS2, evac, xcol0,
                  sink=None, src=None):
        KT = din // 128
        NB = min(4, len(wpool.tiles), len(wsgb.tiles), KT)
        for m in range(ceil_div(dout, 128)):
            banks = [pp.next() for _ in range(ceil_div(TS2, MMF))]
            for kb0 in range(0, KT, NB):
                kbatch = list(range(kb0, min(kb0 + NB, KT)))
                wbs = {}
                if src is not None:
                    jobs = []
                    for j, k in enumerate(kbatch):
                        wb = wsgb.next()
                        wbs[k] = wb
                        jobs.append((DMA_ENGS[j % 3], wb[:, 0:128],
                                     src[k * 128:(k + 1) * 128,
                                         m * 128:(m + 1) * 128]))
                    E.dma_par(jobs)
                else:
                    jobs = []
                    stages = {}
                    for j, k in enumerate(kbatch):
                        wtile = wpool.next()
                        stages[k] = wtile
                        jobs.append((DMA_ENGS[j % 3], wtile[:, 0:128],
                                     w_dram[k * 128:(k + 1) * 128,
                                            m * 128:(m + 1) * 128]))
                    E.dma_par(jobs)
                    for k in kbatch:
                        wb = wsgb.next()
                        wbs[k] = wb
                        E.do("scalar", lambda e, o=wb[:, 0:128],
                             x=stages[k][:, 0:128]: e.activation(o, x,
                                                                 AF.Sign))
                    if sink is not None:
                        E.dma_par([
                            (DMA_ENGS[j % 3],
                             sink[k * 128:(k + 1) * 128,
                                  m * 128:(m + 1) * 128],
                             wbs[k][:, 0:128])
                            for j, k in enumerate(kbatch)])
                for k in kbatch:
                    for c in range(ceil_div(TS2, MMF)):
                        cw = min(MMF, TS2 - c * MMF)
                        E.do("tensor",
                             lambda e, o=banks[c][0:128, 0:cw],
                                    w=wbs[k][:, 0:128],
                                    x=xtiles[k][0:128, xcol0 + c * MMF:
                                                xcol0 + c * MMF + cw],
                                    st=(k == 0), sp=(k == KT - 1):
                                 e.matmul(o, w, x, start=st, stop=sp))
            for c in range(ceil_div(TS2, MMF)):
                cw = min(MMF, TS2 - c * MMF)
                evac(m, c, banks[c][0:128, 0:cw])

    xtb = cast_tiles(P, "xtb", xt, T)
    s1c = P.dram_tmp("s1bf", [1024, 4096], BF16)
    s2c = P.dram_tmp("s2bf", [4096, 1024], BF16)

    for sub in range(NS):
        xc0 = sub * TS

        def evac1(m, c, ps):
            E.do("scalar", lambda e, m=m, o=ht[m][:, 0:TS], p=ps:
                 e.activation(o, p, AF.Gelu, bias=b1[:, m:m + 1],
                              scale=a1[:, m:m + 1]))

        proj_sign(xtb, w1_d.ap(), 1024, 4096, TS, evac1, xc0,
                  **({"sink": s1c} if sub == 0 else {"src": s1c}))

        def evac2(m, c, ps):
            E.do("vector", lambda e, m=m, o=gstage[:, 0:TS], p=ps:
                 e.tensor_scalar(o, p, a2[:, m:m + 1], b2[:, m:m + 1],
                                 ALU.mult, ALU.add))
            E.dma("sync", g_d[m * 128:(m + 1) * 128, xc0:xc0 + TS],
                  gstage[:, 0:TS])

        proj_sign(ht, w2_d.ap(), 4096, 1024, TS, evac2, 0,
                  **({"sink": s2c} if sub == 0 else {"src": s2c}))
    if standalone:
        P.finish()
    return P


def host_l2c(xn_fm, w1, b1, w2, b2):
    P = build_l2c()
    base = Prog.basic_inputs(eye=True)
    w1t = np.ascontiguousarray(w1.T)
    w2t = np.ascontiguousarray(w2.T)
    b1c = b1.reshape(32, 128).T.astype(np.float32).copy()
    b2c = b2.reshape(8, 128).T.astype(np.float32).copy()
    ims = []
    for c in range(NCORES):
        tok = slice(c * TCORE, (c + 1) * TCORE)
        ims.append(_W(P, dict(xn=np.ascontiguousarray(xn_fm[:, tok]),
                        w1t=w1t, w2t=w2t, b1c=b1c, b2c=b2c)) | base)
    res = P.run(ims)
    return np.concatenate([r["l2c_g"] for r in res.results], axis=1)



def build_l3a(P=None):
    T = TCORE
    standalone = P is None
    if standalone:
        P = Prog("l3a")
    P.begin_phase("l3a")
    E = P.E
    x_d = P.dram_in("x", [1024, T])
    wq_d = P.dram_in("wq", [1024, 1024])
    wk_d = P.dram_in("wk", [1024, 1024])
    wv_d = P.dram_in("wv", [1024, 1024])
    wb_d = P.dram_in("wb", [1024, 16])
    ind2_d = P.dram_in("ind2", [2, 128])
    ind2T_d = P.dram_in("ind2T_in", [128, 2])
    xn_d = P.dram_out("xn", [1024, T])
    q_d = P.dram_out("qn", [1024, T])
    k_d = P.dram_out("kn", [1024, T])
    v_d = P.dram_out("v", [1024, T])
    be_d = P.dram_out("beta", [16, T])
    P.load_basics()
    pp = P.psum_pool()
    xt = [P.sbuf(f"x{k}", [128, T]) for k in range(8)]
    scr = P.sbuf("scr", [128, MMF])
    ab = (P.sbuf("abb", [1, T]), P.sbuf("aba", [1, T]))
    wpool = P.sbuf_pool(4, [128, MMF], name="wp")
    stage = P.sbuf("stage", [128, T])
    rn2 = P.sbuf("rn2", [2, T])
    ind2 = P.sbuf("ind2_sb", [2, 128])
    ind2T = P.sbuf("ind2T_sb", [128, 2])
    bst = P.sbuf("bst", [16, T])
    dma_rows_par(P, xt, x_d, 8)
    E.dma("sync", ind2[:, :], ind2_d.ap())
    E.dma("sync", ind2T[:, :], ind2T_d.ap())
    ln_rows(P, pp, xt, T, 1024, scr, ab)
    apply_ab(P, pp, xt, xt, ab, T, 1024)
    for k in range(8):
        E.dma("sync", xn_d[k * 128:(k + 1) * 128, :], xt[k][:, :])

    def norm_evac(out_d):
        def evac(m, c, ps):
            E.do("vector", lambda e, o=stage[:, c * MMF:c * MMF + ps.shape[1]],
                 p=ps: e.tensor_copy(o, p))
            if c == ceil_div(T, MMF) - 1:
                CT = ceil_div(T, MMF)
                for c2 in range(CT):
                    cw = min(MMF, T - c2 * MMF)
                    psA = pp.next()
                    E.do("scalar", lambda e, o=scr[:, 0:cw],
                         x=stage[:, c2 * MMF:c2 * MMF + cw]:
                             e.activation(o, x, AF.Square))
                    E.do("tensor", lambda e, o=psA[0:2, 0:cw], x=scr[:, 0:cw]:
                         e.matmul(o, ind2T[:, 0:2], x, start=True, stop=True))
                    E.do("vector", lambda e, o=rn2[0:2, c2 * MMF:c2 * MMF + cw],
                         p=psA[0:2, 0:cw]: e.tensor_copy(o, p))
                E.do("scalar", lambda e: e.activation(
                    rn2[0:2, 0:T], rn2[0:2, 0:T], AF.Ln,
                    bias=P.ccol("eps1e12")[0:2, :]))
                E.do("scalar", lambda e: e.activation(
                    rn2[0:2, 0:T], rn2[0:2, 0:T], AF.Exp, scale=0.5))
                E.do("vector", lambda e: e.tensor_scalar_max(
                    rn2[0:2, 0:T], rn2[0:2, 0:T], 1e-12))
                E.do("vector", lambda e: e.reciprocal(rn2[0:2, 0:T],
                                                      rn2[0:2, 0:T]))
                for c2 in range(CT):
                    cw = min(MMF, T - c2 * MMF)
                    psb = pp.next()
                    E.do("tensor", lambda e, o=psb[:, 0:cw],
                         r=rn2[0:2, c2 * MMF:c2 * MMF + cw]:
                             e.matmul(o, ind2[:, :], r, start=True, stop=True))
                    E.do("vector", lambda e, o=stage[:, c2 * MMF:c2 * MMF + cw],
                         p=psb[:, 0:cw]: e.tensor_mul(o, o, p))
                mm = evac.m_holder[0]
                E.dma("sync", out_d[mm * 128:(mm + 1) * 128, :],
                      stage[:, 0:T])
        evac.m_holder = [0]
        def wrapper(m, c, ps):
            evac.m_holder[0] = m
            evac(m, c, ps)
        return wrapper

    xtb = cast_tiles(P, "xtb", xt, T)
    wbf = P.sbuf_pool(4, [128, 128], BF16, name="wbf")
    proj_fm(P, pp, wpool, xtb, wq_d.ap(), 1024, 1024, T, norm_evac(q_d),
            bfpool=wbf)
    proj_fm(P, pp, wpool, xtb, wk_d.ap(), 1024, 1024, T, norm_evac(k_d),
            bfpool=wbf)

    def evac_v(m, c, ps):
        E.do("vector", lambda e, o=stage[:, c * MMF:c * MMF + ps.shape[1]],
             p=ps: e.tensor_copy(o, p))
        if c == ceil_div(T, MMF) - 1:
            E.dma("sync", v_d[m * 128:(m + 1) * 128, :], stage[:, 0:T])

    proj_fm(P, pp, wpool, xtb, wv_d.ap(), 1024, 1024, T, evac_v, bfpool=wbf)

    def evac_b(m, c, ps):
        E.do("scalar", lambda e, o=bst[0:16, c * MMF:c * MMF + ps.shape[1]],
             p=ps[0:16, 0:ps.shape[1]]: e.activation(o, p, AF.Sigmoid))
        if c == ceil_div(T, MMF) - 1:
            E.dma("sync", be_d.ap(), bst[:, 0:T])

    proj_fm(P, pp, wpool, xtb, wb_d.ap(), 1024, 16, T, evac_b, bfpool=wbf)
    if standalone:
        P.finish()
    return P


def build_l3b(P=None):
    L = 4096
    NCK = 8
    standalone = P is None
    if standalone:
        P = Prog("l3b")
    P.begin_phase("l3b")
    E = P.E
    q_d = P.dram_in("qn", [1024, TCORE])
    k_d = P.dram_in("kn", [1024, TCORE])
    v_d = P.dram_in("v", [1024, TCORE])
    be_d = P.dram_in("beta", [16, TCORE])
    ssel_d = P.dram_in("self_sel", [128, 64 * 128])
    csel_d = P.dram_in("colsel", [128, 64 * 128])
    psel_d = P.dram_in("pairsel", [8, 4 * 128])
    y_d = P.dram_out("y", [1024, TCORE])
    P.load_basics()
    pp = P.psum_pool(n=6)
    ypsum = P.psum("ypsum_ded", [128, MMF])
    kt = P.sbuf("kt3", [128, L])
    qt = P.sbuf("qt3", [128, L])
    vt3 = P.sbuf("vt3", [128, L])
    ktb = P.sbuf("ktb3", [128, L], BF16)
    vtb3 = P.sbuf("vtb3", [128, L], BF16)
    ytb = P.sbuf("ytb3", [128, MMF], BF16)
    bk = P.sbuf("bk3", [128, L])
    be = P.sbuf("be3", [8, L])
    ssel = P.sbuf("ssel_sb", [128, 64 * 128])
    sselb = P.sbuf("sselb_sb", [128, 64 * 128], BF16)
    cselb = P.sbuf("cselb_sb", [128, 64 * 128], BF16)
    psel = P.sbuf("psel_sb", [8, 4 * 128])
    scol = P.sbuf("scol3", [128, 64])
    t1 = P.sbuf("t1_3", [128, MMF])
    aa = P.sbuf("aa3", [128, MMF])
    uu = P.sbuf("uu3", [128, MMF])
    ss = P.sbuf("ss3", [128, MMF])
    yt = P.sbuf("yt3", [128, MMF])
    ystage = P.sbuf("ystage3", [128, MMF])
    E.dma("sync", be[0:8, 0:TCORE], be_d[0:8, :])
    E.dma("sync", be[0:8, TCORE:L], be_d[8:16, :])
    E.dma("sync", ssel[:, :], ssel_d.ap())
    E.do("scalar", lambda e: e.activation(sselb[:, :], ssel[:, :], AF.Copy))
    E.dma("sync", ssel[:, :], csel_d.ap())
    E.do("scalar", lambda e: e.activation(cselb[:, :], ssel[:, :], AF.Copy))
    E.dma("sync", psel[:, :], psel_d.ap())

    for pg in range(4):
        E.dma_par([
            ("sync", kt[:, 0:TCORE], k_d[pg * 128:(pg + 1) * 128, :]),
            ("gpsimd", qt[:, 0:TCORE], q_d[pg * 128:(pg + 1) * 128, :]),
            ("scalar", vt3[:, 0:TCORE], v_d[pg * 128:(pg + 1) * 128, :])])
        E.dma_par([
            ("sync", kt[:, TCORE:L],
             k_d[512 + pg * 128:512 + (pg + 1) * 128, :]),
            ("gpsimd", qt[:, TCORE:L],
             q_d[512 + pg * 128:512 + (pg + 1) * 128, :]),
            ("scalar", vt3[:, TCORE:L],
             v_d[512 + pg * 128:512 + (pg + 1) * 128, :])])
        E.do("scalar", lambda e: e.activation(ktb[:, :], kt[:, :], AF.Copy))
        E.do("scalar", lambda e: e.activation(vtb3[:, :], vt3[:, :], AF.Copy))
        for c in range(NCK):
            psb = pp.next()
            E.do("tensor", lambda e, o=psb[:, 0:MMF], c=c, pg=pg:
                 e.matmul(o, psel[:, pg * 128:(pg + 1) * 128],
                          be[0:8, c * MMF:(c + 1) * MMF],
                          start=True, stop=True))
            E.do("vector", lambda e, c=c, p=psb[:, 0:MMF]:
                 e.tensor_mul(bk[:, c * MMF:(c + 1) * MMF],
                              kt[:, c * MMF:(c + 1) * MMF], p))
        E.do("vector", lambda e: e.memset(scol[:, :], 0.0))
        for c in range(NCK):
            t0 = c * MMF
            for i in range(64):
                pk = pp.next()
                E.do("tensor", lambda e, o=pk[:, 0:MMF], i=i, t0=t0:
                     e.matmul(o, sselb[:, i * 128:(i + 1) * 128],
                              ktb[:, t0:t0 + MMF], start=True, stop=True))
                pv = pp.next()
                E.do("tensor", lambda e, o=pv[:, 0:MMF], i=i, t0=t0:
                     e.matmul(o, sselb[:, i * 128:(i + 1) * 128],
                              vtb3[:, t0:t0 + MMF], start=True, stop=True))
                E.do("vector", lambda e, p=pk[:, 0:MMF], t0=t0:
                     e.tensor_mul(t1[:, 0:MMF], bk[:, t0:t0 + MMF], p))
                E.do("vector", lambda e:
                     e.tensor_scalar(aa[:, 0:MMF], t1[:, 0:MMF], -1.0, 1.0,
                                     ALU.mult, ALU.add))
                E.do("vector", lambda e, p=pv[:, 0:MMF], t0=t0:
                     e.tensor_mul(uu[:, 0:MMF], bk[:, t0:t0 + MMF], p))
                E.do("vector", lambda e, i=i: e.tensor_tensor_scan(
                    ss[:, 0:MMF], aa[:, 0:MMF], uu[:, 0:MMF],
                    scol[:, i:i + 1], ALU.mult, ALU.add))
                E.do("vector", lambda e, i=i: e.tensor_copy(
                    scol[:, i:i + 1], ss[:, MMF - 1:MMF]))
                E.do("vector", lambda e, t0=t0: e.tensor_mul(
                    ytb[:, 0:MMF], ss[:, 0:MMF], qt[:, t0:t0 + MMF]))
                E.do("tensor", lambda e, o=ypsum[:, 0:MMF], i=i:
                     e.matmul(o, cselb[:, i * 128:(i + 1) * 128],
                              ytb[:, 0:MMF], start=(i == 0), stop=(i == 63)))
            E.do("vector", lambda e, p=ypsum[:, 0:MMF]:
                 e.tensor_copy(ystage[:, 0:MMF], p))
            shard = 0 if c < 4 else 512
            E.dma("sync",
                  y_d[shard + pg * 128:shard + (pg + 1) * 128,
                      (c % 4) * MMF:(c % 4) * MMF + MMF],
                  ystage[:, 0:MMF])
    if standalone:
        P.finish()
    return P


def host_l3b(q_fm, k_fm, v_fm, beta_fm):
    P = build_l3b()
    base = Prog.basic_inputs()
    ssel = np.zeros((128, 64 * 128), np.float32)
    csel = np.zeros((128, 64 * 128), np.float32)
    for i in range(64):
        for p2 in range(2):
            ssel[p2 * 64 + i, i * 128 + p2 * 64:i * 128 + (p2 + 1) * 64] = 1.0
            csel[p2 * 64:(p2 + 1) * 64, i * 128 + p2 * 64 + i] = 1.0
    psel = np.zeros((8, 4 * 128), np.float32)
    for pg in range(4):
        psel[2 * pg, pg * 128:pg * 128 + 64] = 1.0
        psel[2 * pg + 1, pg * 128 + 64:(pg + 1) * 128] = 1.0
    ims = []
    for c in range(NCORES):
        b = c // 2
        hh = (c % 2) * 8
        rows = slice(hh * 64, (hh + 8) * 64)
        t1 = slice(b * 4096, b * 4096 + TCORE)
        t2 = slice(b * 4096 + TCORE, (b + 1) * 4096)
        stk = lambda a: np.concatenate([a[rows, t1], a[rows, t2]], axis=0)
        ims.append(_W(P, dict(qn=np.ascontiguousarray(stk(q_fm)),
                        kn=np.ascontiguousarray(stk(k_fm)),
                        v=np.ascontiguousarray(stk(v_fm)),
                        beta=np.ascontiguousarray(np.concatenate(
                            [beta_fm[hh:hh + 8, t1],
                             beta_fm[hh:hh + 8, t2]], axis=0)),
                        self_sel=ssel, colsel=csel, pairsel=psel)) | base)
    res = P.run(ims)
    y = np.zeros((1024, NTOK), np.float32)
    for c in range(NCORES):
        b = c // 2
        hh = (c % 2) * 8
        yc = res.results[c]["l3b_y"]
        y[hh * 64:(hh + 8) * 64, b * 4096:b * 4096 + TCORE] = yc[0:512, :]
        y[hh * 64:(hh + 8) * 64, b * 4096 + TCORE:(b + 1) * 4096] = \
            yc[512:1024, :]
    return y


def build_l3c(P=None):
    T = TCORE
    standalone = P is None
    if standalone:
        P = Prog("l3c")
    P.begin_phase("l3c")
    E = P.E
    o_d = P.dram_in("o_in", [1024, T])
    xn_d = P.dram_in("xn", [1024, T])
    lngb_d = P.dram_in("lngb", [128, 16])
    wg_d = P.dram_in("wg", [1024, 1024])
    wo_d = P.dram_in("wout", [1024, 1024])
    s_d = P.dram_out("s", [1024, T])
    P.load_basics()
    pp = P.psum_pool()
    ot = [P.sbuf(f"o{k}", [128, T]) for k in range(8)]
    xnt = [P.sbuf(f"xn{k}", [128, T]) for k in range(8)]
    scr = P.sbuf("scr", [128, MMF])
    ab = (P.sbuf("abb", [1, T]), P.sbuf("aba", [1, T]))
    lngb = P.sbuf("lngb_sb", [128, 16])
    wpool = P.sbuf_pool(4, [128, MMF], name="wp")
    stage = P.sbuf("stage", [128, T])
    dma_rows_par(P, ot, o_d, 8)
    dma_rows_par(P, xnt, xn_d, 8)
    E.dma("sync", lngb[:, :], lngb_d.ap())
    ln_rows(P, pp, ot, T, 1024, scr, ab)
    apply_ab(P, pp, ot, ot, ab, T, 1024)
    for k in range(8):
        E.do("vector", lambda e, k=k: e.tensor_scalar(
            ot[k][:, :], ot[k][:, :], lngb[:, k:k + 1], lngb[:, 8 + k:9 + k],
            ALU.mult, ALU.add))

    def evac_g(m, c, ps):
        E.do("scalar", lambda e, o=stage[:, c * MMF:c * MMF + ps.shape[1]],
             p=ps: e.activation(o, p, AF.Silu))
        if c == ceil_div(T, MMF) - 1:
            E.do("vector", lambda e, m=m: e.tensor_mul(
                ot[m][:, :], ot[m][:, :], stage[:, 0:T]))

    xb = cast_tiles(P, "xb", xnt, T)
    wbf = P.sbuf_pool(4, [128, 128], BF16, name="wbf")
    proj_fm(P, pp, wpool, xb, wg_d.ap(), 1024, 1024, T, evac_g, bfpool=wbf)

    def evac_s(m, c, ps):
        E.do("vector", lambda e, o=stage[:, c * MMF:c * MMF + ps.shape[1]],
             p=ps: e.tensor_copy(o, p))
        if c == ceil_div(T, MMF) - 1:
            E.dma("sync", s_d[m * 128:(m + 1) * 128, :], stage[:, 0:T])

    cast_into(P, xb, ot, T)
    proj_fm(P, pp, wpool, xb, wo_d.ap(), 1024, 1024, T, evac_s, bfpool=wbf)
    if standalone:
        P.finish()
    return P


L3D_DEBUG = False


def build_l3d(P=None):
    T = TCORE
    TS = 512
    NS = T // TS
    standalone = P is None
    if standalone:
        P = Prog("l3d")
    P.begin_phase("l3d")
    E = P.E
    xn_d = P.dram_in("xn", [1024, T])
    lt_d = P.dram_in("lt", [1024, 16])
    ltT_d = P.dram_tmp("ltT_t", [16, 1024])
    w1_d = P.dram_in("w1t", [1024, 4096])
    b1_d = P.dram_in("b1c", [128, 32])
    w2_d = P.dram_in("w2t", [4096, 1024])
    b2_d = P.dram_in("b2c", [128, 8])
    sc_d = P.dram_in("scale11", [1, 1])
    sel4_d = P.dram_in("sel4", [4, 4 * 128])
    g_d = P.dram_out("g", [1024, T])
    if L3D_DEBUG:
        dbg_h0 = P.dram_out("dbg_h0", [128, 512])
        dbg_h31 = P.dram_out("dbg_h31", [128, 512])
        dbg_p = P.dram_out("dbg_p", [4, T])
    P.load_basics()
    pp = P.psum_pool(n=6)
    rbded = P.psum("rbded", [128, MMF])
    pbded = P.psum("pbded", [128, MMF])
    xt = [P.sbuf(f"x{k}", [128, T]) for k in range(8)]
    ht = [P.sbuf(f"h{k}", [128, TS], BF16) for k in range(32)]
    hscr = P.sbuf("hscr", [128, TS])
    wbf = P.sbuf_pool(4, [128, MMF], BF16, name="wbf")
    w1c3 = P.dram_tmp("w1bf", [1024, 4096], BF16)
    w2c3 = P.dram_tmp("w2bf", [4096, 1024], BF16)
    lts = [P.sbuf(f"lts{k}", [128, 16]) for k in range(8)]
    ltTs = P.sbuf("ltTs", [16, 1024])
    b1 = P.sbuf("b1d_sb", [128, 32])
    b2 = P.sbuf("b2d_sb", [128, 8])
    sel4 = P.sbuf("sel4_sb", [4, 4 * 128])
    scrow = P.sbuf("scrow", [1, 1])
    sccol = P.sbuf("sccol", [128, 1])
    colset = P.sbuf("colset", [128, 32])
    pstage = P.sbuf("pstage", [4, T])
    rt = [P.sbuf(f"rt{k}", [4, T]) for k in range(4)]
    tsb = P.sbuf("tsb", [128, MMF])
    t2sb = P.sbuf("t2sb", [128, MMF])
    wpool = P.sbuf_pool(4, [128, MMF], name="wp")
    P.load_basics(eye=True)
    dma_rows_par(P, xt, xn_d, 8)
    for k in range(8):
        E.dma("sync", lts[k][:, :], lt_d[k * 128:(k + 1) * 128, :])
    for k in range(8):
        pst = pp.next()
        E.do("tensor", lambda e, k=k, o=pst[0:16, 0:128]:
             e.transpose(o, lts[k][:, :], P.eye[:, :]))
        E.do("vector", lambda e, k=k, p=pst[0:16, 0:128]:
             e.tensor_copy(ltTs[0:16, k * 128:(k + 1) * 128], p))
    E.dma("sync", ltT_d.ap(), ltTs[:, :])
    E.dma("sync", b1[:, :], b1_d.ap())
    E.dma("sync", b2[:, :], b2_d.ap())
    E.dma("sync", sel4[:, :], sel4_d.ap())
    E.dma("sync", scrow[:, :], sc_d.ap())
    for m1 in range(2):
        E.dma("sync",
              bass.AP(colset, m1 * 4, [[32, 128], [8, 4], [1, 4]]),
              bass.AP(ltT_d, 8 * 1024 + m1 * 512,
                      [[4, 128], [2048, 4], [1, 4]]))
    psx = pp.next()
    E.do("tensor", lambda e: e.matmul(psx[:, 0:1], P.ones_row[0:1, :],
                                      scrow[0:1, 0:1], start=True, stop=True))
    E.do("vector", lambda e: e.tensor_copy(sccol[:, :], psx[:, 0:1]))
    for k in range(8):
        ev = bass.AP(lts[k], 0, [[16, 128], [2, 8]])
        E.do("vector", lambda e, ev=ev: e.tensor_scalar(
            ev, ev, sccol[:, 0:1], None, ALU.mult))
    CT = ceil_div(T, MMF)
    for c in range(CT):
        cw = min(MMF, T - c * MMF)
        psp = pp.next()
        for k in range(8):
            E.do("tensor", lambda e, o=psp[0:4, 0:cw],
                 w=bass.AP(lts[k], 1, [[16, 128], [2, 4]]),
                 x=xt[k][:, c * MMF:c * MMF + cw],
                 st=(k == 0): e.matmul(o, w, x, start=st, stop=(k == 7)))
        E.do("vector", lambda e, c=c, cw=cw, p=psp[0:4, 0:cw]:
             e.tensor_copy(pstage[0:4, c * MMF:c * MMF + cw], p))

    xtb = cast_tiles(P, "xtb", xt, T)

    for sub in range(NS):
        xc0 = sub * TS

        def evac1(m, c, ps):
            cw = ps.shape[1]
            pb = pbded
            E.do("tensor", lambda e, o=pb[:, 0:cw],
                 w=sel4[:, (m // 8) * 128:(m // 8 + 1) * 128],
                 x=pstage[0:4, xc0:xc0 + cw]:
                 e.matmul(o, w, x, start=True, stop=True))
            E.do("vector", lambda e, p=pb[:, 0:cw],
                 s=lts[m % 8][:, 2 * (m // 8):2 * (m // 8) + 1]:
                 e.tensor_scalar(tsb[:, 0:cw], p, s, None, ALU.mult))
            E.do("vector", lambda e, p=ps: e.tensor_add(t2sb[0:128, 0:cw],
                                                        tsb[0:128, 0:cw], p))
            E.do("scalar", lambda e, m=m, o=ht[m][:, 0:cw]:
                 e.activation(o, t2sb[0:128, 0:cw], AF.Gelu,
                              bias=b1[:, m:m + 1]))

        proj_fm_wide(P, pp, wpool, xtb, w1_d.ap(), 1024, 4096, TS, evac1,
                     xcol0=xc0, bfpool=wbf,
                     **({"bf_sink": w1c3} if sub == 0 else {"bf_src": w1c3}))
        if L3D_DEBUG and sub == 0:
            E.dma("sync", dbg_h0.ap(), ht[0][:, :])
            E.dma("sync", dbg_h31.ap(), ht[31][:, :])
            E.dma("sync", dbg_p.ap(), pstage[:, :])
        for hb in range(4):
            psr = pp.next()
            for kk in range(8):
                E.do("scalar", lambda e, hb=hb, kk=kk, o=hscr[:, 0:TS]:
                     e.activation(o, ht[hb * 8 + kk][:, 0:TS], AF.Copy))
                E.do("tensor", lambda e, o=psr[0:4, 0:TS],
                     w=bass.AP(lts[kk], 9, [[16, 128], [2, 4]]),
                     x=hscr[:, 0:TS],
                     st=(kk == 0):
                     e.matmul(o, w, x, start=st, stop=(kk == 7)))
            E.do("vector", lambda e, o=rt[hb][0:4, xc0:xc0 + TS],
                 p=psr[0:4, 0:TS]:
                 e.tensor_scalar(o, p, sccol[0:4, 0:1], None, ALU.mult))

        def evac2(m, c, ps):
            cw = ps.shape[1]
            E.do("vector", lambda e, m=m, p=ps:
                 e.tensor_scalar(tsb[:, 0:cw], p, 1.0, b2[:, m:m + 1],
                                 ALU.mult, ALU.add))
            for hb in range(4):
                rb = rbded
                E.do("tensor", lambda e, o=rb[:, 0:cw],
                     w=sel4[:, (m // 2) * 128:(m // 2 + 1) * 128],
                     x=rt[hb][0:4, xc0:xc0 + cw]:
                     e.matmul(o, w, x, start=True, stop=True))
                E.do("vector", lambda e, m=m, hb=hb, p=rb[:, 0:cw]:
                     e.scalar_tensor_tensor(tsb[:, 0:cw], p,
                                            colset[:, m * 4 + hb:m * 4 + hb + 1],
                                            tsb[:, 0:cw], ALU.mult, ALU.add))
            E.dma("sync", g_d[m * 128:(m + 1) * 128, xc0:xc0 + cw],
                  tsb[:, 0:cw])

        proj_fm_wide(P, pp, wpool, ht, w2_d.ap(), 4096, 1024, TS, evac2,
                     bfpool=wbf,
                     **({"bf_sink": w2c3} if sub == 0 else {"bf_src": w2c3}))
    if standalone:
        P.finish()
    return P



def host_l3a(x_fm, Wq, Wk, Wv, Wbeta):
    P = build_l3a()
    base = Prog.basic_inputs()
    ind2 = np.zeros((2, 128), np.float32)
    ind2[0, 0:64] = 1.0
    ind2[1, 64:128] = 1.0
    ims = []
    for c in range(NCORES):
        tok = slice(c * TCORE, (c + 1) * TCORE)
        ims.append(_W(P, dict(x=np.ascontiguousarray(x_fm[:, tok]),
                        wq=np.ascontiguousarray(Wq),
                        wk=np.ascontiguousarray(Wk),
                        wv=np.ascontiguousarray(Wv),
                        wb=np.ascontiguousarray(Wbeta),
                        ind2=ind2,
                        ind2T_in=np.ascontiguousarray(ind2.T))) | base)
    res = P.run(ims)
    out = {}
    for nm in ("xn", "qn", "kn", "v", "beta"):
        out[nm] = np.concatenate([r["l3a_" + nm] for r in res.results],
                                 axis=1)
    return out


def host_l3c(y_fm, xn_fm, ln_g, ln_b, Wg, Wout):
    P = build_l3c()
    base = Prog.basic_inputs()
    lngb = np.zeros((128, 16), np.float32)
    lngb[:, 0:8] = ln_g.reshape(8, 128).T
    lngb[:, 8:16] = ln_b.reshape(8, 128).T
    ims = []
    for c in range(NCORES):
        tok = slice(c * TCORE, (c + 1) * TCORE)
        ims.append(_W(P, dict(o_in=np.ascontiguousarray(y_fm[:, tok]),
                        xn=np.ascontiguousarray(xn_fm[:, tok]),
                        lngb=lngb, wg=np.ascontiguousarray(Wg),
                        wout=np.ascontiguousarray(Wout))) | base)
    res = P.run(ims)
    return np.concatenate([r["l3c_s"] for r in res.results], axis=1)


def host_l3d(xn_fm, W1, b1, W2, b2, scale):
    P = build_l3d()
    base = Prog.basic_inputs(eye=True)
    sel4 = np.zeros((4, 4 * 128), np.float32)
    for k in range(4):
        sel4[k, k * 128:(k + 1) * 128] = 1.0
    w1t = np.ascontiguousarray(W1.T)
    w2t = np.ascontiguousarray(W2.T)
    b1c = b1.reshape(32, 128).T.astype(np.float32).copy()
    b2c = b2.reshape(8, 128).T.astype(np.float32).copy()
    ims = []
    for c in range(NCORES):
        b = c // 2
        tok = slice(c * TCORE, (c + 1) * TCORE)
        lt = np.ascontiguousarray(xn_fm[:, b * 4096 + 1:b * 4096 + 17])
        ims.append(_W(P, dict(xn=np.ascontiguousarray(xn_fm[:, tok]),
                        lt=lt,
                        w1t=w1t, b1c=b1c, w2t=w2t, b2c=b2c,
                        scale11=np.asarray(scale, np.float32).reshape(1, 1),
                        sel4=sel4)) | base)
    res = P.run(ims)
    return np.concatenate([r["l3d_g"] for r in res.results], axis=1)



def kernel_multilaunch(**inputs):
    g = lambda n: np.ascontiguousarray(np.asarray(inputs[n], np.float32))
    x = g("L0_attn_x")
    x_fm = np.ascontiguousarray(x.reshape(NTOK, DM).T)

    o = host_l0a(x_fm, g("L0_attn_Win"), g("L0_attn_conv_w"),
                 g("L0_attn_conv_b"), g("L0_attn_dt_bias"))
    y = host_l0b(o["xssm"], o["dtp"], o["bmat"], o["cmat"],
                 g("L0_attn_A_log"), g("L0_attn_D"))
    ffn0 = {k[7:]: g(k) for k in inputs if k.startswith("L0_ffn_")}
    x_fm = host_l0c(y, o["z"], o["xn"], x_fm, g("L0_attn_norm_w"),
                    g("L0_attn_Wout"), ffn0)

    o1 = host_l1a(x_fm, g("L1_attn_in_proj_w"), g("L1_attn_conv_w"),
                  g("L1_attn_x_proj_w"))
    y1 = host_l1b(o1["dtb"], g("L1_attn_dt_w"), g("L1_attn_dt_b"),
                  g("L1_attn_A_param"))
    s1 = host_l1c(y1, o1["u"], o1["z"], g("L1_attn_D_param"),
                  g("L1_attn_out_w"), g("L1_attn_out_b"))
    xf = o1["xn"]
    for li in range(2):
        xf = host_l1ffn_layer(xf, g("L1_ffn_wqkv")[li],
                              g("L1_ffn_q_scale")[li], g("L1_ffn_k_scale")[li],
                              g("L1_ffn_wo")[li], g("L1_ffn_ln_g")[li],
                              g("L1_ffn_ln_b")[li], g("L1_ffn_w1")[li],
                              g("L1_ffn_b1")[li], g("L1_ffn_w2")[li],
                              g("L1_ffn_b2")[li])
    x_fm = host_combine(x_fm, s1, xf)

    s2, xn2 = host_l2(x_fm, g("L2_attn_rms_weight"), g("L2_attn_W_in"),
                      g("L2_attn_gate_bias"), g("L2_attn_W_out"))
    g2 = host_l2c(xn2, g("L2_ffn_w1"), g("L2_ffn_b1"), g("L2_ffn_w2"),
                  g("L2_ffn_b2"))
    x_fm = host_combine(x_fm, s2, g2)

    o3 = host_l3a(x_fm, g("L3_attn_Wq"), g("L3_attn_Wk"), g("L3_attn_Wv"),
                  g("L3_attn_Wbeta"))
    y3 = host_l3b(o3["qn"], o3["kn"], o3["v"], o3["beta"])
    s3 = host_l3c(y3, o3["xn"], g("L3_attn_ln_g"), g("L3_attn_ln_b"),
                  g("L3_attn_Wg"), g("L3_attn_Wout"))
    g3 = host_l3d(o3["xn"], g("L3_ffn_W1"), g("L3_ffn_b1"), g("L3_ffn_W2"),
                  g("L3_ffn_b2"), g("L3_ffn_scale"))
    x_fm = host_combine(x_fm, s3, g3)

    return np.ascontiguousarray(x_fm.T).reshape(4, 4096, DM)



PAIRS = [[0, 1], [2, 3], [4, 5], [6, 7]]
ALL8 = [[0, 1, 2, 3, 4, 5, 6, 7]]


def _cc(P, kind, contrib, out_shape, name, groups):
    from concourse.aot_env import allow_ici_all_to_all
    out = P.nc.dram_tensor(name, list(out_shape), F32)

    def run(e, contrib=contrib, out=out, kind=kind, groups=groups):
        if kind == "AllToAll":
            with allow_ici_all_to_all():
                return e.collective_compute(
                    kind, ALU.bypass, replica_groups=groups,
                    ins=[contrib.ap().opt()], outs=[out.ap().opt()])
        return e.collective_compute(
            kind, ALU.bypass, replica_groups=groups,
            ins=[contrib.ap().opt()], outs=[out.ap().opt()])

    P.E.do("gpsimd", run, dma=False)
    return out


def glue_blend(P, tag, jobs):
    E = P.E
    P.begin_phase(tag)
    P.load_basics()
    if getattr(P, "bflag", None) is None:
        bd = P.nc.dram_tensor("bflag", [128, 2], F32, kind="ExternalInput")
        P.bflag = P.stack.enter_context(
            P.nc.sbuf_tensor("bflagsb", [128, 2], F32))
        E.dma("sync", P.bflag[:, :], bd.ap())
    a = P.sbuf("ba", [128, 2048])
    b = P.sbuf("bb", [128, 2048])
    for (src_h, a0, b0, dst_h, d0, R, C) in jobs:
        assert C <= 2048
        for r0 in range(0, R, 128):
            rw = min(128, R - r0)
            E.dma_par([
                ("sync", a[0:rw, 0:C], src_h[a0 + r0:a0 + r0 + rw, 0:C]),
                ("gpsimd", b[0:rw, 0:C], src_h[b0 + r0:b0 + r0 + rw, 0:C])])
            E.do("vector", lambda e, rw=rw, C=C: e.tensor_scalar(
                b[0:rw, 0:C], b[0:rw, 0:C], P.bflag[0:rw, 1:2], None,
                ALU.mult))
            E.do("vector", lambda e, rw=rw, C=C: e.scalar_tensor_tensor(
                a[0:rw, 0:C], a[0:rw, 0:C], P.bflag[0:rw, 0:1], b[0:rw, 0:C],
                ALU.mult, ALU.add))
            E.dma("sync", dst_h[d0 + r0:d0 + r0 + rw, 0:C], a[0:rw, 0:C])


def glue_transpose(P, src_h, sr0, sc0, dst_h, dr0, dc0, R, C, tag):
    E = P.E
    P.begin_phase(tag)
    P.load_basics(eye=True)
    av = [P.sbuf(f"ta{j}", [128, 128]) for j in range(3)]
    bv = [P.sbuf(f"tb{j}", [128, 128]) for j in range(3)]
    psv = [P.psum(f"tp{j}", [128, 128]) for j in range(3)]
    cells = [(r0, c0) for r0 in range(0, R, 128) for c0 in range(0, C, 128)]
    for g0 in range(0, len(cells), 3):
        grp = cells[g0:g0 + 3]
        E.dma_par([
            (DMA_ENGS[j], av[j][0:min(128, R - r0), 0:min(128, C - c0)],
             src_h[sr0 + r0:sr0 + r0 + min(128, R - r0),
                   sc0 + c0:sc0 + c0 + min(128, C - c0)])
            for j, (r0, c0) in enumerate(grp)])
        for j, (r0, c0) in enumerate(grp):
            rw = min(128, R - r0)
            cw = min(128, C - c0)
            E.do("tensor", lambda e, j=j, o=psv[j][0:128, 0:128], rw=rw,
                 cw=cw: e.transpose(o[0:cw, 0:rw],
                                    av[j][0:rw, 0:cw], P.eye[0:rw, 0:rw]))
            E.do("vector", lambda e, j=j, rw=rw, cw=cw,
                 p=psv[j][0:128, 0:128]:
                 e.tensor_copy(bv[j][0:cw, 0:rw], p[0:cw, 0:rw]))
        E.dma_par([
            (DMA_ENGS[j],
             dst_h[dr0 + c0:dr0 + c0 + min(128, C - c0),
                   dc0 + r0:dc0 + r0 + min(128, R - r0)],
             bv[j][0:min(128, C - c0), 0:min(128, R - r0)])
            for j, (r0, c0) in enumerate(grp)])


def build_fused():
    P = Prog("fz")
    P.fused = True
    E = P.E
    nc = P.nc
    T = TCORE

    def itn(name, shape):
        return nc.dram_tensor("i_" + name, list(shape), F32)

    xn0 = itn("xn0", [1024, T]); z0 = itn("z0", [2048, T])
    xssm = itn("xssm", [2048, T]); bm = itn("bm", [128, T])
    cm = itn("cm", [128, T]); dtp0 = itn("dtp0", [32, T])
    P.bind = dict(xn=xn0, z=z0, xssm=xssm, bmat=bm, cmat=cm, dtp=dtp0)
    build_l0a(P=P)
    xssm_x = _cc(P, "AllGather", xssm, [4096, T], "cc_xssm", PAIRS)
    dtp_x = _cc(P, "AllGather", dtp0, [64, T], "cc_dtp", PAIRS)
    bm_x = _cc(P, "AllGather", bm, [256, T], "cc_bm", PAIRS)
    cm_x = _cc(P, "AllGather", cm, [256, T], "cc_cm", PAIRS)
    xssmL = itn("xssmL", [1024, 4096]); dtpL = itn("dtpL", [16, 4096])
    bmL = itn("bmL", [128, 4096]); cmL = itn("cmL", [128, 4096])
    xssmT2 = itn("xssmT2", [2048, T])
    glue_blend(P, "bl0", [
        (xssm_x, 0, 1024, xssmT2, 0, 1024, T),
        (xssm_x, 2048, 3072, xssmT2, 1024, 1024, T),
    ])
    E.dma("sync", xssmL[:, 0:T], xssmT2[0:1024, :])
    E.dma("sync", xssmL[:, T:4096], xssmT2[1024:2048, :])
    dtpT2 = itn("dtpT2", [32, T])
    glue_blend(P, "bl0d", [
        (dtp_x, 0, 16, dtpT2, 0, 16, T),
        (dtp_x, 32, 48, dtpT2, 16, 16, T),
    ])
    E.dma("sync", dtpL[:, 0:T], dtpT2[0:16, :])
    E.dma("sync", dtpL[:, T:4096], dtpT2[16:32, :])
    E.dma("sync", bmL[:, 0:T], bm_x[0:128, :])
    E.dma("sync", bmL[:, T:4096], bm_x[128:256, :])
    E.dma("sync", cmL[:, 0:T], cm_x[0:128, :])
    E.dma("sync", cmL[:, T:4096], cm_x[128:256, :])
    dtpTL = itn("dtpTL", [4096, 16])
    glue_transpose(P, dtpL, 0, 0, dtpTL, 0, 0, 16, 4096, "gt0")
    y0 = itn("y0", [4096, 1024])
    P.bind = dict(xssm=xssmL, dtp=dtpL, dtpT=dtpTL, bmat=bmL, cmat=cmL,
                  y_tm=y0)
    build_l0b(P=P)
    y0x = _cc(P, "AllGather", y0, [8192, 1024], "cc_y0", PAIRS)
    ysel = itn("ysel", [4096, 1024])
    glue_blend(P, "bly0", [
        (y0x, 0, 2048, ysel, 0, 2048, 1024),
        (y0x, 4096, 6144, ysel, 2048, 2048, 1024),
    ])
    yfm = itn("yfm", [2048, T])
    glue_transpose(P, ysel, 0, 0, yfm, 0, 0, 2048, 1024, "gt1")
    glue_transpose(P, ysel, 2048, 0, yfm, 1024, 0, 2048, 1024, "gt2")
    s0 = itn("s0", [1024, T])
    P.bind = dict(y=yfm, z=z0, s=s0)
    build_l0c1(P=P)
    x1 = itn("x1", [1024, T])
    P.bind = dict(xn=xn0, s=s0, x1=x1)
    build_l0c2(P=P)

    tail3 = itn("tail3", [1024, 3])
    E.dma("sync", tail3.ap(), x1[:, T - 3:T])
    h3 = _cc(P, "AllGather", tail3, [2048, 3], "cc_h3", PAIRS)
    halo3 = itn("halo3", [1024, 3])
    E.dma("sync", halo3.ap(), h3[0:1024, :])
    xn1 = itn("xn1", [1024, T]); u1 = itn("u1", [2048, T])
    z1 = itn("z1", [2048, T]); dtb1 = itn("dtb1", [17, T])
    P.bind = dict(x=x1, xhalo=halo3, xn=xn1, u=u1, z=z1, dtb=dtb1)
    build_l1a(P=P)
    dtbag = _cc(P, "AllGather", dtb1, [136, T], "cc_dtb", ALL8)
    y1s = itn("y1s", [2048, T])
    P.bind = dict(dtbag=dtbag, y=y1s)
    build_l1b(P=P)
    y1 = _cc(P, "AllToAll", y1s, [2048, T], "cc_y1", ALL8)
    s1 = itn("s1", [1024, T])
    P.bind = dict(y=y1, u=u1, z=z1, s=s1)
    build_l1c(P=P)
    xf = xn1
    for li in range(2):
        t513 = itn(f"t513_{li}", [1024, 513])
        E.dma("sync", t513.ap(), xf[:, T - 513:T])
        h513 = _cc(P, "AllGather", t513, [2048, 513], f"cc_h513_{li}", PAIRS)
        halo513 = itn(f"halo513_{li}", [1024, 513])
        E.dma("sync", halo513.ap(), h513[0:1024, :])
        qq = itn(f"q_{li}", [512, T]); kk = itn(f"k_{li}", [512, 512 + T])
        vv = itn(f"v_{li}", [512, 512 + T])
        P.next_tag = f"l1d1_{li}"
        P.bind = dict(x=xf, xhalo=halo513, q=qq, k=kk, v=vv)
        build_l1d1(P=P)
        oo = itn(f"o_{li}", [512, T])
        P.next_tag = f"l1d2_{li}"
        P.bind = dict(q=qq, k=kk, v=vv, o=oo)
        build_l1d2(P=P)
        xf2 = itn(f"xf2_{li}", [1024, T])
        P.next_tag = f"l1d3_{li}"
        P.bind = dict(o=oo, x=xf, x2=xf2)
        build_l1d3(P=P)
        t1t = itn(f"t1_{li}", [1024, 1])
        E.dma("sync", t1t.ap(), xf2[:, T - 1:T], allow_slow=True)
        h1t = _cc(P, "AllGather", t1t, [2048, 1], f"cc_h1_{li}", PAIRS)
        halo1 = itn(f"halo1_{li}", [1024, 1])
        E.dma("sync", halo1.ap(), h1t[0:1024, :])
        xf3 = itn(f"xf3_{li}", [1024, T])
        P.next_tag = f"l1e_{li}"
        P.bind = dict(x2=xf2, x2halo=halo1, x3=xf3)
        build_l1e(P=P)
        xf = xf3
    x2g = itn("x2g", [1024, T])
    P.next_tag = "combine_1"
    P.bind = dict(x=x1, s=s1, g=xf, xn_out=x2g)
    build_combine(P=P)

    xn2 = itn("xn2", [1024, T]); z2 = itn("z2", [1024, T])
    q2 = itn("q2", [128, T]); ix2 = itn("ix2", [1024, T])
    ik2 = itn("ik2", [1024, T]); cs2 = itn("cs2", [2048, T // 64])
    P.bind = dict(x_raw=x2g, xn=xn2, z=z2, q=q2, ix=ix2, ik=ik2, cs=cs2)
    build_l2a(P=P)
    csag = _cc(P, "AllGather", cs2, [4096, T // 64], "cc_cs", PAIRS)
    s2 = itn("s2", [1024, T])
    P.bind = dict(q=q2, ix=ix2, ik=ik2, cs_b=csag, z=z2, xn=xn2, s=s2)
    build_l2b(P=P)
    g2 = itn("g2", [1024, T])
    P.bind = dict(xn=xn2, g=g2)
    build_l2c(P=P)
    x3g = itn("x3g", [1024, T])
    P.next_tag = "combine_2"
    P.bind = dict(x=x2g, s=s2, g=g2, xn_out=x3g)
    build_combine(P=P)

    xn3 = itn("xn3", [1024, T]); qn3 = itn("qn3", [1024, T])
    kn3 = itn("kn3", [1024, T]); v3 = itn("v3", [1024, T])
    be3 = itn("be3", [16, T])
    P.bind = dict(x=x3g, xn=xn3, qn=qn3, kn=kn3, v=v3, beta=be3)
    build_l3a(P=P)
    qag = _cc(P, "AllGather", qn3, [2048, T], "cc_q3", PAIRS)
    kag = _cc(P, "AllGather", kn3, [2048, T], "cc_k3", PAIRS)
    vag = _cc(P, "AllGather", v3, [2048, T], "cc_v3", PAIRS)
    beag = _cc(P, "AllGather", be3, [32, T], "cc_b3", PAIRS)
    qx = itn("qx", [1024, T]); kx = itn("kx", [1024, T])
    vx = itn("vx", [1024, T]); bex = itn("bex", [16, T])
    jobs3 = []
    for src, dst in ((qag, qx), (kag, kx), (vag, vx)):
        jobs3.append((src, 0, 512, dst, 0, 512, T))
        jobs3.append((src, 1024, 1536, dst, 512, 512, T))
    jobs3.append((beag, 0, 8, bex, 0, 8, T))
    jobs3.append((beag, 16, 24, bex, 8, 8, T))
    glue_blend(P, "bl3", jobs3)
    y3s = itn("y3s", [1024, T])
    P.bind = dict(qn=qx, kn=kx, v=vx, beta=bex, y=y3s)
    build_l3b(P=P)
    yag3 = _cc(P, "AllGather", y3s, [2048, T], "cc_y3", PAIRS)
    y3 = itn("y3r", [1024, T])
    glue_blend(P, "bl3y", [
        (yag3, 0, 512, y3, 0, 512, T),
        (yag3, 1024, 1536, y3, 512, 512, T),
    ])
    s3 = itn("s3", [1024, T])
    P.bind = dict(o_in=y3, xn=xn3, s=s3)
    build_l3c(P=P)
    lt0 = itn("lt0", [1024, 16])
    E.dma("sync", lt0.ap(), xn3[:, 1:17])
    ltag = _cc(P, "AllGather", lt0, [2048, 16], "cc_lt", PAIRS)
    ltL = itn("ltL", [1024, 16])
    E.dma("sync", ltL.ap(), ltag[0:1024, :])
    g3 = itn("g3", [1024, T])
    P.bind = dict(xn=xn3, lt=ltL, g=g3)
    build_l3d(P=P)
    out_final = nc.dram_tensor("out_final", [1024, T], F32,
                               kind="ExternalOutput")
    P.next_tag = "combine_3"
    P.bind = dict(x=x3g, s=s3, g=g3, xn_out=out_final)
    build_combine(P=P)
    P.bind = {}
    P.finish()
    return P

def host_fused(inputs):
    g = lambda n: np.ascontiguousarray(np.asarray(inputs[n], np.float32))
    x = g("L0_attn_x")
    x_fm = np.ascontiguousarray(x.reshape(NTOK, DM).T)
    P = build_fused()
    base = Prog.basic_inputs(eye=True)

    sh = {}
    sh["l0a_wt_in"] = np.ascontiguousarray(g("L0_attn_Win").T)
    sh["l0a_conv_w"] = g("L0_attn_conv_w")
    sh["l0a_conv_b"] = g("L0_attn_conv_b").reshape(-1, 1)
    sh["l0a_dt_bias"] = g("L0_attn_dt_bias").reshape(-1, 1)
    tmask = np.ones((16, 4096), np.float32)
    tmask[:, ::128] = 0.0
    sh["l0b_tmaskc"] = tmask
    sh["l0b_ipatc"] = np.tile(np.eye(128, dtype=np.float32), (1, 32))
    hsel = np.zeros((16, 16 * 128), np.float32)
    for h in range(16):
        hsel[h, h * 128:(h + 1) * 128] = 1.0
    sh["l0b_hselc"] = hsel
    sh["l0c1_norm_w"] = g("L0_attn_norm_w").reshape(-1, 1)
    sh["l0c1_wt_out"] = np.ascontiguousarray(g("L0_attn_Wout").T)
    ffn0 = {k[7:]: g(k) for k in inputs if k.startswith("L0_ffn_")}
    sc2 = np.zeros((128, 136), np.float32)
    for i, nm in enumerate(("w1_s", "w2_s", "h1_s", "h2_s")):
        sc2[:, i * 32:(i + 1) * 32] = ffn0[nm].reshape(32, 128).T
    sc2[:, 128:136] = ffn0["w3_s"].reshape(8, 128).T
    sh["l0c2_scales2"] = sc2
    sh["l0c2_scales"] = np.zeros((128, 9), np.float32)
    for nm in ("w1", "w2", "h1", "h2", "w3"):
        sh[f"l0c2_{nm}t"] = np.ascontiguousarray(ffn0[nm + "_w"].T)

    sh["l1a_wt_in"] = np.ascontiguousarray(g("L1_attn_in_proj_w").T)
    sh["l1a_conv_w"] = np.ascontiguousarray(g("L1_attn_conv_w")[:, 0, :])
    sh["l1a_wt_xproj"] = np.ascontiguousarray(g("L1_attn_x_proj_w").T)
    sh["l1b_hsel16"] = hsel
    sh["l1c_dcol"] = g("L1_attn_D_param").reshape(-1, 1)
    sh["l1c_wt_out"] = np.ascontiguousarray(g("L1_attn_out_w").T)
    sh["l1c_out_b"] = g("L1_attn_out_b").reshape(-1, 1)

    HID = 2730
    for li in range(2):
        wqkv = g("L1_ffn_wqkv")[li]
        sh[f"l1d1_{li}_wq"] = np.ascontiguousarray(wqkv[:, 0:512])
        sh[f"l1d1_{li}_wk"] = np.ascontiguousarray(wqkv[:, 512:1024])
        sh[f"l1d1_{li}_wv"] = np.ascontiguousarray(wqkv[:, 1024:1536])
        sh[f"l1d2_{li}_qs_col"] = np.tile(g("L1_ffn_q_scale")[li], 8)\
            .reshape(-1, 1).astype(np.float32)
        sh[f"l1d2_{li}_ks_col"] = np.tile(g("L1_ffn_k_scale")[li], 8)\
            .reshape(-1, 1).astype(np.float32)
        sh[f"l1d3_{li}_wo"] = np.ascontiguousarray(g("L1_ffn_wo")[li])
        w1 = g("L1_ffn_w1")[li]; b1 = g("L1_ffn_b1")[li]
        w2 = g("L1_ffn_w2")[li]; b2 = g("L1_ffn_b2")[li]
        w1p = np.zeros((1024, 5632), np.float32)
        w1p[:, 0:HID] = w1[:, 0:HID]
        w1p[:, 2816:2816 + HID] = w1[:, HID:2 * HID]
        b1p = np.zeros((128, 44), np.float32)
        b1pad = np.zeros(5632, np.float32)
        b1pad[0:HID] = b1[0:HID]
        b1pad[2816:2816 + HID] = b1[HID:]
        b1p[:, :] = b1pad.reshape(44, 128).T
        w2p = np.zeros((2816, 1024), np.float32)
        w2p[0:HID, :] = w2
        lngb = np.zeros((128, 16), np.float32)
        lngb[:, 0:8] = g("L1_ffn_ln_g")[li].reshape(8, 128).T
        lngb[:, 8:16] = g("L1_ffn_ln_b")[li].reshape(8, 128).T
        sh[f"l1e_{li}_w1p"] = w1p
        sh[f"l1e_{li}_b1p"] = b1p
        sh[f"l1e_{li}_w2p"] = w2p
        sh[f"l1e_{li}_b2c"] = b2.reshape(8, 128).T.astype(np.float32).copy()
        sh[f"l1e_{li}_lngb"] = lngb

    W_in = g("L2_attn_W_in")
    sh["l2a_w_in"] = np.concatenate(
        [W_in[:, 0:1024], W_in[:, 2048:3072], W_in[:, 1024:2048],
         W_in[:, 3072:3328]], axis=1).astype(np.float32).copy()
    sh["l2a_rms_w"] = g("L2_attn_rms_weight").reshape(-1, 1)
    sh["l2a_gate_b"] = g("L2_attn_gate_bias").reshape(-1, 1)
    cmask = np.ones((128, TCORE), np.float32)
    cmask[:, ::64] = 0.0
    sh["l2a_cmask"] = cmask
    sh["l2b_w_out"] = g("L2_attn_W_out")
    sh["l2c_w1t"] = np.ascontiguousarray(g("L2_ffn_w1").T)
    sh["l2c_w2t"] = np.ascontiguousarray(g("L2_ffn_w2").T)
    sh["l2c_b1c"] = g("L2_ffn_b1").reshape(32, 128).T.copy()
    sh["l2c_b2c"] = g("L2_ffn_b2").reshape(8, 128).T.copy()

    ind2 = np.zeros((2, 128), np.float32)
    ind2[0, 0:64] = 1.0
    ind2[1, 64:128] = 1.0
    sh["l3a_wq"] = g("L3_attn_Wq")
    sh["l3a_wk"] = g("L3_attn_Wk")
    sh["l3a_wv"] = g("L3_attn_Wv")
    sh["l3a_wb"] = g("L3_attn_Wbeta")
    sh["l3a_ind2"] = ind2
    sh["l3a_ind2T_in"] = np.ascontiguousarray(ind2.T)
    ssel = np.zeros((128, 64 * 128), np.float32)
    csel = np.zeros((128, 64 * 128), np.float32)
    for i in range(64):
        for p2 in range(2):
            ssel[p2 * 64 + i, i * 128 + p2 * 64:i * 128 + (p2 + 1) * 64] = 1.0
            csel[p2 * 64:(p2 + 1) * 64, i * 128 + p2 * 64 + i] = 1.0
    psel = np.zeros((8, 4 * 128), np.float32)
    for pg in range(4):
        psel[2 * pg, pg * 128:pg * 128 + 64] = 1.0
        psel[2 * pg + 1, pg * 128 + 64:(pg + 1) * 128] = 1.0
    sh["l3b_self_sel"] = ssel
    sh["l3b_colsel"] = csel
    sh["l3b_pairsel"] = psel
    lngb3 = np.zeros((128, 16), np.float32)
    lngb3[:, 0:8] = g("L3_attn_ln_g").reshape(8, 128).T
    lngb3[:, 8:16] = g("L3_attn_ln_b").reshape(8, 128).T
    sh["l3c_lngb"] = lngb3
    sh["l3c_wg"] = g("L3_attn_Wg")
    sh["l3c_wout"] = g("L3_attn_Wout")
    sh["l3d_w1t"] = np.ascontiguousarray(g("L3_ffn_W1").T)
    sh["l3d_w2t"] = np.ascontiguousarray(g("L3_ffn_W2").T)
    sh["l3d_b1c"] = g("L3_ffn_b1").reshape(32, 128).T.copy()
    sh["l3d_b2c"] = g("L3_ffn_b2").reshape(8, 128).T.copy()
    sh["l3d_scale11"] = np.asarray(g("L3_ffn_scale"), np.float32)\
        .reshape(1, 1)
    sel4 = np.zeros((4, 4 * 128), np.float32)
    for k in range(4):
        sel4[k, k * 128:(k + 1) * 128] = 1.0
    sh["l3d_sel4"] = sel4

    A_log = g("L0_attn_A_log"); D0 = g("L0_attn_D")
    dt_w = g("L1_attn_dt_w"); dt_b = g("L1_attn_dt_b")
    A1 = g("L1_attn_A_param")
    in_maps = []
    for c in range(NCORES):
        hg = c % 2
        lo = c * TCORE
        per = {}
        xh = np.zeros((DM, TCORE + 3), np.float32)
        xh[:, 3:] = x_fm[:, lo:lo + TCORE]
        if hg == 1:
            xh[:, 0:3] = x_fm[:, lo - 3:lo]
        per["l0a_x"] = xh
        hh = slice(hg * 16, (hg + 1) * 16)
        per["l0b_a_log"] = np.ascontiguousarray(A_log[hh].reshape(-1, 1))
        per["l0b_d_row"] = np.ascontiguousarray(D0[hh].reshape(1, -1))
        per["l0c2_x"] = np.ascontiguousarray(x_fm[:, lo:lo + TCORE])
        flag = np.full((128, 1), float(hg), np.float32)
        per["l1a_haloflag"] = flag
        dsl = slice(c * 256, (c + 1) * 256)
        per["l1b_dtwb"] = np.ascontiguousarray(
            np.stack([dt_w[dsl, 0], dt_b[dsl]]).astype(np.float32))
        per["l1b_a_sl"] = np.ascontiguousarray(A1[dsl])
        for li in range(2):
            per[f"l1d1_{li}_haloflag"] = flag
            per[f"l1e_{li}_haloflag"] = flag
            tb = _l1ffn_tables(c)
            for knm, v in tb.items():
                per[f"l1d2_{li}_{knm}"] = v
        NCh = 64
        maskc = np.zeros((128, 16 * NCh), np.float32)
        for sub in range(16):
            tglob = hg * 2048 + sub * 128 + np.arange(128)
            allowed = (tglob[:, None] // 64) > np.arange(NCh)[None, :]
            maskc[:, sub * NCh:(sub + 1) * NCh] = np.where(allowed, 0.0,
                                                           -1e30)
        per["l2b_maskc"] = maskc
        bf = np.zeros((128, 2), np.float32)
        bf[:, hg] = 1.0
        per["bflag"] = bf
        in_maps.append(sh | per | base)
    res = P.run(in_maps)
    x_out = np.concatenate([r["out_final"] for r in res.results], axis=1)
    return np.ascontiguousarray(x_out.T).reshape(4, 4096, DM)


FUSED = True


def kernel(**inputs):
    if FUSED:
        return host_fused(inputs)
    return kernel_multilaunch(**inputs)
